# version 01
# baseline (speedup 1.0000x reference)
import numpy as np
from contextlib import ExitStack

import concourse.bass as bass
import concourse.mybir as mybir
from concourse.bass_utils import run_bass_kernel_spmd

F32 = mybir.dt.float32
BF16 = mybir.dt.bfloat16
U8 = mybir.dt.uint8
AF = mybir.ActivationFunctionType
ALU = mybir.AluOpType
AX = mybir.AxisListType

NCORES = 8
D = 1024
NB = 16
T = 2048
H = 16
HD = 64
KT = D // 128
B2 = NB // NCORES
SS = T // NCORES
C_VELT = float((1 << 22) + 1)
LN_EPS = 1e-5


class Prog:
    CE = ("tensor", "vector", "scalar", "gpsimd", "sync")

    def __init__(self, nc):
        self.nc = nc
        self.recs = {e: [] for e in self.CE}
        self.nk = {}
        self.awaited = set()

    def _add(self, eng, fn, deps, kind):
        deps = tuple(d for d in deps if d is not None)
        for d in deps:
            self.awaited.add(d)
        key = (eng, kind)
        self.nk.setdefault(key, 0)
        self.nk[key] += 1
        tok = (eng, kind, self.nk[key])
        self.recs[eng].append((fn, deps, tok))
        return tok

    def op(self, eng, fn, deps=()):
        return self._add(eng, fn, deps, "c")

    def dma(self, eng, fn, deps=(), ch="w128"):
        tok = self._add(eng, fn, deps, "d:" + ch)
        self.awaited.add(tok)
        return tok

    def last(self, eng, kind="c"):
        n = self.nk.get((eng, kind), 0)
        return (eng, kind, n) if n else None

    def fence(self):
        toks = []
        for (e, kind), n in self.nk.items():
            if n:
                toks.append((e, kind, n))
        return toks

    def build(self, stack, block):
        nc = self.nc
        sems = {}
        for (e, kind) in self.nk:
            sems[(e, kind)] = stack.enter_context(nc.semaphore(f"sem_{e}_{kind.replace(':', '_')}"))
        for s in sems.values():
            nc.gpsimd.sem_clear(s)
        nc.all_engine_barrier()
        target = {}
        counters = {}
        for e in self.CE:
            for fn, deps, tok in self.recs[e]:
                if tok in self.awaited:
                    key = (e, tok[1])
                    counters.setdefault(key, 0)
                    counters[key] += 1
                    target[tok] = counters[key] * (16 if tok[1].startswith("d") else 1)

        def mk_emit(name):
            def emit(E):
                seen = {}
                for fn, deps, tok in self.recs[name]:
                    for d in deps:
                        key = (d[0], d[1])
                        tv = target[d]
                        if seen.get(key, -1) < tv:
                            E.wait_ge(sems[key], tv)
                            seen[key] = tv
                    inst = fn(E)
                    if tok in self.awaited:
                        inst.then_inc(sems[(name, tok[1])], 16 if tok[1].startswith("d") else 1)
            return emit

        for name in self.CE:
            getattr(block, name)(mk_emit(name))


def build_l0_main(ln_only=False):
    nc = bass.Bass("TRN2", target_bir_lowering=False)
    x0T_d = nc.dram_tensor("x0T", [D, B2 * T], F32, kind="ExternalInput").ap()
    wqkvT_d = nc.dram_tensor("wqkvT", [D, 3 * D], F32, kind="ExternalInput").ap()
    woutT_d = nc.dram_tensor("woutT", [D, D], F32, kind="ExternalInput").ap()
    bout_d = nc.dram_tensor("bout", [D], F32, kind="ExternalInput").ap()
    g16_d = nc.dram_tensor("g16", [1, H], F32, kind="ExternalInput").ap()
    b16_d = nc.dram_tensor("b16", [1, H], F32, kind="ExternalInput").ap()
    eye_d = nc.dram_tensor("eye", [128, 128], F32, kind="ExternalInput").ap()
    ones_d = nc.dram_tensor("ones", [128, 1], F32, kind="ExternalInput").ap()

    attn0T_d = nc.dram_tensor("attn0T", [D, B2 * T], F32, kind="ExternalOutput").ap()
    xn0T_d = nc.dram_tensor("xn0T", [D, B2 * T], BF16, kind="ExternalOutput").ap()
    sburst_d = nc.dram_tensor("sburst", [2, T], F32, kind="Internal").ap()
    ao_scr_d = nc.dram_tensor("ao_scr", [D, T], BF16, kind="Internal").ap()

    P = Prog(nc)
    with ExitStack() as stack:
        def sb(name, shape, dt):
            return stack.enter_context(nc.sbuf_tensor(name, shape, dt))

        def ps(name, shape, dt=F32):
            return stack.enter_context(nc.psum_tensor(name, shape, dt))

        CH = 512
        xbc = [sb(f"xbc{k}", [128, CH], BF16) for k in range(KT)]
        xnc = [sb(f"xnc{k}", [128, CH], BF16) for k in range(KT)]
        ktc = [sb(f"ktc{k}", [128, CH], BF16) for k in range(KT)]
        vtc = [sb(f"vtc{k}", [128, CH], BF16) for k in range(KT)]
        q_sb = [sb(f"q{k}", [128, T], BF16) for k in range(KT)]
        wq = [sb(f"wq{k}", [128, 3 * D], BF16) for k in range(KT)]
        wo = [sb(f"wo{k}", [128, D], BF16) for k in range(KT)]
        kvrow = sb("kvrow", [128, CH], BF16)
        gkv_all = sb("gkv_all", [128, KT * 128], F32)
        gkvb = sb("gkvb", [128, KT * 128], BF16)
        aostage = sb("aostage", [128, T], BF16)
        aor = [sb(f"aor{k}", [128, CH], BF16) for k in range(KT)]
        aout = sb("aout", [128, T], F32)
        eyeb = sb("eyeb", [128, 128], BF16)
        ones_b = sb("ones_b", [128, 1], BF16)
        onesrow_b = sb("onesrow_b", [1, 128], BF16)
        mb_row = sb("mb_row", [1, CH], BF16)
        rb_row = sb("rb_row", [1, CH], BF16)
        bout_s = sb("bout_s", [128, KT], F32)
        brep = sb("brep", [128, KT], F32)
        g16_s = sb("g16_s", [1, H], F32)
        m_row = sb("m_row", [1, CH], F32)
        v_row = sb("v_row", [1, CH], F32)
        vt_row = sb("vt_row", [1, CH], F32)
        nsq = sb("nsq", [1, T], F32)
        srow = sb("srow", [1, T], F32)
        srep = sb("srep", [128, T], BF16)
        t1 = sb("t1", [128, CH], F32)
        t1bs = [sb(f"t1b{k}", [128, CH], BF16) for k in range(KT)]
        zq = sb("zq", [64, CH], BF16)
        ub = sb("ub", [128, KT], F32)
        ubt = sb("ubt", [128, KT], F32)
        ubb = sb("ubb", [128, KT], BF16)
        btot = sb("btot", [128, KT], F32)

        PQ = [ps(f"PQ{i}", [128, 512]) for i in range(2)]
        PT = [ps(f"PT{i}", [128, 64], BF16) for i in range(2)]
        GK = ps("GK", [64, 128])
        ZO = ps("ZO", [128, 512])
        NS = ps("NS", [1, 512])

        warm = sb("warm", [1, 32], F32)
        w0 = P.op("vector", lambda E: E.memset(warm[:], 1.0))
        w1 = P.op("scalar", lambda E: E.activation(out=warm[:], in_=warm[:], func=AF.Square), deps=[w0])
        w2 = P.op("scalar", lambda E: E.activation(out=warm[:], in_=warm[:], func=AF.Ln), deps=[w1])
        w3 = P.op("scalar", lambda E: E.activation(out=warm[:], in_=warm[:], func=AF.Exp), deps=[w2])
        warm_tok = [w3]

        ldw_q, ldw_o = [], []
        for k in range(KT):
            ldw_q.append(P.dma("gpsimd", lambda E, k=k: E.dma_start(out=wq[k][:], in_=wqkvT_d[k * 128:(k + 1) * 128, :]), ch="cast"))
            ldw_o.append(P.dma("gpsimd", lambda E, k=k: E.dma_start(out=wo[k][:], in_=woutT_d[k * 128:(k + 1) * 128, :]), ch="cast"))
        ld_eyeb = P.dma("gpsimd", lambda E: E.dma_start(out=eyeb[:], in_=eye_d[:]), ch="cast")
        ld_ob = P.dma("gpsimd", lambda E: E.dma_start(out=ones_b[:], in_=ones_d[:]), ch="cast")
        ld_or = P.dma("gpsimd", lambda E: E.dma_start(
            out=onesrow_b[:], in_=ones_d[:, 0].unsqueeze(0).broadcast_to([1, 128])), ch="castp0")
        ld_g = P.dma("sync", lambda E: E.dma_start(out=g16_s[:], in_=g16_d[:]), ch="p0")
        ld_bout = [P.dma("sync", lambda E, o=o: E.dma_start(
            out=bout_s[:, o:o + 1], in_=bout_d[o * 128:(o + 1) * 128].unsqueeze(1))) for o in range(KT)]
        ld_brep = [P.dma("sync", lambda E, h=h: E.dma_start(
            out=brep[(h % 2) * 64:(h % 2) * 64 + 64, h // 2:h // 2 + 1],
            in_=b16_d[0:1, h:h + 1].broadcast_to([64, 1])), ch=("n0" if h % 2 == 0 else "n64")) for h in range(H)]

        out_toks = []
        batch_fence = []
        for bi in range(B2):
            z1 = P.op("vector", lambda E: E.memset(gkv_all[:], 0.0), deps=batch_fence + warm_tok)
            z2 = P.op("vector", lambda E: E.memset(ub[:], 0.0), deps=batch_fence)
            acc_g = [z1]
            acc_u = [z2]
            chunk_fence = list(batch_fence)
            for c in range(4):
                tc = slice(bi * T + c * CH, bi * T + (c + 1) * CH)
                ld_x = [P.dma("gpsimd", lambda E, k=k, tc=tc: E.dma_start(
                    out=xbc[k][:], in_=x0T_d[k * 128:(k + 1) * 128, tc]),
                    deps=chunk_fence) for k in range(KT)]
                mm = None
                for k in range(KT):
                    mm = P.op("tensor", lambda E, k=k: E.matmul(
                        PQ[0][0:1, :], ones_b[:], xbc[k][:], start=(k == 0), stop=(k == KT - 1)),
                        deps=(list(ld_x) + [ld_ob] + warm_tok + chunk_fence) if k == 0 else [])
                em0 = P.op("scalar", lambda E: E.copy(out=m_row[:], in_=PQ[0][0:1, :]), deps=[mm])
                em = P.op("vector", lambda E: E.tensor_scalar_mul(out=m_row[:], in0=m_row[:], scalar1=1.0 / D), deps=[em0])
                sq_toks = [P.op("scalar", lambda E, k=k: E.activation(out=t1bs[k][:], in_=xbc[k][:], func=AF.Square),
                                deps=[ld_x[k]] + chunk_fence) for k in range(KT)]
                mmq = None
                for k in range(KT):
                    mmq = P.op("tensor", lambda E, k=k: E.matmul(
                        PQ[1][0:1, :], ones_b[:], t1bs[k][:], start=(k == 0), stop=(k == KT - 1)),
                        deps=([sq_toks[-1], ld_ob] + chunk_fence) if k == 0 else [])
                ev0 = P.op("scalar", lambda E: E.copy(out=v_row[:], in_=PQ[1][0:1, :]), deps=[mmq])
                ev = P.op("vector", lambda E: E.tensor_scalar_mul(out=v_row[:], in0=v_row[:], scalar1=1.0 / D), deps=[ev0])
                tv1 = P.op("vector", lambda E: E.tensor_mul(out=vt_row[:], in0=m_row[:], in1=m_row[:]), deps=[em])
                tv2 = P.op("vector", lambda E: E.tensor_sub(out=v_row[:], in0=v_row[:], in1=vt_row[:]), deps=[tv1, ev])
                tv3 = P.op("vector", lambda E: E.tensor_scalar_add(out=v_row[:], in0=v_row[:], scalar1=LN_EPS), deps=[tv2])
                tr1 = P.op("scalar", lambda E: E.activation(out=v_row[:], in_=v_row[:], func=AF.Ln), deps=[tv3])
                tr1b = P.op("vector", lambda E: E.tensor_scalar_mul(out=v_row[:], in0=v_row[:], scalar1=-0.5), deps=[tr1])
                tr2 = P.op("scalar", lambda E: E.activation(out=v_row[:], in_=v_row[:], func=AF.Exp), deps=[tr1b])
                cmb = P.op("vector", lambda E: E.tensor_copy(out=mb_row[:], in_=m_row[:]), deps=[em])
                crb = P.op("vector", lambda E: E.tensor_copy(out=rb_row[:], in_=v_row[:]), deps=[tr2])
                t_m = P.op("tensor", lambda E: E.matmul(PQ[0][:], onesrow_b[:], mb_row[:], start=True, stop=True),
                           deps=[cmb, ld_or])
                t_r = P.op("tensor", lambda E: E.matmul(PQ[1][:], onesrow_b[:], rb_row[:], start=True, stop=True),
                           deps=[crb, ld_or])
                ap_toks = []
                for k in range(KT):
                    a1 = P.op("vector", lambda E, k=k: E.tensor_tensor(out=t1[:], in0=xbc[k][:], in1=PQ[0][:], op=ALU.subtract),
                              deps=[t_m, ld_x[k]] + ([ap_toks[-1]] if ap_toks else [mmq]))
                    a2 = P.op("vector", lambda E, k=k: E.tensor_tensor(out=xnc[k][:], in0=t1[:], in1=PQ[1][:], op=ALU.mult),
                              deps=[a1, t_r])
                    ap_toks.append(a2)
                st_xn = [P.dma("gpsimd", lambda E, k=k, tc=tc: E.dma_start(
                    out=xn0T_d[k * 128:(k + 1) * 128, tc], in_=xnc[k][:]), deps=[ap_toks[k]]) for k in range(KT)]
                out_toks += st_xn
                if ln_only:
                    chunk_fence = P.fence()
                    continue
                pq_free = [[ap_toks[-1]], [ap_toks[-1]]]
                ev_q = [None] * KT
                ev_k = [None] * KT
                ev_v = [None] * KT
                for ot in range(3 * KT):
                    pqi = ot % 2
                    mm = None
                    for k in range(KT):
                        mm = P.op("tensor", lambda E, ot=ot, k=k, pqi=pqi: E.matmul(
                            PQ[pqi][:], wq[k][:, ot * 128:(ot + 1) * 128], xnc[k][:],
                            start=(k == 0), stop=(k == KT - 1)),
                            deps=([ap_toks[-1], ldw_q[-1]] + pq_free[pqi]) if k == 0 else [])
                    if ot < KT:
                        evt = P.op("scalar", lambda E, ot=ot, pqi=pqi, c=c: E.copy(
                            out=q_sb[ot][:, c * CH:(c + 1) * CH], in_=PQ[pqi][:]), deps=[mm] + batch_fence)
                    elif ot < 2 * KT:
                        evt = P.op("scalar", lambda E, ot=ot, pqi=pqi: E.copy(
                            out=ktc[ot - KT][:], in_=PQ[pqi][:]), deps=[mm] + chunk_fence)
                        ev_k[ot - KT] = evt
                    else:
                        evt = P.op("scalar", lambda E, ot=ot, pqi=pqi: E.copy(
                            out=vtc[ot - 2 * KT][:], in_=PQ[pqi][:]), deps=[mm] + chunk_fence)
                        ev_v[ot - 2 * KT] = evt
                    pq_free[pqi] = [evt]
                    if ot < KT:
                        ev_q[ot] = evt
                pt_free = [list(chunk_fence), list(chunk_fence)]
                kvrow_free = list(chunk_fence)
                gk_free = []
                ub_toks = []
                for h in range(H):
                    p64 = (h % 2) * 64
                    hp = h // 2
                    kcol = ktc[hp][p64:p64 + 64, :]
                    vcol = vtc[hp][p64:p64 + 64, :]
                    idn = eyeb[p64:p64 + 64, p64:p64 + 64]
                    tr_toks = []
                    for jb in range(4):
                        for which, col in ((0, kcol), (1, vcol)):
                            pti = (jb * 2 + which) % 2
                            tt = P.op("tensor", lambda E, col=col, jb=jb, pti=pti, idn=idn: E.transpose(
                                out=PT[pti][:], in_=col[:, jb * 128:(jb + 1) * 128], identity=idn),
                                deps=[ev_k[hp], ev_v[hp], ld_eyeb] + pt_free[pti])
                            te = P.op("scalar", lambda E, jb=jb, which=which, pti=pti: E.copy(
                                out=kvrow[:, jb * 128 + which * 64: jb * 128 + which * 64 + 64], in_=PT[pti][:]),
                                deps=[tt] + kvrow_free)
                            pt_free[pti] = [te]
                            tr_toks.append(te)
                    kvrow_free = []
                    mmgk = None
                    for jb in range(4):
                        mmgk = P.op("tensor", lambda E, jb=jb: E.matmul(
                            GK[:], kvrow[:, jb * 128: jb * 128 + 64], kvrow[:, jb * 128: jb * 128 + 128],
                            start=(jb == 0), stop=(jb == 3)),
                            deps=([tr_toks[-1]] + gk_free) if jb == 0 else [])
                    kvrow_free = [mmgk]
                    ga = P.op("vector", lambda E, h=h, p64=p64, hp=hp: E.tensor_tensor(
                        out=gkv_all[p64:p64 + 64, hp * 128:(hp + 1) * 128],
                        in0=GK[:], in1=gkv_all[p64:p64 + 64, hp * 128:(hp + 1) * 128], op=ALU.add),
                        deps=[mmgk] + acc_g)
                    gk_free = [ga]
                    ut = P.op("vector", lambda E, h=h, p64=p64, vcol=vcol: E.tensor_reduce(
                        out=ubt[p64:p64 + 64, h // 2:h // 2 + 1], in_=vcol[:], axis=AX.X, op=ALU.add),
                        deps=[ev_v[hp]] + acc_u)
                    ub_toks.append(ut)
                acc_g = [gk_free[0] if gk_free else None] + [None]
                acc_g = [ga]
                ua = P.op("vector", lambda E: E.tensor_add(out=ub[:], in0=ub[:], in1=ubt[:]), deps=ub_toks + acc_u)
                acc_u = [ua]
                chunk_fence = P.fence()

            if ln_only:
                batch_fence = P.fence()
                continue
            cg = P.op("vector", lambda E: E.tensor_copy(out=gkvb[:], in_=gkv_all[:]), deps=acc_g + chunk_fence)
            zo_free, ns_free, zq_free = [cg], [], []
            sburst_free = {0: [], 1: []}
            srep_free = list(batch_fence)
            ao_dma = {0: [], 1: []}
            for h in range(H):
                p64 = (h % 2) * 64
                hp = h // 2
                qcol = q_sb[hp][p64:p64 + 64, :]
                oh_toks = []
                nsq_toks = []
                for c in range(4):
                    ch = slice(c * CH, (c + 1) * CH)
                    mm = P.op("tensor", lambda E, ch=ch, p64=p64, hp=hp, qcol=qcol: E.matmul(
                        ZO[:], gkvb[p64:p64 + 64, hp * 128:(hp + 1) * 128], qcol[:, ch], start=True, stop=True),
                        deps=[cg, ev_q[hp]] + zo_free)
                    tzq = P.op("vector", lambda E, ch=ch, qcol=qcol: E.tensor_tensor(
                        out=zq[:], in0=ZO[0:64, :], in1=qcol[:, ch], op=ALU.mult), deps=[mm] + zq_free)
                    toh = P.op("scalar", lambda E, ch=ch, p64=p64: E.copy(
                        out=aostage[p64:p64 + 64, ch], in_=ZO[64:128, :]), deps=[mm] + ao_dma[h % 2])
                    tns = P.op("tensor", lambda E: E.matmul(
                        NS[:], ones_b[0:64, :], zq[:], start=True, stop=True), deps=[tzq] + ns_free)
                    ten = P.op("scalar", lambda E, ch=ch: E.copy(out=nsq[:, ch], in_=NS[:]), deps=[tns])
                    ns_free = [ten]
                    zq_free = [tns]
                    zo_free = [tzq, toh]
                    oh_toks.append(toh)
                    nsq_toks.append(ten)
                s1 = P.op("scalar", lambda E: E.activation(out=srow[:], in_=nsq[:], func=AF.Ln),
                          deps=nsq_toks + sburst_free[h % 2] + sburst_free[(h + 1) % 2])
                s1b = P.op("vector", lambda E: E.tensor_scalar_mul(out=srow[:], in0=srow[:], scalar1=0.5), deps=[s1])
                s2 = P.op("scalar", lambda E: E.activation(out=srow[:], in_=srow[:], func=AF.Exp), deps=[s1b])
                s3 = P.op("vector", lambda E: E.tensor_scalar_max(out=srow[:], in0=srow[:], scalar1=1e-12), deps=[s2])
                s4 = P.op("vector", lambda E: E.reciprocal(out=srow[:], in_=srow[:]), deps=[s3])
                s5 = P.op("vector", lambda E, h=h: E.tensor_scalar_mul(
                    out=srow[:], in0=srow[:], scalar1=g16_s[0:1, h:h + 1]), deps=[s4, ld_g])
                s6 = P.dma("gpsimd", lambda E, h=h: E.dma_start(out=sburst_d[h % 2:h % 2 + 1, :], in_=srow[0:1, :]),
                           deps=[s5] + sburst_free[h % 2], ch="p0")
                s7 = P.dma("gpsimd", lambda E, h=h: E.dma_start(
                    out=srep[:], in_=sburst_d[h % 2:h % 2 + 1, :].broadcast_to([128, T])),
                    deps=[s6] + srep_free, ch="cast")
                sburst_free[h % 2] = [s7]
                sc_toks = []
                for c in range(4):
                    ch = slice(c * CH, (c + 1) * CH)
                    tok = P.op("vector", lambda E, ch=ch, p64=p64: E.tensor_tensor(
                        out=aostage[p64:p64 + 64, ch], in0=aostage[p64:p64 + 64, ch],
                        in1=srep[p64:p64 + 64, ch], op=ALU.mult), deps=[s7, oh_toks[c]])
                    sc_toks.append(tok)
                srep_free = sc_toks
                da = P.dma("gpsimd", lambda E, h=h, p64=p64: E.dma_start(
                    out=ao_scr_d[h * 64:(h + 1) * 64, :], in_=aostage[p64:p64 + 64, :]), deps=sc_toks,
                    ch=("n0" if p64 == 0 else "n64"))
                ao_dma[h % 2] = [da]
            ao_done = [ao_dma[0][0], ao_dma[1][0]]

            ubm = P.op("vector", lambda E: E.tensor_tensor(out=ub[:], in0=ub[:], in1=brep[:], op=ALU.mult),
                       deps=acc_u + ld_brep)
            ubc = P.op("vector", lambda E: E.tensor_copy(out=ubb[:], in_=ub[:]), deps=[ubm] + batch_fence)
            bt_toks = []
            pq0_free = chunk_fence
            for ot in range(KT):
                mm = None
                for k in range(KT):
                    mm = P.op("tensor", lambda E, ot=ot, k=k: E.matmul(
                        PQ[0][:, 0:1], wo[k][:, ot * 128:(ot + 1) * 128], ubb[:, k:k + 1],
                        start=(k == 0), stop=(k == KT - 1)),
                        deps=([ubc, ldw_o[-1]] + (pq0_free if ot == 0 else [bt_toks[-1]])) if k == 0 else [])
                evb = P.op("scalar", lambda E, ot=ot: E.copy(out=btot[:, ot:ot + 1], in_=PQ[0][:, 0:1]), deps=[mm])
                bt_toks.append(evb)
            btt = P.op("vector", lambda E: E.tensor_add(out=btot[:], in0=btot[:], in1=bout_s[:]),
                       deps=bt_toks + ld_bout)

            pq_free = [[bt_toks[-1]], chunk_fence]
            aor_free = [list(chunk_fence) for _ in range(KT)]
            aout_free = list(batch_fence)
            for c in range(4):
                ch = slice(c * CH, (c + 1) * CH)
                ld_ao = [P.dma("sync", lambda E, k=k, ch=ch: E.dma_start(
                    out=aor[k][:], in_=ao_scr_d[k * 128:(k + 1) * 128, ch]), deps=ao_done + aor_free[k])
                    for k in range(KT)]
                for ot in range(KT):
                    pqi = ot % 2
                    mm = None
                    for k in range(KT):
                        mm = P.op("tensor", lambda E, ot=ot, k=k, pqi=pqi: E.matmul(
                            PQ[pqi][:], wo[k][:, ot * 128:(ot + 1) * 128], aor[k][:],
                            start=(k == 0), stop=(k == KT - 1)),
                            deps=(list(ld_ao) + [ldw_o[-1]] + pq_free[pqi]) if k == 0 else [])
                    evo = P.op("vector", lambda E, ch=ch, pqi=pqi, ot=ot: E.tensor_scalar_add(
                        out=aout[:, ch], in0=PQ[pqi][:], scalar1=btot[:, ot:ot + 1]),
                        deps=[mm, btt] + aout_free)
                    pq_free[pqi] = [evo]
                    std = P.dma("gpsimd", lambda E, ot=ot, ch=ch, bi=bi, c=c: E.dma_start(
                        out=attn0T_d[ot * 128:(ot + 1) * 128, bi * T + c * CH: bi * T + (c + 1) * CH],
                        in_=aout[:, ch]), deps=[evo])
                    aout_free = [std]
                    out_toks.append(std)
                aor_free = [[P.last("tensor")] for _ in range(KT)]
            batch_fence = P.fence()

        P.op("gpsimd", lambda E: E.engine_nop(), deps=out_toks)
        block = stack.enter_context(nc.Block())
        P.build(stack, block)
    return nc


def build_l0_scan(nsteps=NB, out_c=False):
    nc = bass.Bass("TRN2", target_bir_lowering=False)
    TS_ = NB * SS
    x0Ts_d = nc.dram_tensor("x0Ts", [D, TS_], F32, kind="ExternalInput").ap()
    wpackT_d = nc.dram_tensor("wpackT", [D, 7 * D], F32, kind="ExternalInput").ap()
    bias3_d = nc.dram_tensor("bias3", [3, D], F32, kind="ExternalInput").ap()
    inc_d = nc.dram_tensor("incv", [D], F32, kind="ExternalInput").ap()
    ones_d = nc.dram_tensor("ones", [128, 1], F32, kind="ExternalInput").ap()
    zT_d = nc.dram_tensor("zT", [D, SS], F32, kind="ExternalOutput").ap()
    xns_scr_d = nc.dram_tensor("xns_scr", [D, TS_], BF16, kind="Internal").ap()

    P = Prog(nc)
    with ExitStack() as stack:
        def sb(name, shape, dt):
            return stack.enter_context(nc.sbuf_tensor(name, shape, dt))

        def ps(name, shape, dt=F32):
            return stack.enter_context(nc.psum_tensor(name, shape, dt))

        CH = 512
        xbc = [sb(f"xbc{k}", [128, CH], BF16) for k in range(KT)]
        xnc = [sb(f"xnc{k}", [128, CH], BF16) for k in range(KT)]
        t1bs = [sb(f"t1b{k}", [128, CH], BF16) for k in range(KT)]
        wp = [sb(f"wp{k}", [128, 7 * D], BF16) for k in range(KT)]
        ct_ = [sb(f"ct{k}", [128, SS], F32) for k in range(KT)]
        xstep = [[sb(f"xs{j}_{k}", [128, SS], BF16) for k in range(KT)] for j in range(2)]
        fts = [sb(f"fts{k}", [128, SS], F32) for k in range(KT)]
        its = [sb(f"its{k}", [128, SS], F32) for k in range(KT)]
        gts = [sb(f"gts{k}", [128, SS], F32) for k in range(KT)]
        tmpa = sb("tmpa", [128, SS], F32)
        tmpb = sb("tmpb", [128, SS], F32)
        tmpc = sb("tmpc", [128, 512], F32)
        ones_b = sb("ones_b", [128, 1], BF16)
        onesrow_b = sb("onesrow_b", [1, 128], BF16)
        b3 = sb("b3", [128, 3 * KT], F32)
        m_row = sb("m_row", [1, CH], F32)
        v_row = sb("v_row", [1, CH], F32)
        vt_row = sb("vt_row", [1, CH], F32)
        mb_row = sb("mb_row", [1, CH], BF16)
        rb_row = sb("rb_row", [1, CH], BF16)
        cb = [sb(f"cb{k}", [128, SS], BF16) for k in range(KT)]
        inc_sb = sb("inc_sb", [128, KT], F32)
        zstage = sb("zstage", [128, SS], F32)

        PQ = [ps(f"PQ{i}", [128, 512]) for i in range(2)]
        PG = [ps(f"PG{i}", [128, 512]) for i in range(3)]

        warm = sb("warm", [1, 32], F32)
        w0 = P.op("vector", lambda E: E.memset(warm[:], 1.0))
        w1 = P.op("scalar", lambda E: E.activation(out=warm[:], in_=warm[:], func=AF.Square), deps=[w0])
        w2 = P.op("scalar", lambda E: E.activation(out=warm[:], in_=warm[:], func=AF.Ln), deps=[w1])
        w3 = P.op("scalar", lambda E: E.activation(out=warm[:], in_=warm[:], func=AF.Exp), deps=[w2])
        warm_tok = [w3]

        ldw = [P.dma("gpsimd", lambda E, k=k: E.dma_start(out=wp[k][:], in_=wpackT_d[k * 128:(k + 1) * 128, :]), ch="cast")
               for k in range(KT)]
        ld_ob = P.dma("gpsimd", lambda E: E.dma_start(out=ones_b[:], in_=ones_d[:]), ch="cast")
        ld_or = P.dma("gpsimd", lambda E: E.dma_start(
            out=onesrow_b[:], in_=ones_d[:, 0].unsqueeze(0).broadcast_to([1, 128])), ch="castp0")
        ld_b3 = [P.dma("sync", lambda E, g=g, o=o: E.dma_start(
            out=b3[:, g * KT + o:g * KT + o + 1], in_=bias3_d[g, o * 128:(o + 1) * 128].unsqueeze(1)))
            for g in range(3) for o in range(KT)]
        ld_inc = [P.dma("sync", lambda E, k=k: E.dma_start(
            out=inc_sb[:, k:k + 1], in_=inc_d[k * 128:(k + 1) * 128].unsqueeze(1))) for k in range(KT)]
        ld_c0 = [P.op("vector", lambda E, k=k: E.tensor_copy(
            out=ct_[k][:], in_=inc_sb[:, k:k + 1].broadcast_to([128, SS])), deps=[ld_inc[k]])
            for k in range(KT)]

        chunk_fence = []
        st_xn = []
        for c in range(8):
            tc = slice(c * CH, (c + 1) * CH)
            ld_x = [P.dma("gpsimd", lambda E, k=k, tc=tc: E.dma_start(
                out=xbc[k][:], in_=x0Ts_d[k * 128:(k + 1) * 128, tc]), deps=chunk_fence, ch="cast") for k in range(KT)]
            mm = None
            for k in range(KT):
                mm = P.op("tensor", lambda E, k=k: E.matmul(
                    PQ[0][0:1, :], ones_b[:], xbc[k][:], start=(k == 0), stop=(k == KT - 1)),
                    deps=(list(ld_x) + [ld_ob] + warm_tok + chunk_fence) if k == 0 else [])
            em0 = P.op("scalar", lambda E: E.copy(out=m_row[:], in_=PQ[0][0:1, :]), deps=[mm])
            em = P.op("vector", lambda E: E.tensor_scalar_mul(out=m_row[:], in0=m_row[:], scalar1=1.0 / D), deps=[em0])
            sq_toks = [P.op("scalar", lambda E, k=k: E.activation(out=t1bs[k][:], in_=xbc[k][:], func=AF.Square),
                           deps=[ld_x[k]] + chunk_fence) for k in range(KT)]
            mmq = None
            for k in range(KT):
                mmq = P.op("tensor", lambda E, k=k: E.matmul(
                    PQ[1][0:1, :], ones_b[:], t1bs[k][:], start=(k == 0), stop=(k == KT - 1)),
                    deps=([sq_toks[-1], ld_ob] + chunk_fence) if k == 0 else [])
            ev0 = P.op("scalar", lambda E: E.copy(out=v_row[:], in_=PQ[1][0:1, :]), deps=[mmq])
            ev = P.op("vector", lambda E: E.tensor_scalar_mul(out=v_row[:], in0=v_row[:], scalar1=1.0 / D), deps=[ev0])
            tv1 = P.op("vector", lambda E: E.tensor_mul(out=vt_row[:], in0=m_row[:], in1=m_row[:]), deps=[em])
            tv2 = P.op("vector", lambda E: E.tensor_sub(out=v_row[:], in0=v_row[:], in1=vt_row[:]), deps=[tv1, ev])
            tv3 = P.op("vector", lambda E: E.tensor_scalar_add(out=v_row[:], in0=v_row[:], scalar1=LN_EPS), deps=[tv2])
            tr1 = P.op("scalar", lambda E: E.activation(out=v_row[:], in_=v_row[:], func=AF.Ln), deps=[tv3])
            tr1b = P.op("vector", lambda E: E.tensor_scalar_mul(out=v_row[:], in0=v_row[:], scalar1=-0.5), deps=[tr1])
            tr2 = P.op("scalar", lambda E: E.activation(out=v_row[:], in_=v_row[:], func=AF.Exp), deps=[tr1b])
            cmb = P.op("vector", lambda E: E.tensor_copy(out=mb_row[:], in_=m_row[:]), deps=[em])
            crb = P.op("vector", lambda E: E.tensor_copy(out=rb_row[:], in_=v_row[:]), deps=[tr2])
            t_m = P.op("tensor", lambda E: E.matmul(PQ[0][:], onesrow_b[:], mb_row[:], start=True, stop=True),
                       deps=[cmb, ld_or, em0])
            t_r = P.op("tensor", lambda E: E.matmul(PQ[1][:], onesrow_b[:], rb_row[:], start=True, stop=True),
                       deps=[crb, ld_or, ev0])
            ap_toks = []
            for k in range(KT):
                a1 = P.op("vector", lambda E, k=k: E.tensor_tensor(out=tmpc[:], in0=xbc[k][:], in1=PQ[0][:],
                                                                  op=ALU.subtract),
                          deps=[t_m] + ([ap_toks[-1]] if ap_toks else [mmq]))
                a2 = P.op("vector", lambda E, k=k: E.tensor_tensor(out=xnc[k][:], in0=tmpc[:], in1=PQ[1][:],
                                                                  op=ALU.mult),
                          deps=[a1, t_r])
                ap_toks.append(a2)
            st_c = [P.dma("gpsimd", lambda E, k=k, tc=tc: E.dma_start(
                out=xns_scr_d[k * 128:(k + 1) * 128, tc], in_=xnc[k][:]), deps=[ap_toks[k]]) for k in range(KT)]
            st_xn += st_c
            chunk_fence = P.fence()

        GATES = 3
        upd_last = list(ld_c0) + chunk_fence
        cb_toks = []
        xld_prev = []
        ev_toks = []
        mm_toks = []
        for t_i in range(nsteps):
            jbuf = 0
            xld = [P.dma("gpsimd", lambda E, k=k, t_i=t_i, jbuf=jbuf: E.dma_start(
                out=xstep[jbuf][k][:], in_=xns_scr_d[k * 128:(k + 1) * 128, t_i * SS:(t_i + 1) * SS]),
                deps=(st_xn if t_i < 2 else []) + xld_prev + mm_toks[-1:]) for k in range(KT)]
            cb_toks = [P.op("vector", lambda E, k=k: E.tensor_copy(out=cb[k][:], in_=ct_[k][:]),
                            deps=upd_last + cb_toks[-1:]) for k in range(KT)]
            ev_base = len(ev_toks)
            for ot in range(KT):
                for g in range(GATES):
                    mm = None
                    for k in range(2 * KT):
                        if k < KT:
                            lhs = (lambda E, k=k, g=g, ot=ot, jbuf=jbuf: E.matmul(
                                PG[g][:, 0:SS], wp[k][:, g * D + ot * 128: g * D + (ot + 1) * 128],
                                xstep[jbuf][k][:], start=(k == 0), stop=False))
                        else:
                            kk = k - KT
                            lhs = (lambda E, kk=kk, g=g, ot=ot: E.matmul(
                                PG[g][:, 0:SS], wp[kk][:, (3 + g) * D + ot * 128: (3 + g) * D + (ot + 1) * 128],
                                cb[kk][:], start=False, stop=(kk == KT - 1)))
                        mm = P.op("tensor", lhs,
                                  deps=(list(xld) + [cb_toks[-1], ldw[-1]] + ev_toks[-3:]) if k == 0 else [])
                    mm_toks.append(mm)
                    func = AF.Sigmoid if g < 2 else AF.Tanh
                    tok = P.op("scalar", lambda E, g=g, ot=ot, func=func: E.activation(
                        out=(fts if g == 0 else its if g == 1 else gts)[ot][:], in_=PG[g][:, 0:SS],
                        func=func, bias=b3[:, g * KT + ot:g * KT + ot + 1]),
                        deps=[mm] + upd_last)
                    ev_toks.append(tok)
            upd = []
            for ot in range(KT):
                u1 = P.op("vector", lambda E, ot=ot: E.tensor_tensor(out=tmpa[:], in0=fts[ot][:], in1=ct_[ot][:],
                                                                    op=ALU.mult),
                          deps=[ev_toks[ev_base + ot * 3], upd[-1] if upd else None])
                u2 = P.op("vector", lambda E, ot=ot: E.tensor_tensor(out=tmpb[:], in0=its[ot][:], in1=gts[ot][:],
                                                                    op=ALU.mult),
                          deps=[ev_toks[ev_base + ot * 3 + 1], ev_toks[ev_base + ot * 3 + 2]])
                u3 = P.op("vector", lambda E, ot=ot: E.tensor_tensor(out=ct_[ot][:], in0=tmpa[:], in1=tmpb[:],
                                                                    op=ALU.add),
                          deps=[u1, u2])
                upd.append(u3)
            upd_last = [upd[-1]]
            xld_prev = xld

        cbf = [P.op("vector", lambda E, k=k: E.tensor_copy(out=cb[k][:], in_=ct_[k][:]), deps=upd_last + cb_toks[-1:])
               for k in range(KT)]
        if out_c == "f":
            stc = [P.dma("gpsimd", lambda E, k=k: E.dma_start(out=zT_d[k * 128:(k + 1) * 128, :], in_=fts[k][:]),
                         deps=upd_last + cb_toks[-1:]) for k in range(KT)]
            P.op("gpsimd", lambda E: E.engine_nop(), deps=stc)
            block = stack.enter_context(nc.Block())
            P.build(stack, block)
            return nc
        if out_c == "x":
            stc = [P.dma("gpsimd", lambda E, k=k: E.dma_start(out=zT_d[k * 128:(k + 1) * 128, :], in_=xstep[0][k][:]),
                         deps=upd_last + cb_toks[-1:]) for k in range(KT)]
            P.op("gpsimd", lambda E: E.engine_nop(), deps=stc)
            block = stack.enter_context(nc.Block())
            P.build(stack, block)
            return nc
        if out_c:
            stc = [P.dma("gpsimd", lambda E, k=k: E.dma_start(out=zT_d[k * 128:(k + 1) * 128, :], in_=ct_[k][:]),
                         deps=upd_last + cb_toks[-1:]) for k in range(KT)]
            P.op("gpsimd", lambda E: E.engine_nop(), deps=stc)
            block = stack.enter_context(nc.Block())
            P.build(stack, block)
            return nc
        z_toks = []
        pqi_free = [[], []]
        for ot in range(KT):
            mm = None
            for k in range(KT):
                mm = P.op("tensor", lambda E, ot=ot, k=k: E.matmul(
                    PG[0][:, 0:SS], wp[k][:, 6 * D + ot * 128: 6 * D + (ot + 1) * 128], cb[k][:],
                    start=(k == 0), stop=(k == KT - 1)),
                    deps=([cbf[-1], ldw[-1]] + z_toks[-1:] + ev_toks[-3:]) if k == 0 else [])
            evz = P.op("scalar", lambda E, ot=ot: E.copy(out=zstage[:, 0:SS], in_=PG[0][:, 0:SS]), deps=[mm])
            stz = P.dma("gpsimd", lambda E, ot=ot: E.dma_start(
                out=zT_d[ot * 128:(ot + 1) * 128, :], in_=zstage[:, 0:SS]), deps=[evz])
            z_toks.append(stz)
        P.op("gpsimd", lambda E: E.engine_nop(), deps=z_toks)
        block = stack.enter_context(nc.Block())
        P.build(stack, block)
    return nc


def _prep_l0_scan_inputs(inputs):
    x = np.asarray(inputs["L0_attn_x"])
    wpackT = np.ascontiguousarray(np.concatenate(
        [np.asarray(inputs[f"L0_ffn_{n}"]).T for n in ("wf", "wi", "wc", "uf", "ui", "uc", "uo")], axis=1))
    bias3 = np.ascontiguousarray(np.stack([np.asarray(inputs["L0_ffn_bf"]),
                                           np.asarray(inputs["L0_ffn_bi"]),
                                           np.asarray(inputs["L0_ffn_bc"])]))
    incv = np.ascontiguousarray(np.asarray(inputs["L0_ffn_in_c"]))
    ones = np.ones((128, 1), dtype=np.float32)
    in_maps = []
    for c in range(NCORES):
        sl = slice(c * SS, (c + 1) * SS)
        x0Ts = np.ascontiguousarray(np.concatenate([x[b, sl, :].T for b in range(NB)], axis=1))
        in_maps.append({"x0Ts": x0Ts, "wpackT": wpackT, "bias3": bias3, "incv": incv, "ones": ones})
    return in_maps


def run_l0_scan(inputs, nsteps=NB, out_c=False):
    nc = build_l0_scan(nsteps=nsteps, out_c=out_c)
    in_maps = _prep_l0_scan_inputs(inputs)
    res = run_bass_kernel_spmd(nc, in_maps, core_ids=list(range(NCORES)))
    return res.results



def build_x1_ln1():
    nc = bass.Bass("TRN2", target_bir_lowering=False)
    TT_ = B2 * T
    x0T_d = nc.dram_tensor("x0T", [D, TT_], F32, kind="ExternalInput").ap()
    attn0T_d = nc.dram_tensor("attn0T", [D, TT_], F32, kind="ExternalInput").ap()
    xn0T_d = nc.dram_tensor("xn0T", [D, TT_], BF16, kind="ExternalInput").ap()
    zT_d = nc.dram_tensor("zTfull", [D, T], F32, kind="ExternalInput").ap()
    woT_d = nc.dram_tensor("woT", [D, D], F32, kind="ExternalInput").ap()
    bo_d = nc.dram_tensor("bo", [D], F32, kind="ExternalInput").ap()
    ones_d = nc.dram_tensor("ones", [128, 1], F32, kind="ExternalInput").ap()
    x1T_d = nc.dram_tensor("x1T", [D, TT_], F32, kind="ExternalOutput").ap()
    xn1T_d = nc.dram_tensor("xn1T", [D, TT_], F32, kind="ExternalOutput").ap()

    P = Prog(nc)
    with ExitStack() as stack:
        def sb(name, shape, dt):
            return stack.enter_context(nc.sbuf_tensor(name, shape, dt))

        def ps(name, shape, dt=F32):
            return stack.enter_context(nc.psum_tensor(name, shape, dt))

        CH = 512
        wo = [sb(f"wo{k}", [128, D], BF16) for k in range(KT)]
        zs = [sb(f"zs{k}", [128, T], F32) for k in range(KT)]
        xn0c = [sb(f"xn0c{k}", [128, CH], BF16) for k in range(KT)]
        x0c = [sb(f"x0c{k}", [128, CH], F32) for k in range(KT)]
        a0c = [sb(f"a0c{k}", [128, CH], F32) for k in range(KT)]
        x1c = [sb(f"x1c{k}", [128, CH], F32) for k in range(KT)]
        xbc = [sb(f"xbc{k}", [128, CH], BF16) for k in range(KT)]
        t1bs = [sb(f"t1b{k}", [128, CH], BF16) for k in range(KT)]
        xnc = [sb(f"xnc{k}", [128, CH], F32) for k in range(KT)]
        tmp = sb("tmp", [128, CH], F32)
        ht = sb("ht", [128, CH], F32)
        bo_s = sb("bo_s", [128, KT], F32)
        ones_b = sb("ones_b", [128, 1], BF16)
        onesrow_b = sb("onesrow_b", [1, 128], BF16)
        m_row = sb("m_row", [1, CH], F32)
        v_row = sb("v_row", [1, CH], F32)
        vt_row = sb("vt_row", [1, CH], F32)
        mb_row = sb("mb_row", [1, CH], BF16)
        rb_row = sb("rb_row", [1, CH], BF16)
        PH = [ps(f"PH{i}", [128, 512]) for i in range(2)]
        PQ = [ps(f"PQ{i}", [128, 512]) for i in range(2)]

        warm = sb("warm", [1, 32], F32)
        w0 = P.op("vector", lambda E: E.memset(warm[:], 1.0))
        w1 = P.op("scalar", lambda E: E.activation(out=warm[:], in_=warm[:], func=AF.Square), deps=[w0])
        w2 = P.op("scalar", lambda E: E.activation(out=warm[:], in_=warm[:], func=AF.Ln), deps=[w1])
        w3 = P.op("scalar", lambda E: E.activation(out=warm[:], in_=warm[:], func=AF.Tanh), deps=[w2])
        w4 = P.op("scalar", lambda E: E.activation(out=warm[:], in_=warm[:], func=AF.Exp), deps=[w3])
        warm_tok = [w4]

        ldw = [P.dma("gpsimd", lambda E, k=k: E.dma_start(out=wo[k][:], in_=woT_d[k * 128:(k + 1) * 128, :]), ch="cast")
               for k in range(KT)]
        ldz = [P.dma("sync", lambda E, k=k: E.dma_start(out=zs[k][:], in_=zT_d[k * 128:(k + 1) * 128, :]))
               for k in range(KT)]
        ld_ob = P.dma("gpsimd", lambda E: E.dma_start(out=ones_b[:], in_=ones_d[:]), ch="cast")
        ld_or = P.dma("gpsimd", lambda E: E.dma_start(
            out=onesrow_b[:], in_=ones_d[:, 0].unsqueeze(0).broadcast_to([1, 128])), ch="castp0")
        ld_bo = [P.dma("sync", lambda E, o=o: E.dma_start(
            out=bo_s[:, o:o + 1], in_=bo_d[o * 128:(o + 1) * 128].unsqueeze(1))) for o in range(KT)]

        chunk_fence = [ldw[-1], ldz[-1], ld_ob, ld_or] + ld_bo + warm_tok
        out_toks = []
        for c in range(8):
            bi = c // 4
            tc = slice(c * CH, (c + 1) * CH)
            zc = slice((c % 4) * CH, (c % 4 + 1) * CH)
            ld_xn0 = [P.dma("gpsimd", lambda E, k=k, tc=tc: E.dma_start(
                out=xn0c[k][:], in_=xn0T_d[k * 128:(k + 1) * 128, tc]), deps=chunk_fence) for k in range(KT)]
            ld_x0 = [P.dma("sync", lambda E, k=k, tc=tc: E.dma_start(
                out=x0c[k][:], in_=x0T_d[k * 128:(k + 1) * 128, tc]), deps=chunk_fence) for k in range(KT)]
            ld_a0 = [P.dma("sync", lambda E, k=k, tc=tc: E.dma_start(
                out=a0c[k][:], in_=attn0T_d[k * 128:(k + 1) * 128, tc]), deps=chunk_fence) for k in range(KT)]
            x1_toks = []
            phi_free = [list(chunk_fence), list(chunk_fence)]
            for ot in range(KT):
                phi = ot % 2
                mm = None
                for k in range(KT):
                    mm = P.op("tensor", lambda E, ot=ot, k=k, phi=phi: E.matmul(
                        PH[phi][:], wo[k][:, ot * 128:(ot + 1) * 128], xn0c[k][:],
                        start=(k == 0), stop=(k == KT - 1)),
                        deps=(list(ld_xn0) + [ldw[-1]] + warm_tok + phi_free[phi]) if k == 0 else [])
                tz = P.op("vector", lambda E, ot=ot, zc=zc, phi=phi: E.tensor_tensor(
                    out=tmp[:], in0=PH[phi][:], in1=zs[ot][:, zc], op=ALU.add),
                    deps=[mm, ldz[-1], (x1_toks[-1][0] if x1_toks else None),
                          (phi_free[0][0] if phi_free[0] else None),
                          (phi_free[1][0] if phi_free[1] else None)] + chunk_fence)
                th = P.op("scalar", lambda E, ot=ot: E.activation(
                    out=ht[:], in_=tmp[:], func=AF.Tanh, bias=bo_s[:, ot:ot + 1]),
                    deps=[tz, (x1_toks[-1][0] if x1_toks else None)] + ld_bo + chunk_fence)
                phi_free[phi] = [th]
                a1 = P.op("vector", lambda E, ot=ot: E.tensor_tensor(
                    out=x1c[ot][:], in0=x0c[ot][:], in1=a0c[ot][:], op=ALU.add),
                    deps=[ld_x0[ot], ld_a0[ot]] + chunk_fence)
                a2 = P.op("vector", lambda E, ot=ot: E.tensor_tensor(
                    out=x1c[ot][:], in0=x1c[ot][:], in1=ht[:], op=ALU.add), deps=[a1, th])
                st1 = P.dma("gpsimd", lambda E, ot=ot, tc=tc: E.dma_start(
                    out=x1T_d[ot * 128:(ot + 1) * 128, tc], in_=x1c[ot][:]), deps=[a2])
                cb = P.op("vector", lambda E, ot=ot: E.tensor_copy(out=xbc[ot][:], in_=x1c[ot][:]),
                          deps=[a2] + chunk_fence)
                x1_toks.append((a2, cb, st1))
                out_toks.append(st1)
            cps = [t[1] for t in x1_toks]
            mm = None
            for k in range(KT):
                mm = P.op("tensor", lambda E, k=k: E.matmul(
                    PQ[0][0:1, :], ones_b[:], xbc[k][:], start=(k == 0), stop=(k == KT - 1)),
                    deps=([cps[-1], ld_ob] + chunk_fence) if k == 0 else [])
            em0 = P.op("scalar", lambda E: E.copy(out=m_row[:], in_=PQ[0][0:1, :]), deps=[mm])
            em = P.op("vector", lambda E: E.tensor_scalar_mul(out=m_row[:], in0=m_row[:], scalar1=1.0 / D), deps=[em0])
            sq_toks = [P.op("scalar", lambda E, k=k: E.activation(out=t1bs[k][:], in_=xbc[k][:], func=AF.Square),
                           deps=[cps[k]] + chunk_fence) for k in range(KT)]
            mmq = None
            for k in range(KT):
                mmq = P.op("tensor", lambda E, k=k: E.matmul(
                    PQ[1][0:1, :], ones_b[:], t1bs[k][:], start=(k == 0), stop=(k == KT - 1)),
                    deps=([sq_toks[-1], ld_ob] + chunk_fence) if k == 0 else [])
            ev0 = P.op("scalar", lambda E: E.copy(out=v_row[:], in_=PQ[1][0:1, :]), deps=[mmq])
            ev = P.op("vector", lambda E: E.tensor_scalar_mul(out=v_row[:], in0=v_row[:], scalar1=1.0 / D), deps=[ev0])
            tv1 = P.op("vector", lambda E: E.tensor_mul(out=vt_row[:], in0=m_row[:], in1=m_row[:]), deps=[em])
            tv2 = P.op("vector", lambda E: E.tensor_sub(out=v_row[:], in0=v_row[:], in1=vt_row[:]), deps=[tv1, ev])
            tv3 = P.op("vector", lambda E: E.tensor_scalar_add(out=v_row[:], in0=v_row[:], scalar1=LN_EPS), deps=[tv2])
            tr1 = P.op("scalar", lambda E: E.activation(out=v_row[:], in_=v_row[:], func=AF.Ln), deps=[tv3])
            tr1b = P.op("vector", lambda E: E.tensor_scalar_mul(out=v_row[:], in0=v_row[:], scalar1=-0.5), deps=[tr1])
            tr2 = P.op("scalar", lambda E: E.activation(out=v_row[:], in_=v_row[:], func=AF.Exp), deps=[tr1b])
            cmb = P.op("vector", lambda E: E.tensor_copy(out=mb_row[:], in_=m_row[:]), deps=[em])
            crb = P.op("vector", lambda E: E.tensor_copy(out=rb_row[:], in_=v_row[:]), deps=[tr2])
            t_m = P.op("tensor", lambda E: E.matmul(PQ[0][:], onesrow_b[:], mb_row[:], start=True, stop=True),
                       deps=[cmb, ld_or, em0])
            t_r = P.op("tensor", lambda E: E.matmul(PQ[1][:], onesrow_b[:], rb_row[:], start=True, stop=True),
                       deps=[crb, ld_or, ev0])
            ap_toks = []
            for k in range(KT):
                a1 = P.op("vector", lambda E, k=k: E.tensor_tensor(out=tmp[:], in0=x1c[k][:], in1=PQ[0][:],
                                                                  op=ALU.subtract),
                          deps=[t_m, x1_toks[k][0]] + ([ap_toks[-1]] if ap_toks else [mmq]) + [phi_free[0][0], phi_free[1][0]])
                a2 = P.op("vector", lambda E, k=k: E.tensor_tensor(out=xnc[k][:], in0=tmp[:], in1=PQ[1][:],
                                                                  op=ALU.mult),
                          deps=[a1, t_r])
                ap_toks.append(a2)
            st_xn = [P.dma("gpsimd", lambda E, k=k, tc=tc: E.dma_start(
                out=xn1T_d[k * 128:(k + 1) * 128, tc], in_=xnc[k][:]), deps=[ap_toks[k]]) for k in range(KT)]
            out_toks += st_xn
            chunk_fence = P.fence()

        P.op("gpsimd", lambda E: E.engine_nop(), deps=out_toks)
        block = stack.enter_context(nc.Block())
        P.build(stack, block)
    return nc


def run_x1_ln1(inputs, attn0T_list, xn0T_list, zT_full):
    nc = build_x1_ln1()
    x = np.asarray(inputs["L0_attn_x"])
    woT = np.ascontiguousarray(np.asarray(inputs["L0_ffn_wo"]).T)
    bo = np.ascontiguousarray(np.asarray(inputs["L0_ffn_bo"]))
    ones = np.ones((128, 1), dtype=np.float32)
    in_maps = []
    for c in range(NCORES):
        x0T = np.ascontiguousarray(np.concatenate([x[2 * c].T, x[2 * c + 1].T], axis=1))
        in_maps.append({"x0T": x0T, "attn0T": attn0T_list[c], "xn0T": xn0T_list[c],
                        "zTfull": zT_full, "woT": woT, "bo": bo, "ones": ones})
    res = run_bass_kernel_spmd(nc, in_maps, core_ids=list(range(NCORES)))
    return res.results



def build_l1_attn(dbg=False, nch=None):
    nc = bass.Bass("TRN2", target_bir_lowering=False)
    TT_ = B2 * T
    NSt = 16
    CH = 512
    NCH = (T // CH) if nch is None else nch
    xn1T_d = nc.dram_tensor("xn1T", [D, TT_], F32, kind="ExternalInput").ap()
    xprojT_d = nc.dram_tensor("xprojT", [D, 96], F32, kind="ExternalInput").ap()
    dtprojT_d = nc.dram_tensor("dtprojT", [64, D], F32, kind="ExternalInput").ap()
    dtb_d = nc.dram_tensor("dtb", [D], F32, kind="ExternalInput").ap()
    alog_d = nc.dram_tensor("alog", [D, NSt], F32, kind="ExternalInput").ap()
    dvec_d = nc.dram_tensor("dvec", [D], F32, kind="ExternalInput").ap()
    y1T_d = nc.dram_tensor("y1T", [D, TT_], F32, kind="ExternalOutput").ap()
    delta_scr_d = nc.dram_tensor("delta_scr", [D, T], F32, kind=("ExternalOutput" if dbg else "Internal")).ap()
    xp_dbg_d = nc.dram_tensor("xp_dbg", [96, T], F32, kind="ExternalOutput").ap() if dbg else None
    br_dbg_d = nc.dram_tensor("br_dbg", [128, 16 * 512], BF16, kind="ExternalOutput").ap() if dbg else None
    da_dbg_d = nc.dram_tensor("da_dbg", [128, 16 * 512], F32, kind="ExternalOutput").ap() if dbg else None
    h_dbg_d = nc.dram_tensor("h_dbg", [128, 16 * 512], F32, kind="ExternalOutput").ap() if dbg else None
    bu_dbg_d = nc.dram_tensor("bu_dbg", [128, 16 * 512], F32, kind="ExternalOutput").ap() if dbg else None
    hc_dbg_d = nc.dram_tensor("hc_dbg", [128, 16], F32, kind="ExternalOutput").ap() if dbg else None
    zs_dbg_d = nc.dram_tensor("zs_dbg", [128, 16 * 512], BF16, kind="ExternalOutput").ap() if dbg else None
    yc_dbg_d = nc.dram_tensor("yc_dbg", [128, 512], F32, kind="ExternalOutput").ap() if dbg else None
    cr_dbg_d = nc.dram_tensor("cr_dbg", [128, 16 * 512], BF16, kind="ExternalOutput").ap() if dbg else None
    bc_scr_d = nc.dram_tensor("bc_scr", [2, NSt * CH], F32, kind="Internal").ap()

    P = Prog(nc)
    with ExitStack() as stack:
        def sb(name, shape, dt):
            return stack.enter_context(nc.sbuf_tensor(name, shape, dt))

        def ps(name, shape, dt=F32):
            return stack.enter_context(nc.psum_tensor(name, shape, dt))

        xbc = [sb(f"xbc{k}", [128, CH], BF16) for k in range(KT)]
        xprj = sb("xprj", [128, 96], BF16)
        xprjs = [sb(f"xprj{k}", [128, 96], BF16) for k in range(KT)]
        dtp = sb("dtp", [64, D], BF16)
        dtb_s = sb("dtb_s", [128, KT], F32)
        xp_sb = sb("xp_sb", [96, T], F32)
        drb = sb("drb", [64, T], BF16)
        A_s = [sb(f"A{k}", [128, NSt], F32) for k in range(KT)]
        D_s = sb("D_s", [128, KT], F32)
        dc2 = [sb(f"dc{i}", [128, CH], F32) for i in range(2)]
        xc2 = [sb(f"xc{i}", [128, CH], F32) for i in range(2)]
        uc2 = [sb(f"uc{i}", [128, CH], F32) for i in range(2)]
        dA = sb("dA", [128, NSt * CH], F32)
        dBu = sb("dBu", [128, NSt * CH], F32)
        hbuf = sb("hbuf", [128, NSt * CH], F32)
        zb2 = [sb(f"zb{i}", [128, NSt * CH], BF16) for i in range(2)]
        Brep = sb("Brep", [128, NSt * CH], BF16)
        Crep = sb("Crep", [128, NSt * CH], BF16)
        hcarry = [sb(f"hc{k}", [128, NSt], F32) for k in range(KT)]
        y1c2 = [sb(f"y1c{i}", [128, CH], F32) for i in range(2)]
        sp1 = sb("sp1", [128, CH], F32)
        sp2 = sb("sp2", [128, CH], F32)
        PXP = ps("PXP", [96, 512])
        PDT = ps("PDT", [128, 512])

        warm = sb("warm", [1, 32], F32)
        w0 = P.op("vector", lambda E: E.memset(warm[:], 1.0))
        w1 = P.op("scalar", lambda E: E.activation(out=warm[:], in_=warm[:], func=AF.Abs), deps=[w0])
        w2 = P.op("scalar", lambda E: E.activation(out=warm[:], in_=warm[:], func=AF.Ln), deps=[w1])
        w3 = P.op("scalar", lambda E: E.activation(out=warm[:], in_=warm[:], func=AF.Exp), deps=[w2])
        warm_tok = [w3]

        ld_xp = [P.dma("gpsimd", lambda E, k=k: E.dma_start(
            out=xprjs[k][:], in_=xprojT_d[k * 128:(k + 1) * 128, :]), ch="cast") for k in range(KT)]
        ld_dtp = P.dma("gpsimd", lambda E: E.dma_start(out=dtp[:], in_=dtprojT_d[:]), ch="castn0")
        ld_dtb = [P.dma("sync", lambda E, o=o: E.dma_start(
            out=dtb_s[:, o:o + 1], in_=dtb_d[o * 128:(o + 1) * 128].unsqueeze(1))) for o in range(KT)]
        ld_al = [P.dma("sync", lambda E, k=k: E.dma_start(
            out=A_s[k][:], in_=alog_d[k * 128:(k + 1) * 128, :])) for k in range(KT)]
        ld_dv = [P.dma("sync", lambda E, o=o: E.dma_start(
            out=D_s[:, o:o + 1], in_=dvec_d[o * 128:(o + 1) * 128].unsqueeze(1))) for o in range(KT)]
        load_fence = ld_xp + [ld_dtp] + ld_dtb + ld_al + ld_dv + warm_tok
        a_toks = []
        for k in range(KT):
            t1 = P.op("scalar", lambda E, k=k: E.activation(out=A_s[k][:], in_=A_s[k][:], func=AF.Exp),
                      deps=load_fence)
            t2 = P.op("vector", lambda E, k=k: E.tensor_scalar_mul(out=A_s[k][:], in0=A_s[k][:], scalar1=-1.0),
                      deps=[t1])
            a_toks.append(t2)

        out_toks = []
        batch_fence = list(load_fence) + a_toks
        for bi in range(B2):
            xp_toks = []
            pxp_free = list(batch_fence)
            for c in range(NCH):
                tc = slice(bi * T + c * CH, bi * T + (c + 1) * CH)
                ld_x = [P.dma("gpsimd", lambda E, k=k, tc=tc: E.dma_start(
                    out=xbc[k][:], in_=xn1T_d[k * 128:(k + 1) * 128, tc]),
                    deps=batch_fence + xp_toks[-1:], ch="cast") for k in range(KT)]
                mm = None
                for k in range(KT):
                    mm = P.op("tensor", lambda E, k=k: E.matmul(
                        PXP[:], xprjs[k][:], xbc[k][:], start=(k == 0), stop=(k == KT - 1)),
                        deps=(list(ld_x) + pxp_free) if k == 0 else [])
                evt = P.op("scalar", lambda E, c=c: E.copy(out=xp_sb[:, c * CH:(c + 1) * CH], in_=PXP[:]),
                           deps=[mm] + batch_fence)
                pxp_free = [evt]
                xp_toks.append(evt)
            if dbg and bi == 0:
                dx = P.dma("gpsimd", lambda E: E.dma_start(out=xp_dbg_d[:], in_=xp_sb[:]), deps=xp_toks, ch="n0")
                out_toks.append(dx)
            cdr = P.op("vector", lambda E: E.tensor_copy(out=drb[:], in_=xp_sb[0:64, :]), deps=xp_toks + batch_fence)
            del_toks = []
            pdt_free = list(batch_fence)
            for ot in range(KT):
                for c in range(NCH):
                    mm = P.op("tensor", lambda E, ot=ot, c=c: E.matmul(
                        PDT[:], dtp[:, ot * 128:(ot + 1) * 128], drb[:, c * CH:(c + 1) * CH],
                        start=True, stop=True), deps=([cdr] + pdt_free))
                    s0 = P.op("vector", lambda E, ot=ot: E.tensor_scalar_add(
                        out=sp1[:], in0=PDT[:], scalar1=dtb_s[:, ot:ot + 1]), deps=[mm] + del_toks[-1:])
                    pdt_free = [s0]
                    sa = P.op("scalar", lambda E: E.activation(out=sp2[:], in_=sp1[:], func=AF.Abs), deps=[s0] + del_toks[-1:])
                    sn = P.op("vector", lambda E: E.tensor_scalar_mul(out=sp2[:], in0=sp2[:], scalar1=-1.0), deps=[sa])
                    se = P.op("scalar", lambda E: E.activation(out=sp2[:], in_=sp2[:], func=AF.Exp), deps=[sn])
                    s1 = P.op("vector", lambda E: E.tensor_scalar_add(out=sp2[:], in0=sp2[:], scalar1=1.0), deps=[se])
                    sl = P.op("scalar", lambda E: E.activation(out=sp2[:], in_=sp2[:], func=AF.Ln), deps=[s1])
                    sr = P.op("vector", lambda E: E.tensor_scalar_max(out=sp1[:], in0=sp1[:], scalar1=0.0), deps=[sa])
                    sd = P.op("vector", lambda E: E.tensor_add(out=sp1[:], in0=sp1[:], in1=sp2[:]), deps=[sl, sr])
                    std = P.dma("gpsimd", lambda E, ot=ot, c=c: E.dma_start(
                        out=delta_scr_d[ot * 128:(ot + 1) * 128, c * CH:(c + 1) * CH], in_=sp1[:]), deps=[sd])
                    del_toks.append(std)
            rep_toks = []
            scan_fence = list(del_toks)
            prev_dma = {0: [], 1: []}
            y_toks = []
            for c in range(NCH):
                prel = P.fence()
                b1 = P.dma("gpsimd", lambda E, c=c: E.dma_start(
                    out=bc_scr_d[0, :].rearrange("(n t) -> n t", t=CH), in_=xp_sb[64:80, c * CH:(c + 1) * CH]),
                    deps=xp_toks + prel, ch="n64")
                b2 = P.dma("gpsimd", lambda E, c=c: E.dma_start(
                    out=bc_scr_d[1, :].rearrange("(n t) -> n t", t=CH), in_=xp_sb[80:96, c * CH:(c + 1) * CH]),
                    deps=xp_toks + prel, ch="n64")
                b3 = P.dma("gpsimd", lambda E: E.dma_start(
                    out=Brep[:], in_=bc_scr_d[0:1, :].broadcast_to([128, NSt * CH])), deps=[b1, b2] + prel,
                    ch="bcast")
                b4 = P.dma("gpsimd", lambda E: E.dma_start(
                    out=Crep[:], in_=bc_scr_d[1:2, :].broadcast_to([128, NSt * CH])), deps=[b1, b2] + prel,
                    ch="bcast")
                rep_done = [b3, b4]
                dt_toks = []
                for dt in range(KT):
                    pb = (c * KT + dt) % 2
                    dc, xc, uc = dc2[pb], xc2[pb], uc2[pb]
                    tc = slice(bi * T + c * CH, bi * T + (c + 1) * CH)
                    l1 = P.dma("sync", lambda E, dt=dt, c=c, dc=dc: E.dma_start(
                        out=dc[:], in_=delta_scr_d[dt * 128:(dt + 1) * 128, c * CH:(c + 1) * CH]),
                        deps=scan_fence + list(dt_toks))
                    l2 = P.dma("sync", lambda E, dt=dt, tc=tc, xc=xc: E.dma_start(
                        out=xc[:], in_=xn1T_d[dt * 128:(dt + 1) * 128, tc]), deps=scan_fence + list(dt_toks))
                    tu = P.op("vector", lambda E, dc=dc, xc=xc, uc=uc: E.tensor_mul(out=uc[:], in0=dc[:], in1=xc[:]),
                              deps=[l1, l2] + list(dt_toks))
                    targ = P.op("vector", lambda E, dt=dt, dc=dc: E.tensor_tensor(
                        out=dA[:].rearrange("p (n t) -> p n t", n=NSt),
                        in0=dc[:].unsqueeze(1).broadcast_to([128, NSt, CH]),
                        in1=A_s[dt][:].unsqueeze(2).broadcast_to([128, NSt, CH]), op=ALU.mult),
                        deps=[l1] + list(dt_toks) + a_toks)
                    texp = P.op("scalar", lambda E: E.activation(out=dA[:], in_=dA[:], func=AF.Exp), deps=[targ])
                    tbu = P.op("vector", lambda E, uc=uc: E.tensor_tensor(
                        out=dBu[:].rearrange("p (n t) -> p n t", n=NSt),
                        in0=uc[:].unsqueeze(1).broadcast_to([128, NSt, CH]),
                        in1=Brep[:].rearrange("p (n t) -> p n t", n=NSt), op=ALU.mult),
                        deps=[tu] + rep_done + list(dt_toks))
                    sc_toks = []
                    fold_toks = []
                    if c > 0:
                        for n in range(NSt):
                            fold_toks.append(P.op("vector", lambda E, n=n, dt=dt: E.scalar_tensor_tensor(
                                out=dBu[:, n * CH:n * CH + 1], in0=dA[:, n * CH:n * CH + 1],
                                scalar=hcarry[dt][:, n:n + 1], in1=dBu[:, n * CH:n * CH + 1],
                                op0=ALU.mult, op1=ALU.add), deps=[texp, tbu]))
                    for n in range(NSt):
                        sc_toks.append(P.op("vector", lambda E, n=n: E.tensor_tensor_scan(
                            out=hbuf[:, n * CH:(n + 1) * CH], data0=dA[:, n * CH:(n + 1) * CH],
                            data1=dBu[:, n * CH:(n + 1) * CH], initial=0.0, op0=ALU.mult, op1=ALU.add),
                            deps=[texp, tbu] + fold_toks[-1:] + (dt_toks[-1:] if c == 0 else [])))
                    hc = P.op("vector", lambda E, dt=dt: E.tensor_copy(
                        out=hcarry[dt][:].unsqueeze(2),
                        in_=hbuf[:].rearrange("p (n t) -> p n t", n=NSt)[:, :, CH - 1:CH]), deps=sc_toks)
                    ybi0 = (c * KT + dt) % 2
                    zb = zb2[ybi0]
                    tz = P.op("vector", lambda E, zb=zb: E.tensor_tensor(out=zb[:], in0=hbuf[:], in1=Crep[:], op=ALU.mult),
                              deps=sc_toks + rep_done)
                    f1 = P.op("vector", lambda E, zb=zb: E.tensor_tensor(
                        out=zb[:, 0:8 * CH], in0=zb[:, 0:8 * CH], in1=zb[:, 8 * CH:16 * CH], op=ALU.add), deps=[tz])
                    f2 = P.op("vector", lambda E, zb=zb: E.tensor_tensor(
                        out=zb[:, 0:4 * CH], in0=zb[:, 0:4 * CH], in1=zb[:, 4 * CH:8 * CH], op=ALU.add), deps=[f1])
                    f3 = P.op("vector", lambda E, zb=zb: E.tensor_tensor(
                        out=zb[:, 0:2 * CH], in0=zb[:, 0:2 * CH], in1=zb[:, 2 * CH:4 * CH], op=ALU.add), deps=[f2])
                    f4 = P.op("vector", lambda E, zb=zb: E.tensor_tensor(
                        out=zb[:, 0:CH], in0=zb[:, 0:CH], in1=zb[:, CH:2 * CH], op=ALU.add), deps=[f3])
                    ybi = (c * KT + dt) % 2
                    ty = P.op("vector", lambda E, dt=dt, ybi=ybi, zb=zb, xc=xc: E.scalar_tensor_tensor(
                        out=y1c2[ybi][:], in0=xc[:], scalar=D_s[:, dt:dt + 1], in1=zb[:, 0:CH],
                        op0=ALU.mult, op1=ALU.add), deps=[f4, prev_dma[ybi][0] if prev_dma[ybi] else None])
                    sty = P.dma("gpsimd", lambda E, dt=dt, tc=tc, ybi=ybi: E.dma_start(
                        out=y1T_d[dt * 128:(dt + 1) * 128, tc], in_=y1c2[ybi][:]), deps=[ty])
                    if dbg and bi == 0 and c == 0 and dt == KT - 1:
                        dyc = P.dma("sync", lambda E: E.dma_start(out=yc_dbg_d[:], in_=y1c[:]), deps=[ty])
                        out_toks.append(dyc)
                    prev_dma[ybi] = [sty]
                    out_toks.append(sty)
                    dt_toks = [ty, texp, tbu, hc]
                rep_toks = dt_toks
            batch_fence = P.fence()

        if dbg:
            d1 = P.dma("gpsimd", lambda E: E.dma_start(out=br_dbg_d[:], in_=Brep[:]), deps=P.fence())
            d2 = P.dma("gpsimd", lambda E: E.dma_start(out=da_dbg_d[:], in_=dA[:]), deps=P.fence())
            d3 = P.dma("gpsimd", lambda E: E.dma_start(out=h_dbg_d[:], in_=hbuf[:]), deps=P.fence())
            d4 = P.dma("gpsimd", lambda E: E.dma_start(out=bu_dbg_d[:], in_=dBu[:]), deps=P.fence())
            d5 = P.dma("gpsimd", lambda E: E.dma_start(out=hc_dbg_d[:], in_=hcarry[KT - 1][:]), deps=P.fence())
            d6 = P.dma("gpsimd", lambda E: E.dma_start(out=zs_dbg_d[:], in_=zb[:]), deps=P.fence())
            d7 = P.dma("gpsimd", lambda E: E.dma_start(out=cr_dbg_d[:], in_=Crep[:]), deps=P.fence())
            out_toks += [d4, d5, d6, d7]
            out_toks += [d1, d2, d3]
        P.op("gpsimd", lambda E: E.engine_nop(), deps=out_toks)
        block = stack.enter_context(nc.Block())
        P.build(stack, block)
    return nc


def run_l1_attn(inputs, xn1T_list, dbg=False, nch=None):
    nc = build_l1_attn(dbg=dbg, nch=nch)
    xprojT = np.ascontiguousarray(np.asarray(inputs["L1_attn_x_proj_w"]).T)
    dtprojT = np.ascontiguousarray(np.asarray(inputs["L1_attn_dt_proj_w"]).T)
    dtb = np.ascontiguousarray(np.asarray(inputs["L1_attn_dt_proj_b"]))
    alog = np.ascontiguousarray(np.asarray(inputs["L1_attn_A_log"]))
    dvec = np.ascontiguousarray(np.asarray(inputs["L1_attn_D"]))
    in_maps = [{"xn1T": xn1T_list[c], "xprojT": xprojT, "dtprojT": dtprojT,
                "dtb": dtb, "alog": alog, "dvec": dvec} for c in range(NCORES)]
    res = run_bass_kernel_spmd(nc, in_maps, core_ids=list(range(NCORES)))
    return res.results



def build_l1_ffn():
    nc = bass.Bass("TRN2", target_bir_lowering=False)
    TT_ = B2 * T
    IM = 2816
    OT_I = IM // 128
    xn1R_d = nc.dram_tensor("xn1R", [TT_, D], F32, kind="ExternalInput").ap()
    wg_d = nc.dram_tensor("wg", [IM, D], F32, kind="ExternalInput").ap()
    wu_d = nc.dram_tensor("wu", [IM, D], F32, kind="ExternalInput").ap()
    wd_d = nc.dram_tensor("wd", [D, IM], F32, kind="ExternalInput").ap()
    x1T_d = nc.dram_tensor("x1T", [D, TT_], F32, kind="ExternalInput").ap()
    y1T_d = nc.dram_tensor("y1T", [D, TT_], F32, kind="ExternalInput").ap()
    x2T_d = nc.dram_tensor("x2T", [D, TT_], F32, kind="ExternalOutput").ap()
    xq_scr_d = nc.dram_tensor("xq_scr", [TT_, D], BF16, kind="Internal").ap()
    wgq_scr_d = nc.dram_tensor("wgq_scr", [IM, D], BF16, kind="Internal").ap()
    wuq_scr_d = nc.dram_tensor("wuq_scr", [IM, D], BF16, kind="Internal").ap()
    wdq_scr_d = nc.dram_tensor("wdq_scr", [D, IM], BF16, kind="Internal").ap()
    gs_scr_d = nc.dram_tensor("gs_scr", [IM, TT_], BF16, kind="Internal").ap()
    h_scr_d = nc.dram_tensor("h_scr", [IM, TT_], BF16, kind="Internal").ap()
    hq_scr_d = nc.dram_tensor("hq_scr", [TT_, IM], BF16, kind="Internal").ap()

    P = Prog(nc)
    with ExitStack() as stack:
        def sb(name, shape, dt):
            return stack.enter_context(nc.sbuf_tensor(name, shape, dt))

        def ps(name, shape, dt=F32):
            return stack.enter_context(nc.psum_tensor(name, shape, dt))

        CH = 512
        NCH = TT_ // CH
        pool = sb("pool", [128, 8 * IM], BF16)
        qin_f = sb("qin_f", [128, IM], F32)
        qin_b = sb("qin_b", [128, IM], BF16)
        u2 = sb("u2", [128, IM], F32)
        t1q = sb("t1q", [128, IM], F32)
        loq = sb("loq", [128, IM], F32)
        mkq = sb("mkq", [128, IM], U8)
        sgq = sb("sgq", [128, IM], F32)
        qout = sb("qout", [128, IM], BF16)
        bmq = sb("bmq", [128, IM // 16], F32)
        rs2q = sb("rs2q", [128, IM // 16], F32)
        shq = sb("shq", [128, IM // 16], F32)
        xqT = [sb(f"xqT{k}", [128, CH], BF16) for k in range(KT)]
        hqT = [sb(f"hqT{k}", [128, CH], BF16) for k in range(OT_I)]
        gsb = [sb(f"gsb{i}", [128, CH], BF16) for i in range(2)]
        hsb = [sb(f"hsb{i}", [128, CH], BF16) for i in range(2)]
        x1c2 = [sb(f"x1c{i}", [128, CH], F32) for i in range(2)]
        y1c2 = [sb(f"y1c{i}", [128, CH], F32) for i in range(2)]
        x2c2 = [sb(f"x2c{i}", [128, CH], F32) for i in range(2)]
        PG = ps("PG", [128, 512])
        PU = ps("PU", [128, 512])
        PD = ps("PD", [128, 512])

        warm = sb("warmq", [1, 32], F32)
        w0 = P.op("vector", lambda E: E.memset(warm[:], 1.0))
        w1 = P.op("scalar", lambda E: E.activation(out=warm[:], in_=warm[:], func=AF.Sign), deps=[w0])
        w2 = P.op("scalar", lambda E: E.activation(out=warm[:], in_=warm[:], func=AF.Silu), deps=[w1])
        warm_tok = [w2]

        def emit_quant(qin_ap, W, deps):
            NBq = W // 16
            t0 = P.op("vector", lambda E, qin_ap=qin_ap, W=W, NBq=NBq: E.tensor_reduce(
                out=bmq[:, 0:NBq], in_=qin_ap.rearrange("p (nb blk) -> p nb blk", blk=16),
                axis=AX.X, op=ALU.max, apply_absolute_value=True), deps=deps)
            t1_ = P.op("vector", lambda E, NBq=NBq: E.tensor_scalar_max(out=bmq[:, 0:NBq], in0=bmq[:, 0:NBq],
                                                                       scalar1=6e-12), deps=[t0])
            t2 = P.op("vector", lambda E, NBq=NBq: E.reciprocal(out=rs2q[:, 0:NBq], in_=bmq[:, 0:NBq]), deps=[t1_])
            t3 = P.op("vector", lambda E, NBq=NBq: E.tensor_scalar_mul(out=rs2q[:, 0:NBq], in0=rs2q[:, 0:NBq],
                                                                      scalar1=12.0), deps=[t2])
            t4 = P.op("vector", lambda E, NBq=NBq: E.tensor_scalar_mul(out=shq[:, 0:NBq], in0=bmq[:, 0:NBq],
                                                                      scalar1=1.0 / 12.0), deps=[t2])
            t5 = P.op("scalar", lambda E, qin_ap=qin_ap, W=W: E.activation(out=sgq[:, 0:W], in_=qin_ap, func=AF.Sign),
                      deps=deps + warm_tok)
            t6 = P.op("vector", lambda E, qin_ap=qin_ap, W=W, NBq=NBq: E.tensor_tensor(
                out=u2[:, 0:W].rearrange("p (nb blk) -> p nb blk", blk=16),
                in0=qin_ap.rearrange("p (nb blk) -> p nb blk", blk=16),
                in1=rs2q[:, 0:NBq].unsqueeze(2).broadcast_to([128, NBq, 16]), op=ALU.mult), deps=[t3] + deps)
            t7 = P.op("vector", lambda E, W=W: E.scalar_tensor_tensor(
                out=u2[:, 0:W], in0=u2[:, 0:W], scalar=-1.0, in1=u2[:, 0:W], op0=ALU.mult, op1=ALU.max), deps=[t6])
            t8 = P.op("vector", lambda E, W=W: E.scalar_tensor_tensor(
                out=t1q[:, 0:W], in0=u2[:, 0:W], scalar=C_VELT, in1=u2[:, 0:W],
                op0=ALU.mult, op1=ALU.subtract), deps=[t7])
            t9 = P.op("vector", lambda E, W=W: E.scalar_tensor_tensor(
                out=t1q[:, 0:W], in0=u2[:, 0:W], scalar=C_VELT, in1=t1q[:, 0:W],
                op0=ALU.mult, op1=ALU.subtract), deps=[t8])
            ta = P.op("vector", lambda E, W=W: E.tensor_single_scalar(
                out=loq[:, 0:W], in_=u2[:, 0:W], scalar=0.5, op=ALU.is_gt), deps=[t7])
            tb = P.op("vector", lambda E, W=W: E.scalar_tensor_tensor(
                out=loq[:, 0:W], in0=u2[:, 0:W], scalar=1.5, in1=loq[:, 0:W], op0=ALU.is_gt, op1=ALU.add), deps=[ta])
            tcm = P.op("vector", lambda E, W=W: E.tensor_single_scalar(
                out=mkq[:, 0:W], in_=u2[:, 0:W], scalar=2.0, op=ALU.is_gt), deps=[t7])
            td = P.op("vector", lambda E, W=W: E.select(
                out=u2[:, 0:W], mask=mkq[:, 0:W], on_true=t1q[:, 0:W], on_false=loq[:, 0:W]), deps=[t9, tb, tcm])
            te = P.op("vector", lambda E, W=W: E.tensor_tensor(
                out=u2[:, 0:W], in0=u2[:, 0:W], in1=sgq[:, 0:W], op=ALU.mult), deps=[td, t5])
            tf = P.op("vector", lambda E, W=W, NBq=NBq: E.tensor_tensor(
                out=qout[:, 0:W].rearrange("p (nb blk) -> p nb blk", blk=16),
                in0=u2[:, 0:W].rearrange("p (nb blk) -> p nb blk", blk=16),
                in1=shq[:, 0:NBq].unsqueeze(2).broadcast_to([128, NBq, 16]), op=ALU.mult), deps=[te, t4])
            return tf

        out_toks = []
        prevq = list(warm_tok)
        for (src_d, dst_d, nt, W) in ((xn1R_d, xq_scr_d, TT_ // 128, D),
                                      (wg_d, wgq_scr_d, OT_I, D),
                                      (wu_d, wuq_scr_d, OT_I, D),
                                      (wd_d, wdq_scr_d, KT, IM)):
            for rt in range(nt):
                ld = P.dma("sync", lambda E, src_d=src_d, rt=rt, W=W: E.dma_start(
                    out=qin_f[:, 0:W], in_=src_d[rt * 128:(rt + 1) * 128, :]), deps=prevq)
                tf = emit_quant(qin_f[:, 0:W], W, [ld])
                st = P.dma("gpsimd", lambda E, dst_d=dst_d, rt=rt, W=W: E.dma_start(
                    out=dst_d[rt * 128:(rt + 1) * 128, :], in_=qout[:, 0:W]), deps=[tf])
                prevq = [tf, st]
        q_done = P.fence()

        ldp = [P.dma("sync", lambda E, k=k: E.dma_start(
            out=pool[:, k * IM:(k + 1) * IM], in_=wgq_scr_d[:, k * 128:(k + 1) * 128], transpose=True),
            deps=q_done, ch="tp") for k in range(KT)]
        pg_free = []
        gs_toks = []
        for ch in range(NCH):
            ldx = [P.dma("sync", lambda E, k=k, ch=ch: E.dma_start(
                out=xqT[k][:], in_=xq_scr_d[ch * CH:(ch + 1) * CH, k * 128:(k + 1) * 128], transpose=True),
                deps=q_done + gs_toks[-2:], ch="tp") for k in range(KT)]
            for ot in range(OT_I):
                gi = ot % 2
                mm = None
                for k in range(KT):
                    mm = P.op("tensor", lambda E, ot=ot, k=k: E.matmul(
                        PG[:], pool[:, k * IM + ot * 128: k * IM + (ot + 1) * 128], xqT[k][:],
                        start=(k == 0), stop=(k == KT - 1)),
                        deps=(list(ldx) + list(ldp) + pg_free) if k == 0 else [])
                ev = P.op("scalar", lambda E, gi=gi: E.activation(out=gsb[gi][:], in_=PG[:], func=AF.Silu),
                          deps=[mm] + gs_toks[-2:] + warm_tok)
                pg_free = [ev]
                st = P.dma("gpsimd", lambda E, ot=ot, ch=ch, gi=gi: E.dma_start(
                    out=gs_scr_d[ot * 128:(ot + 1) * 128, ch * CH:(ch + 1) * CH], in_=gsb[gi][:]), deps=[ev])
                gs_toks.append(st)
        g_done = P.fence()

        ldp2 = [P.dma("sync", lambda E, k=k: E.dma_start(
            out=pool[:, k * IM:(k + 1) * IM], in_=wuq_scr_d[:, k * 128:(k + 1) * 128], transpose=True),
            deps=g_done, ch="tp") for k in range(KT)]
        pu_free = []
        h_toks = []
        for ch in range(NCH):
            ldx = [P.dma("sync", lambda E, k=k, ch=ch: E.dma_start(
                out=xqT[k][:], in_=xq_scr_d[ch * CH:(ch + 1) * CH, k * 128:(k + 1) * 128], transpose=True),
                deps=g_done + h_toks[-2:], ch="tp") for k in range(KT)]
            for ot in range(OT_I):
                gi = ot % 2
                mm = None
                for k in range(KT):
                    mm = P.op("tensor", lambda E, ot=ot, k=k: E.matmul(
                        PU[:], pool[:, k * IM + ot * 128: k * IM + (ot + 1) * 128], xqT[k][:],
                        start=(k == 0), stop=(k == KT - 1)),
                        deps=(list(ldx) + list(ldp2) + pu_free) if k == 0 else [])
                lg = P.dma("sync", lambda E, ot=ot, ch=ch, gi=gi: E.dma_start(
                    out=gsb[gi][:], in_=gs_scr_d[ot * 128:(ot + 1) * 128, ch * CH:(ch + 1) * CH]),
                    deps=g_done + h_toks[-2:])
                th = P.op("vector", lambda E, gi=gi: E.tensor_tensor(
                    out=hsb[gi][:], in0=PU[:], in1=gsb[gi][:], op=ALU.mult), deps=[mm, lg] + h_toks[-2:])
                pu_free = [th]
                st = P.dma("gpsimd", lambda E, ot=ot, ch=ch, gi=gi: E.dma_start(
                    out=h_scr_d[ot * 128:(ot + 1) * 128, ch * CH:(ch + 1) * CH], in_=hsb[gi][:]), deps=[th])
                h_toks.append(st)
        h_done = P.fence()

        prevq = list(h_done)
        hq_toks = []
        for rt in range(TT_ // 128):
            ld = P.dma("sync", lambda E, rt=rt: E.dma_start(
                out=qin_b[:], in_=h_scr_d[:, rt * 128:(rt + 1) * 128], transpose=True),
                deps=prevq, ch="tp")
            tf = emit_quant(qin_b[:], IM, [ld])
            st = P.dma("gpsimd", lambda E, rt=rt: E.dma_start(
                out=hq_scr_d[rt * 128:(rt + 1) * 128, :], in_=qout[:, 0:IM]), deps=[tf])
            prevq = [tf, st]
            hq_toks.append(st)
        hq_done = P.fence()

        ldp3 = [P.dma("sync", lambda E, kk=kk: E.dma_start(
            out=pool[:, kk * D:(kk + 1) * D], in_=wdq_scr_d[:, kk * 128:(kk + 1) * 128], transpose=True),
            deps=hq_done, ch="tp") for kk in range(OT_I)]
        pd_free = []
        x2_toks = []
        for ch in range(NCH):
            ldh = [P.dma("sync", lambda E, kk=kk, ch=ch: E.dma_start(
                out=hqT[kk][:], in_=hq_scr_d[ch * CH:(ch + 1) * CH, kk * 128:(kk + 1) * 128], transpose=True),
                deps=hq_done + x2_toks[-2:], ch="tp") for kk in range(OT_I)]
            for ot in range(KT):
                gi = ot % 2
                l1 = P.dma("sync", lambda E, ot=ot, ch=ch, gi=gi: E.dma_start(
                    out=x1c2[gi][:], in_=x1T_d[ot * 128:(ot + 1) * 128, ch * CH:(ch + 1) * CH]),
                    deps=hq_done + x2_toks[-2:])
                l2 = P.dma("sync", lambda E, ot=ot, ch=ch, gi=gi: E.dma_start(
                    out=y1c2[gi][:], in_=y1T_d[ot * 128:(ot + 1) * 128, ch * CH:(ch + 1) * CH]),
                    deps=hq_done + x2_toks[-2:])
                mm = None
                for kk in range(OT_I):
                    mm = P.op("tensor", lambda E, ot=ot, kk=kk: E.matmul(
                        PD[:], pool[:, kk * D + ot * 128: kk * D + (ot + 1) * 128], hqT[kk][:],
                        start=(kk == 0), stop=(kk == OT_I - 1)),
                        deps=(list(ldh) + list(ldp3) + pd_free) if kk == 0 else [])
                a1 = P.op("vector", lambda E, gi=gi: E.tensor_tensor(
                    out=x2c2[gi][:], in0=PD[:], in1=x1c2[gi][:], op=ALU.add), deps=[mm, l1] + x2_toks[-2:])
                pd_free = [a1]
                a2 = P.op("vector", lambda E, gi=gi: E.tensor_tensor(
                    out=x2c2[gi][:], in0=x2c2[gi][:], in1=y1c2[gi][:], op=ALU.add), deps=[a1, l2])
                st = P.dma("gpsimd", lambda E, ot=ot, ch=ch, gi=gi: E.dma_start(
                    out=x2T_d[ot * 128:(ot + 1) * 128, ch * CH:(ch + 1) * CH], in_=x2c2[gi][:]), deps=[a2])
                x2_toks.append(st)
                out_toks.append(st)
        P.op("gpsimd", lambda E: E.engine_nop(), deps=out_toks)
        block = stack.enter_context(nc.Block())
        P.build(stack, block)
    return nc


def run_l1_ffn(inputs, xn1T_list, x1T_list, y1T_list):
    nc = build_l1_ffn()
    wg = np.ascontiguousarray(np.asarray(inputs["L1_ffn_w_gate"]))
    wu = np.ascontiguousarray(np.asarray(inputs["L1_ffn_w_up"]))
    wd = np.ascontiguousarray(np.asarray(inputs["L1_ffn_w_down"]))
    in_maps = []
    for c in range(NCORES):
        xn1R = np.ascontiguousarray(xn1T_list[c].T)
        in_maps.append({"xn1R": xn1R, "wg": wg, "wu": wu, "wd": wd,
                        "x1T": x1T_list[c], "y1T": y1T_list[c]})
    res = run_bass_kernel_spmd(nc, in_maps, core_ids=list(range(NCORES)))
    return res.results


def _prep_l0_main_inputs(inputs):
    x = np.asarray(inputs["L0_attn_x"])
    wqkvT = np.ascontiguousarray(np.asarray(inputs["L0_attn_w_qkv"]).T)
    woutT = np.ascontiguousarray(np.asarray(inputs["L0_attn_w_out"]).T)
    bout = np.ascontiguousarray(np.asarray(inputs["L0_attn_b_out"]))
    g16 = np.ascontiguousarray(np.asarray(inputs["L0_attn_norm_g"]).reshape(1, H).astype(np.float32))
    b16 = np.ascontiguousarray(np.asarray(inputs["L0_attn_norm_b"]).reshape(1, H).astype(np.float32))
    eye = np.eye(128, dtype=np.float32)
    ones = np.ones((128, 1), dtype=np.float32)
    in_maps = []
    for c in range(NCORES):
        x0T = np.ascontiguousarray(np.concatenate([x[2 * c].T, x[2 * c + 1].T], axis=1))
        in_maps.append({"x0T": x0T, "wqkvT": wqkvT, "woutT": woutT, "bout": bout,
                        "g16": g16, "b16": b16, "eye": eye, "ones": ones})
    return in_maps


def run_l0_main(inputs, warm=False, ln_only=False):
    nc = build_l0_main(ln_only=ln_only)
    in_maps = _prep_l0_main_inputs(inputs)
    res = run_bass_kernel_spmd(nc, in_maps, core_ids=list(range(NCORES)))
    return res.results


def kernel(**inputs):
    x = np.asarray(inputs["L0_attn_x"])
    r1 = run_l0_main(inputs)
    attn0T = [np.asarray(r1[c]["attn0T"]) for c in range(NCORES)]
    xn0T = [np.asarray(r1[c]["xn0T"]) for c in range(NCORES)]
    r2 = run_l0_scan(inputs)
    zT_full = np.ascontiguousarray(np.concatenate([np.asarray(r2[c]["zT"]) for c in range(NCORES)], axis=1))
    r3 = run_x1_ln1(inputs, attn0T, xn0T, zT_full)
    x1T = [np.asarray(r3[c]["x1T"]) for c in range(NCORES)]
    xn1T = [np.asarray(r3[c]["xn1T"]) for c in range(NCORES)]
    r4 = run_l1_attn(inputs, xn1T)
    y1T = [np.asarray(r4[c]["y1T"]) for c in range(NCORES)]
    r5 = run_l1_ffn(inputs, xn1T, x1T, y1T)
    out = np.empty((NB, T, D), dtype=np.float32)
    for c in range(NCORES):
        x2T = np.asarray(r5[c]["x2T"])
        out[2 * c] = x2T[:, 0:T].T
        out[2 * c + 1] = x2T[:, T:2 * T].T
    return out


# version 05
# speedup vs baseline: 1.0652x; 1.0652x over previous
import numpy as np
from contextlib import ExitStack

import concourse.bass as bass
import concourse.mybir as mybir
from concourse.bass_utils import run_bass_kernel_spmd

F32 = mybir.dt.float32
BF16 = mybir.dt.bfloat16
U8 = mybir.dt.uint8
AF = mybir.ActivationFunctionType
ALU = mybir.AluOpType
AX = mybir.AxisListType

NCORES = 8
D = 1024
NB = 16
T = 2048
H = 16
HD = 64
KT = D // 128
B2 = NB // NCORES
SS = T // NCORES
C_VELT = float((1 << 22) + 1)
LN_EPS = 1e-5


class Prog:
    CE = ("tensor", "vector", "scalar", "gpsimd", "sync")

    def __init__(self, nc):
        self.nc = nc
        self.recs = {e: [] for e in self.CE}
        self.nk = {}
        self.awaited = set()
        self.floor = []

    def set_floor(self, toks):
        self.floor = list(toks)

    def _add(self, eng, fn, deps, kind):
        deps = tuple(d for d in deps if d is not None) + tuple(self.floor)
        for d in deps:
            self.awaited.add(d)
        key = (eng, kind)
        self.nk.setdefault(key, 0)
        self.nk[key] += 1
        tok = (eng, kind, self.nk[key])
        self.recs[eng].append((fn, deps, tok))
        return tok

    def op(self, eng, fn, deps=()):
        return self._add(eng, fn, deps, "c")

    def dma(self, eng, fn, deps=(), ch="w128"):
        tok = self._add(eng, fn, deps, "d:" + ch)
        self.awaited.add(tok)
        return tok

    def last(self, eng, kind="c"):
        n = self.nk.get((eng, kind), 0)
        return (eng, kind, n) if n else None

    def fence(self):
        toks = []
        for (e, kind), n in self.nk.items():
            if n:
                toks.append((e, kind, n))
        return toks

    def build(self, stack, block):
        nc = self.nc
        sems = {}
        for (e, kind) in self.nk:
            sems[(e, kind)] = stack.enter_context(nc.semaphore(f"sem_{e}_{kind.replace(':', '_')}"))
        for s in sems.values():
            nc.gpsimd.sem_clear(s)
        nc.all_engine_barrier()
        target = {}
        counters = {}
        for e in self.CE:
            for fn, deps, tok in self.recs[e]:
                if tok in self.awaited:
                    key = (e, tok[1])
                    counters.setdefault(key, 0)
                    counters[key] += 1
                    target[tok] = counters[key] * (16 if tok[1].startswith("d") else 1)

        def mk_emit(name):
            def emit(E):
                seen = {}
                for fn, deps, tok in self.recs[name]:
                    for d in deps:
                        key = (d[0], d[1])
                        tv = target[d]
                        if seen.get(key, -1) < tv:
                            E.wait_ge(sems[key], tv)
                            seen[key] = tv
                    inst = fn(E)
                    if tok in self.awaited:
                        inst.then_inc(sems[(name, tok[1])], 16 if tok[1].startswith("d") else 1)
            return emit

        for name in self.CE:
            getattr(block, name)(mk_emit(name))


def build_l0_main(ln_only=False):
    nc = bass.Bass("TRN2", target_bir_lowering=False)
    x0T_d = nc.dram_tensor("x0T", [D, B2 * T], F32, kind="ExternalInput").ap()
    wqkvT_d = nc.dram_tensor("wqkvT", [D, 3 * D], F32, kind="ExternalInput").ap()
    woutT_d = nc.dram_tensor("woutT", [D, D], F32, kind="ExternalInput").ap()
    bout_d = nc.dram_tensor("bout", [D], F32, kind="ExternalInput").ap()
    g16_d = nc.dram_tensor("g16", [1, H], F32, kind="ExternalInput").ap()
    b16_d = nc.dram_tensor("b16", [1, H], F32, kind="ExternalInput").ap()
    eye_d = nc.dram_tensor("eye", [128, 128], F32, kind="ExternalInput").ap()
    ones_d = nc.dram_tensor("ones", [128, 1], F32, kind="ExternalInput").ap()

    attn0T_d = nc.dram_tensor("attn0T", [D, B2 * T], F32, kind="ExternalOutput").ap()
    xn0T_d = nc.dram_tensor("xn0T", [D, B2 * T], BF16, kind="ExternalOutput").ap()
    sburst_d = nc.dram_tensor("sburst", [2, T], F32, kind="Internal").ap()
    ao_scr_d = nc.dram_tensor("ao_scr", [D, T], BF16, kind="Internal").ap()

    P = Prog(nc)
    with ExitStack() as stack:
        def sb(name, shape, dt):
            return stack.enter_context(nc.sbuf_tensor(name, shape, dt))

        def ps(name, shape, dt=F32):
            return stack.enter_context(nc.psum_tensor(name, shape, dt))

        CH = 512
        xbc = [sb(f"xbc{k}", [128, CH], BF16) for k in range(KT)]
        xnc = [sb(f"xnc{k}", [128, CH], BF16) for k in range(KT)]
        ktc = [sb(f"ktc{k}", [128, CH], BF16) for k in range(KT)]
        vtc = [sb(f"vtc{k}", [128, CH], BF16) for k in range(KT)]
        q_sb = [sb(f"q{k}", [128, T], BF16) for k in range(KT)]
        wq = [sb(f"wq{k}", [128, 3 * D], BF16) for k in range(KT)]
        wo = [sb(f"wo{k}", [128, D], BF16) for k in range(KT)]
        kvrow = sb("kvrow", [128, CH], BF16)
        gkv_all = sb("gkv_all", [128, KT * 128], F32)
        gkvb = sb("gkvb", [128, KT * 128], BF16)
        aostage = sb("aostage", [128, T], BF16)
        aor = [sb(f"aor{k}", [128, CH], BF16) for k in range(KT)]
        aout = sb("aout", [128, T], F32)
        eyeb = sb("eyeb", [128, 128], BF16)
        ones_b = sb("ones_b", [128, 1], BF16)
        onesrow_b = sb("onesrow_b", [1, 128], BF16)
        mb_row = sb("mb_row", [1, CH], BF16)
        rb_row = sb("rb_row", [1, CH], BF16)
        bout_s = sb("bout_s", [128, KT], F32)
        brep = sb("brep", [128, KT], F32)
        g16_s = sb("g16_s", [1, H], F32)
        m_row = sb("m_row", [1, CH], F32)
        v_row = sb("v_row", [1, CH], F32)
        vt_row = sb("vt_row", [1, CH], F32)
        nsq = sb("nsq", [1, T], F32)
        srow = sb("srow", [1, T], F32)
        srep = sb("srep", [128, T], BF16)
        t1 = sb("t1", [128, CH], F32)
        t1bs = [sb(f"t1b{k}", [128, CH], BF16) for k in range(KT)]
        zq = sb("zq", [64, CH], BF16)
        ub = sb("ub", [128, KT], F32)
        ubt = sb("ubt", [128, KT], F32)
        ubb = sb("ubb", [128, KT], BF16)
        btot = sb("btot", [128, KT], F32)

        PQ = [ps(f"PQ{i}", [128, 512]) for i in range(2)]
        PT = [ps(f"PT{i}", [128, 64], BF16) for i in range(2)]
        GK = ps("GK", [64, 128])
        ZO = ps("ZO", [128, 512])
        NS = ps("NS", [1, 512])

        warm = sb("warm", [1, 32], F32)
        w0 = P.op("vector", lambda E: E.memset(warm[:], 1.0))
        w1 = P.op("scalar", lambda E: E.activation(out=warm[:], in_=warm[:], func=AF.Square), deps=[w0])
        w2 = P.op("scalar", lambda E: E.activation(out=warm[:], in_=warm[:], func=AF.Ln), deps=[w1])
        w3 = P.op("scalar", lambda E: E.activation(out=warm[:], in_=warm[:], func=AF.Exp), deps=[w2])
        warm_tok = [w3]

        ldw_q, ldw_o = [], []
        for k in range(KT):
            ldw_q.append(P.dma("gpsimd", lambda E, k=k: E.dma_start(out=wq[k][:], in_=wqkvT_d[k * 128:(k + 1) * 128, :]), ch="cast"))
            ldw_o.append(P.dma("gpsimd", lambda E, k=k: E.dma_start(out=wo[k][:], in_=woutT_d[k * 128:(k + 1) * 128, :]), ch="cast"))
        ld_eyeb = P.dma("gpsimd", lambda E: E.dma_start(out=eyeb[:], in_=eye_d[:]), ch="cast")
        ld_ob = P.dma("gpsimd", lambda E: E.dma_start(out=ones_b[:], in_=ones_d[:]), ch="cast")
        ld_or = P.dma("gpsimd", lambda E: E.dma_start(
            out=onesrow_b[:], in_=ones_d[:, 0].unsqueeze(0).broadcast_to([1, 128])), ch="castp0")
        ld_g = P.dma("sync", lambda E: E.dma_start(out=g16_s[:], in_=g16_d[:]), ch="p0")
        ld_bout = [P.dma("sync", lambda E, o=o: E.dma_start(
            out=bout_s[:, o:o + 1], in_=bout_d[o * 128:(o + 1) * 128].unsqueeze(1))) for o in range(KT)]
        ld_brep = [P.dma("sync", lambda E, h=h: E.dma_start(
            out=brep[(h % 2) * 64:(h % 2) * 64 + 64, h // 2:h // 2 + 1],
            in_=b16_d[0:1, h:h + 1].broadcast_to([64, 1])), ch=("n0" if h % 2 == 0 else "n64")) for h in range(H)]

        out_toks = []
        batch_fence = []
        for bi in range(B2):
            z1 = P.op("vector", lambda E: E.memset(gkv_all[:], 0.0), deps=batch_fence + warm_tok)
            z2 = P.op("vector", lambda E: E.memset(ub[:], 0.0), deps=batch_fence)
            acc_g = [z1]
            acc_u = [z2]
            chunk_fence = list(batch_fence)
            for c in range(4):
                tc = slice(bi * T + c * CH, bi * T + (c + 1) * CH)
                ld_x = [P.dma("gpsimd", lambda E, k=k, tc=tc: E.dma_start(
                    out=xbc[k][:], in_=x0T_d[k * 128:(k + 1) * 128, tc]),
                    deps=chunk_fence) for k in range(KT)]
                mm = None
                for k in range(KT):
                    mm = P.op("tensor", lambda E, k=k: E.matmul(
                        PQ[0][0:1, :], ones_b[:], xbc[k][:], start=(k == 0), stop=(k == KT - 1)),
                        deps=(list(ld_x) + [ld_ob] + warm_tok + chunk_fence) if k == 0 else [])
                em0 = P.op("scalar", lambda E: E.copy(out=m_row[:], in_=PQ[0][0:1, :]), deps=[mm])
                em = P.op("vector", lambda E: E.tensor_scalar_mul(out=m_row[:], in0=m_row[:], scalar1=1.0 / D), deps=[em0])
                sq_toks = [P.op("scalar", lambda E, k=k: E.activation(out=t1bs[k][:], in_=xbc[k][:], func=AF.Square),
                                deps=[ld_x[k]] + chunk_fence) for k in range(KT)]
                mmq = None
                for k in range(KT):
                    mmq = P.op("tensor", lambda E, k=k: E.matmul(
                        PQ[1][0:1, :], ones_b[:], t1bs[k][:], start=(k == 0), stop=(k == KT - 1)),
                        deps=([sq_toks[-1], ld_ob] + chunk_fence) if k == 0 else [])
                ev0 = P.op("scalar", lambda E: E.copy(out=v_row[:], in_=PQ[1][0:1, :]), deps=[mmq])
                ev = P.op("vector", lambda E: E.tensor_scalar_mul(out=v_row[:], in0=v_row[:], scalar1=1.0 / D), deps=[ev0])
                tv1 = P.op("vector", lambda E: E.tensor_mul(out=vt_row[:], in0=m_row[:], in1=m_row[:]), deps=[em])
                tv2 = P.op("vector", lambda E: E.tensor_sub(out=v_row[:], in0=v_row[:], in1=vt_row[:]), deps=[tv1, ev])
                tv3 = P.op("vector", lambda E: E.tensor_scalar_add(out=v_row[:], in0=v_row[:], scalar1=LN_EPS), deps=[tv2])
                tr1 = P.op("scalar", lambda E: E.activation(out=v_row[:], in_=v_row[:], func=AF.Ln), deps=[tv3])
                tr1b = P.op("vector", lambda E: E.tensor_scalar_mul(out=v_row[:], in0=v_row[:], scalar1=-0.5), deps=[tr1])
                tr2 = P.op("scalar", lambda E: E.activation(out=v_row[:], in_=v_row[:], func=AF.Exp), deps=[tr1b])
                cmb = P.op("vector", lambda E: E.tensor_copy(out=mb_row[:], in_=m_row[:]), deps=[em])
                crb = P.op("vector", lambda E: E.tensor_copy(out=rb_row[:], in_=v_row[:]), deps=[tr2])
                t_m = P.op("tensor", lambda E: E.matmul(PQ[0][:], onesrow_b[:], mb_row[:], start=True, stop=True),
                           deps=[cmb, ld_or])
                t_r = P.op("tensor", lambda E: E.matmul(PQ[1][:], onesrow_b[:], rb_row[:], start=True, stop=True),
                           deps=[crb, ld_or])
                ap_toks = []
                for k in range(KT):
                    a1 = P.op("vector", lambda E, k=k: E.tensor_tensor(out=t1[:], in0=xbc[k][:], in1=PQ[0][:], op=ALU.subtract),
                              deps=[t_m, ld_x[k]] + ([ap_toks[-1]] if ap_toks else [mmq]))
                    a2 = P.op("vector", lambda E, k=k: E.tensor_tensor(out=xnc[k][:], in0=t1[:], in1=PQ[1][:], op=ALU.mult),
                              deps=[a1, t_r])
                    ap_toks.append(a2)
                st_xn = [P.dma("gpsimd", lambda E, k=k, tc=tc: E.dma_start(
                    out=xn0T_d[k * 128:(k + 1) * 128, tc], in_=xnc[k][:]), deps=[ap_toks[k]]) for k in range(KT)]
                out_toks += st_xn
                if ln_only:
                    chunk_fence = P.fence()
                    continue
                pq_free = [[ap_toks[-1]], [ap_toks[-1]]]
                ev_q = [None] * KT
                ev_k = [None] * KT
                ev_v = [None] * KT
                for ot in range(3 * KT):
                    pqi = ot % 2
                    mm = None
                    for k in range(KT):
                        mm = P.op("tensor", lambda E, ot=ot, k=k, pqi=pqi: E.matmul(
                            PQ[pqi][:], wq[k][:, ot * 128:(ot + 1) * 128], xnc[k][:],
                            start=(k == 0), stop=(k == KT - 1)),
                            deps=([ap_toks[-1], ldw_q[-1]] + pq_free[pqi]) if k == 0 else [])
                    if ot < KT:
                        evt = P.op("scalar", lambda E, ot=ot, pqi=pqi, c=c: E.copy(
                            out=q_sb[ot][:, c * CH:(c + 1) * CH], in_=PQ[pqi][:]), deps=[mm] + batch_fence)
                    elif ot < 2 * KT:
                        evt = P.op("scalar", lambda E, ot=ot, pqi=pqi: E.copy(
                            out=ktc[ot - KT][:], in_=PQ[pqi][:]), deps=[mm] + chunk_fence)
                        ev_k[ot - KT] = evt
                    else:
                        evt = P.op("scalar", lambda E, ot=ot, pqi=pqi: E.copy(
                            out=vtc[ot - 2 * KT][:], in_=PQ[pqi][:]), deps=[mm] + chunk_fence)
                        ev_v[ot - 2 * KT] = evt
                    pq_free[pqi] = [evt]
                    if ot < KT:
                        ev_q[ot] = evt
                pt_free = [list(chunk_fence), list(chunk_fence)]
                kvrow_free = list(chunk_fence)
                gk_free = []
                ub_toks = []
                for h in range(H):
                    p64 = (h % 2) * 64
                    hp = h // 2
                    kcol = ktc[hp][p64:p64 + 64, :]
                    vcol = vtc[hp][p64:p64 + 64, :]
                    idn = eyeb[p64:p64 + 64, p64:p64 + 64]
                    tr_toks = []
                    for jb in range(4):
                        for which, col in ((0, kcol), (1, vcol)):
                            pti = (jb * 2 + which) % 2
                            tt = P.op("tensor", lambda E, col=col, jb=jb, pti=pti, idn=idn: E.transpose(
                                out=PT[pti][:], in_=col[:, jb * 128:(jb + 1) * 128], identity=idn),
                                deps=[ev_k[hp], ev_v[hp], ld_eyeb] + pt_free[pti])
                            te = P.op("scalar", lambda E, jb=jb, which=which, pti=pti: E.copy(
                                out=kvrow[:, jb * 128 + which * 64: jb * 128 + which * 64 + 64], in_=PT[pti][:]),
                                deps=[tt] + kvrow_free)
                            pt_free[pti] = [te]
                            tr_toks.append(te)
                    kvrow_free = []
                    mmgk = None
                    for jb in range(4):
                        mmgk = P.op("tensor", lambda E, jb=jb: E.matmul(
                            GK[:], kvrow[:, jb * 128: jb * 128 + 64], kvrow[:, jb * 128: jb * 128 + 128],
                            start=(jb == 0), stop=(jb == 3)),
                            deps=([tr_toks[-1]] + gk_free) if jb == 0 else [])
                    kvrow_free = [mmgk]
                    ga = P.op("vector", lambda E, h=h, p64=p64, hp=hp: E.tensor_tensor(
                        out=gkv_all[p64:p64 + 64, hp * 128:(hp + 1) * 128],
                        in0=GK[:], in1=gkv_all[p64:p64 + 64, hp * 128:(hp + 1) * 128], op=ALU.add),
                        deps=[mmgk] + acc_g)
                    gk_free = [ga]
                    ut = P.op("vector", lambda E, h=h, p64=p64, vcol=vcol: E.tensor_reduce(
                        out=ubt[p64:p64 + 64, h // 2:h // 2 + 1], in_=vcol[:], axis=AX.X, op=ALU.add),
                        deps=[ev_v[hp]] + acc_u)
                    ub_toks.append(ut)
                acc_g = [gk_free[0] if gk_free else None] + [None]
                acc_g = [ga]
                ua = P.op("vector", lambda E: E.tensor_add(out=ub[:], in0=ub[:], in1=ubt[:]), deps=ub_toks + acc_u)
                acc_u = [ua]
                chunk_fence = P.fence()

            if ln_only:
                batch_fence = P.fence()
                continue
            cg = P.op("vector", lambda E: E.tensor_copy(out=gkvb[:], in_=gkv_all[:]), deps=acc_g + chunk_fence)
            zo_free, ns_free, zq_free = [cg], [], []
            sburst_free = {0: [], 1: []}
            srep_free = list(batch_fence)
            ao_dma = {0: [], 1: []}
            for h in range(H):
                p64 = (h % 2) * 64
                hp = h // 2
                qcol = q_sb[hp][p64:p64 + 64, :]
                oh_toks = []
                nsq_toks = []
                for c in range(4):
                    ch = slice(c * CH, (c + 1) * CH)
                    mm = P.op("tensor", lambda E, ch=ch, p64=p64, hp=hp, qcol=qcol: E.matmul(
                        ZO[:], gkvb[p64:p64 + 64, hp * 128:(hp + 1) * 128], qcol[:, ch], start=True, stop=True),
                        deps=[cg, ev_q[hp]] + zo_free)
                    tzq = P.op("vector", lambda E, ch=ch, qcol=qcol: E.tensor_tensor(
                        out=zq[:], in0=ZO[0:64, :], in1=qcol[:, ch], op=ALU.mult), deps=[mm] + zq_free)
                    toh = P.op("scalar", lambda E, ch=ch, p64=p64: E.copy(
                        out=aostage[p64:p64 + 64, ch], in_=ZO[64:128, :]), deps=[mm] + ao_dma[h % 2])
                    tns = P.op("tensor", lambda E: E.matmul(
                        NS[:], ones_b[0:64, :], zq[:], start=True, stop=True), deps=[tzq] + ns_free)
                    ten = P.op("scalar", lambda E, ch=ch: E.copy(out=nsq[:, ch], in_=NS[:]), deps=[tns])
                    ns_free = [ten]
                    zq_free = [tns]
                    zo_free = [tzq, toh]
                    oh_toks.append(toh)
                    nsq_toks.append(ten)
                s1 = P.op("scalar", lambda E: E.activation(out=srow[:], in_=nsq[:], func=AF.Ln),
                          deps=nsq_toks + sburst_free[h % 2] + sburst_free[(h + 1) % 2])
                s1b = P.op("vector", lambda E: E.tensor_scalar_mul(out=srow[:], in0=srow[:], scalar1=0.5), deps=[s1])
                s2 = P.op("scalar", lambda E: E.activation(out=srow[:], in_=srow[:], func=AF.Exp), deps=[s1b])
                s3 = P.op("vector", lambda E: E.tensor_scalar_max(out=srow[:], in0=srow[:], scalar1=1e-12), deps=[s2])
                s4 = P.op("vector", lambda E: E.reciprocal(out=srow[:], in_=srow[:]), deps=[s3])
                s5 = P.op("vector", lambda E, h=h: E.tensor_scalar_mul(
                    out=srow[:], in0=srow[:], scalar1=g16_s[0:1, h:h + 1]), deps=[s4, ld_g])
                s6 = P.dma("gpsimd", lambda E, h=h: E.dma_start(out=sburst_d[h % 2:h % 2 + 1, :], in_=srow[0:1, :]),
                           deps=[s5] + sburst_free[h % 2], ch="p0")
                s7 = P.dma("gpsimd", lambda E, h=h: E.dma_start(
                    out=srep[:], in_=sburst_d[h % 2:h % 2 + 1, :].broadcast_to([128, T])),
                    deps=[s6] + srep_free, ch="cast")
                sburst_free[h % 2] = [s7]
                sc_toks = []
                for c in range(4):
                    ch = slice(c * CH, (c + 1) * CH)
                    tok = P.op("vector", lambda E, ch=ch, p64=p64: E.tensor_tensor(
                        out=aostage[p64:p64 + 64, ch], in0=aostage[p64:p64 + 64, ch],
                        in1=srep[p64:p64 + 64, ch], op=ALU.mult), deps=[s7, oh_toks[c]])
                    sc_toks.append(tok)
                srep_free = sc_toks
                da = P.dma("gpsimd", lambda E, h=h, p64=p64: E.dma_start(
                    out=ao_scr_d[h * 64:(h + 1) * 64, :], in_=aostage[p64:p64 + 64, :]), deps=sc_toks,
                    ch=("n0" if p64 == 0 else "n64"))
                ao_dma[h % 2] = [da]
            ao_done = [ao_dma[0][0], ao_dma[1][0]]

            ubm = P.op("vector", lambda E: E.tensor_tensor(out=ub[:], in0=ub[:], in1=brep[:], op=ALU.mult),
                       deps=acc_u + ld_brep)
            ubc = P.op("vector", lambda E: E.tensor_copy(out=ubb[:], in_=ub[:]), deps=[ubm] + batch_fence)
            bt_toks = []
            pq0_free = chunk_fence
            for ot in range(KT):
                mm = None
                for k in range(KT):
                    mm = P.op("tensor", lambda E, ot=ot, k=k: E.matmul(
                        PQ[0][:, 0:1], wo[k][:, ot * 128:(ot + 1) * 128], ubb[:, k:k + 1],
                        start=(k == 0), stop=(k == KT - 1)),
                        deps=([ubc, ldw_o[-1]] + (pq0_free if ot == 0 else [bt_toks[-1]])) if k == 0 else [])
                evb = P.op("scalar", lambda E, ot=ot: E.copy(out=btot[:, ot:ot + 1], in_=PQ[0][:, 0:1]), deps=[mm])
                bt_toks.append(evb)
            btt = P.op("vector", lambda E: E.tensor_add(out=btot[:], in0=btot[:], in1=bout_s[:]),
                       deps=bt_toks + ld_bout)

            pq_free = [[bt_toks[-1]], chunk_fence]
            aor_free = [list(chunk_fence) for _ in range(KT)]
            aout_free = list(batch_fence)
            for c in range(4):
                ch = slice(c * CH, (c + 1) * CH)
                ld_ao = [P.dma("sync", lambda E, k=k, ch=ch: E.dma_start(
                    out=aor[k][:], in_=ao_scr_d[k * 128:(k + 1) * 128, ch]), deps=ao_done + aor_free[k])
                    for k in range(KT)]
                for ot in range(KT):
                    pqi = ot % 2
                    mm = None
                    for k in range(KT):
                        mm = P.op("tensor", lambda E, ot=ot, k=k, pqi=pqi: E.matmul(
                            PQ[pqi][:], wo[k][:, ot * 128:(ot + 1) * 128], aor[k][:],
                            start=(k == 0), stop=(k == KT - 1)),
                            deps=(list(ld_ao) + [ldw_o[-1]] + pq_free[pqi]) if k == 0 else [])
                    evo = P.op("vector", lambda E, ch=ch, pqi=pqi, ot=ot: E.tensor_scalar_add(
                        out=aout[:, ch], in0=PQ[pqi][:], scalar1=btot[:, ot:ot + 1]),
                        deps=[mm, btt] + aout_free)
                    pq_free[pqi] = [evo]
                    std = P.dma("gpsimd", lambda E, ot=ot, ch=ch, bi=bi, c=c: E.dma_start(
                        out=attn0T_d[ot * 128:(ot + 1) * 128, bi * T + c * CH: bi * T + (c + 1) * CH],
                        in_=aout[:, ch]), deps=[evo])
                    aout_free = [std]
                    out_toks.append(std)
                aor_free = [[P.last("tensor")] for _ in range(KT)]
            batch_fence = P.fence()

        P.op("gpsimd", lambda E: E.engine_nop(), deps=out_toks)
        block = stack.enter_context(nc.Block())
        P.build(stack, block)
    return nc


def build_l0_scan(nsteps=NB, out_c=False):
    nc = bass.Bass("TRN2", target_bir_lowering=False)
    TS_ = NB * SS
    x0Ts_d = nc.dram_tensor("x0Ts", [D, TS_], F32, kind="ExternalInput").ap()
    wpackT_d = nc.dram_tensor("wpackT", [D, 7 * D], F32, kind="ExternalInput").ap()
    bias3_d = nc.dram_tensor("bias3", [3, D], F32, kind="ExternalInput").ap()
    inc_d = nc.dram_tensor("incv", [D], F32, kind="ExternalInput").ap()
    ones_d = nc.dram_tensor("ones", [128, 1], F32, kind="ExternalInput").ap()
    zT_d = nc.dram_tensor("zT", [D, SS], F32, kind="ExternalOutput").ap()
    xns_scr_d = nc.dram_tensor("xns_scr", [D, TS_], BF16, kind="Internal").ap()

    P = Prog(nc)
    with ExitStack() as stack:
        def sb(name, shape, dt):
            return stack.enter_context(nc.sbuf_tensor(name, shape, dt))

        def ps(name, shape, dt=F32):
            return stack.enter_context(nc.psum_tensor(name, shape, dt))

        CH = 512
        xbc = [sb(f"xbc{k}", [128, CH], BF16) for k in range(KT)]
        xnc = [sb(f"xnc{k}", [128, CH], BF16) for k in range(KT)]
        t1bs = [sb(f"t1b{k}", [128, CH], BF16) for k in range(KT)]
        wp = [sb(f"wp{k}", [128, 7 * D], BF16) for k in range(KT)]
        ct_ = [sb(f"ct{k}", [128, SS], F32) for k in range(KT)]
        xstep = [[sb(f"xs{j}_{k}", [128, SS], BF16) for k in range(KT)] for j in range(2)]
        fts = [sb(f"fts{k}", [128, SS], F32) for k in range(KT)]
        its = [sb(f"its{k}", [128, SS], F32) for k in range(KT)]
        gts = [sb(f"gts{k}", [128, SS], F32) for k in range(KT)]
        tmpa = sb("tmpa", [128, SS], F32)
        tmpb = sb("tmpb", [128, SS], F32)
        tmpc = sb("tmpc", [128, 512], F32)
        ones_b = sb("ones_b", [128, 1], BF16)
        onesrow_b = sb("onesrow_b", [1, 128], BF16)
        b3 = sb("b3", [128, 3 * KT], F32)
        m_row = sb("m_row", [1, CH], F32)
        v_row = sb("v_row", [1, CH], F32)
        vt_row = sb("vt_row", [1, CH], F32)
        mb_row = sb("mb_row", [1, CH], BF16)
        rb_row = sb("rb_row", [1, CH], BF16)
        cb = [sb(f"cb{k}", [128, SS], BF16) for k in range(KT)]
        inc_sb = sb("inc_sb", [128, KT], F32)
        zstage = sb("zstage", [128, SS], F32)

        PQ = [ps(f"PQ{i}", [128, 512]) for i in range(2)]
        PG = [ps(f"PG{i}", [128, 512]) for i in range(3)]

        warm = sb("warm", [1, 32], F32)
        w0 = P.op("vector", lambda E: E.memset(warm[:], 1.0))
        w1 = P.op("scalar", lambda E: E.activation(out=warm[:], in_=warm[:], func=AF.Square), deps=[w0])
        w2 = P.op("scalar", lambda E: E.activation(out=warm[:], in_=warm[:], func=AF.Ln), deps=[w1])
        w3 = P.op("scalar", lambda E: E.activation(out=warm[:], in_=warm[:], func=AF.Exp), deps=[w2])
        warm_tok = [w3]

        ldw = [P.dma("gpsimd", lambda E, k=k: E.dma_start(out=wp[k][:], in_=wpackT_d[k * 128:(k + 1) * 128, :]), ch="cast")
               for k in range(KT)]
        ld_ob = P.dma("gpsimd", lambda E: E.dma_start(out=ones_b[:], in_=ones_d[:]), ch="cast")
        ld_or = P.dma("gpsimd", lambda E: E.dma_start(
            out=onesrow_b[:], in_=ones_d[:, 0].unsqueeze(0).broadcast_to([1, 128])), ch="castp0")
        ld_b3 = [P.dma("sync", lambda E, g=g, o=o: E.dma_start(
            out=b3[:, g * KT + o:g * KT + o + 1], in_=bias3_d[g, o * 128:(o + 1) * 128].unsqueeze(1)))
            for g in range(3) for o in range(KT)]
        ld_inc = [P.dma("sync", lambda E, k=k: E.dma_start(
            out=inc_sb[:, k:k + 1], in_=inc_d[k * 128:(k + 1) * 128].unsqueeze(1))) for k in range(KT)]
        ld_c0 = [P.op("vector", lambda E, k=k: E.tensor_copy(
            out=ct_[k][:], in_=inc_sb[:, k:k + 1].broadcast_to([128, SS])), deps=[ld_inc[k]])
            for k in range(KT)]

        chunk_fence = []
        st_xn = []
        for c in range(8):
            tc = slice(c * CH, (c + 1) * CH)
            ld_x = [P.dma("gpsimd", lambda E, k=k, tc=tc: E.dma_start(
                out=xbc[k][:], in_=x0Ts_d[k * 128:(k + 1) * 128, tc]), deps=chunk_fence, ch="cast") for k in range(KT)]
            mm = None
            for k in range(KT):
                mm = P.op("tensor", lambda E, k=k: E.matmul(
                    PQ[0][0:1, :], ones_b[:], xbc[k][:], start=(k == 0), stop=(k == KT - 1)),
                    deps=(list(ld_x) + [ld_ob] + warm_tok + chunk_fence) if k == 0 else [])
            em0 = P.op("scalar", lambda E: E.copy(out=m_row[:], in_=PQ[0][0:1, :]), deps=[mm])
            em = P.op("vector", lambda E: E.tensor_scalar_mul(out=m_row[:], in0=m_row[:], scalar1=1.0 / D), deps=[em0])
            sq_toks = [P.op("scalar", lambda E, k=k: E.activation(out=t1bs[k][:], in_=xbc[k][:], func=AF.Square),
                           deps=[ld_x[k]] + chunk_fence) for k in range(KT)]
            mmq = None
            for k in range(KT):
                mmq = P.op("tensor", lambda E, k=k: E.matmul(
                    PQ[1][0:1, :], ones_b[:], t1bs[k][:], start=(k == 0), stop=(k == KT - 1)),
                    deps=([sq_toks[-1], ld_ob] + chunk_fence) if k == 0 else [])
            ev0 = P.op("scalar", lambda E: E.copy(out=v_row[:], in_=PQ[1][0:1, :]), deps=[mmq])
            ev = P.op("vector", lambda E: E.tensor_scalar_mul(out=v_row[:], in0=v_row[:], scalar1=1.0 / D), deps=[ev0])
            tv1 = P.op("vector", lambda E: E.tensor_mul(out=vt_row[:], in0=m_row[:], in1=m_row[:]), deps=[em])
            tv2 = P.op("vector", lambda E: E.tensor_sub(out=v_row[:], in0=v_row[:], in1=vt_row[:]), deps=[tv1, ev])
            tv3 = P.op("vector", lambda E: E.tensor_scalar_add(out=v_row[:], in0=v_row[:], scalar1=LN_EPS), deps=[tv2])
            tr1 = P.op("scalar", lambda E: E.activation(out=v_row[:], in_=v_row[:], func=AF.Ln), deps=[tv3])
            tr1b = P.op("vector", lambda E: E.tensor_scalar_mul(out=v_row[:], in0=v_row[:], scalar1=-0.5), deps=[tr1])
            tr2 = P.op("scalar", lambda E: E.activation(out=v_row[:], in_=v_row[:], func=AF.Exp), deps=[tr1b])
            cmb = P.op("vector", lambda E: E.tensor_copy(out=mb_row[:], in_=m_row[:]), deps=[em])
            crb = P.op("vector", lambda E: E.tensor_copy(out=rb_row[:], in_=v_row[:]), deps=[tr2])
            t_m = P.op("tensor", lambda E: E.matmul(PQ[0][:], onesrow_b[:], mb_row[:], start=True, stop=True),
                       deps=[cmb, ld_or, em0])
            t_r = P.op("tensor", lambda E: E.matmul(PQ[1][:], onesrow_b[:], rb_row[:], start=True, stop=True),
                       deps=[crb, ld_or, ev0])
            ap_toks = []
            for k in range(KT):
                a1 = P.op("vector", lambda E, k=k: E.tensor_tensor(out=tmpc[:], in0=xbc[k][:], in1=PQ[0][:],
                                                                  op=ALU.subtract),
                          deps=[t_m] + ([ap_toks[-1]] if ap_toks else [mmq]))
                a2 = P.op("vector", lambda E, k=k: E.tensor_tensor(out=xnc[k][:], in0=tmpc[:], in1=PQ[1][:],
                                                                  op=ALU.mult),
                          deps=[a1, t_r])
                ap_toks.append(a2)
            st_c = [P.dma("gpsimd", lambda E, k=k, tc=tc: E.dma_start(
                out=xns_scr_d[k * 128:(k + 1) * 128, tc], in_=xnc[k][:]), deps=[ap_toks[k]]) for k in range(KT)]
            st_xn += st_c
            chunk_fence = P.fence()

        GATES = 3
        upd_last = list(ld_c0) + chunk_fence
        cb_toks = []
        xld_prev = []
        ev_toks = []
        mm_toks = []
        for t_i in range(nsteps):
            jbuf = 0
            xld = [P.dma("gpsimd", lambda E, k=k, t_i=t_i, jbuf=jbuf: E.dma_start(
                out=xstep[jbuf][k][:], in_=xns_scr_d[k * 128:(k + 1) * 128, t_i * SS:(t_i + 1) * SS]),
                deps=(st_xn if t_i < 2 else []) + xld_prev + mm_toks[-1:]) for k in range(KT)]
            cb_toks = [P.op("vector", lambda E, k=k: E.tensor_copy(out=cb[k][:], in_=ct_[k][:]),
                            deps=upd_last + cb_toks[-1:]) for k in range(KT)]
            ev_base = len(ev_toks)
            for ot in range(KT):
                for g in range(GATES):
                    mm = None
                    for k in range(2 * KT):
                        if k < KT:
                            lhs = (lambda E, k=k, g=g, ot=ot, jbuf=jbuf: E.matmul(
                                PG[g][:, 0:SS], wp[k][:, g * D + ot * 128: g * D + (ot + 1) * 128],
                                xstep[jbuf][k][:], start=(k == 0), stop=False))
                        else:
                            kk = k - KT
                            lhs = (lambda E, kk=kk, g=g, ot=ot: E.matmul(
                                PG[g][:, 0:SS], wp[kk][:, (3 + g) * D + ot * 128: (3 + g) * D + (ot + 1) * 128],
                                cb[kk][:], start=False, stop=(kk == KT - 1)))
                        mm = P.op("tensor", lhs,
                                  deps=(list(xld) + [cb_toks[-1], ldw[-1]] + ev_toks[-3:]) if k == 0 else [])
                    mm_toks.append(mm)
                    func = AF.Sigmoid if g < 2 else AF.Tanh
                    tok = P.op("scalar", lambda E, g=g, ot=ot, func=func: E.activation(
                        out=(fts if g == 0 else its if g == 1 else gts)[ot][:], in_=PG[g][:, 0:SS],
                        func=func, bias=b3[:, g * KT + ot:g * KT + ot + 1]),
                        deps=[mm] + upd_last)
                    ev_toks.append(tok)
            upd = []
            for ot in range(KT):
                u1 = P.op("vector", lambda E, ot=ot: E.tensor_tensor(out=tmpa[:], in0=fts[ot][:], in1=ct_[ot][:],
                                                                    op=ALU.mult),
                          deps=[ev_toks[ev_base + ot * 3], upd[-1] if upd else None])
                u2 = P.op("vector", lambda E, ot=ot: E.tensor_tensor(out=tmpb[:], in0=its[ot][:], in1=gts[ot][:],
                                                                    op=ALU.mult),
                          deps=[ev_toks[ev_base + ot * 3 + 1], ev_toks[ev_base + ot * 3 + 2]])
                u3 = P.op("vector", lambda E, ot=ot: E.tensor_tensor(out=ct_[ot][:], in0=tmpa[:], in1=tmpb[:],
                                                                    op=ALU.add),
                          deps=[u1, u2])
                upd.append(u3)
            upd_last = [upd[-1]]
            xld_prev = xld

        cbf = [P.op("vector", lambda E, k=k: E.tensor_copy(out=cb[k][:], in_=ct_[k][:]), deps=upd_last + cb_toks[-1:])
               for k in range(KT)]
        if out_c == "f":
            stc = [P.dma("gpsimd", lambda E, k=k: E.dma_start(out=zT_d[k * 128:(k + 1) * 128, :], in_=fts[k][:]),
                         deps=upd_last + cb_toks[-1:]) for k in range(KT)]
            P.op("gpsimd", lambda E: E.engine_nop(), deps=stc)
            block = stack.enter_context(nc.Block())
            P.build(stack, block)
            return nc
        if out_c == "x":
            stc = [P.dma("gpsimd", lambda E, k=k: E.dma_start(out=zT_d[k * 128:(k + 1) * 128, :], in_=xstep[0][k][:]),
                         deps=upd_last + cb_toks[-1:]) for k in range(KT)]
            P.op("gpsimd", lambda E: E.engine_nop(), deps=stc)
            block = stack.enter_context(nc.Block())
            P.build(stack, block)
            return nc
        if out_c:
            stc = [P.dma("gpsimd", lambda E, k=k: E.dma_start(out=zT_d[k * 128:(k + 1) * 128, :], in_=ct_[k][:]),
                         deps=upd_last + cb_toks[-1:]) for k in range(KT)]
            P.op("gpsimd", lambda E: E.engine_nop(), deps=stc)
            block = stack.enter_context(nc.Block())
            P.build(stack, block)
            return nc
        z_toks = []
        pqi_free = [[], []]
        for ot in range(KT):
            mm = None
            for k in range(KT):
                mm = P.op("tensor", lambda E, ot=ot, k=k: E.matmul(
                    PG[0][:, 0:SS], wp[k][:, 6 * D + ot * 128: 6 * D + (ot + 1) * 128], cb[k][:],
                    start=(k == 0), stop=(k == KT - 1)),
                    deps=([cbf[-1], ldw[-1]] + z_toks[-1:] + ev_toks[-3:]) if k == 0 else [])
            evz = P.op("scalar", lambda E, ot=ot: E.copy(out=zstage[:, 0:SS], in_=PG[0][:, 0:SS]), deps=[mm])
            stz = P.dma("gpsimd", lambda E, ot=ot: E.dma_start(
                out=zT_d[ot * 128:(ot + 1) * 128, :], in_=zstage[:, 0:SS]), deps=[evz])
            z_toks.append(stz)
        P.op("gpsimd", lambda E: E.engine_nop(), deps=z_toks)
        block = stack.enter_context(nc.Block())
        P.build(stack, block)
    return nc


def _prep_l0_scan_inputs(inputs):
    x = np.asarray(inputs["L0_attn_x"])
    wpackT = np.ascontiguousarray(np.concatenate(
        [np.asarray(inputs[f"L0_ffn_{n}"]).T for n in ("wf", "wi", "wc", "uf", "ui", "uc", "uo")], axis=1))
    bias3 = np.ascontiguousarray(np.stack([np.asarray(inputs["L0_ffn_bf"]),
                                           np.asarray(inputs["L0_ffn_bi"]),
                                           np.asarray(inputs["L0_ffn_bc"])]))
    incv = np.ascontiguousarray(np.asarray(inputs["L0_ffn_in_c"]))
    ones = np.ones((128, 1), dtype=np.float32)
    in_maps = []
    for c in range(NCORES):
        sl = slice(c * SS, (c + 1) * SS)
        x0Ts = np.ascontiguousarray(np.concatenate([x[b, sl, :].T for b in range(NB)], axis=1))
        in_maps.append({"x0Ts": x0Ts, "wpackT": wpackT, "bias3": bias3, "incv": incv, "ones": ones})
    return in_maps


def run_l0_scan(inputs, nsteps=NB, out_c=False):
    nc = build_l0_scan(nsteps=nsteps, out_c=out_c)
    in_maps = _prep_l0_scan_inputs(inputs)
    res = run_bass_kernel_spmd(nc, in_maps, core_ids=list(range(NCORES)))
    return res.results



def build_x1_ln1():
    nc = bass.Bass("TRN2", target_bir_lowering=False)
    TT_ = B2 * T
    x0T_d = nc.dram_tensor("x0T", [D, TT_], F32, kind="ExternalInput").ap()
    attn0T_d = nc.dram_tensor("attn0T", [D, TT_], F32, kind="ExternalInput").ap()
    xn0T_d = nc.dram_tensor("xn0T", [D, TT_], BF16, kind="ExternalInput").ap()
    zT_d = nc.dram_tensor("zTfull", [D, T], F32, kind="ExternalInput").ap()
    woT_d = nc.dram_tensor("woT", [D, D], F32, kind="ExternalInput").ap()
    bo_d = nc.dram_tensor("bo", [D], F32, kind="ExternalInput").ap()
    ones_d = nc.dram_tensor("ones", [128, 1], F32, kind="ExternalInput").ap()
    x1T_d = nc.dram_tensor("x1T", [D, TT_], F32, kind="ExternalOutput").ap()
    xn1T_d = nc.dram_tensor("xn1T", [D, TT_], F32, kind="ExternalOutput").ap()

    P = Prog(nc)
    with ExitStack() as stack:
        def sb(name, shape, dt):
            return stack.enter_context(nc.sbuf_tensor(name, shape, dt))

        def ps(name, shape, dt=F32):
            return stack.enter_context(nc.psum_tensor(name, shape, dt))

        CH = 512
        wo = [sb(f"wo{k}", [128, D], BF16) for k in range(KT)]
        zs = [sb(f"zs{k}", [128, T], F32) for k in range(KT)]
        xn0c = [sb(f"xn0c{k}", [128, CH], BF16) for k in range(KT)]
        x0c = [sb(f"x0c{k}", [128, CH], F32) for k in range(KT)]
        a0c = [sb(f"a0c{k}", [128, CH], F32) for k in range(KT)]
        x1c = [sb(f"x1c{k}", [128, CH], F32) for k in range(KT)]
        xbc = [sb(f"xbc{k}", [128, CH], BF16) for k in range(KT)]
        t1bs = [sb(f"t1b{k}", [128, CH], BF16) for k in range(KT)]
        xnc = [sb(f"xnc{k}", [128, CH], F32) for k in range(KT)]
        tmp = sb("tmp", [128, CH], F32)
        ht = sb("ht", [128, CH], F32)
        bo_s = sb("bo_s", [128, KT], F32)
        ones_b = sb("ones_b", [128, 1], BF16)
        onesrow_b = sb("onesrow_b", [1, 128], BF16)
        m_row = sb("m_row", [1, CH], F32)
        v_row = sb("v_row", [1, CH], F32)
        vt_row = sb("vt_row", [1, CH], F32)
        mb_row = sb("mb_row", [1, CH], BF16)
        rb_row = sb("rb_row", [1, CH], BF16)
        PH = [ps(f"PH{i}", [128, 512]) for i in range(2)]
        PQ = [ps(f"PQ{i}", [128, 512]) for i in range(2)]

        warm = sb("warm", [1, 32], F32)
        w0 = P.op("vector", lambda E: E.memset(warm[:], 1.0))
        w1 = P.op("scalar", lambda E: E.activation(out=warm[:], in_=warm[:], func=AF.Square), deps=[w0])
        w2 = P.op("scalar", lambda E: E.activation(out=warm[:], in_=warm[:], func=AF.Ln), deps=[w1])
        w3 = P.op("scalar", lambda E: E.activation(out=warm[:], in_=warm[:], func=AF.Tanh), deps=[w2])
        w4 = P.op("scalar", lambda E: E.activation(out=warm[:], in_=warm[:], func=AF.Exp), deps=[w3])
        warm_tok = [w4]

        ldw = [P.dma("gpsimd", lambda E, k=k: E.dma_start(out=wo[k][:], in_=woT_d[k * 128:(k + 1) * 128, :]), ch="cast")
               for k in range(KT)]
        ldz = [P.dma("sync", lambda E, k=k: E.dma_start(out=zs[k][:], in_=zT_d[k * 128:(k + 1) * 128, :]))
               for k in range(KT)]
        ld_ob = P.dma("gpsimd", lambda E: E.dma_start(out=ones_b[:], in_=ones_d[:]), ch="cast")
        ld_or = P.dma("gpsimd", lambda E: E.dma_start(
            out=onesrow_b[:], in_=ones_d[:, 0].unsqueeze(0).broadcast_to([1, 128])), ch="castp0")
        ld_bo = [P.dma("sync", lambda E, o=o: E.dma_start(
            out=bo_s[:, o:o + 1], in_=bo_d[o * 128:(o + 1) * 128].unsqueeze(1))) for o in range(KT)]

        chunk_fence = [ldw[-1], ldz[-1], ld_ob, ld_or] + ld_bo + warm_tok
        out_toks = []
        for c in range(8):
            bi = c // 4
            tc = slice(c * CH, (c + 1) * CH)
            zc = slice((c % 4) * CH, (c % 4 + 1) * CH)
            ld_xn0 = [P.dma("gpsimd", lambda E, k=k, tc=tc: E.dma_start(
                out=xn0c[k][:], in_=xn0T_d[k * 128:(k + 1) * 128, tc]), deps=chunk_fence) for k in range(KT)]
            ld_x0 = [P.dma("sync", lambda E, k=k, tc=tc: E.dma_start(
                out=x0c[k][:], in_=x0T_d[k * 128:(k + 1) * 128, tc]), deps=chunk_fence) for k in range(KT)]
            ld_a0 = [P.dma("sync", lambda E, k=k, tc=tc: E.dma_start(
                out=a0c[k][:], in_=attn0T_d[k * 128:(k + 1) * 128, tc]), deps=chunk_fence) for k in range(KT)]
            x1_toks = []
            phi_free = [list(chunk_fence), list(chunk_fence)]
            for ot in range(KT):
                phi = ot % 2
                mm = None
                for k in range(KT):
                    mm = P.op("tensor", lambda E, ot=ot, k=k, phi=phi: E.matmul(
                        PH[phi][:], wo[k][:, ot * 128:(ot + 1) * 128], xn0c[k][:],
                        start=(k == 0), stop=(k == KT - 1)),
                        deps=(list(ld_xn0) + [ldw[-1]] + warm_tok + phi_free[phi]) if k == 0 else [])
                tz = P.op("vector", lambda E, ot=ot, zc=zc, phi=phi: E.tensor_tensor(
                    out=tmp[:], in0=PH[phi][:], in1=zs[ot][:, zc], op=ALU.add),
                    deps=[mm, ldz[-1], (x1_toks[-1][0] if x1_toks else None),
                          (phi_free[0][0] if phi_free[0] else None),
                          (phi_free[1][0] if phi_free[1] else None)] + chunk_fence)
                th = P.op("scalar", lambda E, ot=ot: E.activation(
                    out=ht[:], in_=tmp[:], func=AF.Tanh, bias=bo_s[:, ot:ot + 1]),
                    deps=[tz, (x1_toks[-1][0] if x1_toks else None)] + ld_bo + chunk_fence)
                phi_free[phi] = [th]
                a1 = P.op("vector", lambda E, ot=ot: E.tensor_tensor(
                    out=x1c[ot][:], in0=x0c[ot][:], in1=a0c[ot][:], op=ALU.add),
                    deps=[ld_x0[ot], ld_a0[ot]] + chunk_fence)
                a2 = P.op("vector", lambda E, ot=ot: E.tensor_tensor(
                    out=x1c[ot][:], in0=x1c[ot][:], in1=ht[:], op=ALU.add), deps=[a1, th])
                st1 = P.dma("gpsimd", lambda E, ot=ot, tc=tc: E.dma_start(
                    out=x1T_d[ot * 128:(ot + 1) * 128, tc], in_=x1c[ot][:]), deps=[a2])
                cb = P.op("vector", lambda E, ot=ot: E.tensor_copy(out=xbc[ot][:], in_=x1c[ot][:]),
                          deps=[a2] + chunk_fence)
                x1_toks.append((a2, cb, st1))
                out_toks.append(st1)
            cps = [t[1] for t in x1_toks]
            mm = None
            for k in range(KT):
                mm = P.op("tensor", lambda E, k=k: E.matmul(
                    PQ[0][0:1, :], ones_b[:], xbc[k][:], start=(k == 0), stop=(k == KT - 1)),
                    deps=([cps[-1], ld_ob] + chunk_fence) if k == 0 else [])
            em0 = P.op("scalar", lambda E: E.copy(out=m_row[:], in_=PQ[0][0:1, :]), deps=[mm])
            em = P.op("vector", lambda E: E.tensor_scalar_mul(out=m_row[:], in0=m_row[:], scalar1=1.0 / D), deps=[em0])
            sq_toks = [P.op("scalar", lambda E, k=k: E.activation(out=t1bs[k][:], in_=xbc[k][:], func=AF.Square),
                           deps=[cps[k]] + chunk_fence) for k in range(KT)]
            mmq = None
            for k in range(KT):
                mmq = P.op("tensor", lambda E, k=k: E.matmul(
                    PQ[1][0:1, :], ones_b[:], t1bs[k][:], start=(k == 0), stop=(k == KT - 1)),
                    deps=([sq_toks[-1], ld_ob] + chunk_fence) if k == 0 else [])
            ev0 = P.op("scalar", lambda E: E.copy(out=v_row[:], in_=PQ[1][0:1, :]), deps=[mmq])
            ev = P.op("vector", lambda E: E.tensor_scalar_mul(out=v_row[:], in0=v_row[:], scalar1=1.0 / D), deps=[ev0])
            tv1 = P.op("vector", lambda E: E.tensor_mul(out=vt_row[:], in0=m_row[:], in1=m_row[:]), deps=[em])
            tv2 = P.op("vector", lambda E: E.tensor_sub(out=v_row[:], in0=v_row[:], in1=vt_row[:]), deps=[tv1, ev])
            tv3 = P.op("vector", lambda E: E.tensor_scalar_add(out=v_row[:], in0=v_row[:], scalar1=LN_EPS), deps=[tv2])
            tr1 = P.op("scalar", lambda E: E.activation(out=v_row[:], in_=v_row[:], func=AF.Ln), deps=[tv3])
            tr1b = P.op("vector", lambda E: E.tensor_scalar_mul(out=v_row[:], in0=v_row[:], scalar1=-0.5), deps=[tr1])
            tr2 = P.op("scalar", lambda E: E.activation(out=v_row[:], in_=v_row[:], func=AF.Exp), deps=[tr1b])
            cmb = P.op("vector", lambda E: E.tensor_copy(out=mb_row[:], in_=m_row[:]), deps=[em])
            crb = P.op("vector", lambda E: E.tensor_copy(out=rb_row[:], in_=v_row[:]), deps=[tr2])
            t_m = P.op("tensor", lambda E: E.matmul(PQ[0][:], onesrow_b[:], mb_row[:], start=True, stop=True),
                       deps=[cmb, ld_or, em0])
            t_r = P.op("tensor", lambda E: E.matmul(PQ[1][:], onesrow_b[:], rb_row[:], start=True, stop=True),
                       deps=[crb, ld_or, ev0])
            ap_toks = []
            for k in range(KT):
                a1 = P.op("vector", lambda E, k=k: E.tensor_tensor(out=tmp[:], in0=x1c[k][:], in1=PQ[0][:],
                                                                  op=ALU.subtract),
                          deps=[t_m, x1_toks[k][0]] + ([ap_toks[-1]] if ap_toks else [mmq]) + [phi_free[0][0], phi_free[1][0]])
                a2 = P.op("vector", lambda E, k=k: E.tensor_tensor(out=xnc[k][:], in0=tmp[:], in1=PQ[1][:],
                                                                  op=ALU.mult),
                          deps=[a1, t_r])
                ap_toks.append(a2)
            st_xn = [P.dma("gpsimd", lambda E, k=k, tc=tc: E.dma_start(
                out=xn1T_d[k * 128:(k + 1) * 128, tc], in_=xnc[k][:]), deps=[ap_toks[k]]) for k in range(KT)]
            out_toks += st_xn
            chunk_fence = P.fence()

        P.op("gpsimd", lambda E: E.engine_nop(), deps=out_toks)
        block = stack.enter_context(nc.Block())
        P.build(stack, block)
    return nc


def run_x1_ln1(inputs, attn0T_list, xn0T_list, zT_full):
    nc = build_x1_ln1()
    x = np.asarray(inputs["L0_attn_x"])
    woT = np.ascontiguousarray(np.asarray(inputs["L0_ffn_wo"]).T)
    bo = np.ascontiguousarray(np.asarray(inputs["L0_ffn_bo"]))
    ones = np.ones((128, 1), dtype=np.float32)
    in_maps = []
    for c in range(NCORES):
        x0T = np.ascontiguousarray(np.concatenate([x[2 * c].T, x[2 * c + 1].T], axis=1))
        in_maps.append({"x0T": x0T, "attn0T": attn0T_list[c], "xn0T": xn0T_list[c],
                        "zTfull": zT_full, "woT": woT, "bo": bo, "ones": ones})
    res = run_bass_kernel_spmd(nc, in_maps, core_ids=list(range(NCORES)))
    return res.results



def build_l1_attn(dbg=False, nch=None):
    nc = bass.Bass("TRN2", target_bir_lowering=False)
    TT_ = B2 * T
    NSt = 16
    CH = 512
    NCH = (T // CH) if nch is None else nch
    xn1T_d = nc.dram_tensor("xn1T", [D, TT_], F32, kind="ExternalInput").ap()
    xprojT_d = nc.dram_tensor("xprojT", [D, 96], F32, kind="ExternalInput").ap()
    dtprojT_d = nc.dram_tensor("dtprojT", [64, D], F32, kind="ExternalInput").ap()
    dtb_d = nc.dram_tensor("dtb", [D], F32, kind="ExternalInput").ap()
    alog_d = nc.dram_tensor("alog", [D, NSt], F32, kind="ExternalInput").ap()
    dvec_d = nc.dram_tensor("dvec", [D], F32, kind="ExternalInput").ap()
    y1T_d = nc.dram_tensor("y1T", [D, TT_], F32, kind="ExternalOutput").ap()
    delta_scr_d = nc.dram_tensor("delta_scr", [D, T], F32, kind=("ExternalOutput" if dbg else "Internal")).ap()
    xp_dbg_d = nc.dram_tensor("xp_dbg", [96, T], F32, kind="ExternalOutput").ap() if dbg else None
    br_dbg_d = nc.dram_tensor("br_dbg", [128, 16 * 512], BF16, kind="ExternalOutput").ap() if dbg else None
    da_dbg_d = nc.dram_tensor("da_dbg", [128, 16 * 512], F32, kind="ExternalOutput").ap() if dbg else None
    h_dbg_d = nc.dram_tensor("h_dbg", [128, 16 * 512], F32, kind="ExternalOutput").ap() if dbg else None
    bu_dbg_d = nc.dram_tensor("bu_dbg", [128, 16 * 512], F32, kind="ExternalOutput").ap() if dbg else None
    hc_dbg_d = nc.dram_tensor("hc_dbg", [128, 16], F32, kind="ExternalOutput").ap() if dbg else None
    zs_dbg_d = nc.dram_tensor("zs_dbg", [128, 16 * 512], BF16, kind="ExternalOutput").ap() if dbg else None
    yc_dbg_d = nc.dram_tensor("yc_dbg", [128, 512], F32, kind="ExternalOutput").ap() if dbg else None
    cr_dbg_d = nc.dram_tensor("cr_dbg", [128, 16 * 512], BF16, kind="ExternalOutput").ap() if dbg else None
    bc_scr_d = nc.dram_tensor("bc_scr", [2, NSt * CH], F32, kind="Internal").ap()

    P = Prog(nc)
    with ExitStack() as stack:
        def sb(name, shape, dt):
            return stack.enter_context(nc.sbuf_tensor(name, shape, dt))

        def ps(name, shape, dt=F32):
            return stack.enter_context(nc.psum_tensor(name, shape, dt))

        xbc = [sb(f"xbc{k}", [128, CH], BF16) for k in range(KT)]
        xprj = sb("xprj", [128, 96], BF16)
        xprjs = [sb(f"xprj{k}", [128, 96], BF16) for k in range(KT)]
        dtp = sb("dtp", [64, D], BF16)
        dtb_s = sb("dtb_s", [128, KT], F32)
        xp_sb = sb("xp_sb", [96, T], F32)
        drb = sb("drb", [64, T], BF16)
        A_s = [sb(f"A{k}", [128, NSt], F32) for k in range(KT)]
        D_s = sb("D_s", [128, KT], F32)
        dc2 = [sb(f"dc{i}", [128, CH], F32) for i in range(2)]
        xc2 = [sb(f"xc{i}", [128, CH], F32) for i in range(2)]
        uc2 = [sb(f"uc{i}", [128, CH], F32) for i in range(2)]
        dA = sb("dA", [128, NSt * CH], F32)
        dBu = sb("dBu", [128, NSt * CH], F32)
        hbuf = sb("hbuf", [128, NSt * CH], F32)
        zb2 = [sb(f"zb{i}", [128, NSt * CH], BF16) for i in range(2)]
        Brep = sb("Brep", [128, NSt * CH], BF16)
        Crep = sb("Crep", [128, NSt * CH], BF16)
        hcarry = [sb(f"hc{k}", [128, NSt], F32) for k in range(KT)]
        y1c2 = [sb(f"y1c{i}", [128, CH], F32) for i in range(2)]
        sp1 = sb("sp1", [128, CH], F32)
        sp2 = sb("sp2", [128, CH], F32)
        PXP = ps("PXP", [96, 512])
        PDT = ps("PDT", [128, 512])

        warm = sb("warm", [1, 32], F32)
        w0 = P.op("vector", lambda E: E.memset(warm[:], 1.0))
        w1 = P.op("scalar", lambda E: E.activation(out=warm[:], in_=warm[:], func=AF.Abs), deps=[w0])
        w2 = P.op("scalar", lambda E: E.activation(out=warm[:], in_=warm[:], func=AF.Ln), deps=[w1])
        w3 = P.op("scalar", lambda E: E.activation(out=warm[:], in_=warm[:], func=AF.Exp), deps=[w2])
        warm_tok = [w3]

        ld_xp = [P.dma("gpsimd", lambda E, k=k: E.dma_start(
            out=xprjs[k][:], in_=xprojT_d[k * 128:(k + 1) * 128, :]), ch="cast") for k in range(KT)]
        ld_dtp = P.dma("gpsimd", lambda E: E.dma_start(out=dtp[:], in_=dtprojT_d[:]), ch="castn0")
        ld_dtb = [P.dma("sync", lambda E, o=o: E.dma_start(
            out=dtb_s[:, o:o + 1], in_=dtb_d[o * 128:(o + 1) * 128].unsqueeze(1))) for o in range(KT)]
        ld_al = [P.dma("sync", lambda E, k=k: E.dma_start(
            out=A_s[k][:], in_=alog_d[k * 128:(k + 1) * 128, :])) for k in range(KT)]
        ld_dv = [P.dma("sync", lambda E, o=o: E.dma_start(
            out=D_s[:, o:o + 1], in_=dvec_d[o * 128:(o + 1) * 128].unsqueeze(1))) for o in range(KT)]
        load_fence = ld_xp + [ld_dtp] + ld_dtb + ld_al + ld_dv + warm_tok
        a_toks = []
        for k in range(KT):
            t1 = P.op("scalar", lambda E, k=k: E.activation(out=A_s[k][:], in_=A_s[k][:], func=AF.Exp),
                      deps=load_fence)
            t2 = P.op("vector", lambda E, k=k: E.tensor_scalar_mul(out=A_s[k][:], in0=A_s[k][:], scalar1=-1.0),
                      deps=[t1])
            a_toks.append(t2)

        out_toks = []
        batch_fence = list(load_fence) + a_toks
        for bi in range(B2):
            xp_toks = []
            pxp_free = list(batch_fence)
            for c in range(NCH):
                tc = slice(bi * T + c * CH, bi * T + (c + 1) * CH)
                ld_x = [P.dma("gpsimd", lambda E, k=k, tc=tc: E.dma_start(
                    out=xbc[k][:], in_=xn1T_d[k * 128:(k + 1) * 128, tc]),
                    deps=batch_fence + xp_toks[-1:], ch="cast") for k in range(KT)]
                mm = None
                for k in range(KT):
                    mm = P.op("tensor", lambda E, k=k: E.matmul(
                        PXP[:], xprjs[k][:], xbc[k][:], start=(k == 0), stop=(k == KT - 1)),
                        deps=(list(ld_x) + pxp_free) if k == 0 else [])
                evt = P.op("scalar", lambda E, c=c: E.copy(out=xp_sb[:, c * CH:(c + 1) * CH], in_=PXP[:]),
                           deps=[mm] + batch_fence)
                pxp_free = [evt]
                xp_toks.append(evt)
            if dbg and bi == 0:
                dx = P.dma("gpsimd", lambda E: E.dma_start(out=xp_dbg_d[:], in_=xp_sb[:]), deps=xp_toks, ch="n0")
                out_toks.append(dx)
            cdr = P.op("vector", lambda E: E.tensor_copy(out=drb[:], in_=xp_sb[0:64, :]), deps=xp_toks + batch_fence)
            del_toks = []
            pdt_free = list(batch_fence)
            for ot in range(KT):
                for c in range(NCH):
                    mm = P.op("tensor", lambda E, ot=ot, c=c: E.matmul(
                        PDT[:], dtp[:, ot * 128:(ot + 1) * 128], drb[:, c * CH:(c + 1) * CH],
                        start=True, stop=True), deps=([cdr] + pdt_free))
                    s0 = P.op("vector", lambda E, ot=ot: E.tensor_scalar_add(
                        out=sp1[:], in0=PDT[:], scalar1=dtb_s[:, ot:ot + 1]), deps=[mm] + del_toks[-1:])
                    pdt_free = [s0]
                    sa = P.op("scalar", lambda E: E.activation(out=sp2[:], in_=sp1[:], func=AF.Abs), deps=[s0] + del_toks[-1:])
                    sn = P.op("vector", lambda E: E.tensor_scalar_mul(out=sp2[:], in0=sp2[:], scalar1=-1.0), deps=[sa])
                    se = P.op("scalar", lambda E: E.activation(out=sp2[:], in_=sp2[:], func=AF.Exp), deps=[sn])
                    s1 = P.op("vector", lambda E: E.tensor_scalar_add(out=sp2[:], in0=sp2[:], scalar1=1.0), deps=[se])
                    sl = P.op("scalar", lambda E: E.activation(out=sp2[:], in_=sp2[:], func=AF.Ln), deps=[s1])
                    sr = P.op("vector", lambda E: E.tensor_scalar_max(out=sp1[:], in0=sp1[:], scalar1=0.0), deps=[sa])
                    sd = P.op("vector", lambda E: E.tensor_add(out=sp1[:], in0=sp1[:], in1=sp2[:]), deps=[sl, sr])
                    std = P.dma("gpsimd", lambda E, ot=ot, c=c: E.dma_start(
                        out=delta_scr_d[ot * 128:(ot + 1) * 128, c * CH:(c + 1) * CH], in_=sp1[:]), deps=[sd])
                    del_toks.append(std)
            rep_toks = []
            scan_fence = list(del_toks)
            prev_dma = {0: [], 1: []}
            y_toks = []
            for c in range(NCH):
                prel = P.fence()
                b1 = P.dma("gpsimd", lambda E, c=c: E.dma_start(
                    out=bc_scr_d[0, :].rearrange("(n t) -> n t", t=CH), in_=xp_sb[64:80, c * CH:(c + 1) * CH]),
                    deps=xp_toks + prel, ch="n64")
                b2 = P.dma("gpsimd", lambda E, c=c: E.dma_start(
                    out=bc_scr_d[1, :].rearrange("(n t) -> n t", t=CH), in_=xp_sb[80:96, c * CH:(c + 1) * CH]),
                    deps=xp_toks + prel, ch="n64")
                b3 = P.dma("gpsimd", lambda E: E.dma_start(
                    out=Brep[:], in_=bc_scr_d[0:1, :].broadcast_to([128, NSt * CH])), deps=[b1, b2] + prel,
                    ch="bcast")
                b4 = P.dma("gpsimd", lambda E: E.dma_start(
                    out=Crep[:], in_=bc_scr_d[1:2, :].broadcast_to([128, NSt * CH])), deps=[b1, b2] + prel,
                    ch="bcast")
                rep_done = [b3, b4]
                dt_toks = []
                for dt in range(KT):
                    pb = (c * KT + dt) % 2
                    dc, xc, uc = dc2[pb], xc2[pb], uc2[pb]
                    tc = slice(bi * T + c * CH, bi * T + (c + 1) * CH)
                    l1 = P.dma("sync", lambda E, dt=dt, c=c, dc=dc: E.dma_start(
                        out=dc[:], in_=delta_scr_d[dt * 128:(dt + 1) * 128, c * CH:(c + 1) * CH]),
                        deps=scan_fence + list(dt_toks))
                    l2 = P.dma("sync", lambda E, dt=dt, tc=tc, xc=xc: E.dma_start(
                        out=xc[:], in_=xn1T_d[dt * 128:(dt + 1) * 128, tc]), deps=scan_fence + list(dt_toks))
                    tu = P.op("vector", lambda E, dc=dc, xc=xc, uc=uc: E.tensor_mul(out=uc[:], in0=dc[:], in1=xc[:]),
                              deps=[l1, l2] + list(dt_toks))
                    targ = P.op("vector", lambda E, dt=dt, dc=dc: E.tensor_tensor(
                        out=dA[:].rearrange("p (n t) -> p n t", n=NSt),
                        in0=dc[:].unsqueeze(1).broadcast_to([128, NSt, CH]),
                        in1=A_s[dt][:].unsqueeze(2).broadcast_to([128, NSt, CH]), op=ALU.mult),
                        deps=[l1] + list(dt_toks) + a_toks)
                    texp = P.op("scalar", lambda E: E.activation(out=dA[:], in_=dA[:], func=AF.Exp), deps=[targ])
                    tbu = P.op("vector", lambda E, uc=uc: E.tensor_tensor(
                        out=dBu[:].rearrange("p (n t) -> p n t", n=NSt),
                        in0=uc[:].unsqueeze(1).broadcast_to([128, NSt, CH]),
                        in1=Brep[:].rearrange("p (n t) -> p n t", n=NSt), op=ALU.mult),
                        deps=[tu] + rep_done + list(dt_toks))
                    sc_toks = []
                    fold_toks = []
                    if c > 0:
                        for n in range(NSt):
                            fold_toks.append(P.op("vector", lambda E, n=n, dt=dt: E.scalar_tensor_tensor(
                                out=dBu[:, n * CH:n * CH + 1], in0=dA[:, n * CH:n * CH + 1],
                                scalar=hcarry[dt][:, n:n + 1], in1=dBu[:, n * CH:n * CH + 1],
                                op0=ALU.mult, op1=ALU.add), deps=[texp, tbu]))
                    for n in range(NSt):
                        sc_toks.append(P.op("vector", lambda E, n=n: E.tensor_tensor_scan(
                            out=hbuf[:, n * CH:(n + 1) * CH], data0=dA[:, n * CH:(n + 1) * CH],
                            data1=dBu[:, n * CH:(n + 1) * CH], initial=0.0, op0=ALU.mult, op1=ALU.add),
                            deps=[texp, tbu] + fold_toks[-1:] + (dt_toks[-1:] if c == 0 else [])))
                    hc = P.op("vector", lambda E, dt=dt: E.tensor_copy(
                        out=hcarry[dt][:].unsqueeze(2),
                        in_=hbuf[:].rearrange("p (n t) -> p n t", n=NSt)[:, :, CH - 1:CH]), deps=sc_toks)
                    ybi0 = (c * KT + dt) % 2
                    zb = zb2[ybi0]
                    tz = P.op("vector", lambda E, zb=zb: E.tensor_tensor(out=zb[:], in0=hbuf[:], in1=Crep[:], op=ALU.mult),
                              deps=sc_toks + rep_done)
                    f1 = P.op("vector", lambda E, zb=zb: E.tensor_tensor(
                        out=zb[:, 0:8 * CH], in0=zb[:, 0:8 * CH], in1=zb[:, 8 * CH:16 * CH], op=ALU.add), deps=[tz])
                    f2 = P.op("vector", lambda E, zb=zb: E.tensor_tensor(
                        out=zb[:, 0:4 * CH], in0=zb[:, 0:4 * CH], in1=zb[:, 4 * CH:8 * CH], op=ALU.add), deps=[f1])
                    f3 = P.op("vector", lambda E, zb=zb: E.tensor_tensor(
                        out=zb[:, 0:2 * CH], in0=zb[:, 0:2 * CH], in1=zb[:, 2 * CH:4 * CH], op=ALU.add), deps=[f2])
                    f4 = P.op("vector", lambda E, zb=zb: E.tensor_tensor(
                        out=zb[:, 0:CH], in0=zb[:, 0:CH], in1=zb[:, CH:2 * CH], op=ALU.add), deps=[f3])
                    ybi = (c * KT + dt) % 2
                    ty = P.op("vector", lambda E, dt=dt, ybi=ybi, zb=zb, xc=xc: E.scalar_tensor_tensor(
                        out=y1c2[ybi][:], in0=xc[:], scalar=D_s[:, dt:dt + 1], in1=zb[:, 0:CH],
                        op0=ALU.mult, op1=ALU.add), deps=[f4, prev_dma[ybi][0] if prev_dma[ybi] else None])
                    sty = P.dma("gpsimd", lambda E, dt=dt, tc=tc, ybi=ybi: E.dma_start(
                        out=y1T_d[dt * 128:(dt + 1) * 128, tc], in_=y1c2[ybi][:]), deps=[ty])
                    if dbg and bi == 0 and c == 0 and dt == KT - 1:
                        dyc = P.dma("sync", lambda E: E.dma_start(out=yc_dbg_d[:], in_=y1c[:]), deps=[ty])
                        out_toks.append(dyc)
                    prev_dma[ybi] = [sty]
                    out_toks.append(sty)
                    dt_toks = [ty, texp, tbu, hc]
                rep_toks = dt_toks
            batch_fence = P.fence()

        if dbg:
            d1 = P.dma("gpsimd", lambda E: E.dma_start(out=br_dbg_d[:], in_=Brep[:]), deps=P.fence())
            d2 = P.dma("gpsimd", lambda E: E.dma_start(out=da_dbg_d[:], in_=dA[:]), deps=P.fence())
            d3 = P.dma("gpsimd", lambda E: E.dma_start(out=h_dbg_d[:], in_=hbuf[:]), deps=P.fence())
            d4 = P.dma("gpsimd", lambda E: E.dma_start(out=bu_dbg_d[:], in_=dBu[:]), deps=P.fence())
            d5 = P.dma("gpsimd", lambda E: E.dma_start(out=hc_dbg_d[:], in_=hcarry[KT - 1][:]), deps=P.fence())
            d6 = P.dma("gpsimd", lambda E: E.dma_start(out=zs_dbg_d[:], in_=zb[:]), deps=P.fence())
            d7 = P.dma("gpsimd", lambda E: E.dma_start(out=cr_dbg_d[:], in_=Crep[:]), deps=P.fence())
            out_toks += [d4, d5, d6, d7]
            out_toks += [d1, d2, d3]
        P.op("gpsimd", lambda E: E.engine_nop(), deps=out_toks)
        block = stack.enter_context(nc.Block())
        P.build(stack, block)
    return nc


def run_l1_attn(inputs, xn1T_list, dbg=False, nch=None):
    nc = build_l1_attn(dbg=dbg, nch=nch)
    xprojT = np.ascontiguousarray(np.asarray(inputs["L1_attn_x_proj_w"]).T)
    dtprojT = np.ascontiguousarray(np.asarray(inputs["L1_attn_dt_proj_w"]).T)
    dtb = np.ascontiguousarray(np.asarray(inputs["L1_attn_dt_proj_b"]))
    alog = np.ascontiguousarray(np.asarray(inputs["L1_attn_A_log"]))
    dvec = np.ascontiguousarray(np.asarray(inputs["L1_attn_D"]))
    in_maps = [{"xn1T": xn1T_list[c], "xprojT": xprojT, "dtprojT": dtprojT,
                "dtb": dtb, "alog": alog, "dvec": dvec} for c in range(NCORES)]
    res = run_bass_kernel_spmd(nc, in_maps, core_ids=list(range(NCORES)))
    return res.results



def build_l1_ffn():
    nc = bass.Bass("TRN2", target_bir_lowering=False)
    TT_ = B2 * T
    IM = 2816
    OT_I = IM // 128
    xn1R_d = nc.dram_tensor("xn1R", [TT_, D], F32, kind="ExternalInput").ap()
    wg_d = nc.dram_tensor("wg", [IM, D], F32, kind="ExternalInput").ap()
    wu_d = nc.dram_tensor("wu", [IM, D], F32, kind="ExternalInput").ap()
    wd_d = nc.dram_tensor("wd", [D, IM], F32, kind="ExternalInput").ap()
    x1T_d = nc.dram_tensor("x1T", [D, TT_], F32, kind="ExternalInput").ap()
    y1T_d = nc.dram_tensor("y1T", [D, TT_], F32, kind="ExternalInput").ap()
    x2T_d = nc.dram_tensor("x2T", [D, TT_], F32, kind="ExternalOutput").ap()
    xq_scr_d = nc.dram_tensor("xq_scr", [TT_, D], BF16, kind="Internal").ap()
    wgq_scr_d = nc.dram_tensor("wgq_scr", [IM, D], BF16, kind="Internal").ap()
    wuq_scr_d = nc.dram_tensor("wuq_scr", [IM, D], BF16, kind="Internal").ap()
    wdq_scr_d = nc.dram_tensor("wdq_scr", [D, IM], BF16, kind="Internal").ap()
    gs_scr_d = nc.dram_tensor("gs_scr", [IM, TT_], BF16, kind="Internal").ap()
    h_scr_d = nc.dram_tensor("h_scr", [IM, TT_], BF16, kind="Internal").ap()
    hq_scr_d = nc.dram_tensor("hq_scr", [TT_, IM], BF16, kind="Internal").ap()

    P = Prog(nc)
    with ExitStack() as stack:
        def sb(name, shape, dt):
            return stack.enter_context(nc.sbuf_tensor(name, shape, dt))

        def ps(name, shape, dt=F32):
            return stack.enter_context(nc.psum_tensor(name, shape, dt))

        CH = 512
        NCH = TT_ // CH
        pool = sb("pool", [128, 8 * IM], BF16)
        qin_f = sb("qin_f", [128, IM], F32)
        qin_b = sb("qin_b", [128, IM], BF16)
        u2 = sb("u2", [128, IM], F32)
        t1q = sb("t1q", [128, IM], F32)
        loq = sb("loq", [128, IM], F32)
        mkq = sb("mkq", [128, IM], U8)
        sgq = sb("sgq", [128, IM], F32)
        qout = sb("qout", [128, IM], BF16)
        bmq = sb("bmq", [128, IM // 16], F32)
        rs2q = sb("rs2q", [128, IM // 16], F32)
        shq = sb("shq", [128, IM // 16], F32)
        xqT = [sb(f"xqT{k}", [128, CH], BF16) for k in range(KT)]
        hqT = [sb(f"hqT{k}", [128, CH], BF16) for k in range(OT_I)]
        gsb = [sb(f"gsb{i}", [128, CH], BF16) for i in range(2)]
        hsb = [sb(f"hsb{i}", [128, CH], BF16) for i in range(2)]
        x1c2 = [sb(f"x1c{i}", [128, CH], F32) for i in range(2)]
        y1c2 = [sb(f"y1c{i}", [128, CH], F32) for i in range(2)]
        x2c2 = [sb(f"x2c{i}", [128, CH], F32) for i in range(2)]
        PG = ps("PG", [128, 512])
        PU = ps("PU", [128, 512])
        PD = ps("PD", [128, 512])

        warm = sb("warmq", [1, 32], F32)
        w0 = P.op("vector", lambda E: E.memset(warm[:], 1.0))
        w1 = P.op("scalar", lambda E: E.activation(out=warm[:], in_=warm[:], func=AF.Sign), deps=[w0])
        w2 = P.op("scalar", lambda E: E.activation(out=warm[:], in_=warm[:], func=AF.Silu), deps=[w1])
        warm_tok = [w2]

        def emit_quant(qin_ap, W, deps):
            NBq = W // 16
            t0 = P.op("vector", lambda E, qin_ap=qin_ap, W=W, NBq=NBq: E.tensor_reduce(
                out=bmq[:, 0:NBq], in_=qin_ap.rearrange("p (nb blk) -> p nb blk", blk=16),
                axis=AX.X, op=ALU.max, apply_absolute_value=True), deps=deps)
            t1_ = P.op("vector", lambda E, NBq=NBq: E.tensor_scalar_max(out=bmq[:, 0:NBq], in0=bmq[:, 0:NBq],
                                                                       scalar1=6e-12), deps=[t0])
            t2 = P.op("vector", lambda E, NBq=NBq: E.reciprocal(out=rs2q[:, 0:NBq], in_=bmq[:, 0:NBq]), deps=[t1_])
            t3 = P.op("vector", lambda E, NBq=NBq: E.tensor_scalar_mul(out=rs2q[:, 0:NBq], in0=rs2q[:, 0:NBq],
                                                                      scalar1=12.0), deps=[t2])
            t4 = P.op("vector", lambda E, NBq=NBq: E.tensor_scalar_mul(out=shq[:, 0:NBq], in0=bmq[:, 0:NBq],
                                                                      scalar1=1.0 / 12.0), deps=[t2])
            t5 = P.op("scalar", lambda E, qin_ap=qin_ap, W=W: E.activation(out=sgq[:, 0:W], in_=qin_ap, func=AF.Sign),
                      deps=deps + warm_tok)
            t6 = P.op("vector", lambda E, qin_ap=qin_ap, W=W, NBq=NBq: E.tensor_tensor(
                out=u2[:, 0:W].rearrange("p (nb blk) -> p nb blk", blk=16),
                in0=qin_ap.rearrange("p (nb blk) -> p nb blk", blk=16),
                in1=rs2q[:, 0:NBq].unsqueeze(2).broadcast_to([128, NBq, 16]), op=ALU.mult), deps=[t3] + deps)
            t7 = P.op("vector", lambda E, W=W: E.scalar_tensor_tensor(
                out=u2[:, 0:W], in0=u2[:, 0:W], scalar=-1.0, in1=u2[:, 0:W], op0=ALU.mult, op1=ALU.max), deps=[t6])
            t8 = P.op("vector", lambda E, W=W: E.scalar_tensor_tensor(
                out=t1q[:, 0:W], in0=u2[:, 0:W], scalar=C_VELT, in1=u2[:, 0:W],
                op0=ALU.mult, op1=ALU.subtract), deps=[t7])
            t9 = P.op("vector", lambda E, W=W: E.scalar_tensor_tensor(
                out=t1q[:, 0:W], in0=u2[:, 0:W], scalar=C_VELT, in1=t1q[:, 0:W],
                op0=ALU.mult, op1=ALU.subtract), deps=[t8])
            ta = P.op("vector", lambda E, W=W: E.tensor_single_scalar(
                out=loq[:, 0:W], in_=u2[:, 0:W], scalar=0.5, op=ALU.is_gt), deps=[t7])
            tb = P.op("vector", lambda E, W=W: E.scalar_tensor_tensor(
                out=loq[:, 0:W], in0=u2[:, 0:W], scalar=1.5, in1=loq[:, 0:W], op0=ALU.is_gt, op1=ALU.add), deps=[ta])
            tcm = P.op("vector", lambda E, W=W: E.tensor_single_scalar(
                out=mkq[:, 0:W], in_=u2[:, 0:W], scalar=2.0, op=ALU.is_gt), deps=[t7])
            td = P.op("vector", lambda E, W=W: E.select(
                out=u2[:, 0:W], mask=mkq[:, 0:W], on_true=t1q[:, 0:W], on_false=loq[:, 0:W]), deps=[t9, tb, tcm])
            te = P.op("vector", lambda E, W=W: E.tensor_tensor(
                out=u2[:, 0:W], in0=u2[:, 0:W], in1=sgq[:, 0:W], op=ALU.mult), deps=[td, t5])
            tf = P.op("vector", lambda E, W=W, NBq=NBq: E.tensor_tensor(
                out=qout[:, 0:W].rearrange("p (nb blk) -> p nb blk", blk=16),
                in0=u2[:, 0:W].rearrange("p (nb blk) -> p nb blk", blk=16),
                in1=shq[:, 0:NBq].unsqueeze(2).broadcast_to([128, NBq, 16]), op=ALU.mult), deps=[te, t4])
            return tf

        out_toks = []
        prevq = list(warm_tok)
        for (src_d, dst_d, nt, W) in ((xn1R_d, xq_scr_d, TT_ // 128, D),
                                      (wg_d, wgq_scr_d, OT_I, D),
                                      (wu_d, wuq_scr_d, OT_I, D),
                                      (wd_d, wdq_scr_d, KT, IM)):
            for rt in range(nt):
                ld = P.dma("sync", lambda E, src_d=src_d, rt=rt, W=W: E.dma_start(
                    out=qin_f[:, 0:W], in_=src_d[rt * 128:(rt + 1) * 128, :]), deps=prevq)
                tf = emit_quant(qin_f[:, 0:W], W, [ld])
                st = P.dma("gpsimd", lambda E, dst_d=dst_d, rt=rt, W=W: E.dma_start(
                    out=dst_d[rt * 128:(rt + 1) * 128, :], in_=qout[:, 0:W]), deps=[tf])
                prevq = [tf, st]
        q_done = P.fence()

        ldp = [P.dma("sync", lambda E, k=k: E.dma_start(
            out=pool[:, k * IM:(k + 1) * IM], in_=wgq_scr_d[:, k * 128:(k + 1) * 128], transpose=True),
            deps=q_done, ch="tp") for k in range(KT)]
        pg_free = []
        gs_toks = []
        for ch in range(NCH):
            ldx = [P.dma("sync", lambda E, k=k, ch=ch: E.dma_start(
                out=xqT[k][:], in_=xq_scr_d[ch * CH:(ch + 1) * CH, k * 128:(k + 1) * 128], transpose=True),
                deps=q_done + gs_toks[-2:], ch="tp") for k in range(KT)]
            for ot in range(OT_I):
                gi = ot % 2
                mm = None
                for k in range(KT):
                    mm = P.op("tensor", lambda E, ot=ot, k=k: E.matmul(
                        PG[:], pool[:, k * IM + ot * 128: k * IM + (ot + 1) * 128], xqT[k][:],
                        start=(k == 0), stop=(k == KT - 1)),
                        deps=(list(ldx) + list(ldp) + pg_free) if k == 0 else [])
                ev = P.op("scalar", lambda E, gi=gi: E.activation(out=gsb[gi][:], in_=PG[:], func=AF.Silu),
                          deps=[mm] + gs_toks[-2:] + warm_tok)
                pg_free = [ev]
                st = P.dma("gpsimd", lambda E, ot=ot, ch=ch, gi=gi: E.dma_start(
                    out=gs_scr_d[ot * 128:(ot + 1) * 128, ch * CH:(ch + 1) * CH], in_=gsb[gi][:]), deps=[ev])
                gs_toks.append(st)
        g_done = P.fence()

        ldp2 = [P.dma("sync", lambda E, k=k: E.dma_start(
            out=pool[:, k * IM:(k + 1) * IM], in_=wuq_scr_d[:, k * 128:(k + 1) * 128], transpose=True),
            deps=g_done, ch="tp") for k in range(KT)]
        pu_free = []
        h_toks = []
        for ch in range(NCH):
            ldx = [P.dma("sync", lambda E, k=k, ch=ch: E.dma_start(
                out=xqT[k][:], in_=xq_scr_d[ch * CH:(ch + 1) * CH, k * 128:(k + 1) * 128], transpose=True),
                deps=g_done + h_toks[-2:], ch="tp") for k in range(KT)]
            for ot in range(OT_I):
                gi = ot % 2
                mm = None
                for k in range(KT):
                    mm = P.op("tensor", lambda E, ot=ot, k=k: E.matmul(
                        PU[:], pool[:, k * IM + ot * 128: k * IM + (ot + 1) * 128], xqT[k][:],
                        start=(k == 0), stop=(k == KT - 1)),
                        deps=(list(ldx) + list(ldp2) + pu_free) if k == 0 else [])
                lg = P.dma("sync", lambda E, ot=ot, ch=ch, gi=gi: E.dma_start(
                    out=gsb[gi][:], in_=gs_scr_d[ot * 128:(ot + 1) * 128, ch * CH:(ch + 1) * CH]),
                    deps=g_done + h_toks[-2:])
                th = P.op("vector", lambda E, gi=gi: E.tensor_tensor(
                    out=hsb[gi][:], in0=PU[:], in1=gsb[gi][:], op=ALU.mult), deps=[mm, lg] + h_toks[-2:])
                pu_free = [th]
                st = P.dma("gpsimd", lambda E, ot=ot, ch=ch, gi=gi: E.dma_start(
                    out=h_scr_d[ot * 128:(ot + 1) * 128, ch * CH:(ch + 1) * CH], in_=hsb[gi][:]), deps=[th])
                h_toks.append(st)
        h_done = P.fence()

        prevq = list(h_done)
        hq_toks = []
        for rt in range(TT_ // 128):
            ld = P.dma("sync", lambda E, rt=rt: E.dma_start(
                out=qin_b[:], in_=h_scr_d[:, rt * 128:(rt + 1) * 128], transpose=True),
                deps=prevq, ch="tp")
            tf = emit_quant(qin_b[:], IM, [ld])
            st = P.dma("gpsimd", lambda E, rt=rt: E.dma_start(
                out=hq_scr_d[rt * 128:(rt + 1) * 128, :], in_=qout[:, 0:IM]), deps=[tf])
            prevq = [tf, st]
            hq_toks.append(st)
        hq_done = P.fence()

        ldp3 = [P.dma("sync", lambda E, kk=kk: E.dma_start(
            out=pool[:, kk * D:(kk + 1) * D], in_=wdq_scr_d[:, kk * 128:(kk + 1) * 128], transpose=True),
            deps=hq_done, ch="tp") for kk in range(OT_I)]
        pd_free = []
        x2_toks = []
        for ch in range(NCH):
            ldh = [P.dma("sync", lambda E, kk=kk, ch=ch: E.dma_start(
                out=hqT[kk][:], in_=hq_scr_d[ch * CH:(ch + 1) * CH, kk * 128:(kk + 1) * 128], transpose=True),
                deps=hq_done + x2_toks[-2:], ch="tp") for kk in range(OT_I)]
            for ot in range(KT):
                gi = ot % 2
                l1 = P.dma("sync", lambda E, ot=ot, ch=ch, gi=gi: E.dma_start(
                    out=x1c2[gi][:], in_=x1T_d[ot * 128:(ot + 1) * 128, ch * CH:(ch + 1) * CH]),
                    deps=hq_done + x2_toks[-2:])
                l2 = P.dma("sync", lambda E, ot=ot, ch=ch, gi=gi: E.dma_start(
                    out=y1c2[gi][:], in_=y1T_d[ot * 128:(ot + 1) * 128, ch * CH:(ch + 1) * CH]),
                    deps=hq_done + x2_toks[-2:])
                mm = None
                for kk in range(OT_I):
                    mm = P.op("tensor", lambda E, ot=ot, kk=kk: E.matmul(
                        PD[:], pool[:, kk * D + ot * 128: kk * D + (ot + 1) * 128], hqT[kk][:],
                        start=(kk == 0), stop=(kk == OT_I - 1)),
                        deps=(list(ldh) + list(ldp3) + pd_free) if kk == 0 else [])
                a1 = P.op("vector", lambda E, gi=gi: E.tensor_tensor(
                    out=x2c2[gi][:], in0=PD[:], in1=x1c2[gi][:], op=ALU.add), deps=[mm, l1] + x2_toks[-2:])
                pd_free = [a1]
                a2 = P.op("vector", lambda E, gi=gi: E.tensor_tensor(
                    out=x2c2[gi][:], in0=x2c2[gi][:], in1=y1c2[gi][:], op=ALU.add), deps=[a1, l2])
                st = P.dma("gpsimd", lambda E, ot=ot, ch=ch, gi=gi: E.dma_start(
                    out=x2T_d[ot * 128:(ot + 1) * 128, ch * CH:(ch + 1) * CH], in_=x2c2[gi][:]), deps=[a2])
                x2_toks.append(st)
                out_toks.append(st)
        P.op("gpsimd", lambda E: E.engine_nop(), deps=out_toks)
        block = stack.enter_context(nc.Block())
        P.build(stack, block)
    return nc


def run_l1_ffn(inputs, xn1T_list, x1T_list, y1T_list):
    nc = build_l1_ffn()
    wg = np.ascontiguousarray(np.asarray(inputs["L1_ffn_w_gate"]))
    wu = np.ascontiguousarray(np.asarray(inputs["L1_ffn_w_up"]))
    wd = np.ascontiguousarray(np.asarray(inputs["L1_ffn_w_down"]))
    in_maps = []
    for c in range(NCORES):
        xn1R = np.ascontiguousarray(xn1T_list[c].T)
        in_maps.append({"xn1R": xn1R, "wg": wg, "wu": wu, "wd": wd,
                        "x1T": x1T_list[c], "y1T": y1T_list[c]})
    res = run_bass_kernel_spmd(nc, in_maps, core_ids=list(range(NCORES)))
    return res.results




def build_fused():
    nc = bass.Bass("TRN2", target_bir_lowering=False)
    TT_ = B2 * T
    TS_ = NB * SS
    NSt = 16
    IM = 2816
    OT_I = IM // 128
    x0T_d = nc.dram_tensor("x0T", [D, TT_], F32, kind="ExternalInput").ap()
    x0Ts_d = nc.dram_tensor("x0Ts", [D, TS_], F32, kind="ExternalInput").ap()
    wqkvT_d = nc.dram_tensor("wqkvT", [D, 3 * D], F32, kind="ExternalInput").ap()
    woutT_d = nc.dram_tensor("woutT", [D, D], F32, kind="ExternalInput").ap()
    bout_d = nc.dram_tensor("bout", [D], F32, kind="ExternalInput").ap()
    g16_d = nc.dram_tensor("g16", [1, H], F32, kind="ExternalInput").ap()
    b16_d = nc.dram_tensor("b16", [1, H], F32, kind="ExternalInput").ap()
    eye_d = nc.dram_tensor("eye", [128, 128], F32, kind="ExternalInput").ap()
    ones_d = nc.dram_tensor("ones", [128, 1], F32, kind="ExternalInput").ap()
    wpackT_d = nc.dram_tensor("wpackT", [D, 7 * D], F32, kind="ExternalInput").ap()
    bias3_d = nc.dram_tensor("bias3", [3, D], F32, kind="ExternalInput").ap()
    inc_d = nc.dram_tensor("incv", [D], F32, kind="ExternalInput").ap()
    woT_d = nc.dram_tensor("woT", [D, D], F32, kind="ExternalInput").ap()
    bo_d = nc.dram_tensor("bo", [D], F32, kind="ExternalInput").ap()
    xprojT_d = nc.dram_tensor("xprojT", [D, 96], F32, kind="ExternalInput").ap()
    dtprojT_d = nc.dram_tensor("dtprojT", [64, D], F32, kind="ExternalInput").ap()
    dtb_d = nc.dram_tensor("dtb", [D], F32, kind="ExternalInput").ap()
    alog_d = nc.dram_tensor("alog", [D, NSt], F32, kind="ExternalInput").ap()
    dvec_d = nc.dram_tensor("dvec", [D], F32, kind="ExternalInput").ap()
    wg_d = nc.dram_tensor("wg", [IM, D], F32, kind="ExternalInput").ap()
    wu_d = nc.dram_tensor("wu", [IM, D], F32, kind="ExternalInput").ap()
    wd_d = nc.dram_tensor("wd", [D, IM], F32, kind="ExternalInput").ap()
    x2T_d = nc.dram_tensor("x2T", [D, TT_], F32, kind="ExternalOutput").ap()
    attn0T_d = nc.dram_tensor("i_attn0T", [D, TT_], F32, kind="Internal").ap()
    xn0T_d = nc.dram_tensor("i_xn0T", [D, TT_], BF16, kind="Internal").ap()
    sburst_d = nc.dram_tensor("i_sburst", [2, T], F32, kind="Internal").ap()
    ao_scr_d = nc.dram_tensor("i_ao_scr", [D, T], BF16, kind="Internal").ap()
    xns_scr_d = nc.dram_tensor("i_xns", [D, TS_], BF16, kind="Internal").ap()
    zsh_d = nc.dram_tensor("i_zsh", [D, SS], F32, kind="Internal").ap()
    zag_d = nc.dram_tensor("i_zag", [NCORES * D, SS], F32, kind="Internal", addr_space="Shared").ap()
    x1T_d = nc.dram_tensor("i_x1T", [D, TT_], F32, kind="Internal").ap()
    xn1T_d = nc.dram_tensor("i_xn1T", [D, TT_], F32, kind="Internal").ap()
    xn1Tb_d = nc.dram_tensor("i_xn1Tb", [D, TT_], BF16, kind="Internal").ap()
    y1T_d = nc.dram_tensor("i_y1T", [D, TT_], F32, kind="Internal").ap()
    delta_scr_d = nc.dram_tensor("i_delta", [D, T], F32, kind="Internal").ap()
    bc_scr_d = nc.dram_tensor("i_bc", [2, NSt * 512], F32, kind="Internal").ap()
    xq_scr_d = nc.dram_tensor("i_xq", [TT_, D], BF16, kind="Internal").ap()
    wgq_scr_d = nc.dram_tensor("i_wgq", [IM, D], BF16, kind="Internal").ap()
    wuq_scr_d = nc.dram_tensor("i_wuq", [IM, D], BF16, kind="Internal").ap()
    wdq_scr_d = nc.dram_tensor("i_wdq", [D, IM], BF16, kind="Internal").ap()
    gs_scr_d = nc.dram_tensor("i_gs", [IM, TT_], BF16, kind="Internal").ap()
    h_scr_d = nc.dram_tensor("i_h", [IM, TT_], BF16, kind="Internal").ap()
    hq_scr_d = nc.dram_tensor("i_hq", [TT_, IM], BF16, kind="Internal").ap()

    P = Prog(nc)
    def _phase1():
        with ExitStack() as stack:
            def sb(name, shape, dt):
                return stack.enter_context(nc.sbuf_tensor("p1_" + name, shape, dt))

            def ps(name, shape, dt=F32):
                return stack.enter_context(nc.psum_tensor("p1_" + name, shape, dt))

            CH = 512
            xbc = [sb(f"xbc{k}", [128, CH], BF16) for k in range(KT)]
            xnc = [sb(f"xnc{k}", [128, CH], BF16) for k in range(KT)]
            ktc = [sb(f"ktc{k}", [128, CH], BF16) for k in range(KT)]
            vtc = [sb(f"vtc{k}", [128, CH], BF16) for k in range(KT)]
            q_sb = [sb(f"q{k}", [128, T], BF16) for k in range(KT)]
            wq = [sb(f"wq{k}", [128, 3 * D], BF16) for k in range(KT)]
            wo = [sb(f"wo{k}", [128, D], BF16) for k in range(KT)]
            kvrow = sb("kvrow", [128, CH], BF16)
            gkv_all = sb("gkv_all", [128, KT * 128], F32)
            gkvb = sb("gkvb", [128, KT * 128], BF16)
            aostage = sb("aostage", [128, T], BF16)
            aor = [sb(f"aor{k}", [128, CH], BF16) for k in range(KT)]
            aout = sb("aout", [128, T], F32)
            eyeb = sb("eyeb", [128, 128], BF16)
            ones_b = sb("ones_b", [128, 1], BF16)
            onesrow_b = sb("onesrow_b", [1, 128], BF16)
            mb_row = sb("mb_row", [1, CH], BF16)
            rb_row = sb("rb_row", [1, CH], BF16)
            bout_s = sb("bout_s", [128, KT], F32)
            brep = sb("brep", [128, KT], F32)
            g16_s = sb("g16_s", [1, H], F32)
            m_row = sb("m_row", [1, CH], F32)
            v_row = sb("v_row", [1, CH], F32)
            vt_row = sb("vt_row", [1, CH], F32)
            nsq = sb("nsq", [1, T], F32)
            srow = sb("srow", [1, T], F32)
            srep = sb("srep", [128, T], BF16)
            t1 = sb("t1", [128, CH], F32)
            t1bs = [sb(f"t1b{k}", [128, CH], BF16) for k in range(KT)]
            zq = sb("zq", [64, CH], BF16)
            ub = sb("ub", [128, KT], F32)
            ubt = sb("ubt", [128, KT], F32)
            ubb = sb("ubb", [128, KT], BF16)
            btot = sb("btot", [128, KT], F32)

            PQ = [ps(f"PQ{i}", [128, 512]) for i in range(2)]
            PT = [ps(f"PT{i}", [128, 64], BF16) for i in range(2)]
            GK = ps("GK", [64, 128])
            ZO = ps("ZO", [128, 512])
            NS = ps("NS", [1, 512])

            warm = sb("warm", [1, 32], F32)
            w0 = P.op("vector", lambda E: E.memset(warm[:], 1.0))
            w1 = P.op("scalar", lambda E: E.activation(out=warm[:], in_=warm[:], func=AF.Square), deps=[w0])
            w2 = P.op("scalar", lambda E: E.activation(out=warm[:], in_=warm[:], func=AF.Ln), deps=[w1])
            w3 = P.op("scalar", lambda E: E.activation(out=warm[:], in_=warm[:], func=AF.Exp), deps=[w2])
            warm_tok = [w3]

            ldw_q, ldw_o = [], []
            for k in range(KT):
                ldw_q.append(P.dma("gpsimd", lambda E, k=k: E.dma_start(out=wq[k][:], in_=wqkvT_d[k * 128:(k + 1) * 128, :]), ch="cast"))
                ldw_o.append(P.dma("gpsimd", lambda E, k=k: E.dma_start(out=wo[k][:], in_=woutT_d[k * 128:(k + 1) * 128, :]), ch="cast"))
            ld_eyeb = P.dma("gpsimd", lambda E: E.dma_start(out=eyeb[:], in_=eye_d[:]), ch="cast")
            ld_ob = P.dma("gpsimd", lambda E: E.dma_start(out=ones_b[:], in_=ones_d[:]), ch="cast")
            ld_or = P.dma("gpsimd", lambda E: E.dma_start(
                out=onesrow_b[:], in_=ones_d[:, 0].unsqueeze(0).broadcast_to([1, 128])), ch="castp0")
            ld_g = P.dma("sync", lambda E: E.dma_start(out=g16_s[:], in_=g16_d[:]), ch="p0")
            ld_bout = [P.dma("sync", lambda E, o=o: E.dma_start(
                out=bout_s[:, o:o + 1], in_=bout_d[o * 128:(o + 1) * 128].unsqueeze(1))) for o in range(KT)]
            ld_brep = [P.dma("sync", lambda E, h=h: E.dma_start(
                out=brep[(h % 2) * 64:(h % 2) * 64 + 64, h // 2:h // 2 + 1],
                in_=b16_d[0:1, h:h + 1].broadcast_to([64, 1])), ch=("n0" if h % 2 == 0 else "n64")) for h in range(H)]

            out_toks = []
            batch_fence = []
            for bi in range(B2):
                z1 = P.op("vector", lambda E: E.memset(gkv_all[:], 0.0), deps=batch_fence + warm_tok)
                z2 = P.op("vector", lambda E: E.memset(ub[:], 0.0), deps=batch_fence)
                acc_g = [z1]
                acc_u = [z2]
                chunk_fence = list(batch_fence)
                for c in range(4):
                    tc = slice(bi * T + c * CH, bi * T + (c + 1) * CH)
                    ld_x = [P.dma("gpsimd", lambda E, k=k, tc=tc: E.dma_start(
                        out=xbc[k][:], in_=x0T_d[k * 128:(k + 1) * 128, tc]),
                        deps=chunk_fence) for k in range(KT)]
                    mm = None
                    for k in range(KT):
                        mm = P.op("tensor", lambda E, k=k: E.matmul(
                            PQ[0][0:1, :], ones_b[:], xbc[k][:], start=(k == 0), stop=(k == KT - 1)),
                            deps=(list(ld_x) + [ld_ob] + warm_tok + chunk_fence) if k == 0 else [])
                    em0 = P.op("scalar", lambda E: E.copy(out=m_row[:], in_=PQ[0][0:1, :]), deps=[mm])
                    em = P.op("vector", lambda E: E.tensor_scalar_mul(out=m_row[:], in0=m_row[:], scalar1=1.0 / D), deps=[em0])
                    sq_toks = [P.op("scalar", lambda E, k=k: E.activation(out=t1bs[k][:], in_=xbc[k][:], func=AF.Square),
                                    deps=[ld_x[k]] + chunk_fence) for k in range(KT)]
                    mmq = None
                    for k in range(KT):
                        mmq = P.op("tensor", lambda E, k=k: E.matmul(
                            PQ[1][0:1, :], ones_b[:], t1bs[k][:], start=(k == 0), stop=(k == KT - 1)),
                            deps=([sq_toks[-1], ld_ob] + chunk_fence) if k == 0 else [])
                    ev0 = P.op("scalar", lambda E: E.copy(out=v_row[:], in_=PQ[1][0:1, :]), deps=[mmq])
                    ev = P.op("vector", lambda E: E.tensor_scalar_mul(out=v_row[:], in0=v_row[:], scalar1=1.0 / D), deps=[ev0])
                    tv1 = P.op("vector", lambda E: E.tensor_mul(out=vt_row[:], in0=m_row[:], in1=m_row[:]), deps=[em])
                    tv2 = P.op("vector", lambda E: E.tensor_sub(out=v_row[:], in0=v_row[:], in1=vt_row[:]), deps=[tv1, ev])
                    tv3 = P.op("vector", lambda E: E.tensor_scalar_add(out=v_row[:], in0=v_row[:], scalar1=LN_EPS), deps=[tv2])
                    tr1 = P.op("scalar", lambda E: E.activation(out=v_row[:], in_=v_row[:], func=AF.Ln), deps=[tv3])
                    tr1b = P.op("vector", lambda E: E.tensor_scalar_mul(out=v_row[:], in0=v_row[:], scalar1=-0.5), deps=[tr1])
                    tr2 = P.op("scalar", lambda E: E.activation(out=v_row[:], in_=v_row[:], func=AF.Exp), deps=[tr1b])
                    cmb = P.op("vector", lambda E: E.tensor_copy(out=mb_row[:], in_=m_row[:]), deps=[em])
                    crb = P.op("vector", lambda E: E.tensor_copy(out=rb_row[:], in_=v_row[:]), deps=[tr2])
                    t_m = P.op("tensor", lambda E: E.matmul(PQ[0][:], onesrow_b[:], mb_row[:], start=True, stop=True),
                               deps=[cmb, ld_or])
                    t_r = P.op("tensor", lambda E: E.matmul(PQ[1][:], onesrow_b[:], rb_row[:], start=True, stop=True),
                               deps=[crb, ld_or])
                    ap_toks = []
                    for k in range(KT):
                        a1 = P.op("vector", lambda E, k=k: E.tensor_tensor(out=t1[:], in0=xbc[k][:], in1=PQ[0][:], op=ALU.subtract),
                                  deps=[t_m, ld_x[k]] + ([ap_toks[-1]] if ap_toks else [mmq]))
                        a2 = P.op("vector", lambda E, k=k: E.tensor_tensor(out=xnc[k][:], in0=t1[:], in1=PQ[1][:], op=ALU.mult),
                                  deps=[a1, t_r])
                        ap_toks.append(a2)
                    st_xn = [P.dma("gpsimd", lambda E, k=k, tc=tc: E.dma_start(
                        out=xn0T_d[k * 128:(k + 1) * 128, tc], in_=xnc[k][:]), deps=[ap_toks[k]]) for k in range(KT)]
                    out_toks += st_xn
                    pq_free = [[ap_toks[-1]], [ap_toks[-1]]]
                    ev_q = [None] * KT
                    ev_k = [None] * KT
                    ev_v = [None] * KT
                    for ot in range(3 * KT):
                        pqi = ot % 2
                        mm = None
                        for k in range(KT):
                            mm = P.op("tensor", lambda E, ot=ot, k=k, pqi=pqi: E.matmul(
                                PQ[pqi][:], wq[k][:, ot * 128:(ot + 1) * 128], xnc[k][:],
                                start=(k == 0), stop=(k == KT - 1)),
                                deps=([ap_toks[-1], ldw_q[-1]] + pq_free[pqi]) if k == 0 else [])
                        if ot < KT:
                            evt = P.op("scalar", lambda E, ot=ot, pqi=pqi, c=c: E.copy(
                                out=q_sb[ot][:, c * CH:(c + 1) * CH], in_=PQ[pqi][:]), deps=[mm] + batch_fence)
                        elif ot < 2 * KT:
                            evt = P.op("scalar", lambda E, ot=ot, pqi=pqi: E.copy(
                                out=ktc[ot - KT][:], in_=PQ[pqi][:]), deps=[mm] + chunk_fence)
                            ev_k[ot - KT] = evt
                        else:
                            evt = P.op("scalar", lambda E, ot=ot, pqi=pqi: E.copy(
                                out=vtc[ot - 2 * KT][:], in_=PQ[pqi][:]), deps=[mm] + chunk_fence)
                            ev_v[ot - 2 * KT] = evt
                        pq_free[pqi] = [evt]
                        if ot < KT:
                            ev_q[ot] = evt
                    pt_free = [list(chunk_fence), list(chunk_fence)]
                    kvrow_free = list(chunk_fence)
                    gk_free = []
                    ub_toks = []
                    for h in range(H):
                        p64 = (h % 2) * 64
                        hp = h // 2
                        kcol = ktc[hp][p64:p64 + 64, :]
                        vcol = vtc[hp][p64:p64 + 64, :]
                        idn = eyeb[p64:p64 + 64, p64:p64 + 64]
                        tr_toks = []
                        for jb in range(4):
                            for which, col in ((0, kcol), (1, vcol)):
                                pti = (jb * 2 + which) % 2
                                tt = P.op("tensor", lambda E, col=col, jb=jb, pti=pti, idn=idn: E.transpose(
                                    out=PT[pti][:], in_=col[:, jb * 128:(jb + 1) * 128], identity=idn),
                                    deps=[ev_k[hp], ev_v[hp], ld_eyeb] + pt_free[pti])
                                te = P.op("scalar", lambda E, jb=jb, which=which, pti=pti: E.copy(
                                    out=kvrow[:, jb * 128 + which * 64: jb * 128 + which * 64 + 64], in_=PT[pti][:]),
                                    deps=[tt] + kvrow_free)
                                pt_free[pti] = [te]
                                tr_toks.append(te)
                        kvrow_free = []
                        mmgk = None
                        for jb in range(4):
                            mmgk = P.op("tensor", lambda E, jb=jb: E.matmul(
                                GK[:], kvrow[:, jb * 128: jb * 128 + 64], kvrow[:, jb * 128: jb * 128 + 128],
                                start=(jb == 0), stop=(jb == 3)),
                                deps=([tr_toks[-1]] + gk_free) if jb == 0 else [])
                        kvrow_free = [mmgk]
                        ga = P.op("vector", lambda E, h=h, p64=p64, hp=hp: E.tensor_tensor(
                            out=gkv_all[p64:p64 + 64, hp * 128:(hp + 1) * 128],
                            in0=GK[:], in1=gkv_all[p64:p64 + 64, hp * 128:(hp + 1) * 128], op=ALU.add),
                            deps=[mmgk] + acc_g)
                        gk_free = [ga]
                        ut = P.op("vector", lambda E, h=h, p64=p64, vcol=vcol: E.tensor_reduce(
                            out=ubt[p64:p64 + 64, h // 2:h // 2 + 1], in_=vcol[:], axis=AX.X, op=ALU.add),
                            deps=[ev_v[hp]] + acc_u)
                        ub_toks.append(ut)
                    acc_g = [gk_free[0] if gk_free else None] + [None]
                    acc_g = [ga]
                    ua = P.op("vector", lambda E: E.tensor_add(out=ub[:], in0=ub[:], in1=ubt[:]), deps=ub_toks + acc_u)
                    acc_u = [ua]
                    chunk_fence = P.fence()

                cg = P.op("vector", lambda E: E.tensor_copy(out=gkvb[:], in_=gkv_all[:]), deps=acc_g + chunk_fence)
                zo_free, ns_free, zq_free = [cg], [], []
                sburst_free = {0: [], 1: []}
                srep_free = list(batch_fence)
                ao_dma = {0: [], 1: []}
                for h in range(H):
                    p64 = (h % 2) * 64
                    hp = h // 2
                    qcol = q_sb[hp][p64:p64 + 64, :]
                    oh_toks = []
                    nsq_toks = []
                    for c in range(4):
                        ch = slice(c * CH, (c + 1) * CH)
                        mm = P.op("tensor", lambda E, ch=ch, p64=p64, hp=hp, qcol=qcol: E.matmul(
                            ZO[:], gkvb[p64:p64 + 64, hp * 128:(hp + 1) * 128], qcol[:, ch], start=True, stop=True),
                            deps=[cg, ev_q[hp]] + zo_free)
                        tzq = P.op("vector", lambda E, ch=ch, qcol=qcol: E.tensor_tensor(
                            out=zq[:], in0=ZO[0:64, :], in1=qcol[:, ch], op=ALU.mult), deps=[mm] + zq_free)
                        toh = P.op("scalar", lambda E, ch=ch, p64=p64: E.copy(
                            out=aostage[p64:p64 + 64, ch], in_=ZO[64:128, :]), deps=[mm] + ao_dma[h % 2])
                        tns = P.op("tensor", lambda E: E.matmul(
                            NS[:], ones_b[0:64, :], zq[:], start=True, stop=True), deps=[tzq] + ns_free)
                        ten = P.op("scalar", lambda E, ch=ch: E.copy(out=nsq[:, ch], in_=NS[:]), deps=[tns])
                        ns_free = [ten]
                        zq_free = [tns]
                        zo_free = [tzq, toh]
                        oh_toks.append(toh)
                        nsq_toks.append(ten)
                    s0 = P.op("vector", lambda E: E.tensor_scalar_max(out=srow[:], in0=nsq[:], scalar1=1e-24),
                              deps=nsq_toks + sburst_free[h % 2] + sburst_free[(h + 1) % 2])
                    s1 = P.op("scalar", lambda E: E.activation(out=srow[:], in_=srow[:], func=AF.Ln), deps=[s0])
                    s1b = P.op("vector", lambda E: E.tensor_scalar_mul(out=srow[:], in0=srow[:], scalar1=-0.5), deps=[s1])
                    s2 = P.op("scalar", lambda E: E.activation(out=srow[:], in_=srow[:], func=AF.Exp), deps=[s1b])
                    s5 = P.op("vector", lambda E, h=h: E.tensor_scalar_mul(
                        out=srow[:], in0=srow[:], scalar1=g16_s[0:1, h:h + 1]), deps=[s2, ld_g])
                    s6 = P.dma("gpsimd", lambda E, h=h: E.dma_start(out=sburst_d[h % 2:h % 2 + 1, :], in_=srow[0:1, :]),
                               deps=[s5] + sburst_free[h % 2], ch="p0")
                    s7 = P.dma("gpsimd", lambda E, h=h: E.dma_start(
                        out=srep[:], in_=sburst_d[h % 2:h % 2 + 1, :].broadcast_to([128, T])),
                        deps=[s6] + srep_free, ch="cast")
                    sburst_free[h % 2] = [s7]
                    sc_toks = []
                    for c in range(4):
                        ch = slice(c * CH, (c + 1) * CH)
                        tok = P.op("vector", lambda E, ch=ch, p64=p64: E.tensor_tensor(
                            out=aostage[p64:p64 + 64, ch], in0=aostage[p64:p64 + 64, ch],
                            in1=srep[p64:p64 + 64, ch], op=ALU.mult), deps=[s7, oh_toks[c]])
                        sc_toks.append(tok)
                    srep_free = sc_toks
                    da = P.dma("gpsimd", lambda E, h=h, p64=p64: E.dma_start(
                        out=ao_scr_d[h * 64:(h + 1) * 64, :], in_=aostage[p64:p64 + 64, :]), deps=sc_toks,
                        ch=("n0" if p64 == 0 else "n64"))
                    ao_dma[h % 2] = [da]
                ao_done = [ao_dma[0][0], ao_dma[1][0]]

                ubm = P.op("vector", lambda E: E.tensor_tensor(out=ub[:], in0=ub[:], in1=brep[:], op=ALU.mult),
                           deps=acc_u + ld_brep)
                ubc = P.op("vector", lambda E: E.tensor_copy(out=ubb[:], in_=ub[:]), deps=[ubm] + batch_fence)
                bt_toks = []
                pq0_free = chunk_fence
                for ot in range(KT):
                    mm = None
                    for k in range(KT):
                        mm = P.op("tensor", lambda E, ot=ot, k=k: E.matmul(
                            PQ[0][:, 0:1], wo[k][:, ot * 128:(ot + 1) * 128], ubb[:, k:k + 1],
                            start=(k == 0), stop=(k == KT - 1)),
                            deps=([ubc, ldw_o[-1]] + (pq0_free if ot == 0 else [bt_toks[-1]])) if k == 0 else [])
                    evb = P.op("scalar", lambda E, ot=ot: E.copy(out=btot[:, ot:ot + 1], in_=PQ[0][:, 0:1]), deps=[mm])
                    bt_toks.append(evb)
                btt = P.op("vector", lambda E: E.tensor_add(out=btot[:], in0=btot[:], in1=bout_s[:]),
                           deps=bt_toks + ld_bout)

                pq_free = [[bt_toks[-1]], chunk_fence]
                aor_free = [list(chunk_fence) for _ in range(KT)]
                aout_free = list(batch_fence)
                for c in range(4):
                    ch = slice(c * CH, (c + 1) * CH)
                    ld_ao = [P.dma("sync", lambda E, k=k, ch=ch: E.dma_start(
                        out=aor[k][:], in_=ao_scr_d[k * 128:(k + 1) * 128, ch]), deps=ao_done + aor_free[k])
                        for k in range(KT)]
                    for ot in range(KT):
                        pqi = ot % 2
                        mm = None
                        for k in range(KT):
                            mm = P.op("tensor", lambda E, ot=ot, k=k, pqi=pqi: E.matmul(
                                PQ[pqi][:], wo[k][:, ot * 128:(ot + 1) * 128], aor[k][:],
                                start=(k == 0), stop=(k == KT - 1)),
                                deps=(list(ld_ao) + [ldw_o[-1]] + pq_free[pqi]) if k == 0 else [])
                        evo = P.op("vector", lambda E, ch=ch, pqi=pqi, ot=ot: E.tensor_scalar_add(
                            out=aout[:, ch], in0=PQ[pqi][:], scalar1=btot[:, ot:ot + 1]),
                            deps=[mm, btt] + aout_free)
                        pq_free[pqi] = [evo]
                        std = P.dma("gpsimd", lambda E, ot=ot, ch=ch, bi=bi, c=c: E.dma_start(
                            out=attn0T_d[ot * 128:(ot + 1) * 128, bi * T + c * CH: bi * T + (c + 1) * CH],
                            in_=aout[:, ch]), deps=[evo])
                        aout_free = [std]
                        out_toks.append(std)
                    aor_free = [[P.last("tensor")] for _ in range(KT)]
                batch_fence = P.fence()

            P.op("gpsimd", lambda E: E.engine_nop(), deps=out_toks)
    _phase1()
    P.set_floor(P.fence())
    def _phase2():
        with ExitStack() as stack:
            def sb(name, shape, dt):
                return stack.enter_context(nc.sbuf_tensor("p2_" + name, shape, dt))

            def ps(name, shape, dt=F32):
                return stack.enter_context(nc.psum_tensor("p2_" + name, shape, dt))

            CH = 512
            xbc = [sb(f"xbc{k}", [128, CH], BF16) for k in range(KT)]
            xnc = [sb(f"xnc{k}", [128, CH], BF16) for k in range(KT)]
            t1bs = [sb(f"t1b{k}", [128, CH], BF16) for k in range(KT)]
            wp = [sb(f"wp{k}", [128, 7 * D], BF16) for k in range(KT)]
            ct_ = [sb(f"ct{k}", [128, SS], F32) for k in range(KT)]
            xstep = [[sb(f"xs{j}_{k}", [128, SS], BF16) for k in range(KT)] for j in range(2)]
            fts = [sb(f"fts{k}", [128, SS], F32) for k in range(KT)]
            its = [sb(f"its{k}", [128, SS], F32) for k in range(KT)]
            gts = [sb(f"gts{k}", [128, SS], F32) for k in range(KT)]
            tmpa = sb("tmpa", [128, SS], F32)
            tmpb = sb("tmpb", [128, SS], F32)
            tmpc = sb("tmpc", [128, 512], F32)
            ones_b = sb("ones_b", [128, 1], BF16)
            onesrow_b = sb("onesrow_b", [1, 128], BF16)
            b3 = sb("b3", [128, 3 * KT], F32)
            m_row = sb("m_row", [1, CH], F32)
            v_row = sb("v_row", [1, CH], F32)
            vt_row = sb("vt_row", [1, CH], F32)
            mb_row = sb("mb_row", [1, CH], BF16)
            rb_row = sb("rb_row", [1, CH], BF16)
            cb = [sb(f"cb{k}", [128, SS], BF16) for k in range(KT)]
            inc_sb = sb("inc_sb", [128, KT], F32)
            zstage = sb("zstage", [128, SS], F32)

            PQ = [ps(f"PQ{i}", [128, 512]) for i in range(2)]
            PG = [ps(f"PG{i}", [128, 512]) for i in range(3)]

            warm = sb("warm", [1, 32], F32)
            w0 = P.op("vector", lambda E: E.memset(warm[:], 1.0))
            w1 = P.op("scalar", lambda E: E.activation(out=warm[:], in_=warm[:], func=AF.Square), deps=[w0])
            w2 = P.op("scalar", lambda E: E.activation(out=warm[:], in_=warm[:], func=AF.Ln), deps=[w1])
            w3 = P.op("scalar", lambda E: E.activation(out=warm[:], in_=warm[:], func=AF.Exp), deps=[w2])
            warm_tok = [w3]

            ldw = [P.dma("gpsimd", lambda E, k=k: E.dma_start(out=wp[k][:], in_=wpackT_d[k * 128:(k + 1) * 128, :]), ch="cast")
                   for k in range(KT)]
            ld_ob = P.dma("gpsimd", lambda E: E.dma_start(out=ones_b[:], in_=ones_d[:]), ch="cast")
            ld_or = P.dma("gpsimd", lambda E: E.dma_start(
                out=onesrow_b[:], in_=ones_d[:, 0].unsqueeze(0).broadcast_to([1, 128])), ch="castp0")
            ld_b3 = [P.dma("sync", lambda E, g=g, o=o: E.dma_start(
                out=b3[:, g * KT + o:g * KT + o + 1], in_=bias3_d[g, o * 128:(o + 1) * 128].unsqueeze(1)))
                for g in range(3) for o in range(KT)]
            ld_inc = [P.dma("sync", lambda E, k=k: E.dma_start(
                out=inc_sb[:, k:k + 1], in_=inc_d[k * 128:(k + 1) * 128].unsqueeze(1))) for k in range(KT)]
            ld_c0 = [P.op("vector", lambda E, k=k: E.tensor_copy(
                out=ct_[k][:], in_=inc_sb[:, k:k + 1].broadcast_to([128, SS])), deps=[ld_inc[k]])
                for k in range(KT)]

            chunk_fence = []
            st_xn = []
            for c in range(8):
                tc = slice(c * CH, (c + 1) * CH)
                ld_x = [P.dma("gpsimd", lambda E, k=k, tc=tc: E.dma_start(
                    out=xbc[k][:], in_=x0Ts_d[k * 128:(k + 1) * 128, tc]), deps=chunk_fence, ch="cast") for k in range(KT)]
                mm = None
                for k in range(KT):
                    mm = P.op("tensor", lambda E, k=k: E.matmul(
                        PQ[0][0:1, :], ones_b[:], xbc[k][:], start=(k == 0), stop=(k == KT - 1)),
                        deps=(list(ld_x) + [ld_ob] + warm_tok + chunk_fence) if k == 0 else [])
                em0 = P.op("scalar", lambda E: E.copy(out=m_row[:], in_=PQ[0][0:1, :]), deps=[mm])
                em = P.op("vector", lambda E: E.tensor_scalar_mul(out=m_row[:], in0=m_row[:], scalar1=1.0 / D), deps=[em0])
                sq_toks = [P.op("scalar", lambda E, k=k: E.activation(out=t1bs[k][:], in_=xbc[k][:], func=AF.Square),
                               deps=[ld_x[k]] + chunk_fence) for k in range(KT)]
                mmq = None
                for k in range(KT):
                    mmq = P.op("tensor", lambda E, k=k: E.matmul(
                        PQ[1][0:1, :], ones_b[:], t1bs[k][:], start=(k == 0), stop=(k == KT - 1)),
                        deps=([sq_toks[-1], ld_ob] + chunk_fence) if k == 0 else [])
                ev0 = P.op("scalar", lambda E: E.copy(out=v_row[:], in_=PQ[1][0:1, :]), deps=[mmq])
                ev = P.op("vector", lambda E: E.tensor_scalar_mul(out=v_row[:], in0=v_row[:], scalar1=1.0 / D), deps=[ev0])
                tv1 = P.op("vector", lambda E: E.tensor_mul(out=vt_row[:], in0=m_row[:], in1=m_row[:]), deps=[em])
                tv2 = P.op("vector", lambda E: E.tensor_sub(out=v_row[:], in0=v_row[:], in1=vt_row[:]), deps=[tv1, ev])
                tv3 = P.op("vector", lambda E: E.tensor_scalar_add(out=v_row[:], in0=v_row[:], scalar1=LN_EPS), deps=[tv2])
                tr1 = P.op("scalar", lambda E: E.activation(out=v_row[:], in_=v_row[:], func=AF.Ln), deps=[tv3])
                tr1b = P.op("vector", lambda E: E.tensor_scalar_mul(out=v_row[:], in0=v_row[:], scalar1=-0.5), deps=[tr1])
                tr2 = P.op("scalar", lambda E: E.activation(out=v_row[:], in_=v_row[:], func=AF.Exp), deps=[tr1b])
                cmb = P.op("vector", lambda E: E.tensor_copy(out=mb_row[:], in_=m_row[:]), deps=[em])
                crb = P.op("vector", lambda E: E.tensor_copy(out=rb_row[:], in_=v_row[:]), deps=[tr2])
                t_m = P.op("tensor", lambda E: E.matmul(PQ[0][:], onesrow_b[:], mb_row[:], start=True, stop=True),
                           deps=[cmb, ld_or, em0])
                t_r = P.op("tensor", lambda E: E.matmul(PQ[1][:], onesrow_b[:], rb_row[:], start=True, stop=True),
                           deps=[crb, ld_or, ev0])
                ap_toks = []
                for k in range(KT):
                    a1 = P.op("vector", lambda E, k=k: E.tensor_tensor(out=tmpc[:], in0=xbc[k][:], in1=PQ[0][:],
                                                                      op=ALU.subtract),
                              deps=[t_m] + ([ap_toks[-1]] if ap_toks else [mmq]))
                    a2 = P.op("vector", lambda E, k=k: E.tensor_tensor(out=xnc[k][:], in0=tmpc[:], in1=PQ[1][:],
                                                                      op=ALU.mult),
                              deps=[a1, t_r])
                    ap_toks.append(a2)
                st_c = [P.dma("gpsimd", lambda E, k=k, tc=tc: E.dma_start(
                    out=xns_scr_d[k * 128:(k + 1) * 128, tc], in_=xnc[k][:]), deps=[ap_toks[k]]) for k in range(KT)]
                st_xn += st_c
                chunk_fence = P.fence()

            GATES = 3
            upd_last = list(ld_c0) + chunk_fence
            cb_toks = []
            xld_prev = []
            ev_toks = []
            mm_toks = []
            for t_i in range(NB):
                jbuf = 0
                xld = [P.dma("gpsimd", lambda E, k=k, t_i=t_i, jbuf=jbuf: E.dma_start(
                    out=xstep[jbuf][k][:], in_=xns_scr_d[k * 128:(k + 1) * 128, t_i * SS:(t_i + 1) * SS]),
                    deps=(st_xn if t_i < 2 else []) + xld_prev + mm_toks[-1:]) for k in range(KT)]
                cb_toks = [P.op("vector", lambda E, k=k: E.tensor_copy(out=cb[k][:], in_=ct_[k][:]),
                                deps=upd_last + cb_toks[-1:]) for k in range(KT)]
                ev_base = len(ev_toks)
                for ot in range(KT):
                    for g in range(GATES):
                        mm = None
                        for k in range(2 * KT):
                            if k < KT:
                                lhs = (lambda E, k=k, g=g, ot=ot, jbuf=jbuf: E.matmul(
                                    PG[g][:, 0:SS], wp[k][:, g * D + ot * 128: g * D + (ot + 1) * 128],
                                    xstep[jbuf][k][:], start=(k == 0), stop=False))
                            else:
                                kk = k - KT
                                lhs = (lambda E, kk=kk, g=g, ot=ot: E.matmul(
                                    PG[g][:, 0:SS], wp[kk][:, (3 + g) * D + ot * 128: (3 + g) * D + (ot + 1) * 128],
                                    cb[kk][:], start=False, stop=(kk == KT - 1)))
                            mm = P.op("tensor", lhs,
                                      deps=(list(xld) + [cb_toks[-1], ldw[-1]] + ev_toks[-3:]) if k == 0 else [])
                        mm_toks.append(mm)
                        func = AF.Sigmoid if g < 2 else AF.Tanh
                        tok = P.op("scalar", lambda E, g=g, ot=ot, func=func: E.activation(
                            out=(fts if g == 0 else its if g == 1 else gts)[ot][:], in_=PG[g][:, 0:SS],
                            func=func, bias=b3[:, g * KT + ot:g * KT + ot + 1]),
                            deps=[mm] + upd_last)
                        ev_toks.append(tok)
                upd = []
                for ot in range(KT):
                    u1 = P.op("vector", lambda E, ot=ot: E.tensor_tensor(out=tmpa[:], in0=fts[ot][:], in1=ct_[ot][:],
                                                                        op=ALU.mult),
                              deps=[ev_toks[ev_base + ot * 3], upd[-1] if upd else None])
                    u2 = P.op("vector", lambda E, ot=ot: E.tensor_tensor(out=tmpb[:], in0=its[ot][:], in1=gts[ot][:],
                                                                        op=ALU.mult),
                              deps=[ev_toks[ev_base + ot * 3 + 1], ev_toks[ev_base + ot * 3 + 2]])
                    u3 = P.op("vector", lambda E, ot=ot: E.tensor_tensor(out=ct_[ot][:], in0=tmpa[:], in1=tmpb[:],
                                                                        op=ALU.add),
                              deps=[u1, u2])
                    upd.append(u3)
                upd_last = [upd[-1]]
                xld_prev = xld

            cbf = [P.op("vector", lambda E, k=k: E.tensor_copy(out=cb[k][:], in_=ct_[k][:]), deps=upd_last + cb_toks[-1:])
                   for k in range(KT)]
            cbf = [P.op("vector", lambda E, k=k: E.tensor_copy(out=cb[k][:], in_=ct_[k][:]), deps=upd_last + cb_toks[-1:])
                   for k in range(KT)]
            z_toks = []
            pqi_free = [[], []]
            for ot in range(KT):
                mm = None
                for k in range(KT):
                    mm = P.op("tensor", lambda E, ot=ot, k=k: E.matmul(
                        PG[0][:, 0:SS], wp[k][:, 6 * D + ot * 128: 6 * D + (ot + 1) * 128], cb[k][:],
                        start=(k == 0), stop=(k == KT - 1)),
                        deps=([cbf[-1], ldw[-1]] + z_toks[-1:] + ev_toks[-3:]) if k == 0 else [])
                evz = P.op("scalar", lambda E, ot=ot: E.copy(out=zstage[:, 0:SS], in_=PG[0][:, 0:SS]), deps=[mm])
                stz = P.dma("gpsimd", lambda E, ot=ot: E.dma_start(
                    out=zsh_d[ot * 128:(ot + 1) * 128, :], in_=zstage[:, 0:SS]), deps=[evz])
                z_toks.append(stz)
            P.op("gpsimd", lambda E: E.engine_nop(), deps=z_toks)
    _phase2()
    P.set_floor(P.fence())
    cc = P.op("gpsimd", lambda E: E.collective_compute(
        "AllGather", ALU.bypass, replica_groups=[list(range(NCORES))],
        ins=[zsh_d[:]], outs=[zag_d[:]]))
    P.set_floor(P.fence())
    def _phase3():
        with ExitStack() as stack:
            def sb(name, shape, dt):
                return stack.enter_context(nc.sbuf_tensor("p3_" + name, shape, dt))

            def ps(name, shape, dt=F32):
                return stack.enter_context(nc.psum_tensor("p3_" + name, shape, dt))

            CH = 512
            wo = [sb(f"wo{k}", [128, D], BF16) for k in range(KT)]
            zs = [sb(f"zs{k}", [128, T], F32) for k in range(KT)]
            xn0c = [sb(f"xn0c{k}", [128, CH], BF16) for k in range(KT)]
            x0c = [sb(f"x0c{k}", [128, CH], F32) for k in range(KT)]
            a0c = [sb(f"a0c{k}", [128, CH], F32) for k in range(KT)]
            x1c = [sb(f"x1c{k}", [128, CH], F32) for k in range(KT)]
            xbc = [sb(f"xbc{k}", [128, CH], BF16) for k in range(KT)]
            t1bs = [sb(f"t1b{k}", [128, CH], BF16) for k in range(KT)]
            xnc = [sb(f"xnc{k}", [128, CH], F32) for k in range(KT)]
            tmp = sb("tmp", [128, CH], F32)
            ht = sb("ht", [128, CH], F32)
            bo_s = sb("bo_s", [128, KT], F32)
            ones_b = sb("ones_b", [128, 1], BF16)
            onesrow_b = sb("onesrow_b", [1, 128], BF16)
            m_row = sb("m_row", [1, CH], F32)
            v_row = sb("v_row", [1, CH], F32)
            vt_row = sb("vt_row", [1, CH], F32)
            mb_row = sb("mb_row", [1, CH], BF16)
            rb_row = sb("rb_row", [1, CH], BF16)
            PH = [ps(f"PH{i}", [128, 512]) for i in range(2)]
            PQ = [ps(f"PQ{i}", [128, 512]) for i in range(2)]

            warm = sb("warm", [1, 32], F32)
            w0 = P.op("vector", lambda E: E.memset(warm[:], 1.0))
            w1 = P.op("scalar", lambda E: E.activation(out=warm[:], in_=warm[:], func=AF.Square), deps=[w0])
            w2 = P.op("scalar", lambda E: E.activation(out=warm[:], in_=warm[:], func=AF.Ln), deps=[w1])
            w3 = P.op("scalar", lambda E: E.activation(out=warm[:], in_=warm[:], func=AF.Tanh), deps=[w2])
            w4 = P.op("scalar", lambda E: E.activation(out=warm[:], in_=warm[:], func=AF.Exp), deps=[w3])
            warm_tok = [w4]

            ldw = [P.dma("gpsimd", lambda E, k=k: E.dma_start(out=wo[k][:], in_=woT_d[k * 128:(k + 1) * 128, :]), ch="cast")
                   for k in range(KT)]
            ldz = [P.dma("sync", lambda E, k=k, r=r: E.dma_start(
                out=zs[k][:, r * SS:(r + 1) * SS], in_=zag_d[r * D + k * 128: r * D + (k + 1) * 128, :]))
                   for k in range(KT) for r in range(NCORES)]
            ld_ob = P.dma("gpsimd", lambda E: E.dma_start(out=ones_b[:], in_=ones_d[:]), ch="cast")
            ld_or = P.dma("gpsimd", lambda E: E.dma_start(
                out=onesrow_b[:], in_=ones_d[:, 0].unsqueeze(0).broadcast_to([1, 128])), ch="castp0")
            ld_bo = [P.dma("sync", lambda E, o=o: E.dma_start(
                out=bo_s[:, o:o + 1], in_=bo_d[o * 128:(o + 1) * 128].unsqueeze(1))) for o in range(KT)]

            chunk_fence = [ldw[-1], ldz[-1], ld_ob, ld_or] + ld_bo + warm_tok
            out_toks = []
            for c in range(8):
                bi = c // 4
                tc = slice(c * CH, (c + 1) * CH)
                zc = slice((c % 4) * CH, (c % 4 + 1) * CH)
                ld_xn0 = [P.dma("gpsimd", lambda E, k=k, tc=tc: E.dma_start(
                    out=xn0c[k][:], in_=xn0T_d[k * 128:(k + 1) * 128, tc]), deps=chunk_fence) for k in range(KT)]
                ld_x0 = [P.dma("sync", lambda E, k=k, tc=tc: E.dma_start(
                    out=x0c[k][:], in_=x0T_d[k * 128:(k + 1) * 128, tc]), deps=chunk_fence) for k in range(KT)]
                ld_a0 = [P.dma("sync", lambda E, k=k, tc=tc: E.dma_start(
                    out=a0c[k][:], in_=attn0T_d[k * 128:(k + 1) * 128, tc]), deps=chunk_fence) for k in range(KT)]
                x1_toks = []
                phi_free = [list(chunk_fence), list(chunk_fence)]
                for ot in range(KT):
                    phi = ot % 2
                    mm = None
                    for k in range(KT):
                        mm = P.op("tensor", lambda E, ot=ot, k=k, phi=phi: E.matmul(
                            PH[phi][:], wo[k][:, ot * 128:(ot + 1) * 128], xn0c[k][:],
                            start=(k == 0), stop=(k == KT - 1)),
                            deps=(list(ld_xn0) + [ldw[-1]] + warm_tok + phi_free[phi]) if k == 0 else [])
                    tz = P.op("vector", lambda E, ot=ot, zc=zc, phi=phi: E.tensor_tensor(
                        out=tmp[:], in0=PH[phi][:], in1=zs[ot][:, zc], op=ALU.add),
                        deps=[mm, ldz[-1], (x1_toks[-1][0] if x1_toks else None),
                              (phi_free[0][0] if phi_free[0] else None),
                              (phi_free[1][0] if phi_free[1] else None)] + chunk_fence)
                    th = P.op("scalar", lambda E, ot=ot: E.activation(
                        out=ht[:], in_=tmp[:], func=AF.Tanh, bias=bo_s[:, ot:ot + 1]),
                        deps=[tz, (x1_toks[-1][0] if x1_toks else None)] + ld_bo + chunk_fence)
                    phi_free[phi] = [th]
                    a1 = P.op("vector", lambda E, ot=ot: E.tensor_tensor(
                        out=x1c[ot][:], in0=x0c[ot][:], in1=a0c[ot][:], op=ALU.add),
                        deps=[ld_x0[ot], ld_a0[ot]] + chunk_fence)
                    a2 = P.op("vector", lambda E, ot=ot: E.tensor_tensor(
                        out=x1c[ot][:], in0=x1c[ot][:], in1=ht[:], op=ALU.add), deps=[a1, th])
                    st1 = P.dma("gpsimd", lambda E, ot=ot, tc=tc: E.dma_start(
                        out=x1T_d[ot * 128:(ot + 1) * 128, tc], in_=x1c[ot][:]), deps=[a2])
                    cb = P.op("vector", lambda E, ot=ot: E.tensor_copy(out=xbc[ot][:], in_=x1c[ot][:]),
                              deps=[a2] + chunk_fence)
                    x1_toks.append((a2, cb, st1))
                    out_toks.append(st1)
                cps = [t[1] for t in x1_toks]
                mm = None
                for k in range(KT):
                    mm = P.op("tensor", lambda E, k=k: E.matmul(
                        PQ[0][0:1, :], ones_b[:], xbc[k][:], start=(k == 0), stop=(k == KT - 1)),
                        deps=([cps[-1], ld_ob] + chunk_fence) if k == 0 else [])
                em0 = P.op("scalar", lambda E: E.copy(out=m_row[:], in_=PQ[0][0:1, :]), deps=[mm])
                em = P.op("vector", lambda E: E.tensor_scalar_mul(out=m_row[:], in0=m_row[:], scalar1=1.0 / D), deps=[em0])
                sq_toks = [P.op("scalar", lambda E, k=k: E.activation(out=t1bs[k][:], in_=xbc[k][:], func=AF.Square),
                               deps=[cps[k]] + chunk_fence) for k in range(KT)]
                mmq = None
                for k in range(KT):
                    mmq = P.op("tensor", lambda E, k=k: E.matmul(
                        PQ[1][0:1, :], ones_b[:], t1bs[k][:], start=(k == 0), stop=(k == KT - 1)),
                        deps=([sq_toks[-1], ld_ob] + chunk_fence) if k == 0 else [])
                ev0 = P.op("scalar", lambda E: E.copy(out=v_row[:], in_=PQ[1][0:1, :]), deps=[mmq])
                ev = P.op("vector", lambda E: E.tensor_scalar_mul(out=v_row[:], in0=v_row[:], scalar1=1.0 / D), deps=[ev0])
                tv1 = P.op("vector", lambda E: E.tensor_mul(out=vt_row[:], in0=m_row[:], in1=m_row[:]), deps=[em])
                tv2 = P.op("vector", lambda E: E.tensor_sub(out=v_row[:], in0=v_row[:], in1=vt_row[:]), deps=[tv1, ev])
                tv3 = P.op("vector", lambda E: E.tensor_scalar_add(out=v_row[:], in0=v_row[:], scalar1=LN_EPS), deps=[tv2])
                tr1 = P.op("scalar", lambda E: E.activation(out=v_row[:], in_=v_row[:], func=AF.Ln), deps=[tv3])
                tr1b = P.op("vector", lambda E: E.tensor_scalar_mul(out=v_row[:], in0=v_row[:], scalar1=-0.5), deps=[tr1])
                tr2 = P.op("scalar", lambda E: E.activation(out=v_row[:], in_=v_row[:], func=AF.Exp), deps=[tr1b])
                cmb = P.op("vector", lambda E: E.tensor_copy(out=mb_row[:], in_=m_row[:]), deps=[em])
                crb = P.op("vector", lambda E: E.tensor_copy(out=rb_row[:], in_=v_row[:]), deps=[tr2])
                t_m = P.op("tensor", lambda E: E.matmul(PQ[0][:], onesrow_b[:], mb_row[:], start=True, stop=True),
                           deps=[cmb, ld_or, em0])
                t_r = P.op("tensor", lambda E: E.matmul(PQ[1][:], onesrow_b[:], rb_row[:], start=True, stop=True),
                           deps=[crb, ld_or, ev0])
                ap_toks = []
                for k in range(KT):
                    a1 = P.op("vector", lambda E, k=k: E.tensor_tensor(out=tmp[:], in0=x1c[k][:], in1=PQ[0][:],
                                                                      op=ALU.subtract),
                              deps=[t_m, x1_toks[k][0]] + ([ap_toks[-1]] if ap_toks else [mmq]) + [phi_free[0][0], phi_free[1][0]])
                    a2 = P.op("vector", lambda E, k=k: E.tensor_tensor(out=xnc[k][:], in0=tmp[:], in1=PQ[1][:],
                                                                      op=ALU.mult),
                              deps=[a1, t_r])
                    ap_toks.append(a2)
                st_xn = [P.dma("gpsimd", lambda E, k=k, tc=tc: E.dma_start(
                    out=xn1T_d[k * 128:(k + 1) * 128, tc], in_=xnc[k][:]), deps=[ap_toks[k]]) for k in range(KT)]
                st_xnb = [P.dma("gpsimd", lambda E, k=k, tc=tc: E.dma_start(
                    out=xn1Tb_d[k * 128:(k + 1) * 128, tc], in_=xnc[k][:]), deps=[ap_toks[k]], ch="cast")
                    for k in range(KT)]
                out_toks += st_xn + st_xnb
                chunk_fence = P.fence()

            P.op("gpsimd", lambda E: E.engine_nop(), deps=out_toks)
    _phase3()
    P.set_floor(P.fence())
    def _phase4():
        with ExitStack() as stack:
            CH = 512
            NCH = T // CH
            NCH = T // CH
            def sb(name, shape, dt):
                return stack.enter_context(nc.sbuf_tensor("p4_" + name, shape, dt))

            def ps(name, shape, dt=F32):
                return stack.enter_context(nc.psum_tensor("p4_" + name, shape, dt))

            xbc = [sb(f"xbc{k}", [128, CH], BF16) for k in range(KT)]
            xprj = sb("xprj", [128, 96], BF16)
            xprjs = [sb(f"xprj{k}", [128, 96], BF16) for k in range(KT)]
            dtp = sb("dtp", [64, D], BF16)
            dtb_s = sb("dtb_s", [128, KT], F32)
            xp_sb = sb("xp_sb", [96, T], F32)
            drb = sb("drb", [64, T], BF16)
            A_s = [sb(f"A{k}", [128, NSt], F32) for k in range(KT)]
            D_s = sb("D_s", [128, KT], F32)
            dc2 = [sb(f"dc{i}", [128, CH], F32) for i in range(2)]
            xc2 = [sb(f"xc{i}", [128, CH], F32) for i in range(2)]
            uc2 = [sb(f"uc{i}", [128, CH], F32) for i in range(2)]
            dA = sb("dA", [128, NSt * CH], F32)
            dBu = sb("dBu", [128, NSt * CH], F32)
            hbuf = sb("hbuf", [128, NSt * CH], F32)
            zb2 = [sb(f"zb{i}", [128, NSt * CH], BF16) for i in range(2)]
            Brep = sb("Brep", [128, NSt * CH], BF16)
            Crep = sb("Crep", [128, NSt * CH], BF16)
            hcarry = [sb(f"hc{k}", [128, NSt], F32) for k in range(KT)]
            y1c2 = [sb(f"y1c{i}", [128, CH], F32) for i in range(2)]
            sp1 = sb("sp1", [128, CH], F32)
            sp2 = sb("sp2", [128, CH], F32)
            PXP = ps("PXP", [96, 512])
            PDT = ps("PDT", [128, 512])

            warm = sb("warm", [1, 32], F32)
            w0 = P.op("vector", lambda E: E.memset(warm[:], 1.0))
            w1 = P.op("scalar", lambda E: E.activation(out=warm[:], in_=warm[:], func=AF.Abs), deps=[w0])
            w2 = P.op("scalar", lambda E: E.activation(out=warm[:], in_=warm[:], func=AF.Ln), deps=[w1])
            w3 = P.op("scalar", lambda E: E.activation(out=warm[:], in_=warm[:], func=AF.Exp), deps=[w2])
            warm_tok = [w3]

            ld_xp = [P.dma("gpsimd", lambda E, k=k: E.dma_start(
                out=xprjs[k][:], in_=xprojT_d[k * 128:(k + 1) * 128, :]), ch="cast") for k in range(KT)]
            ld_dtp = P.dma("gpsimd", lambda E: E.dma_start(out=dtp[:], in_=dtprojT_d[:]), ch="castn0")
            ld_dtb = [P.dma("sync", lambda E, o=o: E.dma_start(
                out=dtb_s[:, o:o + 1], in_=dtb_d[o * 128:(o + 1) * 128].unsqueeze(1))) for o in range(KT)]
            ld_al = [P.dma("sync", lambda E, k=k: E.dma_start(
                out=A_s[k][:], in_=alog_d[k * 128:(k + 1) * 128, :])) for k in range(KT)]
            ld_dv = [P.dma("sync", lambda E, o=o: E.dma_start(
                out=D_s[:, o:o + 1], in_=dvec_d[o * 128:(o + 1) * 128].unsqueeze(1))) for o in range(KT)]
            load_fence = ld_xp + [ld_dtp] + ld_dtb + ld_al + ld_dv + warm_tok
            a_toks = []
            for k in range(KT):
                t1 = P.op("scalar", lambda E, k=k: E.activation(out=A_s[k][:], in_=A_s[k][:], func=AF.Exp),
                          deps=load_fence)
                t2 = P.op("vector", lambda E, k=k: E.tensor_scalar_mul(out=A_s[k][:], in0=A_s[k][:], scalar1=-1.0),
                          deps=[t1])
                a_toks.append(t2)

            out_toks = []
            batch_fence = list(load_fence) + a_toks
            for bi in range(B2):
                xp_toks = []
                pxp_free = list(batch_fence)
                for c in range(NCH):
                    tc = slice(bi * T + c * CH, bi * T + (c + 1) * CH)
                    ld_x = [P.dma("gpsimd", lambda E, k=k, tc=tc: E.dma_start(
                        out=xbc[k][:], in_=xn1T_d[k * 128:(k + 1) * 128, tc]),
                        deps=batch_fence + xp_toks[-1:], ch="cast") for k in range(KT)]
                    mm = None
                    for k in range(KT):
                        mm = P.op("tensor", lambda E, k=k: E.matmul(
                            PXP[:], xprjs[k][:], xbc[k][:], start=(k == 0), stop=(k == KT - 1)),
                            deps=(list(ld_x) + pxp_free) if k == 0 else [])
                    evt = P.op("scalar", lambda E, c=c: E.copy(out=xp_sb[:, c * CH:(c + 1) * CH], in_=PXP[:]),
                               deps=[mm] + batch_fence)
                    pxp_free = [evt]
                    xp_toks.append(evt)
                cdr = P.op("vector", lambda E: E.tensor_copy(out=drb[:], in_=xp_sb[0:64, :]), deps=xp_toks + batch_fence)
                del_toks = []
                pdt_free = list(batch_fence)
                for ot in range(KT):
                    for c in range(NCH):
                        mm = P.op("tensor", lambda E, ot=ot, c=c: E.matmul(
                            PDT[:], dtp[:, ot * 128:(ot + 1) * 128], drb[:, c * CH:(c + 1) * CH],
                            start=True, stop=True), deps=([cdr] + pdt_free))
                        s0 = P.op("vector", lambda E, ot=ot: E.tensor_scalar_add(
                            out=sp1[:], in0=PDT[:], scalar1=dtb_s[:, ot:ot + 1]), deps=[mm] + del_toks[-1:])
                        pdt_free = [s0]
                        sa = P.op("scalar", lambda E: E.activation(out=sp2[:], in_=sp1[:], func=AF.Abs), deps=[s0] + del_toks[-1:])
                        sn = P.op("vector", lambda E: E.tensor_scalar_mul(out=sp2[:], in0=sp2[:], scalar1=-1.0), deps=[sa])
                        se = P.op("scalar", lambda E: E.activation(out=sp2[:], in_=sp2[:], func=AF.Exp), deps=[sn])
                        s1 = P.op("vector", lambda E: E.tensor_scalar_add(out=sp2[:], in0=sp2[:], scalar1=1.0), deps=[se])
                        sl = P.op("scalar", lambda E: E.activation(out=sp2[:], in_=sp2[:], func=AF.Ln), deps=[s1])
                        sr = P.op("vector", lambda E: E.tensor_scalar_max(out=sp1[:], in0=sp1[:], scalar1=0.0), deps=[sa])
                        sd = P.op("vector", lambda E: E.tensor_add(out=sp1[:], in0=sp1[:], in1=sp2[:]), deps=[sl, sr])
                        std = P.dma("gpsimd", lambda E, ot=ot, c=c: E.dma_start(
                            out=delta_scr_d[ot * 128:(ot + 1) * 128, c * CH:(c + 1) * CH], in_=sp1[:]), deps=[sd])
                        del_toks.append(std)
                rep_toks = []
                scan_fence = list(del_toks)
                prev_dma = {0: [], 1: []}
                y_toks = []
                for c in range(NCH):
                    prel = P.fence()
                    b1 = P.dma("gpsimd", lambda E, c=c: E.dma_start(
                        out=bc_scr_d[0, :].rearrange("(n t) -> n t", t=CH), in_=xp_sb[64:80, c * CH:(c + 1) * CH]),
                        deps=xp_toks + prel, ch="n64")
                    b2 = P.dma("gpsimd", lambda E, c=c: E.dma_start(
                        out=bc_scr_d[1, :].rearrange("(n t) -> n t", t=CH), in_=xp_sb[80:96, c * CH:(c + 1) * CH]),
                        deps=xp_toks + prel, ch="n64")
                    b3 = P.dma("gpsimd", lambda E: E.dma_start(
                        out=Brep[:], in_=bc_scr_d[0:1, :].broadcast_to([128, NSt * CH])), deps=[b1, b2] + prel,
                        ch="bcast")
                    b4 = P.dma("gpsimd", lambda E: E.dma_start(
                        out=Crep[:], in_=bc_scr_d[1:2, :].broadcast_to([128, NSt * CH])), deps=[b1, b2] + prel,
                        ch="bcast")
                    rep_done = [b3, b4]
                    dt_toks = []
                    for dt in range(KT):
                        pb = (c * KT + dt) % 2
                        dc, xc, uc = dc2[pb], xc2[pb], uc2[pb]
                        tc = slice(bi * T + c * CH, bi * T + (c + 1) * CH)
                        l1 = P.dma("sync", lambda E, dt=dt, c=c, dc=dc: E.dma_start(
                            out=dc[:], in_=delta_scr_d[dt * 128:(dt + 1) * 128, c * CH:(c + 1) * CH]),
                            deps=scan_fence + list(dt_toks))
                        l2 = P.dma("sync", lambda E, dt=dt, tc=tc, xc=xc: E.dma_start(
                            out=xc[:], in_=xn1T_d[dt * 128:(dt + 1) * 128, tc]), deps=scan_fence + list(dt_toks))
                        tu = P.op("vector", lambda E, dc=dc, xc=xc, uc=uc: E.tensor_mul(out=uc[:], in0=dc[:], in1=xc[:]),
                                  deps=[l1, l2] + list(dt_toks))
                        ex_toks = []
                        for n in range(NSt):
                            ex_toks.append(P.op("scalar", lambda E, n=n, dt=dt, dc=dc: E.activation(
                                out=dA[:, n * CH:(n + 1) * CH], in_=dc[:], func=AF.Exp,
                                scale=A_s[dt][:, n:n + 1]),
                                deps=[l1] + list(dt_toks) + a_toks + (list(rep_toks) if dt == 0 else [])))
                        texp = ex_toks[-1]
                        tbu = P.op("vector", lambda E, uc=uc: E.tensor_tensor(
                            out=dBu[:].rearrange("p (n t) -> p n t", n=NSt),
                            in0=uc[:].unsqueeze(1).broadcast_to([128, NSt, CH]),
                            in1=Brep[:].rearrange("p (n t) -> p n t", n=NSt), op=ALU.mult),
                            deps=[tu] + rep_done + list(dt_toks))
                        sc_toks = []
                        fold_toks = []
                        if c > 0:
                            for n in range(NSt):
                                fold_toks.append(P.op("vector", lambda E, n=n, dt=dt: E.scalar_tensor_tensor(
                                    out=dBu[:, n * CH:n * CH + 1], in0=dA[:, n * CH:n * CH + 1],
                                    scalar=hcarry[dt][:, n:n + 1], in1=dBu[:, n * CH:n * CH + 1],
                                    op0=ALU.mult, op1=ALU.add), deps=[ex_toks[n], tbu]))
                        for n in range(NSt):
                            sc_toks.append(P.op("vector", lambda E, n=n: E.tensor_tensor_scan(
                                out=hbuf[:, n * CH:(n + 1) * CH], data0=dA[:, n * CH:(n + 1) * CH],
                                data1=dBu[:, n * CH:(n + 1) * CH], initial=0.0, op0=ALU.mult, op1=ALU.add),
                                deps=[ex_toks[n], tbu] + fold_toks[-1:] + (dt_toks[-1:] if c == 0 else [])))
                        hc = P.op("vector", lambda E, dt=dt: E.tensor_copy(
                            out=hcarry[dt][:].unsqueeze(2),
                            in_=hbuf[:].rearrange("p (n t) -> p n t", n=NSt)[:, :, CH - 1:CH]), deps=sc_toks)
                        ybi0 = (c * KT + dt) % 2
                        zb = zb2[ybi0]
                        tz = P.op("vector", lambda E, zb=zb: E.tensor_tensor(out=zb[:], in0=hbuf[:], in1=Crep[:], op=ALU.mult),
                                  deps=sc_toks + rep_done)
                        f1 = P.op("vector", lambda E, zb=zb: E.tensor_tensor(
                            out=zb[:, 0:8 * CH], in0=zb[:, 0:8 * CH], in1=zb[:, 8 * CH:16 * CH], op=ALU.add), deps=[tz])
                        f2 = P.op("vector", lambda E, zb=zb: E.tensor_tensor(
                            out=zb[:, 0:4 * CH], in0=zb[:, 0:4 * CH], in1=zb[:, 4 * CH:8 * CH], op=ALU.add), deps=[f1])
                        f3 = P.op("vector", lambda E, zb=zb: E.tensor_tensor(
                            out=zb[:, 0:2 * CH], in0=zb[:, 0:2 * CH], in1=zb[:, 2 * CH:4 * CH], op=ALU.add), deps=[f2])
                        f4 = P.op("vector", lambda E, zb=zb: E.tensor_tensor(
                            out=zb[:, 0:CH], in0=zb[:, 0:CH], in1=zb[:, CH:2 * CH], op=ALU.add), deps=[f3])
                        ybi = (c * KT + dt) % 2
                        ty = P.op("vector", lambda E, dt=dt, ybi=ybi, zb=zb, xc=xc: E.scalar_tensor_tensor(
                            out=y1c2[ybi][:], in0=xc[:], scalar=D_s[:, dt:dt + 1], in1=zb[:, 0:CH],
                            op0=ALU.mult, op1=ALU.add), deps=[f4, prev_dma[ybi][0] if prev_dma[ybi] else None])
                        sty = P.dma("gpsimd", lambda E, dt=dt, tc=tc, ybi=ybi: E.dma_start(
                            out=y1T_d[dt * 128:(dt + 1) * 128, tc], in_=y1c2[ybi][:]), deps=[ty])
                        prev_dma[ybi] = [sty]
                        out_toks.append(sty)
                        dt_toks = [ty, texp, tbu, hc]
                    rep_toks = dt_toks
                batch_fence = P.fence()

            P.op("gpsimd", lambda E: E.engine_nop(), deps=out_toks)
    _phase4()
    P.set_floor(P.fence())
    def _phase5():
        with ExitStack() as stack:
            def sb(name, shape, dt):
                return stack.enter_context(nc.sbuf_tensor("p5_" + name, shape, dt))

            def ps(name, shape, dt=F32):
                return stack.enter_context(nc.psum_tensor("p5_" + name, shape, dt))

            CH = 512
            NCH = TT_ // CH
            pool = sb("pool", [128, 8 * IM], BF16)
            qin_f = sb("qin_f", [128, IM], F32)
            qin_b = sb("qin_b", [128, IM], BF16)
            u2 = sb("u2", [128, IM], F32)
            t1q = sb("t1q", [128, IM], F32)
            loq = sb("loq", [128, IM], F32)
            mkq = sb("mkq", [128, IM], U8)
            sgq = sb("sgq", [128, IM], F32)
            qout = sb("qout", [128, IM], BF16)
            bmq = sb("bmq", [128, IM // 16], F32)
            rs2q = sb("rs2q", [128, IM // 16], F32)
            shq = sb("shq", [128, IM // 16], F32)
            xqT = [sb(f"xqT{k}", [128, CH], BF16) for k in range(KT)]
            hqT = [sb(f"hqT{k}", [128, CH], BF16) for k in range(OT_I)]
            gsb = [sb(f"gsb{i}", [128, CH], BF16) for i in range(2)]
            hsb = [sb(f"hsb{i}", [128, CH], BF16) for i in range(2)]
            x1c2 = [sb(f"x1c{i}", [128, CH], F32) for i in range(2)]
            y1c2 = [sb(f"y1c{i}", [128, CH], F32) for i in range(2)]
            x2c2 = [sb(f"x2c{i}", [128, CH], F32) for i in range(2)]
            PG = ps("PG", [128, 512])
            PU = ps("PU", [128, 512])
            PD = ps("PD", [128, 512])

            warm = sb("warmq", [1, 32], F32)
            w0 = P.op("vector", lambda E: E.memset(warm[:], 1.0))
            w1 = P.op("scalar", lambda E: E.activation(out=warm[:], in_=warm[:], func=AF.Sign), deps=[w0])
            w2 = P.op("scalar", lambda E: E.activation(out=warm[:], in_=warm[:], func=AF.Silu), deps=[w1])
            warm_tok = [w2]

            def emit_quant(qin_ap, W, deps):
                NBq = W // 16
                t0 = P.op("vector", lambda E, qin_ap=qin_ap, W=W, NBq=NBq: E.tensor_reduce(
                    out=bmq[:, 0:NBq], in_=qin_ap.rearrange("p (nb blk) -> p nb blk", blk=16),
                    axis=AX.X, op=ALU.max, apply_absolute_value=True), deps=deps)
                t1_ = P.op("vector", lambda E, NBq=NBq: E.tensor_scalar_max(out=bmq[:, 0:NBq], in0=bmq[:, 0:NBq],
                                                                           scalar1=6e-12), deps=[t0])
                t2 = P.op("vector", lambda E, NBq=NBq: E.reciprocal(out=rs2q[:, 0:NBq], in_=bmq[:, 0:NBq]), deps=[t1_])
                t3 = P.op("vector", lambda E, NBq=NBq: E.tensor_scalar_mul(out=rs2q[:, 0:NBq], in0=rs2q[:, 0:NBq],
                                                                          scalar1=12.0), deps=[t2])
                t4 = P.op("vector", lambda E, NBq=NBq: E.tensor_scalar_mul(out=shq[:, 0:NBq], in0=bmq[:, 0:NBq],
                                                                          scalar1=1.0 / 12.0), deps=[t2])
                t5 = P.op("scalar", lambda E, qin_ap=qin_ap, W=W: E.activation(out=sgq[:, 0:W], in_=qin_ap, func=AF.Sign),
                          deps=deps + warm_tok)
                t6 = P.op("vector", lambda E, qin_ap=qin_ap, W=W, NBq=NBq: E.tensor_tensor(
                    out=u2[:, 0:W].rearrange("p (nb blk) -> p nb blk", blk=16),
                    in0=qin_ap.rearrange("p (nb blk) -> p nb blk", blk=16),
                    in1=rs2q[:, 0:NBq].unsqueeze(2).broadcast_to([128, NBq, 16]), op=ALU.mult), deps=[t3] + deps)
                t7 = P.op("vector", lambda E, W=W: E.scalar_tensor_tensor(
                    out=u2[:, 0:W], in0=u2[:, 0:W], scalar=-1.0, in1=u2[:, 0:W], op0=ALU.mult, op1=ALU.max), deps=[t6])
                t8 = P.op("vector", lambda E, W=W: E.scalar_tensor_tensor(
                    out=t1q[:, 0:W], in0=u2[:, 0:W], scalar=C_VELT, in1=u2[:, 0:W],
                    op0=ALU.mult, op1=ALU.subtract), deps=[t7])
                t9 = P.op("vector", lambda E, W=W: E.scalar_tensor_tensor(
                    out=t1q[:, 0:W], in0=u2[:, 0:W], scalar=C_VELT, in1=t1q[:, 0:W],
                    op0=ALU.mult, op1=ALU.subtract), deps=[t8])
                ta = P.op("vector", lambda E, W=W: E.tensor_single_scalar(
                    out=loq[:, 0:W], in_=u2[:, 0:W], scalar=0.5, op=ALU.is_gt), deps=[t7])
                tb = P.op("vector", lambda E, W=W: E.scalar_tensor_tensor(
                    out=loq[:, 0:W], in0=u2[:, 0:W], scalar=1.5, in1=loq[:, 0:W], op0=ALU.is_gt, op1=ALU.add), deps=[ta])
                tcm = P.op("vector", lambda E, W=W: E.tensor_single_scalar(
                    out=mkq[:, 0:W], in_=u2[:, 0:W], scalar=2.0, op=ALU.is_gt), deps=[t7])
                td = P.op("vector", lambda E, W=W: E.select(
                    out=u2[:, 0:W], mask=mkq[:, 0:W], on_true=t1q[:, 0:W], on_false=loq[:, 0:W]), deps=[t9, tb, tcm])
                te = P.op("vector", lambda E, W=W: E.tensor_tensor(
                    out=u2[:, 0:W], in0=u2[:, 0:W], in1=sgq[:, 0:W], op=ALU.mult), deps=[td, t5])
                tf = P.op("vector", lambda E, W=W, NBq=NBq: E.tensor_tensor(
                    out=qout[:, 0:W].rearrange("p (nb blk) -> p nb blk", blk=16),
                    in0=u2[:, 0:W].rearrange("p (nb blk) -> p nb blk", blk=16),
                    in1=shq[:, 0:NBq].unsqueeze(2).broadcast_to([128, NBq, 16]), op=ALU.mult), deps=[te, t4])
                return tf

            out_toks = []
            prevq = list(warm_tok)
            for rt in range(TT_ // 128):
                ld = P.dma("sync", lambda E, rt=rt: E.dma_start(
                    out=qin_b[:, 0:D], in_=xn1Tb_d[:, rt * 128:(rt + 1) * 128], transpose=True),
                    deps=prevq, ch="tp")
                tf = emit_quant(qin_b[:, 0:D], D, [ld])
                st = P.dma("gpsimd", lambda E, rt=rt: E.dma_start(
                    out=xq_scr_d[rt * 128:(rt + 1) * 128, :], in_=qout[:, 0:D]), deps=[tf])
                prevq = [tf, st]
            for rt in range(OT_I):
                ld = P.dma("sync", lambda E, rt=rt: E.dma_start(
                    out=qin_f[:, 0:D], in_=wg_d[rt * 128:(rt + 1) * 128, :]), deps=prevq)
                tf = emit_quant(qin_f[:, 0:D], D, [ld])
                st = P.dma("gpsimd", lambda E, rt=rt: E.dma_start(
                    out=wgq_scr_d[rt * 128:(rt + 1) * 128, :], in_=qout[:, 0:D]), deps=[tf])
                prevq = [tf, st]
            gx_done = P.fence()
            for (src_d, dst_d, nt, W) in ((wu_d, wuq_scr_d, OT_I, D),
                                          (wd_d, wdq_scr_d, KT, IM)):
                for rt in range(nt):
                    ld = P.dma("gpsimd", lambda E, src_d=src_d, rt=rt, W=W: E.dma_start(
                        out=qin_f[:, 0:W], in_=src_d[rt * 128:(rt + 1) * 128, :]), deps=prevq, ch="wql")
                    tf = emit_quant(qin_f[:, 0:W], W, [ld])
                    st = P.dma("gpsimd", lambda E, dst_d=dst_d, rt=rt, W=W: E.dma_start(
                        out=dst_d[rt * 128:(rt + 1) * 128, :], in_=qout[:, 0:W]), deps=[tf])
                    prevq = [tf, st]

            ldp = [P.dma("sync", lambda E, k=k: E.dma_start(
                out=pool[:, k * IM:(k + 1) * IM], in_=wgq_scr_d[:, k * 128:(k + 1) * 128], transpose=True),
                deps=gx_done, ch="tp") for k in range(KT)]
            pg_free = []
            gs_toks = []
            for ch in range(NCH):
                ldx = [P.dma("sync", lambda E, k=k, ch=ch: E.dma_start(
                    out=xqT[k][:], in_=xq_scr_d[ch * CH:(ch + 1) * CH, k * 128:(k + 1) * 128], transpose=True),
                    deps=gx_done + gs_toks[-2:], ch="tp") for k in range(KT)]
                for ot in range(OT_I):
                    gi = ot % 2
                    mm = None
                    for k in range(KT):
                        mm = P.op("tensor", lambda E, ot=ot, k=k: E.matmul(
                            PG[:], pool[:, k * IM + ot * 128: k * IM + (ot + 1) * 128], xqT[k][:],
                            start=(k == 0), stop=(k == KT - 1)),
                            deps=(list(ldx) + list(ldp) + pg_free) if k == 0 else [])
                    ev = P.op("scalar", lambda E, gi=gi: E.activation(out=gsb[gi][:], in_=PG[:], func=AF.Silu),
                              deps=[mm] + gs_toks[-2:] + warm_tok)
                    pg_free = [ev]
                    st = P.dma("scalar", lambda E, ot=ot, ch=ch, gi=gi: E.dma_start(
                        out=gs_scr_d[ot * 128:(ot + 1) * 128, ch * CH:(ch + 1) * CH], in_=gsb[gi][:]), deps=[ev],
                        ch="sst")
                    gs_toks.append(st)
            g_done = P.fence()

            ldp2 = [P.dma("sync", lambda E, k=k: E.dma_start(
                out=pool[:, k * IM:(k + 1) * IM], in_=wuq_scr_d[:, k * 128:(k + 1) * 128], transpose=True),
                deps=g_done, ch="tp") for k in range(KT)]
            pu_free = []
            h_toks = []
            hq_toks = []
            for ch in range(NCH):
                ldx = [P.dma("sync", lambda E, k=k, ch=ch: E.dma_start(
                    out=xqT[k][:], in_=xq_scr_d[ch * CH:(ch + 1) * CH, k * 128:(k + 1) * 128], transpose=True),
                    deps=g_done + h_toks[-2:], ch="tp") for k in range(KT)]
                ch_sts = []
                for ot in range(OT_I):
                    gi = ot % 2
                    mm = None
                    for k in range(KT):
                        mm = P.op("tensor", lambda E, ot=ot, k=k: E.matmul(
                            PU[:], pool[:, k * IM + ot * 128: k * IM + (ot + 1) * 128], xqT[k][:],
                            start=(k == 0), stop=(k == KT - 1)),
                            deps=(list(ldx) + list(ldp2) + pu_free) if k == 0 else [])
                    lg = P.dma("sync", lambda E, ot=ot, ch=ch, gi=gi: E.dma_start(
                        out=gsb[gi][:], in_=gs_scr_d[ot * 128:(ot + 1) * 128, ch * CH:(ch + 1) * CH]),
                        deps=g_done + h_toks[-2:])
                    th = P.op("vector", lambda E, gi=gi: E.tensor_tensor(
                        out=hsb[gi][:], in0=PU[:], in1=gsb[gi][:], op=ALU.mult), deps=[mm, lg] + h_toks[-2:])
                    pu_free = [th]
                    st = P.dma("gpsimd", lambda E, ot=ot, ch=ch, gi=gi: E.dma_start(
                        out=h_scr_d[ot * 128:(ot + 1) * 128, ch * CH:(ch + 1) * CH], in_=hsb[gi][:]), deps=[th])
                    h_toks.append(st)
                    ch_sts.append(st)
                for rt in range(CH // 128):
                    gt = ch * (CH // 128) + rt
                    ld = P.dma("sync", lambda E, gt=gt: E.dma_start(
                        out=qin_b[:], in_=h_scr_d[:, gt * 128:(gt + 1) * 128], transpose=True),
                        deps=ch_sts + prevq, ch="tp")
                    tf = emit_quant(qin_b[:], IM, [ld])
                    st2 = P.dma("gpsimd", lambda E, gt=gt: E.dma_start(
                        out=hq_scr_d[gt * 128:(gt + 1) * 128, :], in_=qout[:, 0:IM]), deps=[tf])
                    prevq = [tf, st2]
                    hq_toks.append(st2)
            hq_done = P.fence()

            ldp3 = [P.dma("sync", lambda E, kk=kk: E.dma_start(
                out=pool[:, kk * D:(kk + 1) * D], in_=wdq_scr_d[:, kk * 128:(kk + 1) * 128], transpose=True),
                deps=hq_done, ch="tp") for kk in range(OT_I)]
            pd_free = []
            x2_toks = []
            for ch in range(NCH):
                ldh = [P.dma("sync", lambda E, kk=kk, ch=ch: E.dma_start(
                    out=hqT[kk][:], in_=hq_scr_d[ch * CH:(ch + 1) * CH, kk * 128:(kk + 1) * 128], transpose=True),
                    deps=hq_done + x2_toks[-2:], ch="tp") for kk in range(OT_I)]
                for ot in range(KT):
                    gi = ot % 2
                    l1 = P.dma("sync", lambda E, ot=ot, ch=ch, gi=gi: E.dma_start(
                        out=x1c2[gi][:], in_=x1T_d[ot * 128:(ot + 1) * 128, ch * CH:(ch + 1) * CH]),
                        deps=hq_done + x2_toks[-2:])
                    l2 = P.dma("sync", lambda E, ot=ot, ch=ch, gi=gi: E.dma_start(
                        out=y1c2[gi][:], in_=y1T_d[ot * 128:(ot + 1) * 128, ch * CH:(ch + 1) * CH]),
                        deps=hq_done + x2_toks[-2:])
                    mm = None
                    for kk in range(OT_I):
                        mm = P.op("tensor", lambda E, ot=ot, kk=kk: E.matmul(
                            PD[:], pool[:, kk * D + ot * 128: kk * D + (ot + 1) * 128], hqT[kk][:],
                            start=(kk == 0), stop=(kk == OT_I - 1)),
                            deps=(list(ldh) + list(ldp3) + pd_free) if kk == 0 else [])
                    a1 = P.op("vector", lambda E, gi=gi: E.tensor_tensor(
                        out=x2c2[gi][:], in0=PD[:], in1=x1c2[gi][:], op=ALU.add), deps=[mm, l1] + x2_toks[-2:])
                    pd_free = [a1]
                    a2 = P.op("vector", lambda E, gi=gi: E.tensor_tensor(
                        out=x2c2[gi][:], in0=x2c2[gi][:], in1=y1c2[gi][:], op=ALU.add), deps=[a1, l2])
                    st = P.dma("gpsimd", lambda E, ot=ot, ch=ch, gi=gi: E.dma_start(
                        out=x2T_d[ot * 128:(ot + 1) * 128, ch * CH:(ch + 1) * CH], in_=x2c2[gi][:]), deps=[a2])
                    x2_toks.append(st)
                    out_toks.append(st)
            P.op("gpsimd", lambda E: E.engine_nop(), deps=out_toks)
    _phase5()
    with ExitStack() as stack:
        block = stack.enter_context(nc.Block())
        P.build(stack, block)
    return nc


def kernel(**inputs):
    x = np.asarray(inputs["L0_attn_x"])
    wqkvT = np.ascontiguousarray(np.asarray(inputs["L0_attn_w_qkv"]).T)
    woutT = np.ascontiguousarray(np.asarray(inputs["L0_attn_w_out"]).T)
    bout = np.ascontiguousarray(np.asarray(inputs["L0_attn_b_out"]))
    g16 = np.ascontiguousarray(np.asarray(inputs["L0_attn_norm_g"]).reshape(1, H).astype(np.float32))
    b16 = np.ascontiguousarray(np.asarray(inputs["L0_attn_norm_b"]).reshape(1, H).astype(np.float32))
    eye = np.eye(128, dtype=np.float32)
    ones = np.ones((128, 1), dtype=np.float32)
    wpackT = np.ascontiguousarray(np.concatenate(
        [np.asarray(inputs[f"L0_ffn_{n}"]).T for n in ("wf", "wi", "wc", "uf", "ui", "uc", "uo")], axis=1))
    bias3 = np.ascontiguousarray(np.stack([np.asarray(inputs["L0_ffn_bf"]),
                                           np.asarray(inputs["L0_ffn_bi"]),
                                           np.asarray(inputs["L0_ffn_bc"])]))
    incv = np.ascontiguousarray(np.asarray(inputs["L0_ffn_in_c"]))
    woT = np.ascontiguousarray(np.asarray(inputs["L0_ffn_wo"]).T)
    bo = np.ascontiguousarray(np.asarray(inputs["L0_ffn_bo"]))
    xprojT = np.ascontiguousarray(np.asarray(inputs["L1_attn_x_proj_w"]).T)
    dtprojT = np.ascontiguousarray(np.asarray(inputs["L1_attn_dt_proj_w"]).T)
    dtb = np.ascontiguousarray(np.asarray(inputs["L1_attn_dt_proj_b"]))
    alog = np.ascontiguousarray(np.asarray(inputs["L1_attn_A_log"]))
    dvec = np.ascontiguousarray(np.asarray(inputs["L1_attn_D"]))
    wg = np.ascontiguousarray(np.asarray(inputs["L1_ffn_w_gate"]))
    wu = np.ascontiguousarray(np.asarray(inputs["L1_ffn_w_up"]))
    wd = np.ascontiguousarray(np.asarray(inputs["L1_ffn_w_down"]))
    nc = build_fused()
    in_maps = []
    for c in range(NCORES):
        x0T = np.ascontiguousarray(np.concatenate([x[2 * c].T, x[2 * c + 1].T], axis=1))
        sl = slice(c * SS, (c + 1) * SS)
        x0Ts = np.ascontiguousarray(np.concatenate([x[b, sl, :].T for b in range(NB)], axis=1))
        in_maps.append({"x0T": x0T, "x0Ts": x0Ts, "wqkvT": wqkvT, "woutT": woutT, "bout": bout,
                        "g16": g16, "b16": b16, "eye": eye, "ones": ones, "wpackT": wpackT,
                        "bias3": bias3, "incv": incv, "woT": woT, "bo": bo, "xprojT": xprojT,
                        "dtprojT": dtprojT, "dtb": dtb, "alog": alog, "dvec": dvec,
                        "wg": wg, "wu": wu, "wd": wd})
    res = run_bass_kernel_spmd(nc, in_maps, core_ids=list(range(NCORES)))
    out = np.empty((NB, T, D), dtype=np.float32)
    for c in range(NCORES):
        x2T = np.asarray(res.results[c]["x2T"])
        out[2 * c] = x2T[:, 0:T].T
        out[2 * c + 1] = x2T[:, T:2 * T].T
    return out


# version 06 (final)
# speedup vs baseline: 1.1174x; 1.0490x over previous
import numpy as np
from contextlib import ExitStack

import concourse.bass as bass
import concourse.mybir as mybir
from concourse.bass_utils import run_bass_kernel_spmd

F32 = mybir.dt.float32
BF16 = mybir.dt.bfloat16
U8 = mybir.dt.uint8
AF = mybir.ActivationFunctionType
ALU = mybir.AluOpType
AX = mybir.AxisListType

NCORES = 8
D = 1024
NB = 16
T = 2048
H = 16
HD = 64
KT = D // 128
B2 = NB // NCORES
SS = T // NCORES
C_VELT = float((1 << 22) + 1)
LN_EPS = 1e-5


class Prog:
    CE = ("tensor", "vector", "scalar", "gpsimd", "sync")

    def __init__(self, nc):
        self.nc = nc
        self.recs = {e: [] for e in self.CE}
        self.nk = {}
        self.awaited = set()
        self.floor = []

    def set_floor(self, toks):
        self.floor = list(toks)

    def _add(self, eng, fn, deps, kind):
        deps = tuple(d for d in deps if d is not None) + tuple(self.floor)
        for d in deps:
            self.awaited.add(d)
        key = (eng, kind)
        self.nk.setdefault(key, 0)
        self.nk[key] += 1
        tok = (eng, kind, self.nk[key])
        self.recs[eng].append((fn, deps, tok))
        return tok

    def op(self, eng, fn, deps=()):
        return self._add(eng, fn, deps, "c")

    def dma(self, eng, fn, deps=(), ch="w128"):
        tok = self._add(eng, fn, deps, "d:" + ch)
        self.awaited.add(tok)
        return tok

    def last(self, eng, kind="c"):
        n = self.nk.get((eng, kind), 0)
        return (eng, kind, n) if n else None

    def fence(self):
        toks = []
        for (e, kind), n in self.nk.items():
            if n:
                toks.append((e, kind, n))
        return toks

    def build(self, stack, block):
        nc = self.nc
        sems = {}
        for (e, kind) in self.nk:
            sems[(e, kind)] = stack.enter_context(nc.semaphore(f"sem_{e}_{kind.replace(':', '_')}"))
        for s in sems.values():
            nc.gpsimd.sem_clear(s)
        nc.all_engine_barrier()
        target = {}
        counters = {}
        for e in self.CE:
            for fn, deps, tok in self.recs[e]:
                if tok in self.awaited:
                    key = (e, tok[1])
                    counters.setdefault(key, 0)
                    counters[key] += 1
                    target[tok] = counters[key] * (16 if tok[1].startswith("d") else 1)

        def mk_emit(name):
            def emit(E):
                seen = {}
                for fn, deps, tok in self.recs[name]:
                    for d in deps:
                        key = (d[0], d[1])
                        tv = target[d]
                        if seen.get(key, -1) < tv:
                            E.wait_ge(sems[key], tv)
                            seen[key] = tv
                    inst = fn(E)
                    if tok in self.awaited:
                        inst.then_inc(sems[(name, tok[1])], 16 if tok[1].startswith("d") else 1)
            return emit

        for name in self.CE:
            getattr(block, name)(mk_emit(name))


def build_l0_main(ln_only=False):
    nc = bass.Bass("TRN2", target_bir_lowering=False)
    x0T_d = nc.dram_tensor("x0T", [D, B2 * T], F32, kind="ExternalInput").ap()
    wqkvT_d = nc.dram_tensor("wqkvT", [D, 3 * D], F32, kind="ExternalInput").ap()
    woutT_d = nc.dram_tensor("woutT", [D, D], F32, kind="ExternalInput").ap()
    bout_d = nc.dram_tensor("bout", [D], F32, kind="ExternalInput").ap()
    g16_d = nc.dram_tensor("g16", [1, H], F32, kind="ExternalInput").ap()
    b16_d = nc.dram_tensor("b16", [1, H], F32, kind="ExternalInput").ap()
    eye_d = nc.dram_tensor("eye", [128, 128], F32, kind="ExternalInput").ap()
    ones_d = nc.dram_tensor("ones", [128, 1], F32, kind="ExternalInput").ap()

    attn0T_d = nc.dram_tensor("attn0T", [D, B2 * T], F32, kind="ExternalOutput").ap()
    xn0T_d = nc.dram_tensor("xn0T", [D, B2 * T], BF16, kind="ExternalOutput").ap()
    sburst_d = nc.dram_tensor("sburst", [2, T], F32, kind="Internal").ap()
    ao_scr_d = nc.dram_tensor("ao_scr", [D, T], BF16, kind="Internal").ap()

    P = Prog(nc)
    with ExitStack() as stack:
        def sb(name, shape, dt):
            return stack.enter_context(nc.sbuf_tensor(name, shape, dt))

        def ps(name, shape, dt=F32):
            return stack.enter_context(nc.psum_tensor(name, shape, dt))

        CH = 512
        xbc = [sb(f"xbc{k}", [128, CH], BF16) for k in range(KT)]
        xnc = [sb(f"xnc{k}", [128, CH], BF16) for k in range(KT)]
        ktc = [sb(f"ktc{k}", [128, CH], BF16) for k in range(KT)]
        vtc = [sb(f"vtc{k}", [128, CH], BF16) for k in range(KT)]
        q_sb = [sb(f"q{k}", [128, T], BF16) for k in range(KT)]
        wq = [sb(f"wq{k}", [128, 3 * D], BF16) for k in range(KT)]
        wo = [sb(f"wo{k}", [128, D], BF16) for k in range(KT)]
        kvrow = sb("kvrow", [128, CH], BF16)
        gkv_all = sb("gkv_all", [128, KT * 128], F32)
        gkvb = sb("gkvb", [128, KT * 128], BF16)
        aostage = sb("aostage", [128, T], BF16)
        aor = [sb(f"aor{k}", [128, CH], BF16) for k in range(KT)]
        aout = sb("aout", [128, T], F32)
        eyeb = sb("eyeb", [128, 128], BF16)
        ones_b = sb("ones_b", [128, 1], BF16)
        onesrow_b = sb("onesrow_b", [1, 128], BF16)
        mb_row = sb("mb_row", [1, CH], BF16)
        rb_row = sb("rb_row", [1, CH], BF16)
        bout_s = sb("bout_s", [128, KT], F32)
        brep = sb("brep", [128, KT], F32)
        g16_s = sb("g16_s", [1, H], F32)
        m_row = sb("m_row", [1, CH], F32)
        v_row = sb("v_row", [1, CH], F32)
        vt_row = sb("vt_row", [1, CH], F32)
        nsq = sb("nsq", [1, T], F32)
        srow = sb("srow", [1, T], F32)
        srep = sb("srep", [128, T], BF16)
        t1 = sb("t1", [128, CH], F32)
        t1bs = [sb(f"t1b{k}", [128, CH], BF16) for k in range(KT)]
        zq = sb("zq", [64, CH], BF16)
        ub = sb("ub", [128, KT], F32)
        ubt = sb("ubt", [128, KT], F32)
        ubb = sb("ubb", [128, KT], BF16)
        btot = sb("btot", [128, KT], F32)

        PQ = [ps(f"PQ{i}", [128, 512]) for i in range(2)]
        PT = [ps(f"PT{i}", [128, 64], BF16) for i in range(2)]
        GK = ps("GK", [64, 128])
        ZO = ps("ZO", [128, 512])
        NS = ps("NS", [1, 512])

        warm = sb("warm", [1, 32], F32)
        w0 = P.op("vector", lambda E: E.memset(warm[:], 1.0))
        w1 = P.op("scalar", lambda E: E.activation(out=warm[:], in_=warm[:], func=AF.Square), deps=[w0])
        w2 = P.op("scalar", lambda E: E.activation(out=warm[:], in_=warm[:], func=AF.Ln), deps=[w1])
        w3 = P.op("scalar", lambda E: E.activation(out=warm[:], in_=warm[:], func=AF.Exp), deps=[w2])
        warm_tok = [w3]

        ldw_q, ldw_o = [], []
        for k in range(KT):
            ldw_q.append(P.dma("gpsimd", lambda E, k=k: E.dma_start(out=wq[k][:], in_=wqkvT_d[k * 128:(k + 1) * 128, :]), ch="cast"))
            ldw_o.append(P.dma("gpsimd", lambda E, k=k: E.dma_start(out=wo[k][:], in_=woutT_d[k * 128:(k + 1) * 128, :]), ch="cast"))
        ld_eyeb = P.dma("gpsimd", lambda E: E.dma_start(out=eyeb[:], in_=eye_d[:]), ch="cast")
        ld_ob = P.dma("gpsimd", lambda E: E.dma_start(out=ones_b[:], in_=ones_d[:]), ch="cast")
        ld_or = P.dma("gpsimd", lambda E: E.dma_start(
            out=onesrow_b[:], in_=ones_d[:, 0].unsqueeze(0).broadcast_to([1, 128])), ch="castp0")
        ld_g = P.dma("sync", lambda E: E.dma_start(out=g16_s[:], in_=g16_d[:]), ch="p0")
        ld_bout = [P.dma("sync", lambda E, o=o: E.dma_start(
            out=bout_s[:, o:o + 1], in_=bout_d[o * 128:(o + 1) * 128].unsqueeze(1))) for o in range(KT)]
        ld_brep = [P.dma("sync", lambda E, h=h: E.dma_start(
            out=brep[(h % 2) * 64:(h % 2) * 64 + 64, h // 2:h // 2 + 1],
            in_=b16_d[0:1, h:h + 1].broadcast_to([64, 1])), ch=("n0" if h % 2 == 0 else "n64")) for h in range(H)]

        out_toks = []
        batch_fence = []
        for bi in range(B2):
            z1 = P.op("vector", lambda E: E.memset(gkv_all[:], 0.0), deps=batch_fence + warm_tok)
            z2 = P.op("vector", lambda E: E.memset(ub[:], 0.0), deps=batch_fence)
            acc_g = [z1]
            acc_u = [z2]
            chunk_fence = list(batch_fence)
            for c in range(4):
                tc = slice(bi * T + c * CH, bi * T + (c + 1) * CH)
                ld_x = [P.dma("gpsimd", lambda E, k=k, tc=tc: E.dma_start(
                    out=xbc[k][:], in_=x0T_d[k * 128:(k + 1) * 128, tc]),
                    deps=chunk_fence) for k in range(KT)]
                mm = None
                for k in range(KT):
                    mm = P.op("tensor", lambda E, k=k: E.matmul(
                        PQ[0][0:1, :], ones_b[:], xbc[k][:], start=(k == 0), stop=(k == KT - 1)),
                        deps=(list(ld_x) + [ld_ob] + warm_tok + chunk_fence) if k == 0 else [])
                em0 = P.op("scalar", lambda E: E.copy(out=m_row[:], in_=PQ[0][0:1, :]), deps=[mm])
                em = P.op("vector", lambda E: E.tensor_scalar_mul(out=m_row[:], in0=m_row[:], scalar1=1.0 / D), deps=[em0])
                sq_toks = [P.op("scalar", lambda E, k=k: E.activation(out=t1bs[k][:], in_=xbc[k][:], func=AF.Square),
                                deps=[ld_x[k]] + chunk_fence) for k in range(KT)]
                mmq = None
                for k in range(KT):
                    mmq = P.op("tensor", lambda E, k=k: E.matmul(
                        PQ[1][0:1, :], ones_b[:], t1bs[k][:], start=(k == 0), stop=(k == KT - 1)),
                        deps=([sq_toks[-1], ld_ob] + chunk_fence) if k == 0 else [])
                ev0 = P.op("scalar", lambda E: E.copy(out=v_row[:], in_=PQ[1][0:1, :]), deps=[mmq])
                ev = P.op("vector", lambda E: E.tensor_scalar_mul(out=v_row[:], in0=v_row[:], scalar1=1.0 / D), deps=[ev0])
                tv1 = P.op("vector", lambda E: E.tensor_mul(out=vt_row[:], in0=m_row[:], in1=m_row[:]), deps=[em])
                tv2 = P.op("vector", lambda E: E.tensor_sub(out=v_row[:], in0=v_row[:], in1=vt_row[:]), deps=[tv1, ev])
                tv3 = P.op("vector", lambda E: E.tensor_scalar_add(out=v_row[:], in0=v_row[:], scalar1=LN_EPS), deps=[tv2])
                tr1 = P.op("scalar", lambda E: E.activation(out=v_row[:], in_=v_row[:], func=AF.Ln), deps=[tv3])
                tr1b = P.op("vector", lambda E: E.tensor_scalar_mul(out=v_row[:], in0=v_row[:], scalar1=-0.5), deps=[tr1])
                tr2 = P.op("scalar", lambda E: E.activation(out=v_row[:], in_=v_row[:], func=AF.Exp), deps=[tr1b])
                cmb = P.op("vector", lambda E: E.tensor_copy(out=mb_row[:], in_=m_row[:]), deps=[em])
                crb = P.op("vector", lambda E: E.tensor_copy(out=rb_row[:], in_=v_row[:]), deps=[tr2])
                t_m = P.op("tensor", lambda E: E.matmul(PQ[0][:], onesrow_b[:], mb_row[:], start=True, stop=True),
                           deps=[cmb, ld_or])
                t_r = P.op("tensor", lambda E: E.matmul(PQ[1][:], onesrow_b[:], rb_row[:], start=True, stop=True),
                           deps=[crb, ld_or])
                ap_toks = []
                for k in range(KT):
                    a1 = P.op("vector", lambda E, k=k: E.tensor_tensor(out=t1[:], in0=xbc[k][:], in1=PQ[0][:], op=ALU.subtract),
                              deps=[t_m, ld_x[k]] + ([ap_toks[-1]] if ap_toks else [mmq]))
                    a2 = P.op("vector", lambda E, k=k: E.tensor_tensor(out=xnc[k][:], in0=t1[:], in1=PQ[1][:], op=ALU.mult),
                              deps=[a1, t_r])
                    ap_toks.append(a2)
                st_xn = [P.dma("gpsimd", lambda E, k=k, tc=tc: E.dma_start(
                    out=xn0T_d[k * 128:(k + 1) * 128, tc], in_=xnc[k][:]), deps=[ap_toks[k]]) for k in range(KT)]
                out_toks += st_xn
                if ln_only:
                    chunk_fence = P.fence()
                    continue
                pq_free = [[ap_toks[-1]], [ap_toks[-1]]]
                ev_q = [None] * KT
                ev_k = [None] * KT
                ev_v = [None] * KT
                for ot in range(3 * KT):
                    pqi = ot % 2
                    mm = None
                    for k in range(KT):
                        mm = P.op("tensor", lambda E, ot=ot, k=k, pqi=pqi: E.matmul(
                            PQ[pqi][:], wq[k][:, ot * 128:(ot + 1) * 128], xnc[k][:],
                            start=(k == 0), stop=(k == KT - 1)),
                            deps=([ap_toks[-1], ldw_q[-1]] + pq_free[pqi]) if k == 0 else [])
                    if ot < KT:
                        evt = P.op("scalar", lambda E, ot=ot, pqi=pqi, c=c: E.copy(
                            out=q_sb[ot][:, c * CH:(c + 1) * CH], in_=PQ[pqi][:]), deps=[mm] + batch_fence)
                    elif ot < 2 * KT:
                        evt = P.op("scalar", lambda E, ot=ot, pqi=pqi: E.copy(
                            out=ktc[ot - KT][:], in_=PQ[pqi][:]), deps=[mm] + chunk_fence)
                        ev_k[ot - KT] = evt
                    else:
                        evt = P.op("scalar", lambda E, ot=ot, pqi=pqi: E.copy(
                            out=vtc[ot - 2 * KT][:], in_=PQ[pqi][:]), deps=[mm] + chunk_fence)
                        ev_v[ot - 2 * KT] = evt
                    pq_free[pqi] = [evt]
                    if ot < KT:
                        ev_q[ot] = evt
                pt_free = [list(chunk_fence), list(chunk_fence)]
                kvrow_free = list(chunk_fence)
                gk_free = []
                ub_toks = []
                for h in range(H):
                    p64 = (h % 2) * 64
                    hp = h // 2
                    kcol = ktc[hp][p64:p64 + 64, :]
                    vcol = vtc[hp][p64:p64 + 64, :]
                    idn = eyeb[p64:p64 + 64, p64:p64 + 64]
                    tr_toks = []
                    for jb in range(4):
                        for which, col in ((0, kcol), (1, vcol)):
                            pti = (jb * 2 + which) % 2
                            tt = P.op("tensor", lambda E, col=col, jb=jb, pti=pti, idn=idn: E.transpose(
                                out=PT[pti][:], in_=col[:, jb * 128:(jb + 1) * 128], identity=idn),
                                deps=[ev_k[hp], ev_v[hp], ld_eyeb] + pt_free[pti])
                            te = P.op("scalar", lambda E, jb=jb, which=which, pti=pti: E.copy(
                                out=kvrow[:, jb * 128 + which * 64: jb * 128 + which * 64 + 64], in_=PT[pti][:]),
                                deps=[tt] + kvrow_free)
                            pt_free[pti] = [te]
                            tr_toks.append(te)
                    kvrow_free = []
                    mmgk = None
                    for jb in range(4):
                        mmgk = P.op("tensor", lambda E, jb=jb: E.matmul(
                            GK[:], kvrow[:, jb * 128: jb * 128 + 64], kvrow[:, jb * 128: jb * 128 + 128],
                            start=(jb == 0), stop=(jb == 3)),
                            deps=([tr_toks[-1]] + gk_free) if jb == 0 else [])
                    kvrow_free = [mmgk]
                    ga = P.op("vector", lambda E, h=h, p64=p64, hp=hp: E.tensor_tensor(
                        out=gkv_all[p64:p64 + 64, hp * 128:(hp + 1) * 128],
                        in0=GK[:], in1=gkv_all[p64:p64 + 64, hp * 128:(hp + 1) * 128], op=ALU.add),
                        deps=[mmgk] + acc_g)
                    gk_free = [ga]
                    ut = P.op("vector", lambda E, h=h, p64=p64, vcol=vcol: E.tensor_reduce(
                        out=ubt[p64:p64 + 64, h // 2:h // 2 + 1], in_=vcol[:], axis=AX.X, op=ALU.add),
                        deps=[ev_v[hp]] + acc_u)
                    ub_toks.append(ut)
                acc_g = [gk_free[0] if gk_free else None] + [None]
                acc_g = [ga]
                ua = P.op("vector", lambda E: E.tensor_add(out=ub[:], in0=ub[:], in1=ubt[:]), deps=ub_toks + acc_u)
                acc_u = [ua]
                chunk_fence = P.fence()

            if ln_only:
                batch_fence = P.fence()
                continue
            cg = P.op("vector", lambda E: E.tensor_copy(out=gkvb[:], in_=gkv_all[:]), deps=acc_g + chunk_fence)
            zo_free, ns_free, zq_free = [cg], [], []
            sburst_free = {0: [], 1: []}
            srep_free = list(batch_fence)
            ao_dma = {0: [], 1: []}
            for h in range(H):
                p64 = (h % 2) * 64
                hp = h // 2
                qcol = q_sb[hp][p64:p64 + 64, :]
                oh_toks = []
                nsq_toks = []
                for c in range(4):
                    ch = slice(c * CH, (c + 1) * CH)
                    mm = P.op("tensor", lambda E, ch=ch, p64=p64, hp=hp, qcol=qcol: E.matmul(
                        ZO[:], gkvb[p64:p64 + 64, hp * 128:(hp + 1) * 128], qcol[:, ch], start=True, stop=True),
                        deps=[cg, ev_q[hp]] + zo_free)
                    tzq = P.op("vector", lambda E, ch=ch, qcol=qcol: E.tensor_tensor(
                        out=zq[:], in0=ZO[0:64, :], in1=qcol[:, ch], op=ALU.mult), deps=[mm] + zq_free)
                    toh = P.op("scalar", lambda E, ch=ch, p64=p64: E.copy(
                        out=aostage[p64:p64 + 64, ch], in_=ZO[64:128, :]), deps=[mm] + ao_dma[h % 2])
                    tns = P.op("tensor", lambda E: E.matmul(
                        NS[:], ones_b[0:64, :], zq[:], start=True, stop=True), deps=[tzq] + ns_free)
                    ten = P.op("scalar", lambda E, ch=ch: E.copy(out=nsq[:, ch], in_=NS[:]), deps=[tns])
                    ns_free = [ten]
                    zq_free = [tns]
                    zo_free = [tzq, toh]
                    oh_toks.append(toh)
                    nsq_toks.append(ten)
                s1 = P.op("scalar", lambda E: E.activation(out=srow[:], in_=nsq[:], func=AF.Ln),
                          deps=nsq_toks + sburst_free[h % 2] + sburst_free[(h + 1) % 2])
                s1b = P.op("vector", lambda E: E.tensor_scalar_mul(out=srow[:], in0=srow[:], scalar1=0.5), deps=[s1])
                s2 = P.op("scalar", lambda E: E.activation(out=srow[:], in_=srow[:], func=AF.Exp), deps=[s1b])
                s3 = P.op("vector", lambda E: E.tensor_scalar_max(out=srow[:], in0=srow[:], scalar1=1e-12), deps=[s2])
                s4 = P.op("vector", lambda E: E.reciprocal(out=srow[:], in_=srow[:]), deps=[s3])
                s5 = P.op("vector", lambda E, h=h: E.tensor_scalar_mul(
                    out=srow[:], in0=srow[:], scalar1=g16_s[0:1, h:h + 1]), deps=[s4, ld_g])
                s6 = P.dma("gpsimd", lambda E, h=h: E.dma_start(out=sburst_d[h % 2:h % 2 + 1, :], in_=srow[0:1, :]),
                           deps=[s5] + sburst_free[h % 2], ch="p0")
                s7 = P.dma("gpsimd", lambda E, h=h: E.dma_start(
                    out=srep[:], in_=sburst_d[h % 2:h % 2 + 1, :].broadcast_to([128, T])),
                    deps=[s6] + srep_free, ch="cast")
                sburst_free[h % 2] = [s7]
                sc_toks = []
                for c in range(4):
                    ch = slice(c * CH, (c + 1) * CH)
                    tok = P.op("vector", lambda E, ch=ch, p64=p64: E.tensor_tensor(
                        out=aostage[p64:p64 + 64, ch], in0=aostage[p64:p64 + 64, ch],
                        in1=srep[p64:p64 + 64, ch], op=ALU.mult), deps=[s7, oh_toks[c]])
                    sc_toks.append(tok)
                srep_free = sc_toks
                da = P.dma("gpsimd", lambda E, h=h, p64=p64: E.dma_start(
                    out=ao_scr_d[h * 64:(h + 1) * 64, :], in_=aostage[p64:p64 + 64, :]), deps=sc_toks,
                    ch=("n0" if p64 == 0 else "n64"))
                ao_dma[h % 2] = [da]
            ao_done = [ao_dma[0][0], ao_dma[1][0]]

            ubm = P.op("vector", lambda E: E.tensor_tensor(out=ub[:], in0=ub[:], in1=brep[:], op=ALU.mult),
                       deps=acc_u + ld_brep)
            ubc = P.op("vector", lambda E: E.tensor_copy(out=ubb[:], in_=ub[:]), deps=[ubm] + batch_fence)
            bt_toks = []
            pq0_free = chunk_fence
            for ot in range(KT):
                mm = None
                for k in range(KT):
                    mm = P.op("tensor", lambda E, ot=ot, k=k: E.matmul(
                        PQ[0][:, 0:1], wo[k][:, ot * 128:(ot + 1) * 128], ubb[:, k:k + 1],
                        start=(k == 0), stop=(k == KT - 1)),
                        deps=([ubc, ldw_o[-1]] + (pq0_free if ot == 0 else [bt_toks[-1]])) if k == 0 else [])
                evb = P.op("scalar", lambda E, ot=ot: E.copy(out=btot[:, ot:ot + 1], in_=PQ[0][:, 0:1]), deps=[mm])
                bt_toks.append(evb)
            btt = P.op("vector", lambda E: E.tensor_add(out=btot[:], in0=btot[:], in1=bout_s[:]),
                       deps=bt_toks + ld_bout)

            pq_free = [[bt_toks[-1]], chunk_fence]
            aor_free = [list(chunk_fence) for _ in range(KT)]
            aout_free = list(batch_fence)
            for c in range(4):
                ch = slice(c * CH, (c + 1) * CH)
                ld_ao = [P.dma("sync", lambda E, k=k, ch=ch: E.dma_start(
                    out=aor[k][:], in_=ao_scr_d[k * 128:(k + 1) * 128, ch]), deps=ao_done + aor_free[k])
                    for k in range(KT)]
                for ot in range(KT):
                    pqi = ot % 2
                    mm = None
                    for k in range(KT):
                        mm = P.op("tensor", lambda E, ot=ot, k=k, pqi=pqi: E.matmul(
                            PQ[pqi][:], wo[k][:, ot * 128:(ot + 1) * 128], aor[k][:],
                            start=(k == 0), stop=(k == KT - 1)),
                            deps=(list(ld_ao) + [ldw_o[-1]] + pq_free[pqi]) if k == 0 else [])
                    evo = P.op("vector", lambda E, ch=ch, pqi=pqi, ot=ot: E.tensor_scalar_add(
                        out=aout[:, ch], in0=PQ[pqi][:], scalar1=btot[:, ot:ot + 1]),
                        deps=[mm, btt] + aout_free)
                    pq_free[pqi] = [evo]
                    std = P.dma("gpsimd", lambda E, ot=ot, ch=ch, bi=bi, c=c: E.dma_start(
                        out=attn0T_d[ot * 128:(ot + 1) * 128, bi * T + c * CH: bi * T + (c + 1) * CH],
                        in_=aout[:, ch]), deps=[evo])
                    aout_free = [std]
                    out_toks.append(std)
                aor_free = [[P.last("tensor")] for _ in range(KT)]
            batch_fence = P.fence()

        P.op("gpsimd", lambda E: E.engine_nop(), deps=out_toks)
        block = stack.enter_context(nc.Block())
        P.build(stack, block)
    return nc


def build_l0_scan(nsteps=NB, out_c=False):
    nc = bass.Bass("TRN2", target_bir_lowering=False)
    TS_ = NB * SS
    x0Ts_d = nc.dram_tensor("x0Ts", [D, TS_], F32, kind="ExternalInput").ap()
    wpackT_d = nc.dram_tensor("wpackT", [D, 7 * D], F32, kind="ExternalInput").ap()
    bias3_d = nc.dram_tensor("bias3", [3, D], F32, kind="ExternalInput").ap()
    inc_d = nc.dram_tensor("incv", [D], F32, kind="ExternalInput").ap()
    ones_d = nc.dram_tensor("ones", [128, 1], F32, kind="ExternalInput").ap()
    zT_d = nc.dram_tensor("zT", [D, SS], F32, kind="ExternalOutput").ap()
    xns_scr_d = nc.dram_tensor("xns_scr", [D, TS_], BF16, kind="Internal").ap()

    P = Prog(nc)
    with ExitStack() as stack:
        def sb(name, shape, dt):
            return stack.enter_context(nc.sbuf_tensor(name, shape, dt))

        def ps(name, shape, dt=F32):
            return stack.enter_context(nc.psum_tensor(name, shape, dt))

        CH = 512
        xbc = [sb(f"xbc{k}", [128, CH], BF16) for k in range(KT)]
        xnc = [sb(f"xnc{k}", [128, CH], BF16) for k in range(KT)]
        t1bs = [sb(f"t1b{k}", [128, CH], BF16) for k in range(KT)]
        wp = [sb(f"wp{k}", [128, 7 * D], BF16) for k in range(KT)]
        ct_ = [sb(f"ct{k}", [128, SS], F32) for k in range(KT)]
        xstep = [[sb(f"xs{j}_{k}", [128, SS], BF16) for k in range(KT)] for j in range(2)]
        fts = [sb(f"fts{k}", [128, SS], F32) for k in range(KT)]
        its = [sb(f"its{k}", [128, SS], F32) for k in range(KT)]
        gts = [sb(f"gts{k}", [128, SS], F32) for k in range(KT)]
        tmpa = sb("tmpa", [128, SS], F32)
        tmpb = sb("tmpb", [128, SS], F32)
        tmpc = sb("tmpc", [128, 512], F32)
        ones_b = sb("ones_b", [128, 1], BF16)
        onesrow_b = sb("onesrow_b", [1, 128], BF16)
        b3 = sb("b3", [128, 3 * KT], F32)
        m_row = sb("m_row", [1, CH], F32)
        v_row = sb("v_row", [1, CH], F32)
        vt_row = sb("vt_row", [1, CH], F32)
        mb_row = sb("mb_row", [1, CH], BF16)
        rb_row = sb("rb_row", [1, CH], BF16)
        cb = [sb(f"cb{k}", [128, SS], BF16) for k in range(KT)]
        inc_sb = sb("inc_sb", [128, KT], F32)
        zstage = sb("zstage", [128, SS], F32)

        PQ = [ps(f"PQ{i}", [128, 512]) for i in range(2)]
        PG = [ps(f"PG{i}", [128, 512]) for i in range(3)]

        warm = sb("warm", [1, 32], F32)
        w0 = P.op("vector", lambda E: E.memset(warm[:], 1.0))
        w1 = P.op("scalar", lambda E: E.activation(out=warm[:], in_=warm[:], func=AF.Square), deps=[w0])
        w2 = P.op("scalar", lambda E: E.activation(out=warm[:], in_=warm[:], func=AF.Ln), deps=[w1])
        w3 = P.op("scalar", lambda E: E.activation(out=warm[:], in_=warm[:], func=AF.Exp), deps=[w2])
        warm_tok = [w3]

        ldw = [P.dma("gpsimd", lambda E, k=k: E.dma_start(out=wp[k][:], in_=wpackT_d[k * 128:(k + 1) * 128, :]), ch="cast")
               for k in range(KT)]
        ld_ob = P.dma("gpsimd", lambda E: E.dma_start(out=ones_b[:], in_=ones_d[:]), ch="cast")
        ld_or = P.dma("gpsimd", lambda E: E.dma_start(
            out=onesrow_b[:], in_=ones_d[:, 0].unsqueeze(0).broadcast_to([1, 128])), ch="castp0")
        ld_b3 = [P.dma("sync", lambda E, g=g, o=o: E.dma_start(
            out=b3[:, g * KT + o:g * KT + o + 1], in_=bias3_d[g, o * 128:(o + 1) * 128].unsqueeze(1)))
            for g in range(3) for o in range(KT)]
        ld_inc = [P.dma("sync", lambda E, k=k: E.dma_start(
            out=inc_sb[:, k:k + 1], in_=inc_d[k * 128:(k + 1) * 128].unsqueeze(1))) for k in range(KT)]
        ld_c0 = [P.op("vector", lambda E, k=k: E.tensor_copy(
            out=ct_[k][:], in_=inc_sb[:, k:k + 1].broadcast_to([128, SS])), deps=[ld_inc[k]])
            for k in range(KT)]

        chunk_fence = []
        st_xn = []
        for c in range(8):
            tc = slice(c * CH, (c + 1) * CH)
            ld_x = [P.dma("gpsimd", lambda E, k=k, tc=tc: E.dma_start(
                out=xbc[k][:], in_=x0Ts_d[k * 128:(k + 1) * 128, tc]), deps=chunk_fence, ch="cast") for k in range(KT)]
            mm = None
            for k in range(KT):
                mm = P.op("tensor", lambda E, k=k: E.matmul(
                    PQ[0][0:1, :], ones_b[:], xbc[k][:], start=(k == 0), stop=(k == KT - 1)),
                    deps=(list(ld_x) + [ld_ob] + warm_tok + chunk_fence) if k == 0 else [])
            em0 = P.op("scalar", lambda E: E.copy(out=m_row[:], in_=PQ[0][0:1, :]), deps=[mm])
            em = P.op("vector", lambda E: E.tensor_scalar_mul(out=m_row[:], in0=m_row[:], scalar1=1.0 / D), deps=[em0])
            sq_toks = [P.op("scalar", lambda E, k=k: E.activation(out=t1bs[k][:], in_=xbc[k][:], func=AF.Square),
                           deps=[ld_x[k]] + chunk_fence) for k in range(KT)]
            mmq = None
            for k in range(KT):
                mmq = P.op("tensor", lambda E, k=k: E.matmul(
                    PQ[1][0:1, :], ones_b[:], t1bs[k][:], start=(k == 0), stop=(k == KT - 1)),
                    deps=([sq_toks[-1], ld_ob] + chunk_fence) if k == 0 else [])
            ev0 = P.op("scalar", lambda E: E.copy(out=v_row[:], in_=PQ[1][0:1, :]), deps=[mmq])
            ev = P.op("vector", lambda E: E.tensor_scalar_mul(out=v_row[:], in0=v_row[:], scalar1=1.0 / D), deps=[ev0])
            tv1 = P.op("vector", lambda E: E.tensor_mul(out=vt_row[:], in0=m_row[:], in1=m_row[:]), deps=[em])
            tv2 = P.op("vector", lambda E: E.tensor_sub(out=v_row[:], in0=v_row[:], in1=vt_row[:]), deps=[tv1, ev])
            tv3 = P.op("vector", lambda E: E.tensor_scalar_add(out=v_row[:], in0=v_row[:], scalar1=LN_EPS), deps=[tv2])
            tr1 = P.op("scalar", lambda E: E.activation(out=v_row[:], in_=v_row[:], func=AF.Ln), deps=[tv3])
            tr1b = P.op("vector", lambda E: E.tensor_scalar_mul(out=v_row[:], in0=v_row[:], scalar1=-0.5), deps=[tr1])
            tr2 = P.op("scalar", lambda E: E.activation(out=v_row[:], in_=v_row[:], func=AF.Exp), deps=[tr1b])
            cmb = P.op("vector", lambda E: E.tensor_copy(out=mb_row[:], in_=m_row[:]), deps=[em])
            crb = P.op("vector", lambda E: E.tensor_copy(out=rb_row[:], in_=v_row[:]), deps=[tr2])
            t_m = P.op("tensor", lambda E: E.matmul(PQ[0][:], onesrow_b[:], mb_row[:], start=True, stop=True),
                       deps=[cmb, ld_or, em0])
            t_r = P.op("tensor", lambda E: E.matmul(PQ[1][:], onesrow_b[:], rb_row[:], start=True, stop=True),
                       deps=[crb, ld_or, ev0])
            ap_toks = []
            for k in range(KT):
                a1 = P.op("vector", lambda E, k=k: E.tensor_tensor(out=tmpc[:], in0=xbc[k][:], in1=PQ[0][:],
                                                                  op=ALU.subtract),
                          deps=[t_m] + ([ap_toks[-1]] if ap_toks else [mmq]))
                a2 = P.op("vector", lambda E, k=k: E.tensor_tensor(out=xnc[k][:], in0=tmpc[:], in1=PQ[1][:],
                                                                  op=ALU.mult),
                          deps=[a1, t_r])
                ap_toks.append(a2)
            st_c = [P.dma("gpsimd", lambda E, k=k, tc=tc: E.dma_start(
                out=xns_scr_d[k * 128:(k + 1) * 128, tc], in_=xnc[k][:]), deps=[ap_toks[k]]) for k in range(KT)]
            st_xn += st_c
            chunk_fence = P.fence()

        GATES = 3
        upd_last = list(ld_c0) + chunk_fence
        cb_toks = []
        xld_prev = []
        ev_toks = []
        mm_toks = []
        for t_i in range(nsteps):
            jbuf = 0
            xld = [P.dma("gpsimd", lambda E, k=k, t_i=t_i, jbuf=jbuf: E.dma_start(
                out=xstep[jbuf][k][:], in_=xns_scr_d[k * 128:(k + 1) * 128, t_i * SS:(t_i + 1) * SS]),
                deps=(st_xn if t_i < 2 else []) + xld_prev + mm_toks[-1:]) for k in range(KT)]
            cb_toks = [P.op("vector", lambda E, k=k: E.tensor_copy(out=cb[k][:], in_=ct_[k][:]),
                            deps=upd_last + cb_toks[-1:]) for k in range(KT)]
            ev_base = len(ev_toks)
            for ot in range(KT):
                for g in range(GATES):
                    mm = None
                    for k in range(2 * KT):
                        if k < KT:
                            lhs = (lambda E, k=k, g=g, ot=ot, jbuf=jbuf: E.matmul(
                                PG[g][:, 0:SS], wp[k][:, g * D + ot * 128: g * D + (ot + 1) * 128],
                                xstep[jbuf][k][:], start=(k == 0), stop=False))
                        else:
                            kk = k - KT
                            lhs = (lambda E, kk=kk, g=g, ot=ot: E.matmul(
                                PG[g][:, 0:SS], wp[kk][:, (3 + g) * D + ot * 128: (3 + g) * D + (ot + 1) * 128],
                                cb[kk][:], start=False, stop=(kk == KT - 1)))
                        mm = P.op("tensor", lhs,
                                  deps=(list(xld) + [cb_toks[-1], ldw[-1]] + ev_toks[-3:]) if k == 0 else [])
                    mm_toks.append(mm)
                    func = AF.Sigmoid if g < 2 else AF.Tanh
                    tok = P.op("scalar", lambda E, g=g, ot=ot, func=func: E.activation(
                        out=(fts if g == 0 else its if g == 1 else gts)[ot][:], in_=PG[g][:, 0:SS],
                        func=func, bias=b3[:, g * KT + ot:g * KT + ot + 1]),
                        deps=[mm] + upd_last)
                    ev_toks.append(tok)
            upd = []
            for ot in range(KT):
                u1 = P.op("vector", lambda E, ot=ot: E.tensor_tensor(out=tmpa[:], in0=fts[ot][:], in1=ct_[ot][:],
                                                                    op=ALU.mult),
                          deps=[ev_toks[ev_base + ot * 3], upd[-1] if upd else None])
                u2 = P.op("vector", lambda E, ot=ot: E.tensor_tensor(out=tmpb[:], in0=its[ot][:], in1=gts[ot][:],
                                                                    op=ALU.mult),
                          deps=[ev_toks[ev_base + ot * 3 + 1], ev_toks[ev_base + ot * 3 + 2]])
                u3 = P.op("vector", lambda E, ot=ot: E.tensor_tensor(out=ct_[ot][:], in0=tmpa[:], in1=tmpb[:],
                                                                    op=ALU.add),
                          deps=[u1, u2])
                upd.append(u3)
            upd_last = [upd[-1]]
            xld_prev = xld

        cbf = [P.op("vector", lambda E, k=k: E.tensor_copy(out=cb[k][:], in_=ct_[k][:]), deps=upd_last + cb_toks[-1:])
               for k in range(KT)]
        if out_c == "f":
            stc = [P.dma("gpsimd", lambda E, k=k: E.dma_start(out=zT_d[k * 128:(k + 1) * 128, :], in_=fts[k][:]),
                         deps=upd_last + cb_toks[-1:]) for k in range(KT)]
            P.op("gpsimd", lambda E: E.engine_nop(), deps=stc)
            block = stack.enter_context(nc.Block())
            P.build(stack, block)
            return nc
        if out_c == "x":
            stc = [P.dma("gpsimd", lambda E, k=k: E.dma_start(out=zT_d[k * 128:(k + 1) * 128, :], in_=xstep[0][k][:]),
                         deps=upd_last + cb_toks[-1:]) for k in range(KT)]
            P.op("gpsimd", lambda E: E.engine_nop(), deps=stc)
            block = stack.enter_context(nc.Block())
            P.build(stack, block)
            return nc
        if out_c:
            stc = [P.dma("gpsimd", lambda E, k=k: E.dma_start(out=zT_d[k * 128:(k + 1) * 128, :], in_=ct_[k][:]),
                         deps=upd_last + cb_toks[-1:]) for k in range(KT)]
            P.op("gpsimd", lambda E: E.engine_nop(), deps=stc)
            block = stack.enter_context(nc.Block())
            P.build(stack, block)
            return nc
        z_toks = []
        pqi_free = [[], []]
        for ot in range(KT):
            mm = None
            for k in range(KT):
                mm = P.op("tensor", lambda E, ot=ot, k=k: E.matmul(
                    PG[0][:, 0:SS], wp[k][:, 6 * D + ot * 128: 6 * D + (ot + 1) * 128], cb[k][:],
                    start=(k == 0), stop=(k == KT - 1)),
                    deps=([cbf[-1], ldw[-1]] + z_toks[-1:] + ev_toks[-3:]) if k == 0 else [])
            evz = P.op("scalar", lambda E, ot=ot: E.copy(out=zstage[:, 0:SS], in_=PG[0][:, 0:SS]), deps=[mm])
            stz = P.dma("gpsimd", lambda E, ot=ot: E.dma_start(
                out=zT_d[ot * 128:(ot + 1) * 128, :], in_=zstage[:, 0:SS]), deps=[evz])
            z_toks.append(stz)
        P.op("gpsimd", lambda E: E.engine_nop(), deps=z_toks)
        block = stack.enter_context(nc.Block())
        P.build(stack, block)
    return nc


def _prep_l0_scan_inputs(inputs):
    x = np.asarray(inputs["L0_attn_x"])
    wpackT = np.ascontiguousarray(np.concatenate(
        [np.asarray(inputs[f"L0_ffn_{n}"]).T for n in ("wf", "wi", "wc", "uf", "ui", "uc", "uo")], axis=1))
    bias3 = np.ascontiguousarray(np.stack([np.asarray(inputs["L0_ffn_bf"]),
                                           np.asarray(inputs["L0_ffn_bi"]),
                                           np.asarray(inputs["L0_ffn_bc"])]))
    incv = np.ascontiguousarray(np.asarray(inputs["L0_ffn_in_c"]))
    ones = np.ones((128, 1), dtype=np.float32)
    in_maps = []
    for c in range(NCORES):
        sl = slice(c * SS, (c + 1) * SS)
        x0Ts = np.ascontiguousarray(np.concatenate([x[b, sl, :].T for b in range(NB)], axis=1))
        in_maps.append({"x0Ts": x0Ts, "wpackT": wpackT, "bias3": bias3, "incv": incv, "ones": ones})
    return in_maps


def run_l0_scan(inputs, nsteps=NB, out_c=False):
    nc = build_l0_scan(nsteps=nsteps, out_c=out_c)
    in_maps = _prep_l0_scan_inputs(inputs)
    res = run_bass_kernel_spmd(nc, in_maps, core_ids=list(range(NCORES)))
    return res.results



def build_x1_ln1():
    nc = bass.Bass("TRN2", target_bir_lowering=False)
    TT_ = B2 * T
    x0T_d = nc.dram_tensor("x0T", [D, TT_], F32, kind="ExternalInput").ap()
    attn0T_d = nc.dram_tensor("attn0T", [D, TT_], F32, kind="ExternalInput").ap()
    xn0T_d = nc.dram_tensor("xn0T", [D, TT_], BF16, kind="ExternalInput").ap()
    zT_d = nc.dram_tensor("zTfull", [D, T], F32, kind="ExternalInput").ap()
    woT_d = nc.dram_tensor("woT", [D, D], F32, kind="ExternalInput").ap()
    bo_d = nc.dram_tensor("bo", [D], F32, kind="ExternalInput").ap()
    ones_d = nc.dram_tensor("ones", [128, 1], F32, kind="ExternalInput").ap()
    x1T_d = nc.dram_tensor("x1T", [D, TT_], F32, kind="ExternalOutput").ap()
    xn1T_d = nc.dram_tensor("xn1T", [D, TT_], F32, kind="ExternalOutput").ap()

    P = Prog(nc)
    with ExitStack() as stack:
        def sb(name, shape, dt):
            return stack.enter_context(nc.sbuf_tensor(name, shape, dt))

        def ps(name, shape, dt=F32):
            return stack.enter_context(nc.psum_tensor(name, shape, dt))

        CH = 512
        wo = [sb(f"wo{k}", [128, D], BF16) for k in range(KT)]
        zs = [sb(f"zs{k}", [128, T], F32) for k in range(KT)]
        xn0c = [sb(f"xn0c{k}", [128, CH], BF16) for k in range(KT)]
        x0c = [sb(f"x0c{k}", [128, CH], F32) for k in range(KT)]
        a0c = [sb(f"a0c{k}", [128, CH], F32) for k in range(KT)]
        x1c = [sb(f"x1c{k}", [128, CH], F32) for k in range(KT)]
        xbc = [sb(f"xbc{k}", [128, CH], BF16) for k in range(KT)]
        t1bs = [sb(f"t1b{k}", [128, CH], BF16) for k in range(KT)]
        xnc = [sb(f"xnc{k}", [128, CH], F32) for k in range(KT)]
        tmp = sb("tmp", [128, CH], F32)
        ht = sb("ht", [128, CH], F32)
        bo_s = sb("bo_s", [128, KT], F32)
        ones_b = sb("ones_b", [128, 1], BF16)
        onesrow_b = sb("onesrow_b", [1, 128], BF16)
        m_row = sb("m_row", [1, CH], F32)
        v_row = sb("v_row", [1, CH], F32)
        vt_row = sb("vt_row", [1, CH], F32)
        mb_row = sb("mb_row", [1, CH], BF16)
        rb_row = sb("rb_row", [1, CH], BF16)
        PH = [ps(f"PH{i}", [128, 512]) for i in range(2)]
        PQ = [ps(f"PQ{i}", [128, 512]) for i in range(2)]

        warm = sb("warm", [1, 32], F32)
        w0 = P.op("vector", lambda E: E.memset(warm[:], 1.0))
        w1 = P.op("scalar", lambda E: E.activation(out=warm[:], in_=warm[:], func=AF.Square), deps=[w0])
        w2 = P.op("scalar", lambda E: E.activation(out=warm[:], in_=warm[:], func=AF.Ln), deps=[w1])
        w3 = P.op("scalar", lambda E: E.activation(out=warm[:], in_=warm[:], func=AF.Tanh), deps=[w2])
        w4 = P.op("scalar", lambda E: E.activation(out=warm[:], in_=warm[:], func=AF.Exp), deps=[w3])
        warm_tok = [w4]

        ldw = [P.dma("gpsimd", lambda E, k=k: E.dma_start(out=wo[k][:], in_=woT_d[k * 128:(k + 1) * 128, :]), ch="cast")
               for k in range(KT)]
        ldz = [P.dma("sync", lambda E, k=k: E.dma_start(out=zs[k][:], in_=zT_d[k * 128:(k + 1) * 128, :]))
               for k in range(KT)]
        ld_ob = P.dma("gpsimd", lambda E: E.dma_start(out=ones_b[:], in_=ones_d[:]), ch="cast")
        ld_or = P.dma("gpsimd", lambda E: E.dma_start(
            out=onesrow_b[:], in_=ones_d[:, 0].unsqueeze(0).broadcast_to([1, 128])), ch="castp0")
        ld_bo = [P.dma("sync", lambda E, o=o: E.dma_start(
            out=bo_s[:, o:o + 1], in_=bo_d[o * 128:(o + 1) * 128].unsqueeze(1))) for o in range(KT)]

        chunk_fence = [ldw[-1], ldz[-1], ld_ob, ld_or] + ld_bo + warm_tok
        out_toks = []
        for c in range(8):
            bi = c // 4
            tc = slice(c * CH, (c + 1) * CH)
            zc = slice((c % 4) * CH, (c % 4 + 1) * CH)
            ld_xn0 = [P.dma("gpsimd", lambda E, k=k, tc=tc: E.dma_start(
                out=xn0c[k][:], in_=xn0T_d[k * 128:(k + 1) * 128, tc]), deps=chunk_fence) for k in range(KT)]
            ld_x0 = [P.dma("sync", lambda E, k=k, tc=tc: E.dma_start(
                out=x0c[k][:], in_=x0T_d[k * 128:(k + 1) * 128, tc]), deps=chunk_fence) for k in range(KT)]
            ld_a0 = [P.dma("sync", lambda E, k=k, tc=tc: E.dma_start(
                out=a0c[k][:], in_=attn0T_d[k * 128:(k + 1) * 128, tc]), deps=chunk_fence) for k in range(KT)]
            x1_toks = []
            phi_free = [list(chunk_fence), list(chunk_fence)]
            for ot in range(KT):
                phi = ot % 2
                mm = None
                for k in range(KT):
                    mm = P.op("tensor", lambda E, ot=ot, k=k, phi=phi: E.matmul(
                        PH[phi][:], wo[k][:, ot * 128:(ot + 1) * 128], xn0c[k][:],
                        start=(k == 0), stop=(k == KT - 1)),
                        deps=(list(ld_xn0) + [ldw[-1]] + warm_tok + phi_free[phi]) if k == 0 else [])
                tz = P.op("vector", lambda E, ot=ot, zc=zc, phi=phi: E.tensor_tensor(
                    out=tmp[:], in0=PH[phi][:], in1=zs[ot][:, zc], op=ALU.add),
                    deps=[mm, ldz[-1], (x1_toks[-1][0] if x1_toks else None),
                          (phi_free[0][0] if phi_free[0] else None),
                          (phi_free[1][0] if phi_free[1] else None)] + chunk_fence)
                th = P.op("scalar", lambda E, ot=ot: E.activation(
                    out=ht[:], in_=tmp[:], func=AF.Tanh, bias=bo_s[:, ot:ot + 1]),
                    deps=[tz, (x1_toks[-1][0] if x1_toks else None)] + ld_bo + chunk_fence)
                phi_free[phi] = [th]
                a1 = P.op("vector", lambda E, ot=ot: E.tensor_tensor(
                    out=x1c[ot][:], in0=x0c[ot][:], in1=a0c[ot][:], op=ALU.add),
                    deps=[ld_x0[ot], ld_a0[ot]] + chunk_fence)
                a2 = P.op("vector", lambda E, ot=ot: E.tensor_tensor(
                    out=x1c[ot][:], in0=x1c[ot][:], in1=ht[:], op=ALU.add), deps=[a1, th])
                st1 = P.dma("gpsimd", lambda E, ot=ot, tc=tc: E.dma_start(
                    out=x1T_d[ot * 128:(ot + 1) * 128, tc], in_=x1c[ot][:]), deps=[a2])
                cb = P.op("vector", lambda E, ot=ot: E.tensor_copy(out=xbc[ot][:], in_=x1c[ot][:]),
                          deps=[a2] + chunk_fence)
                x1_toks.append((a2, cb, st1))
                out_toks.append(st1)
            cps = [t[1] for t in x1_toks]
            mm = None
            for k in range(KT):
                mm = P.op("tensor", lambda E, k=k: E.matmul(
                    PQ[0][0:1, :], ones_b[:], xbc[k][:], start=(k == 0), stop=(k == KT - 1)),
                    deps=([cps[-1], ld_ob] + chunk_fence) if k == 0 else [])
            em0 = P.op("scalar", lambda E: E.copy(out=m_row[:], in_=PQ[0][0:1, :]), deps=[mm])
            em = P.op("vector", lambda E: E.tensor_scalar_mul(out=m_row[:], in0=m_row[:], scalar1=1.0 / D), deps=[em0])
            sq_toks = [P.op("scalar", lambda E, k=k: E.activation(out=t1bs[k][:], in_=xbc[k][:], func=AF.Square),
                           deps=[cps[k]] + chunk_fence) for k in range(KT)]
            mmq = None
            for k in range(KT):
                mmq = P.op("tensor", lambda E, k=k: E.matmul(
                    PQ[1][0:1, :], ones_b[:], t1bs[k][:], start=(k == 0), stop=(k == KT - 1)),
                    deps=([sq_toks[-1], ld_ob] + chunk_fence) if k == 0 else [])
            ev0 = P.op("scalar", lambda E: E.copy(out=v_row[:], in_=PQ[1][0:1, :]), deps=[mmq])
            ev = P.op("vector", lambda E: E.tensor_scalar_mul(out=v_row[:], in0=v_row[:], scalar1=1.0 / D), deps=[ev0])
            tv1 = P.op("vector", lambda E: E.tensor_mul(out=vt_row[:], in0=m_row[:], in1=m_row[:]), deps=[em])
            tv2 = P.op("vector", lambda E: E.tensor_sub(out=v_row[:], in0=v_row[:], in1=vt_row[:]), deps=[tv1, ev])
            tv3 = P.op("vector", lambda E: E.tensor_scalar_add(out=v_row[:], in0=v_row[:], scalar1=LN_EPS), deps=[tv2])
            tr1 = P.op("scalar", lambda E: E.activation(out=v_row[:], in_=v_row[:], func=AF.Ln), deps=[tv3])
            tr1b = P.op("vector", lambda E: E.tensor_scalar_mul(out=v_row[:], in0=v_row[:], scalar1=-0.5), deps=[tr1])
            tr2 = P.op("scalar", lambda E: E.activation(out=v_row[:], in_=v_row[:], func=AF.Exp), deps=[tr1b])
            cmb = P.op("vector", lambda E: E.tensor_copy(out=mb_row[:], in_=m_row[:]), deps=[em])
            crb = P.op("vector", lambda E: E.tensor_copy(out=rb_row[:], in_=v_row[:]), deps=[tr2])
            t_m = P.op("tensor", lambda E: E.matmul(PQ[0][:], onesrow_b[:], mb_row[:], start=True, stop=True),
                       deps=[cmb, ld_or, em0])
            t_r = P.op("tensor", lambda E: E.matmul(PQ[1][:], onesrow_b[:], rb_row[:], start=True, stop=True),
                       deps=[crb, ld_or, ev0])
            ap_toks = []
            for k in range(KT):
                a1 = P.op("vector", lambda E, k=k: E.tensor_tensor(out=tmp[:], in0=x1c[k][:], in1=PQ[0][:],
                                                                  op=ALU.subtract),
                          deps=[t_m, x1_toks[k][0]] + ([ap_toks[-1]] if ap_toks else [mmq]) + [phi_free[0][0], phi_free[1][0]])
                a2 = P.op("vector", lambda E, k=k: E.tensor_tensor(out=xnc[k][:], in0=tmp[:], in1=PQ[1][:],
                                                                  op=ALU.mult),
                          deps=[a1, t_r])
                ap_toks.append(a2)
            st_xn = [P.dma("gpsimd", lambda E, k=k, tc=tc: E.dma_start(
                out=xn1T_d[k * 128:(k + 1) * 128, tc], in_=xnc[k][:]), deps=[ap_toks[k]]) for k in range(KT)]
            out_toks += st_xn
            chunk_fence = P.fence()

        P.op("gpsimd", lambda E: E.engine_nop(), deps=out_toks)
        block = stack.enter_context(nc.Block())
        P.build(stack, block)
    return nc


def run_x1_ln1(inputs, attn0T_list, xn0T_list, zT_full):
    nc = build_x1_ln1()
    x = np.asarray(inputs["L0_attn_x"])
    woT = np.ascontiguousarray(np.asarray(inputs["L0_ffn_wo"]).T)
    bo = np.ascontiguousarray(np.asarray(inputs["L0_ffn_bo"]))
    ones = np.ones((128, 1), dtype=np.float32)
    in_maps = []
    for c in range(NCORES):
        x0T = np.ascontiguousarray(np.concatenate([x[2 * c].T, x[2 * c + 1].T], axis=1))
        in_maps.append({"x0T": x0T, "attn0T": attn0T_list[c], "xn0T": xn0T_list[c],
                        "zTfull": zT_full, "woT": woT, "bo": bo, "ones": ones})
    res = run_bass_kernel_spmd(nc, in_maps, core_ids=list(range(NCORES)))
    return res.results



def build_l1_attn(dbg=False, nch=None):
    nc = bass.Bass("TRN2", target_bir_lowering=False)
    TT_ = B2 * T
    NSt = 16
    CH = 512
    NCH = (T // CH) if nch is None else nch
    xn1T_d = nc.dram_tensor("xn1T", [D, TT_], F32, kind="ExternalInput").ap()
    xprojT_d = nc.dram_tensor("xprojT", [D, 96], F32, kind="ExternalInput").ap()
    dtprojT_d = nc.dram_tensor("dtprojT", [64, D], F32, kind="ExternalInput").ap()
    dtb_d = nc.dram_tensor("dtb", [D], F32, kind="ExternalInput").ap()
    alog_d = nc.dram_tensor("alog", [D, NSt], F32, kind="ExternalInput").ap()
    dvec_d = nc.dram_tensor("dvec", [D], F32, kind="ExternalInput").ap()
    y1T_d = nc.dram_tensor("y1T", [D, TT_], F32, kind="ExternalOutput").ap()
    delta_scr_d = nc.dram_tensor("delta_scr", [D, T], F32, kind=("ExternalOutput" if dbg else "Internal")).ap()
    xp_dbg_d = nc.dram_tensor("xp_dbg", [96, T], F32, kind="ExternalOutput").ap() if dbg else None
    br_dbg_d = nc.dram_tensor("br_dbg", [128, 16 * 512], BF16, kind="ExternalOutput").ap() if dbg else None
    da_dbg_d = nc.dram_tensor("da_dbg", [128, 16 * 512], F32, kind="ExternalOutput").ap() if dbg else None
    h_dbg_d = nc.dram_tensor("h_dbg", [128, 16 * 512], F32, kind="ExternalOutput").ap() if dbg else None
    bu_dbg_d = nc.dram_tensor("bu_dbg", [128, 16 * 512], F32, kind="ExternalOutput").ap() if dbg else None
    hc_dbg_d = nc.dram_tensor("hc_dbg", [128, 16], F32, kind="ExternalOutput").ap() if dbg else None
    zs_dbg_d = nc.dram_tensor("zs_dbg", [128, 16 * 512], BF16, kind="ExternalOutput").ap() if dbg else None
    yc_dbg_d = nc.dram_tensor("yc_dbg", [128, 512], F32, kind="ExternalOutput").ap() if dbg else None
    cr_dbg_d = nc.dram_tensor("cr_dbg", [128, 16 * 512], BF16, kind="ExternalOutput").ap() if dbg else None
    bc_scr_d = nc.dram_tensor("bc_scr", [2, NSt * CH], F32, kind="Internal").ap()

    P = Prog(nc)
    with ExitStack() as stack:
        def sb(name, shape, dt):
            return stack.enter_context(nc.sbuf_tensor(name, shape, dt))

        def ps(name, shape, dt=F32):
            return stack.enter_context(nc.psum_tensor(name, shape, dt))

        xbc = [sb(f"xbc{k}", [128, CH], BF16) for k in range(KT)]
        xprj = sb("xprj", [128, 96], BF16)
        xprjs = [sb(f"xprj{k}", [128, 96], BF16) for k in range(KT)]
        dtp = sb("dtp", [64, D], BF16)
        dtb_s = sb("dtb_s", [128, KT], F32)
        xp_sb = sb("xp_sb", [96, T], F32)
        drb = sb("drb", [64, T], BF16)
        A_s = [sb(f"A{k}", [128, NSt], F32) for k in range(KT)]
        D_s = sb("D_s", [128, KT], F32)
        dc2 = [sb(f"dc{i}", [128, CH], F32) for i in range(2)]
        xc2 = [sb(f"xc{i}", [128, CH], F32) for i in range(2)]
        uc2 = [sb(f"uc{i}", [128, CH], F32) for i in range(2)]
        dA = sb("dA", [128, NSt * CH], F32)
        dBu = sb("dBu", [128, NSt * CH], F32)
        hbuf = sb("hbuf", [128, NSt * CH], F32)
        zb2 = [sb(f"zb{i}", [128, NSt * CH], BF16) for i in range(2)]
        Brep = sb("Brep", [128, NSt * CH], BF16)
        Crep = sb("Crep", [128, NSt * CH], BF16)
        hcarry = [sb(f"hc{k}", [128, NSt], F32) for k in range(KT)]
        y1c2 = [sb(f"y1c{i}", [128, CH], F32) for i in range(2)]
        sp1 = sb("sp1", [128, CH], F32)
        sp2 = sb("sp2", [128, CH], F32)
        PXP = ps("PXP", [96, 512])
        PDT = ps("PDT", [128, 512])

        warm = sb("warm", [1, 32], F32)
        w0 = P.op("vector", lambda E: E.memset(warm[:], 1.0))
        w1 = P.op("scalar", lambda E: E.activation(out=warm[:], in_=warm[:], func=AF.Abs), deps=[w0])
        w2 = P.op("scalar", lambda E: E.activation(out=warm[:], in_=warm[:], func=AF.Ln), deps=[w1])
        w3 = P.op("scalar", lambda E: E.activation(out=warm[:], in_=warm[:], func=AF.Exp), deps=[w2])
        warm_tok = [w3]

        ld_xp = [P.dma("gpsimd", lambda E, k=k: E.dma_start(
            out=xprjs[k][:], in_=xprojT_d[k * 128:(k + 1) * 128, :]), ch="cast") for k in range(KT)]
        ld_dtp = P.dma("gpsimd", lambda E: E.dma_start(out=dtp[:], in_=dtprojT_d[:]), ch="castn0")
        ld_dtb = [P.dma("sync", lambda E, o=o: E.dma_start(
            out=dtb_s[:, o:o + 1], in_=dtb_d[o * 128:(o + 1) * 128].unsqueeze(1))) for o in range(KT)]
        ld_al = [P.dma("sync", lambda E, k=k: E.dma_start(
            out=A_s[k][:], in_=alog_d[k * 128:(k + 1) * 128, :])) for k in range(KT)]
        ld_dv = [P.dma("sync", lambda E, o=o: E.dma_start(
            out=D_s[:, o:o + 1], in_=dvec_d[o * 128:(o + 1) * 128].unsqueeze(1))) for o in range(KT)]
        load_fence = ld_xp + [ld_dtp] + ld_dtb + ld_al + ld_dv + warm_tok
        a_toks = []
        for k in range(KT):
            t1 = P.op("scalar", lambda E, k=k: E.activation(out=A_s[k][:], in_=A_s[k][:], func=AF.Exp),
                      deps=load_fence)
            t2 = P.op("vector", lambda E, k=k: E.tensor_scalar_mul(out=A_s[k][:], in0=A_s[k][:], scalar1=-1.0),
                      deps=[t1])
            a_toks.append(t2)

        out_toks = []
        batch_fence = list(load_fence) + a_toks
        for bi in range(B2):
            xp_toks = []
            pxp_free = list(batch_fence)
            for c in range(NCH):
                tc = slice(bi * T + c * CH, bi * T + (c + 1) * CH)
                ld_x = [P.dma("gpsimd", lambda E, k=k, tc=tc: E.dma_start(
                    out=xbc[k][:], in_=xn1T_d[k * 128:(k + 1) * 128, tc]),
                    deps=batch_fence + xp_toks[-1:], ch="cast") for k in range(KT)]
                mm = None
                for k in range(KT):
                    mm = P.op("tensor", lambda E, k=k: E.matmul(
                        PXP[:], xprjs[k][:], xbc[k][:], start=(k == 0), stop=(k == KT - 1)),
                        deps=(list(ld_x) + pxp_free) if k == 0 else [])
                evt = P.op("scalar", lambda E, c=c: E.copy(out=xp_sb[:, c * CH:(c + 1) * CH], in_=PXP[:]),
                           deps=[mm] + batch_fence)
                pxp_free = [evt]
                xp_toks.append(evt)
            if dbg and bi == 0:
                dx = P.dma("gpsimd", lambda E: E.dma_start(out=xp_dbg_d[:], in_=xp_sb[:]), deps=xp_toks, ch="n0")
                out_toks.append(dx)
            cdr = P.op("vector", lambda E: E.tensor_copy(out=drb[:], in_=xp_sb[0:64, :]), deps=xp_toks + batch_fence)
            del_toks = []
            pdt_free = list(batch_fence)
            for ot in range(KT):
                for c in range(NCH):
                    mm = P.op("tensor", lambda E, ot=ot, c=c: E.matmul(
                        PDT[:], dtp[:, ot * 128:(ot + 1) * 128], drb[:, c * CH:(c + 1) * CH],
                        start=True, stop=True), deps=([cdr] + pdt_free))
                    s0 = P.op("vector", lambda E, ot=ot: E.tensor_scalar_add(
                        out=sp1[:], in0=PDT[:], scalar1=dtb_s[:, ot:ot + 1]), deps=[mm] + del_toks[-1:])
                    pdt_free = [s0]
                    sa = P.op("scalar", lambda E: E.activation(out=sp2[:], in_=sp1[:], func=AF.Abs), deps=[s0] + del_toks[-1:])
                    sn = P.op("vector", lambda E: E.tensor_scalar_mul(out=sp2[:], in0=sp2[:], scalar1=-1.0), deps=[sa])
                    se = P.op("scalar", lambda E: E.activation(out=sp2[:], in_=sp2[:], func=AF.Exp), deps=[sn])
                    s1 = P.op("vector", lambda E: E.tensor_scalar_add(out=sp2[:], in0=sp2[:], scalar1=1.0), deps=[se])
                    sl = P.op("scalar", lambda E: E.activation(out=sp2[:], in_=sp2[:], func=AF.Ln), deps=[s1])
                    sr = P.op("vector", lambda E: E.tensor_scalar_max(out=sp1[:], in0=sp1[:], scalar1=0.0), deps=[sa])
                    sd = P.op("vector", lambda E: E.tensor_add(out=sp1[:], in0=sp1[:], in1=sp2[:]), deps=[sl, sr])
                    std = P.dma("gpsimd", lambda E, ot=ot, c=c: E.dma_start(
                        out=delta_scr_d[ot * 128:(ot + 1) * 128, c * CH:(c + 1) * CH], in_=sp1[:]), deps=[sd])
                    del_toks.append(std)
            rep_toks = []
            scan_fence = list(del_toks)
            prev_dma = {0: [], 1: []}
            y_toks = []
            for c in range(NCH):
                prel = P.fence()
                b1 = P.dma("gpsimd", lambda E, c=c: E.dma_start(
                    out=bc_scr_d[0, :].rearrange("(n t) -> n t", t=CH), in_=xp_sb[64:80, c * CH:(c + 1) * CH]),
                    deps=xp_toks + prel, ch="n64")
                b2 = P.dma("gpsimd", lambda E, c=c: E.dma_start(
                    out=bc_scr_d[1, :].rearrange("(n t) -> n t", t=CH), in_=xp_sb[80:96, c * CH:(c + 1) * CH]),
                    deps=xp_toks + prel, ch="n64")
                b3 = P.dma("gpsimd", lambda E: E.dma_start(
                    out=Brep[:], in_=bc_scr_d[0:1, :].broadcast_to([128, NSt * CH])), deps=[b1, b2] + prel,
                    ch="bcast")
                b4 = P.dma("gpsimd", lambda E: E.dma_start(
                    out=Crep[:], in_=bc_scr_d[1:2, :].broadcast_to([128, NSt * CH])), deps=[b1, b2] + prel,
                    ch="bcast")
                rep_done = [b3, b4]
                dt_toks = []
                for dt in range(KT):
                    pb = (c * KT + dt) % 2
                    dc, xc, uc = dc2[pb], xc2[pb], uc2[pb]
                    tc = slice(bi * T + c * CH, bi * T + (c + 1) * CH)
                    l1 = P.dma("sync", lambda E, dt=dt, c=c, dc=dc: E.dma_start(
                        out=dc[:], in_=delta_scr_d[dt * 128:(dt + 1) * 128, c * CH:(c + 1) * CH]),
                        deps=scan_fence + list(dt_toks))
                    l2 = P.dma("sync", lambda E, dt=dt, tc=tc, xc=xc: E.dma_start(
                        out=xc[:], in_=xn1T_d[dt * 128:(dt + 1) * 128, tc]), deps=scan_fence + list(dt_toks))
                    tu = P.op("vector", lambda E, dc=dc, xc=xc, uc=uc: E.tensor_mul(out=uc[:], in0=dc[:], in1=xc[:]),
                              deps=[l1, l2] + list(dt_toks))
                    targ = P.op("vector", lambda E, dt=dt, dc=dc: E.tensor_tensor(
                        out=dA[:].rearrange("p (n t) -> p n t", n=NSt),
                        in0=dc[:].unsqueeze(1).broadcast_to([128, NSt, CH]),
                        in1=A_s[dt][:].unsqueeze(2).broadcast_to([128, NSt, CH]), op=ALU.mult),
                        deps=[l1] + list(dt_toks) + a_toks)
                    texp = P.op("scalar", lambda E: E.activation(out=dA[:], in_=dA[:], func=AF.Exp), deps=[targ])
                    tbu = P.op("vector", lambda E, uc=uc: E.tensor_tensor(
                        out=dBu[:].rearrange("p (n t) -> p n t", n=NSt),
                        in0=uc[:].unsqueeze(1).broadcast_to([128, NSt, CH]),
                        in1=Brep[:].rearrange("p (n t) -> p n t", n=NSt), op=ALU.mult),
                        deps=[tu] + rep_done + list(dt_toks))
                    sc_toks = []
                    fold_toks = []
                    if c > 0:
                        for n in range(NSt):
                            fold_toks.append(P.op("vector", lambda E, n=n, dt=dt: E.scalar_tensor_tensor(
                                out=dBu[:, n * CH:n * CH + 1], in0=dA[:, n * CH:n * CH + 1],
                                scalar=hcarry[dt][:, n:n + 1], in1=dBu[:, n * CH:n * CH + 1],
                                op0=ALU.mult, op1=ALU.add), deps=[texp, tbu]))
                    for n in range(NSt):
                        sc_toks.append(P.op("vector", lambda E, n=n: E.tensor_tensor_scan(
                            out=hbuf[:, n * CH:(n + 1) * CH], data0=dA[:, n * CH:(n + 1) * CH],
                            data1=dBu[:, n * CH:(n + 1) * CH], initial=0.0, op0=ALU.mult, op1=ALU.add),
                            deps=[texp, tbu] + fold_toks[-1:] + (dt_toks[-1:] if c == 0 else [])))
                    hc = P.op("vector", lambda E, dt=dt: E.tensor_copy(
                        out=hcarry[dt][:].unsqueeze(2),
                        in_=hbuf[:].rearrange("p (n t) -> p n t", n=NSt)[:, :, CH - 1:CH]), deps=sc_toks)
                    ybi0 = (c * KT + dt) % 2
                    zb = zb2[ybi0]
                    tz = P.op("vector", lambda E, zb=zb: E.tensor_tensor(out=zb[:], in0=hbuf[:], in1=Crep[:], op=ALU.mult),
                              deps=sc_toks + rep_done)
                    f1 = P.op("vector", lambda E, zb=zb: E.tensor_tensor(
                        out=zb[:, 0:8 * CH], in0=zb[:, 0:8 * CH], in1=zb[:, 8 * CH:16 * CH], op=ALU.add), deps=[tz])
                    f2 = P.op("vector", lambda E, zb=zb: E.tensor_tensor(
                        out=zb[:, 0:4 * CH], in0=zb[:, 0:4 * CH], in1=zb[:, 4 * CH:8 * CH], op=ALU.add), deps=[f1])
                    f3 = P.op("vector", lambda E, zb=zb: E.tensor_tensor(
                        out=zb[:, 0:2 * CH], in0=zb[:, 0:2 * CH], in1=zb[:, 2 * CH:4 * CH], op=ALU.add), deps=[f2])
                    f4 = P.op("vector", lambda E, zb=zb: E.tensor_tensor(
                        out=zb[:, 0:CH], in0=zb[:, 0:CH], in1=zb[:, CH:2 * CH], op=ALU.add), deps=[f3])
                    ybi = (c * KT + dt) % 2
                    ty = P.op("vector", lambda E, dt=dt, ybi=ybi, zb=zb, xc=xc: E.scalar_tensor_tensor(
                        out=y1c2[ybi][:], in0=xc[:], scalar=D_s[:, dt:dt + 1], in1=zb[:, 0:CH],
                        op0=ALU.mult, op1=ALU.add), deps=[f4, prev_dma[ybi][0] if prev_dma[ybi] else None])
                    sty = P.dma("gpsimd", lambda E, dt=dt, tc=tc, ybi=ybi: E.dma_start(
                        out=y1T_d[dt * 128:(dt + 1) * 128, tc], in_=y1c2[ybi][:]), deps=[ty])
                    if dbg and bi == 0 and c == 0 and dt == KT - 1:
                        dyc = P.dma("sync", lambda E: E.dma_start(out=yc_dbg_d[:], in_=y1c[:]), deps=[ty])
                        out_toks.append(dyc)
                    prev_dma[ybi] = [sty]
                    out_toks.append(sty)
                    dt_toks = [ty, texp, tbu, hc]
                rep_toks = dt_toks
            batch_fence = P.fence()

        if dbg:
            d1 = P.dma("gpsimd", lambda E: E.dma_start(out=br_dbg_d[:], in_=Brep[:]), deps=P.fence())
            d2 = P.dma("gpsimd", lambda E: E.dma_start(out=da_dbg_d[:], in_=dA[:]), deps=P.fence())
            d3 = P.dma("gpsimd", lambda E: E.dma_start(out=h_dbg_d[:], in_=hbuf[:]), deps=P.fence())
            d4 = P.dma("gpsimd", lambda E: E.dma_start(out=bu_dbg_d[:], in_=dBu[:]), deps=P.fence())
            d5 = P.dma("gpsimd", lambda E: E.dma_start(out=hc_dbg_d[:], in_=hcarry[KT - 1][:]), deps=P.fence())
            d6 = P.dma("gpsimd", lambda E: E.dma_start(out=zs_dbg_d[:], in_=zb[:]), deps=P.fence())
            d7 = P.dma("gpsimd", lambda E: E.dma_start(out=cr_dbg_d[:], in_=Crep[:]), deps=P.fence())
            out_toks += [d4, d5, d6, d7]
            out_toks += [d1, d2, d3]
        P.op("gpsimd", lambda E: E.engine_nop(), deps=out_toks)
        block = stack.enter_context(nc.Block())
        P.build(stack, block)
    return nc


def run_l1_attn(inputs, xn1T_list, dbg=False, nch=None):
    nc = build_l1_attn(dbg=dbg, nch=nch)
    xprojT = np.ascontiguousarray(np.asarray(inputs["L1_attn_x_proj_w"]).T)
    dtprojT = np.ascontiguousarray(np.asarray(inputs["L1_attn_dt_proj_w"]).T)
    dtb = np.ascontiguousarray(np.asarray(inputs["L1_attn_dt_proj_b"]))
    alog = np.ascontiguousarray(np.asarray(inputs["L1_attn_A_log"]))
    dvec = np.ascontiguousarray(np.asarray(inputs["L1_attn_D"]))
    in_maps = [{"xn1T": xn1T_list[c], "xprojT": xprojT, "dtprojT": dtprojT,
                "dtb": dtb, "alog": alog, "dvec": dvec} for c in range(NCORES)]
    res = run_bass_kernel_spmd(nc, in_maps, core_ids=list(range(NCORES)))
    return res.results



def build_l1_ffn():
    nc = bass.Bass("TRN2", target_bir_lowering=False)
    TT_ = B2 * T
    IM = 2816
    OT_I = IM // 128
    xn1R_d = nc.dram_tensor("xn1R", [TT_, D], F32, kind="ExternalInput").ap()
    wg_d = nc.dram_tensor("wg", [IM, D], F32, kind="ExternalInput").ap()
    wu_d = nc.dram_tensor("wu", [IM, D], F32, kind="ExternalInput").ap()
    wd_d = nc.dram_tensor("wd", [D, IM], F32, kind="ExternalInput").ap()
    x1T_d = nc.dram_tensor("x1T", [D, TT_], F32, kind="ExternalInput").ap()
    y1T_d = nc.dram_tensor("y1T", [D, TT_], F32, kind="ExternalInput").ap()
    x2T_d = nc.dram_tensor("x2T", [D, TT_], F32, kind="ExternalOutput").ap()
    xq_scr_d = nc.dram_tensor("xq_scr", [TT_, D], BF16, kind="Internal").ap()
    wgq_scr_d = nc.dram_tensor("wgq_scr", [IM, D], BF16, kind="Internal").ap()
    wuq_scr_d = nc.dram_tensor("wuq_scr", [IM, D], BF16, kind="Internal").ap()
    wdq_scr_d = nc.dram_tensor("wdq_scr", [D, IM], BF16, kind="Internal").ap()
    gs_scr_d = nc.dram_tensor("gs_scr", [IM, TT_], BF16, kind="Internal").ap()
    h_scr_d = nc.dram_tensor("h_scr", [IM, TT_], BF16, kind="Internal").ap()
    hq_scr_d = nc.dram_tensor("hq_scr", [TT_, IM], BF16, kind="Internal").ap()

    P = Prog(nc)
    with ExitStack() as stack:
        def sb(name, shape, dt):
            return stack.enter_context(nc.sbuf_tensor(name, shape, dt))

        def ps(name, shape, dt=F32):
            return stack.enter_context(nc.psum_tensor(name, shape, dt))

        CH = 512
        NCH = TT_ // CH
        pool = sb("pool", [128, 8 * IM], BF16)
        qin_f = sb("qin_f", [128, IM], F32)
        qin_b = sb("qin_b", [128, IM], BF16)
        u2 = sb("u2", [128, IM], F32)
        t1q = sb("t1q", [128, IM], F32)
        loq = sb("loq", [128, IM], F32)
        mkq = sb("mkq", [128, IM], U8)
        sgq = sb("sgq", [128, IM], F32)
        qout = sb("qout", [128, IM], BF16)
        bmq = sb("bmq", [128, IM // 16], F32)
        rs2q = sb("rs2q", [128, IM // 16], F32)
        shq = sb("shq", [128, IM // 16], F32)
        xqT = [sb(f"xqT{k}", [128, CH], BF16) for k in range(KT)]
        hqT = [sb(f"hqT{k}", [128, CH], BF16) for k in range(OT_I)]
        gsb = [sb(f"gsb{i}", [128, CH], BF16) for i in range(2)]
        hsb = [sb(f"hsb{i}", [128, CH], BF16) for i in range(2)]
        x1c2 = [sb(f"x1c{i}", [128, CH], F32) for i in range(2)]
        y1c2 = [sb(f"y1c{i}", [128, CH], F32) for i in range(2)]
        x2c2 = [sb(f"x2c{i}", [128, CH], F32) for i in range(2)]
        PG = ps("PG", [128, 512])
        PU = ps("PU", [128, 512])
        PD = ps("PD", [128, 512])

        warm = sb("warmq", [1, 32], F32)
        w0 = P.op("vector", lambda E: E.memset(warm[:], 1.0))
        w1 = P.op("scalar", lambda E: E.activation(out=warm[:], in_=warm[:], func=AF.Sign), deps=[w0])
        w2 = P.op("scalar", lambda E: E.activation(out=warm[:], in_=warm[:], func=AF.Silu), deps=[w1])
        warm_tok = [w2]

        def emit_quant(qin_ap, W, deps):
            NBq = W // 16
            t0 = P.op("vector", lambda E, qin_ap=qin_ap, W=W, NBq=NBq: E.tensor_reduce(
                out=bmq[:, 0:NBq], in_=qin_ap.rearrange("p (nb blk) -> p nb blk", blk=16),
                axis=AX.X, op=ALU.max, apply_absolute_value=True), deps=deps)
            t1_ = P.op("vector", lambda E, NBq=NBq: E.tensor_scalar_max(out=bmq[:, 0:NBq], in0=bmq[:, 0:NBq],
                                                                       scalar1=6e-12), deps=[t0])
            t2 = P.op("vector", lambda E, NBq=NBq: E.reciprocal(out=rs2q[:, 0:NBq], in_=bmq[:, 0:NBq]), deps=[t1_])
            t3 = P.op("vector", lambda E, NBq=NBq: E.tensor_scalar_mul(out=rs2q[:, 0:NBq], in0=rs2q[:, 0:NBq],
                                                                      scalar1=12.0), deps=[t2])
            t4 = P.op("vector", lambda E, NBq=NBq: E.tensor_scalar_mul(out=shq[:, 0:NBq], in0=bmq[:, 0:NBq],
                                                                      scalar1=1.0 / 12.0), deps=[t2])
            t5 = P.op("scalar", lambda E, qin_ap=qin_ap, W=W: E.activation(out=sgq[:, 0:W], in_=qin_ap, func=AF.Sign),
                      deps=deps + warm_tok)
            t6 = P.op("vector", lambda E, qin_ap=qin_ap, W=W, NBq=NBq: E.tensor_tensor(
                out=u2[:, 0:W].rearrange("p (nb blk) -> p nb blk", blk=16),
                in0=qin_ap.rearrange("p (nb blk) -> p nb blk", blk=16),
                in1=rs2q[:, 0:NBq].unsqueeze(2).broadcast_to([128, NBq, 16]), op=ALU.mult), deps=[t3] + deps)
            t7 = P.op("vector", lambda E, W=W: E.scalar_tensor_tensor(
                out=u2[:, 0:W], in0=u2[:, 0:W], scalar=-1.0, in1=u2[:, 0:W], op0=ALU.mult, op1=ALU.max), deps=[t6])
            t8 = P.op("vector", lambda E, W=W: E.scalar_tensor_tensor(
                out=t1q[:, 0:W], in0=u2[:, 0:W], scalar=C_VELT, in1=u2[:, 0:W],
                op0=ALU.mult, op1=ALU.subtract), deps=[t7])
            t9 = P.op("vector", lambda E, W=W: E.scalar_tensor_tensor(
                out=t1q[:, 0:W], in0=u2[:, 0:W], scalar=C_VELT, in1=t1q[:, 0:W],
                op0=ALU.mult, op1=ALU.subtract), deps=[t8])
            ta = P.op("vector", lambda E, W=W: E.tensor_single_scalar(
                out=loq[:, 0:W], in_=u2[:, 0:W], scalar=0.5, op=ALU.is_gt), deps=[t7])
            tb = P.op("vector", lambda E, W=W: E.scalar_tensor_tensor(
                out=loq[:, 0:W], in0=u2[:, 0:W], scalar=1.5, in1=loq[:, 0:W], op0=ALU.is_gt, op1=ALU.add), deps=[ta])
            tcm = P.op("vector", lambda E, W=W: E.tensor_single_scalar(
                out=mkq[:, 0:W], in_=u2[:, 0:W], scalar=2.0, op=ALU.is_gt), deps=[t7])
            td = P.op("vector", lambda E, W=W: E.select(
                out=u2[:, 0:W], mask=mkq[:, 0:W], on_true=t1q[:, 0:W], on_false=loq[:, 0:W]), deps=[t9, tb, tcm])
            te = P.op("vector", lambda E, W=W: E.tensor_tensor(
                out=u2[:, 0:W], in0=u2[:, 0:W], in1=sgq[:, 0:W], op=ALU.mult), deps=[td, t5])
            tf = P.op("vector", lambda E, W=W, NBq=NBq: E.tensor_tensor(
                out=qout[:, 0:W].rearrange("p (nb blk) -> p nb blk", blk=16),
                in0=u2[:, 0:W].rearrange("p (nb blk) -> p nb blk", blk=16),
                in1=shq[:, 0:NBq].unsqueeze(2).broadcast_to([128, NBq, 16]), op=ALU.mult), deps=[te, t4])
            return tf

        out_toks = []
        prevq = list(warm_tok)
        for (src_d, dst_d, nt, W) in ((xn1R_d, xq_scr_d, TT_ // 128, D),
                                      (wg_d, wgq_scr_d, OT_I, D),
                                      (wu_d, wuq_scr_d, OT_I, D),
                                      (wd_d, wdq_scr_d, KT, IM)):
            for rt in range(nt):
                ld = P.dma("sync", lambda E, src_d=src_d, rt=rt, W=W: E.dma_start(
                    out=qin_f[:, 0:W], in_=src_d[rt * 128:(rt + 1) * 128, :]), deps=prevq)
                tf = emit_quant(qin_f[:, 0:W], W, [ld])
                st = P.dma("gpsimd", lambda E, dst_d=dst_d, rt=rt, W=W: E.dma_start(
                    out=dst_d[rt * 128:(rt + 1) * 128, :], in_=qout[:, 0:W]), deps=[tf])
                prevq = [tf, st]
        q_done = P.fence()

        ldp = [P.dma("sync", lambda E, k=k: E.dma_start(
            out=pool[:, k * IM:(k + 1) * IM], in_=wgq_scr_d[:, k * 128:(k + 1) * 128], transpose=True),
            deps=q_done, ch="tp") for k in range(KT)]
        pg_free = []
        gs_toks = []
        for ch in range(NCH):
            ldx = [P.dma("sync", lambda E, k=k, ch=ch: E.dma_start(
                out=xqT[k][:], in_=xq_scr_d[ch * CH:(ch + 1) * CH, k * 128:(k + 1) * 128], transpose=True),
                deps=q_done + gs_toks[-2:], ch="tp") for k in range(KT)]
            for ot in range(OT_I):
                gi = ot % 2
                mm = None
                for k in range(KT):
                    mm = P.op("tensor", lambda E, ot=ot, k=k: E.matmul(
                        PG[:], pool[:, k * IM + ot * 128: k * IM + (ot + 1) * 128], xqT[k][:],
                        start=(k == 0), stop=(k == KT - 1)),
                        deps=(list(ldx) + list(ldp) + pg_free) if k == 0 else [])
                ev = P.op("scalar", lambda E, gi=gi: E.activation(out=gsb[gi][:], in_=PG[:], func=AF.Silu),
                          deps=[mm] + gs_toks[-2:] + warm_tok)
                pg_free = [ev]
                st = P.dma("gpsimd", lambda E, ot=ot, ch=ch, gi=gi: E.dma_start(
                    out=gs_scr_d[ot * 128:(ot + 1) * 128, ch * CH:(ch + 1) * CH], in_=gsb[gi][:]), deps=[ev])
                gs_toks.append(st)
        g_done = P.fence()

        ldp2 = [P.dma("sync", lambda E, k=k: E.dma_start(
            out=pool[:, k * IM:(k + 1) * IM], in_=wuq_scr_d[:, k * 128:(k + 1) * 128], transpose=True),
            deps=g_done, ch="tp") for k in range(KT)]
        pu_free = []
        h_toks = []
        for ch in range(NCH):
            ldx = [P.dma("sync", lambda E, k=k, ch=ch: E.dma_start(
                out=xqT[k][:], in_=xq_scr_d[ch * CH:(ch + 1) * CH, k * 128:(k + 1) * 128], transpose=True),
                deps=g_done + h_toks[-2:], ch="tp") for k in range(KT)]
            for ot in range(OT_I):
                gi = ot % 2
                mm = None
                for k in range(KT):
                    mm = P.op("tensor", lambda E, ot=ot, k=k: E.matmul(
                        PU[:], pool[:, k * IM + ot * 128: k * IM + (ot + 1) * 128], xqT[k][:],
                        start=(k == 0), stop=(k == KT - 1)),
                        deps=(list(ldx) + list(ldp2) + pu_free) if k == 0 else [])
                lg = P.dma("sync", lambda E, ot=ot, ch=ch, gi=gi: E.dma_start(
                    out=gsb[gi][:], in_=gs_scr_d[ot * 128:(ot + 1) * 128, ch * CH:(ch + 1) * CH]),
                    deps=g_done + h_toks[-2:])
                th = P.op("vector", lambda E, gi=gi: E.tensor_tensor(
                    out=hsb[gi][:], in0=PU[:], in1=gsb[gi][:], op=ALU.mult), deps=[mm, lg] + h_toks[-2:])
                pu_free = [th]
                st = P.dma("gpsimd", lambda E, ot=ot, ch=ch, gi=gi: E.dma_start(
                    out=h_scr_d[ot * 128:(ot + 1) * 128, ch * CH:(ch + 1) * CH], in_=hsb[gi][:]), deps=[th])
                h_toks.append(st)
        h_done = P.fence()

        prevq = list(h_done)
        hq_toks = []
        for rt in range(TT_ // 128):
            ld = P.dma("sync", lambda E, rt=rt: E.dma_start(
                out=qin_b[:], in_=h_scr_d[:, rt * 128:(rt + 1) * 128], transpose=True),
                deps=prevq, ch="tp")
            tf = emit_quant(qin_b[:], IM, [ld])
            st = P.dma("gpsimd", lambda E, rt=rt: E.dma_start(
                out=hq_scr_d[rt * 128:(rt + 1) * 128, :], in_=qout[:, 0:IM]), deps=[tf])
            prevq = [tf, st]
            hq_toks.append(st)
        hq_done = P.fence()

        ldp3 = [P.dma("sync", lambda E, kk=kk: E.dma_start(
            out=pool[:, kk * D:(kk + 1) * D], in_=wdq_scr_d[:, kk * 128:(kk + 1) * 128], transpose=True),
            deps=hq_done, ch="tp") for kk in range(OT_I)]
        pd_free = []
        x2_toks = []
        for ch in range(NCH):
            ldh = [P.dma("sync", lambda E, kk=kk, ch=ch: E.dma_start(
                out=hqT[kk][:], in_=hq_scr_d[ch * CH:(ch + 1) * CH, kk * 128:(kk + 1) * 128], transpose=True),
                deps=hq_done + x2_toks[-2:], ch="tp") for kk in range(OT_I)]
            for ot in range(KT):
                gi = ot % 2
                l1 = P.dma("sync", lambda E, ot=ot, ch=ch, gi=gi: E.dma_start(
                    out=x1c2[gi][:], in_=x1T_d[ot * 128:(ot + 1) * 128, ch * CH:(ch + 1) * CH]),
                    deps=hq_done + x2_toks[-2:])
                l2 = P.dma("sync", lambda E, ot=ot, ch=ch, gi=gi: E.dma_start(
                    out=y1c2[gi][:], in_=y1T_d[ot * 128:(ot + 1) * 128, ch * CH:(ch + 1) * CH]),
                    deps=hq_done + x2_toks[-2:])
                mm = None
                for kk in range(OT_I):
                    mm = P.op("tensor", lambda E, ot=ot, kk=kk: E.matmul(
                        PD[:], pool[:, kk * D + ot * 128: kk * D + (ot + 1) * 128], hqT[kk][:],
                        start=(kk == 0), stop=(kk == OT_I - 1)),
                        deps=(list(ldh) + list(ldp3) + pd_free) if kk == 0 else [])
                a1 = P.op("vector", lambda E, gi=gi: E.tensor_tensor(
                    out=x2c2[gi][:], in0=PD[:], in1=x1c2[gi][:], op=ALU.add), deps=[mm, l1] + x2_toks[-2:])
                pd_free = [a1]
                a2 = P.op("vector", lambda E, gi=gi: E.tensor_tensor(
                    out=x2c2[gi][:], in0=x2c2[gi][:], in1=y1c2[gi][:], op=ALU.add), deps=[a1, l2])
                st = P.dma("gpsimd", lambda E, ot=ot, ch=ch, gi=gi: E.dma_start(
                    out=x2T_d[ot * 128:(ot + 1) * 128, ch * CH:(ch + 1) * CH], in_=x2c2[gi][:]), deps=[a2])
                x2_toks.append(st)
                out_toks.append(st)
        P.op("gpsimd", lambda E: E.engine_nop(), deps=out_toks)
        block = stack.enter_context(nc.Block())
        P.build(stack, block)
    return nc


def run_l1_ffn(inputs, xn1T_list, x1T_list, y1T_list):
    nc = build_l1_ffn()
    wg = np.ascontiguousarray(np.asarray(inputs["L1_ffn_w_gate"]))
    wu = np.ascontiguousarray(np.asarray(inputs["L1_ffn_w_up"]))
    wd = np.ascontiguousarray(np.asarray(inputs["L1_ffn_w_down"]))
    in_maps = []
    for c in range(NCORES):
        xn1R = np.ascontiguousarray(xn1T_list[c].T)
        in_maps.append({"xn1R": xn1R, "wg": wg, "wu": wu, "wd": wd,
                        "x1T": x1T_list[c], "y1T": y1T_list[c]})
    res = run_bass_kernel_spmd(nc, in_maps, core_ids=list(range(NCORES)))
    return res.results




def build_fused():
    nc = bass.Bass("TRN2", target_bir_lowering=False)
    TT_ = B2 * T
    TS_ = NB * SS
    NSt = 16
    IM = 2816
    OT_I = IM // 128
    x0T_d = nc.dram_tensor("x0T", [D, TT_], F32, kind="ExternalInput").ap()
    x0Ts_d = nc.dram_tensor("x0Ts", [D, TS_], F32, kind="ExternalInput").ap()
    wqkvT_d = nc.dram_tensor("wqkvT", [D, 3 * D], F32, kind="ExternalInput").ap()
    woutT_d = nc.dram_tensor("woutT", [D, D], F32, kind="ExternalInput").ap()
    bout_d = nc.dram_tensor("bout", [D], F32, kind="ExternalInput").ap()
    g16_d = nc.dram_tensor("g16", [1, H], F32, kind="ExternalInput").ap()
    b16_d = nc.dram_tensor("b16", [1, H], F32, kind="ExternalInput").ap()
    eye_d = nc.dram_tensor("eye", [128, 128], F32, kind="ExternalInput").ap()
    ones_d = nc.dram_tensor("ones", [128, 1], F32, kind="ExternalInput").ap()
    wpackT_d = nc.dram_tensor("wpackT", [D, 7 * D], F32, kind="ExternalInput").ap()
    bias3_d = nc.dram_tensor("bias3", [3, D], F32, kind="ExternalInput").ap()
    inc_d = nc.dram_tensor("incv", [D], F32, kind="ExternalInput").ap()
    woT_d = nc.dram_tensor("woT", [D, D], F32, kind="ExternalInput").ap()
    bo_d = nc.dram_tensor("bo", [D], F32, kind="ExternalInput").ap()
    xprojT_d = nc.dram_tensor("xprojT", [D, 96], F32, kind="ExternalInput").ap()
    dtprojT_d = nc.dram_tensor("dtprojT", [64, D], F32, kind="ExternalInput").ap()
    dtb_d = nc.dram_tensor("dtb", [D], F32, kind="ExternalInput").ap()
    alog_d = nc.dram_tensor("alog", [D, NSt], F32, kind="ExternalInput").ap()
    dvec_d = nc.dram_tensor("dvec", [D], F32, kind="ExternalInput").ap()
    wg_d = nc.dram_tensor("wg", [IM, D], F32, kind="ExternalInput").ap()
    wu_d = nc.dram_tensor("wu", [IM, D], F32, kind="ExternalInput").ap()
    wd_d = nc.dram_tensor("wd", [D, IM], F32, kind="ExternalInput").ap()
    x2T_d = nc.dram_tensor("x2T", [D, TT_], F32, kind="ExternalOutput").ap()
    attn0T_d = nc.dram_tensor("i_attn0T", [D, TT_], F32, kind="Internal").ap()
    xn0T_d = nc.dram_tensor("i_xn0T", [D, TT_], BF16, kind="Internal").ap()
    sburst_d = nc.dram_tensor("i_sburst", [2, T], F32, kind="Internal").ap()
    ao_scr_d = nc.dram_tensor("i_ao_scr", [D, T], BF16, kind="Internal").ap()
    xns_scr_d = nc.dram_tensor("i_xns", [D, TS_], BF16, kind="Internal").ap()
    zsh_d = nc.dram_tensor("i_zsh", [D, SS], F32, kind="Internal").ap()
    zag_d = nc.dram_tensor("i_zag", [NCORES * D, SS], F32, kind="Internal", addr_space="Shared").ap()
    x1T_d = nc.dram_tensor("i_x1T", [D, TT_], F32, kind="Internal").ap()
    xn1T_d = nc.dram_tensor("i_xn1T", [D, TT_], F32, kind="Internal").ap()
    xn1Tb_d = nc.dram_tensor("i_xn1Tb", [D, TT_], BF16, kind="Internal").ap()
    y1T_d = nc.dram_tensor("i_y1T", [D, TT_], F32, kind="Internal").ap()
    delta_scr_d = nc.dram_tensor("i_delta", [D, T], F32, kind="Internal").ap()
    bc_scr_d = nc.dram_tensor("i_bc", [2, NSt * 512], F32, kind="Internal").ap()
    xq_scr_d = nc.dram_tensor("i_xq", [TT_, D], BF16, kind="Internal").ap()
    wgq_scr_d = nc.dram_tensor("i_wgq", [IM, D], BF16, kind="Internal").ap()
    wuq_scr_d = nc.dram_tensor("i_wuq", [IM, D], BF16, kind="Internal").ap()
    wdq_scr_d = nc.dram_tensor("i_wdq", [D, IM], BF16, kind="Internal").ap()
    gs_scr_d = nc.dram_tensor("i_gs", [IM, TT_], BF16, kind="Internal").ap()
    h_scr_d = nc.dram_tensor("i_h", [IM, TT_], BF16, kind="Internal").ap()
    hq_scr_d = nc.dram_tensor("i_hq", [TT_, IM], BF16, kind="Internal").ap()

    P = Prog(nc)
    def _phase1():
        with ExitStack() as stack:
            def sb(name, shape, dt):
                return stack.enter_context(nc.sbuf_tensor("p1_" + name, shape, dt))

            def ps(name, shape, dt=F32):
                return stack.enter_context(nc.psum_tensor("p1_" + name, shape, dt))

            CH = 512
            xbc = [sb(f"xbc{k}", [128, CH], BF16) for k in range(KT)]
            xnc = [sb(f"xnc{k}", [128, CH], BF16) for k in range(KT)]
            ktc = [sb(f"ktc{k}", [128, CH], BF16) for k in range(KT)]
            vtc = [sb(f"vtc{k}", [128, CH], BF16) for k in range(KT)]
            q_sb = [sb(f"q{k}", [128, T], BF16) for k in range(KT)]
            wq = [sb(f"wq{k}", [128, 3 * D], BF16) for k in range(KT)]
            wo = [sb(f"wo{k}", [128, D], BF16) for k in range(KT)]
            kvrow = sb("kvrow", [128, CH], BF16)
            gkv_all = sb("gkv_all", [128, KT * 128], F32)
            gkvb = sb("gkvb", [128, KT * 128], BF16)
            aostage = sb("aostage", [128, T], BF16)
            aor = [sb(f"aor{k}", [128, CH], BF16) for k in range(KT)]
            aout = sb("aout", [128, T], F32)
            eyeb = sb("eyeb", [128, 128], BF16)
            ones_b = sb("ones_b", [128, 1], BF16)
            onesrow_b = sb("onesrow_b", [1, 128], BF16)
            mb_row = sb("mb_row", [1, CH], BF16)
            rb_row = sb("rb_row", [1, CH], BF16)
            bout_s = sb("bout_s", [128, KT], F32)
            brep = sb("brep", [128, KT], F32)
            g16_s = sb("g16_s", [1, H], F32)
            m_row = sb("m_row", [1, CH], F32)
            v_row = sb("v_row", [1, CH], F32)
            vt_row = sb("vt_row", [1, CH], F32)
            nsq = sb("nsq", [1, T], F32)
            srow = sb("srow", [1, T], F32)
            srep = sb("srep", [128, T], BF16)
            t1 = sb("t1", [128, CH], F32)
            t1bs = [sb(f"t1b{k}", [128, CH], BF16) for k in range(KT)]
            zq = sb("zq", [64, CH], BF16)
            ub = sb("ub", [128, KT], F32)
            ubt = sb("ubt", [128, KT], F32)
            ubb = sb("ubb", [128, KT], BF16)
            btot = sb("btot", [128, KT], F32)

            PQ = [ps(f"PQ{i}", [128, 512]) for i in range(2)]
            PT = [ps(f"PT{i}", [128, 64], BF16) for i in range(2)]
            GK = ps("GK", [64, 128])
            ZO = ps("ZO", [128, 512])
            NS = ps("NS", [1, 512])

            warm = sb("warm", [1, 32], F32)
            w0 = P.op("vector", lambda E: E.memset(warm[:], 1.0))
            w1 = P.op("scalar", lambda E: E.activation(out=warm[:], in_=warm[:], func=AF.Square), deps=[w0])
            w2 = P.op("scalar", lambda E: E.activation(out=warm[:], in_=warm[:], func=AF.Ln), deps=[w1])
            w3 = P.op("scalar", lambda E: E.activation(out=warm[:], in_=warm[:], func=AF.Exp), deps=[w2])
            warm_tok = [w3]

            ldw_q, ldw_o = [], []
            for k in range(KT):
                ldw_q.append(P.dma("gpsimd", lambda E, k=k: E.dma_start(out=wq[k][:], in_=wqkvT_d[k * 128:(k + 1) * 128, :]), ch="cast"))
                ldw_o.append(P.dma("gpsimd", lambda E, k=k: E.dma_start(out=wo[k][:], in_=woutT_d[k * 128:(k + 1) * 128, :]), ch="cast"))
            ld_eyeb = P.dma("gpsimd", lambda E: E.dma_start(out=eyeb[:], in_=eye_d[:]), ch="cast")
            ld_ob = P.dma("gpsimd", lambda E: E.dma_start(out=ones_b[:], in_=ones_d[:]), ch="cast")
            ld_or = P.dma("gpsimd", lambda E: E.dma_start(
                out=onesrow_b[:], in_=ones_d[:, 0].unsqueeze(0).broadcast_to([1, 128])), ch="castp0")
            ld_g = P.dma("sync", lambda E: E.dma_start(out=g16_s[:], in_=g16_d[:]), ch="p0")
            ld_bout = [P.dma("sync", lambda E, o=o: E.dma_start(
                out=bout_s[:, o:o + 1], in_=bout_d[o * 128:(o + 1) * 128].unsqueeze(1))) for o in range(KT)]
            ld_brep = [P.dma("sync", lambda E, h=h: E.dma_start(
                out=brep[(h % 2) * 64:(h % 2) * 64 + 64, h // 2:h // 2 + 1],
                in_=b16_d[0:1, h:h + 1].broadcast_to([64, 1])), ch=("n0" if h % 2 == 0 else "n64")) for h in range(H)]

            out_toks = []
            batch_fence = []
            for bi in range(B2):
                z1 = P.op("vector", lambda E: E.memset(gkv_all[:], 0.0), deps=batch_fence + warm_tok)
                z2 = P.op("vector", lambda E: E.memset(ub[:], 0.0), deps=batch_fence)
                acc_g = [z1]
                acc_u = [z2]
                chunk_fence = list(batch_fence)
                for c in range(4):
                    tc = slice(bi * T + c * CH, bi * T + (c + 1) * CH)
                    ld_x = [P.dma("gpsimd", lambda E, k=k, tc=tc: E.dma_start(
                        out=xbc[k][:], in_=x0T_d[k * 128:(k + 1) * 128, tc]),
                        deps=chunk_fence) for k in range(KT)]
                    mm = None
                    for k in range(KT):
                        mm = P.op("tensor", lambda E, k=k: E.matmul(
                            PQ[0][0:1, :], ones_b[:], xbc[k][:], start=(k == 0), stop=(k == KT - 1)),
                            deps=(list(ld_x) + [ld_ob] + warm_tok + chunk_fence) if k == 0 else [])
                    em0 = P.op("scalar", lambda E: E.copy(out=m_row[:], in_=PQ[0][0:1, :]), deps=[mm])
                    em = P.op("vector", lambda E: E.tensor_scalar_mul(out=m_row[:], in0=m_row[:], scalar1=1.0 / D), deps=[em0])
                    sq_toks = [P.op("scalar", lambda E, k=k: E.activation(out=t1bs[k][:], in_=xbc[k][:], func=AF.Square),
                                    deps=[ld_x[k]] + chunk_fence) for k in range(KT)]
                    mmq = None
                    for k in range(KT):
                        mmq = P.op("tensor", lambda E, k=k: E.matmul(
                            PQ[1][0:1, :], ones_b[:], t1bs[k][:], start=(k == 0), stop=(k == KT - 1)),
                            deps=([sq_toks[-1], ld_ob] + chunk_fence) if k == 0 else [])
                    ev0 = P.op("scalar", lambda E: E.copy(out=v_row[:], in_=PQ[1][0:1, :]), deps=[mmq])
                    ev = P.op("vector", lambda E: E.tensor_scalar_mul(out=v_row[:], in0=v_row[:], scalar1=1.0 / D), deps=[ev0])
                    tv1 = P.op("vector", lambda E: E.tensor_mul(out=vt_row[:], in0=m_row[:], in1=m_row[:]), deps=[em])
                    tv2 = P.op("vector", lambda E: E.tensor_sub(out=v_row[:], in0=v_row[:], in1=vt_row[:]), deps=[tv1, ev])
                    tv3 = P.op("vector", lambda E: E.tensor_scalar_add(out=v_row[:], in0=v_row[:], scalar1=LN_EPS), deps=[tv2])
                    tr1 = P.op("scalar", lambda E: E.activation(out=v_row[:], in_=v_row[:], func=AF.Ln), deps=[tv3])
                    tr1b = P.op("vector", lambda E: E.tensor_scalar_mul(out=v_row[:], in0=v_row[:], scalar1=-0.5), deps=[tr1])
                    tr2 = P.op("scalar", lambda E: E.activation(out=v_row[:], in_=v_row[:], func=AF.Exp), deps=[tr1b])
                    cmb = P.op("vector", lambda E: E.tensor_copy(out=mb_row[:], in_=m_row[:]), deps=[em])
                    crb = P.op("vector", lambda E: E.tensor_copy(out=rb_row[:], in_=v_row[:]), deps=[tr2])
                    t_m = P.op("tensor", lambda E: E.matmul(PQ[0][:], onesrow_b[:], mb_row[:], start=True, stop=True),
                               deps=[cmb, ld_or])
                    t_r = P.op("tensor", lambda E: E.matmul(PQ[1][:], onesrow_b[:], rb_row[:], start=True, stop=True),
                               deps=[crb, ld_or])
                    ap_toks = []
                    for k in range(KT):
                        a1 = P.op("vector", lambda E, k=k: E.tensor_tensor(out=t1[:], in0=xbc[k][:], in1=PQ[0][:], op=ALU.subtract),
                                  deps=[t_m, ld_x[k]] + ([ap_toks[-1]] if ap_toks else [mmq]))
                        a2 = P.op("vector", lambda E, k=k: E.tensor_tensor(out=xnc[k][:], in0=t1[:], in1=PQ[1][:], op=ALU.mult),
                                  deps=[a1, t_r])
                        ap_toks.append(a2)
                    st_xn = [P.dma("gpsimd", lambda E, k=k, tc=tc: E.dma_start(
                        out=xn0T_d[k * 128:(k + 1) * 128, tc], in_=xnc[k][:]), deps=[ap_toks[k]]) for k in range(KT)]
                    out_toks += st_xn
                    pq_free = [[ap_toks[-1]], [ap_toks[-1]]]
                    ev_q = [None] * KT
                    ev_k = [None] * KT
                    ev_v = [None] * KT
                    for ot in range(3 * KT):
                        pqi = ot % 2
                        mm = None
                        for k in range(KT):
                            mm = P.op("tensor", lambda E, ot=ot, k=k, pqi=pqi: E.matmul(
                                PQ[pqi][:], wq[k][:, ot * 128:(ot + 1) * 128], xnc[k][:],
                                start=(k == 0), stop=(k == KT - 1)),
                                deps=([ap_toks[-1], ldw_q[-1]] + pq_free[pqi]) if k == 0 else [])
                        if ot < KT:
                            evt = P.op("scalar", lambda E, ot=ot, pqi=pqi, c=c: E.copy(
                                out=q_sb[ot][:, c * CH:(c + 1) * CH], in_=PQ[pqi][:]), deps=[mm] + batch_fence)
                        elif ot < 2 * KT:
                            evt = P.op("scalar", lambda E, ot=ot, pqi=pqi: E.copy(
                                out=ktc[ot - KT][:], in_=PQ[pqi][:]), deps=[mm] + chunk_fence)
                            ev_k[ot - KT] = evt
                        else:
                            evt = P.op("scalar", lambda E, ot=ot, pqi=pqi: E.copy(
                                out=vtc[ot - 2 * KT][:], in_=PQ[pqi][:]), deps=[mm] + chunk_fence)
                            ev_v[ot - 2 * KT] = evt
                        pq_free[pqi] = [evt]
                        if ot < KT:
                            ev_q[ot] = evt
                    pt_free = [list(chunk_fence), list(chunk_fence)]
                    kvrow_free = list(chunk_fence)
                    gk_free = []
                    ub_toks = []
                    for h in range(H):
                        p64 = (h % 2) * 64
                        hp = h // 2
                        kcol = ktc[hp][p64:p64 + 64, :]
                        vcol = vtc[hp][p64:p64 + 64, :]
                        idn = eyeb[p64:p64 + 64, p64:p64 + 64]
                        tr_toks = []
                        for jb in range(4):
                            for which, col in ((0, kcol), (1, vcol)):
                                pti = (jb * 2 + which) % 2
                                tt = P.op("tensor", lambda E, col=col, jb=jb, pti=pti, idn=idn: E.transpose(
                                    out=PT[pti][:], in_=col[:, jb * 128:(jb + 1) * 128], identity=idn),
                                    deps=[ev_k[hp], ev_v[hp], ld_eyeb] + pt_free[pti])
                                te = P.op("scalar", lambda E, jb=jb, which=which, pti=pti: E.copy(
                                    out=kvrow[:, jb * 128 + which * 64: jb * 128 + which * 64 + 64], in_=PT[pti][:]),
                                    deps=[tt] + kvrow_free)
                                pt_free[pti] = [te]
                                tr_toks.append(te)
                        kvrow_free = []
                        mmgk = None
                        for jb in range(4):
                            mmgk = P.op("tensor", lambda E, jb=jb: E.matmul(
                                GK[:], kvrow[:, jb * 128: jb * 128 + 64], kvrow[:, jb * 128: jb * 128 + 128],
                                start=(jb == 0), stop=(jb == 3)),
                                deps=([tr_toks[-1]] + gk_free) if jb == 0 else [])
                        kvrow_free = [mmgk]
                        ga = P.op("vector", lambda E, h=h, p64=p64, hp=hp: E.tensor_tensor(
                            out=gkv_all[p64:p64 + 64, hp * 128:(hp + 1) * 128],
                            in0=GK[:], in1=gkv_all[p64:p64 + 64, hp * 128:(hp + 1) * 128], op=ALU.add),
                            deps=[mmgk] + acc_g)
                        gk_free = [ga]
                        ut = P.op("vector", lambda E, h=h, p64=p64, vcol=vcol: E.tensor_reduce(
                            out=ubt[p64:p64 + 64, h // 2:h // 2 + 1], in_=vcol[:], axis=AX.X, op=ALU.add),
                            deps=[ev_v[hp]] + acc_u)
                        ub_toks.append(ut)
                    acc_g = [gk_free[0] if gk_free else None] + [None]
                    acc_g = [ga]
                    ua = P.op("vector", lambda E: E.tensor_add(out=ub[:], in0=ub[:], in1=ubt[:]), deps=ub_toks + acc_u)
                    acc_u = [ua]
                    chunk_fence = P.fence()

                cg = P.op("vector", lambda E: E.tensor_copy(out=gkvb[:], in_=gkv_all[:]), deps=acc_g + chunk_fence)
                zo_free, ns_free, zq_free = [cg], [], []
                sburst_free = {0: [], 1: []}
                srep_free = list(batch_fence)
                ao_dma = {0: [], 1: []}
                for h in range(H):
                    p64 = (h % 2) * 64
                    hp = h // 2
                    qcol = q_sb[hp][p64:p64 + 64, :]
                    oh_toks = []
                    nsq_toks = []
                    for c in range(4):
                        ch = slice(c * CH, (c + 1) * CH)
                        mm = P.op("tensor", lambda E, ch=ch, p64=p64, hp=hp, qcol=qcol: E.matmul(
                            ZO[:], gkvb[p64:p64 + 64, hp * 128:(hp + 1) * 128], qcol[:, ch], start=True, stop=True),
                            deps=[cg, ev_q[hp]] + zo_free)
                        tzq = P.op("vector", lambda E, ch=ch, qcol=qcol: E.tensor_tensor(
                            out=zq[:], in0=ZO[0:64, :], in1=qcol[:, ch], op=ALU.mult), deps=[mm] + zq_free)
                        toh = P.op("scalar", lambda E, ch=ch, p64=p64: E.copy(
                            out=aostage[p64:p64 + 64, ch], in_=ZO[64:128, :]), deps=[mm] + ao_dma[h % 2])
                        tns = P.op("tensor", lambda E: E.matmul(
                            NS[:], ones_b[0:64, :], zq[:], start=True, stop=True), deps=[tzq] + ns_free)
                        ten = P.op("scalar", lambda E, ch=ch: E.copy(out=nsq[:, ch], in_=NS[:]), deps=[tns])
                        ns_free = [ten]
                        zq_free = [tns]
                        zo_free = [tzq, toh]
                        oh_toks.append(toh)
                        nsq_toks.append(ten)
                    s0 = P.op("vector", lambda E: E.tensor_scalar_max(out=srow[:], in0=nsq[:], scalar1=1e-24),
                              deps=nsq_toks + sburst_free[h % 2] + sburst_free[(h + 1) % 2])
                    s1 = P.op("scalar", lambda E: E.activation(out=srow[:], in_=srow[:], func=AF.Ln), deps=[s0])
                    s1b = P.op("vector", lambda E: E.tensor_scalar_mul(out=srow[:], in0=srow[:], scalar1=-0.5), deps=[s1])
                    s2 = P.op("scalar", lambda E: E.activation(out=srow[:], in_=srow[:], func=AF.Exp), deps=[s1b])
                    s5 = P.op("vector", lambda E, h=h: E.tensor_scalar_mul(
                        out=srow[:], in0=srow[:], scalar1=g16_s[0:1, h:h + 1]), deps=[s2, ld_g])
                    s6 = P.dma("gpsimd", lambda E, h=h: E.dma_start(out=sburst_d[h % 2:h % 2 + 1, :], in_=srow[0:1, :]),
                               deps=[s5] + sburst_free[h % 2], ch="p0")
                    s7 = P.dma("gpsimd", lambda E, h=h: E.dma_start(
                        out=srep[:], in_=sburst_d[h % 2:h % 2 + 1, :].broadcast_to([128, T])),
                        deps=[s6] + srep_free, ch="cast")
                    sburst_free[h % 2] = [s7]
                    sc_toks = []
                    for c in range(4):
                        ch = slice(c * CH, (c + 1) * CH)
                        tok = P.op("vector", lambda E, ch=ch, p64=p64: E.tensor_tensor(
                            out=aostage[p64:p64 + 64, ch], in0=aostage[p64:p64 + 64, ch],
                            in1=srep[p64:p64 + 64, ch], op=ALU.mult), deps=[s7, oh_toks[c]])
                        sc_toks.append(tok)
                    srep_free = sc_toks
                    da = P.dma("gpsimd", lambda E, h=h, p64=p64: E.dma_start(
                        out=ao_scr_d[h * 64:(h + 1) * 64, :], in_=aostage[p64:p64 + 64, :]), deps=sc_toks,
                        ch=("n0" if p64 == 0 else "n64"))
                    ao_dma[h % 2] = [da]
                ao_done = [ao_dma[0][0], ao_dma[1][0]]

                ubm = P.op("vector", lambda E: E.tensor_tensor(out=ub[:], in0=ub[:], in1=brep[:], op=ALU.mult),
                           deps=acc_u + ld_brep)
                ubc = P.op("vector", lambda E: E.tensor_copy(out=ubb[:], in_=ub[:]), deps=[ubm] + batch_fence)
                bt_toks = []
                pq0_free = chunk_fence
                for ot in range(KT):
                    mm = None
                    for k in range(KT):
                        mm = P.op("tensor", lambda E, ot=ot, k=k: E.matmul(
                            PQ[0][:, 0:1], wo[k][:, ot * 128:(ot + 1) * 128], ubb[:, k:k + 1],
                            start=(k == 0), stop=(k == KT - 1)),
                            deps=([ubc, ldw_o[-1]] + (pq0_free if ot == 0 else [bt_toks[-1]])) if k == 0 else [])
                    evb = P.op("scalar", lambda E, ot=ot: E.copy(out=btot[:, ot:ot + 1], in_=PQ[0][:, 0:1]), deps=[mm])
                    bt_toks.append(evb)
                btt = P.op("vector", lambda E: E.tensor_add(out=btot[:], in0=btot[:], in1=bout_s[:]),
                           deps=bt_toks + ld_bout)

                pq_free = [[bt_toks[-1]], chunk_fence]
                aor_free = [list(chunk_fence) for _ in range(KT)]
                aout_free = list(batch_fence)
                for c in range(4):
                    ch = slice(c * CH, (c + 1) * CH)
                    ld_ao = [P.dma("sync", lambda E, k=k, ch=ch: E.dma_start(
                        out=aor[k][:], in_=ao_scr_d[k * 128:(k + 1) * 128, ch]), deps=ao_done + aor_free[k])
                        for k in range(KT)]
                    for ot in range(KT):
                        pqi = ot % 2
                        mm = None
                        for k in range(KT):
                            mm = P.op("tensor", lambda E, ot=ot, k=k, pqi=pqi: E.matmul(
                                PQ[pqi][:], wo[k][:, ot * 128:(ot + 1) * 128], aor[k][:],
                                start=(k == 0), stop=(k == KT - 1)),
                                deps=(list(ld_ao) + [ldw_o[-1]] + pq_free[pqi]) if k == 0 else [])
                        evo = P.op("vector", lambda E, ch=ch, pqi=pqi, ot=ot: E.tensor_scalar_add(
                            out=aout[:, ch], in0=PQ[pqi][:], scalar1=btot[:, ot:ot + 1]),
                            deps=[mm, btt] + aout_free)
                        pq_free[pqi] = [evo]
                        std = P.dma("gpsimd", lambda E, ot=ot, ch=ch, bi=bi, c=c: E.dma_start(
                            out=attn0T_d[ot * 128:(ot + 1) * 128, bi * T + c * CH: bi * T + (c + 1) * CH],
                            in_=aout[:, ch]), deps=[evo])
                        aout_free = [std]
                        out_toks.append(std)
                    aor_free = [[P.last("tensor")] for _ in range(KT)]
                batch_fence = P.fence()

            P.op("gpsimd", lambda E: E.engine_nop(), deps=out_toks)
    _phase1()
    P.set_floor(P.fence())
    def _phase2():
        with ExitStack() as stack:
            def sb(name, shape, dt):
                return stack.enter_context(nc.sbuf_tensor("p2_" + name, shape, dt))

            def ps(name, shape, dt=F32):
                return stack.enter_context(nc.psum_tensor("p2_" + name, shape, dt))

            CH = 512
            xbc = [sb(f"xbc{k}", [128, CH], BF16) for k in range(KT)]
            xnc = [sb(f"xnc{k}", [128, CH], BF16) for k in range(KT)]
            t1bs = [sb(f"t1b{k}", [128, CH], BF16) for k in range(KT)]
            wp = [sb(f"wp{k}", [128, 7 * D], BF16) for k in range(KT)]
            ct_ = [sb(f"ct{k}", [128, SS], F32) for k in range(KT)]
            xstep = [[sb(f"xs{j}_{k}", [128, SS], BF16) for k in range(KT)] for j in range(2)]
            fts = [sb(f"fts{k}", [128, SS], F32) for k in range(KT)]
            its = [sb(f"its{k}", [128, SS], F32) for k in range(KT)]
            gts = [sb(f"gts{k}", [128, SS], F32) for k in range(KT)]
            tmpa = sb("tmpa", [128, SS], F32)
            tmpb = sb("tmpb", [128, SS], F32)
            tmpc = sb("tmpc", [128, 512], F32)
            ones_b = sb("ones_b", [128, 1], BF16)
            onesrow_b = sb("onesrow_b", [1, 128], BF16)
            b3 = sb("b3", [128, 3 * KT], F32)
            m_row = sb("m_row", [1, CH], F32)
            v_row = sb("v_row", [1, CH], F32)
            vt_row = sb("vt_row", [1, CH], F32)
            mb_row = sb("mb_row", [1, CH], BF16)
            rb_row = sb("rb_row", [1, CH], BF16)
            cb = [sb(f"cb{k}", [128, SS], BF16) for k in range(KT)]
            inc_sb = sb("inc_sb", [128, KT], F32)
            zstage = sb("zstage", [128, SS], F32)

            PQ = [ps(f"PQ{i}", [128, 512]) for i in range(2)]
            PG = [ps(f"PG{i}", [128, 512]) for i in range(3)]

            warm = sb("warm", [1, 32], F32)
            w0 = P.op("vector", lambda E: E.memset(warm[:], 1.0))
            w1 = P.op("scalar", lambda E: E.activation(out=warm[:], in_=warm[:], func=AF.Square), deps=[w0])
            w2 = P.op("scalar", lambda E: E.activation(out=warm[:], in_=warm[:], func=AF.Ln), deps=[w1])
            w3 = P.op("scalar", lambda E: E.activation(out=warm[:], in_=warm[:], func=AF.Exp), deps=[w2])
            warm_tok = [w3]

            ldw = [P.dma("gpsimd", lambda E, k=k: E.dma_start(out=wp[k][:], in_=wpackT_d[k * 128:(k + 1) * 128, :]), ch="cast")
                   for k in range(KT)]
            ld_ob = P.dma("gpsimd", lambda E: E.dma_start(out=ones_b[:], in_=ones_d[:]), ch="cast")
            ld_or = P.dma("gpsimd", lambda E: E.dma_start(
                out=onesrow_b[:], in_=ones_d[:, 0].unsqueeze(0).broadcast_to([1, 128])), ch="castp0")
            ld_b3 = [P.dma("sync", lambda E, g=g, o=o: E.dma_start(
                out=b3[:, g * KT + o:g * KT + o + 1], in_=bias3_d[g, o * 128:(o + 1) * 128].unsqueeze(1)))
                for g in range(3) for o in range(KT)]
            ld_inc = [P.dma("sync", lambda E, k=k: E.dma_start(
                out=inc_sb[:, k:k + 1], in_=inc_d[k * 128:(k + 1) * 128].unsqueeze(1))) for k in range(KT)]
            ld_c0 = [P.op("vector", lambda E, k=k: E.tensor_copy(
                out=ct_[k][:], in_=inc_sb[:, k:k + 1].broadcast_to([128, SS])), deps=[ld_inc[k]])
                for k in range(KT)]

            chunk_fence = []
            st_xn = []
            for c in range(8):
                tc = slice(c * CH, (c + 1) * CH)
                ld_x = [P.dma("gpsimd", lambda E, k=k, tc=tc: E.dma_start(
                    out=xbc[k][:], in_=x0Ts_d[k * 128:(k + 1) * 128, tc]), deps=chunk_fence, ch="cast") for k in range(KT)]
                mm = None
                for k in range(KT):
                    mm = P.op("tensor", lambda E, k=k: E.matmul(
                        PQ[0][0:1, :], ones_b[:], xbc[k][:], start=(k == 0), stop=(k == KT - 1)),
                        deps=(list(ld_x) + [ld_ob] + warm_tok + chunk_fence) if k == 0 else [])
                em0 = P.op("scalar", lambda E: E.copy(out=m_row[:], in_=PQ[0][0:1, :]), deps=[mm])
                em = P.op("vector", lambda E: E.tensor_scalar_mul(out=m_row[:], in0=m_row[:], scalar1=1.0 / D), deps=[em0])
                sq_toks = [P.op("scalar", lambda E, k=k: E.activation(out=t1bs[k][:], in_=xbc[k][:], func=AF.Square),
                               deps=[ld_x[k]] + chunk_fence) for k in range(KT)]
                mmq = None
                for k in range(KT):
                    mmq = P.op("tensor", lambda E, k=k: E.matmul(
                        PQ[1][0:1, :], ones_b[:], t1bs[k][:], start=(k == 0), stop=(k == KT - 1)),
                        deps=([sq_toks[-1], ld_ob] + chunk_fence) if k == 0 else [])
                ev0 = P.op("scalar", lambda E: E.copy(out=v_row[:], in_=PQ[1][0:1, :]), deps=[mmq])
                ev = P.op("vector", lambda E: E.tensor_scalar_mul(out=v_row[:], in0=v_row[:], scalar1=1.0 / D), deps=[ev0])
                tv1 = P.op("vector", lambda E: E.tensor_mul(out=vt_row[:], in0=m_row[:], in1=m_row[:]), deps=[em])
                tv2 = P.op("vector", lambda E: E.tensor_sub(out=v_row[:], in0=v_row[:], in1=vt_row[:]), deps=[tv1, ev])
                tv3 = P.op("vector", lambda E: E.tensor_scalar_add(out=v_row[:], in0=v_row[:], scalar1=LN_EPS), deps=[tv2])
                tr1 = P.op("scalar", lambda E: E.activation(out=v_row[:], in_=v_row[:], func=AF.Ln), deps=[tv3])
                tr1b = P.op("vector", lambda E: E.tensor_scalar_mul(out=v_row[:], in0=v_row[:], scalar1=-0.5), deps=[tr1])
                tr2 = P.op("scalar", lambda E: E.activation(out=v_row[:], in_=v_row[:], func=AF.Exp), deps=[tr1b])
                cmb = P.op("vector", lambda E: E.tensor_copy(out=mb_row[:], in_=m_row[:]), deps=[em])
                crb = P.op("vector", lambda E: E.tensor_copy(out=rb_row[:], in_=v_row[:]), deps=[tr2])
                t_m = P.op("tensor", lambda E: E.matmul(PQ[0][:], onesrow_b[:], mb_row[:], start=True, stop=True),
                           deps=[cmb, ld_or, em0])
                t_r = P.op("tensor", lambda E: E.matmul(PQ[1][:], onesrow_b[:], rb_row[:], start=True, stop=True),
                           deps=[crb, ld_or, ev0])
                ap_toks = []
                for k in range(KT):
                    a1 = P.op("vector", lambda E, k=k: E.tensor_tensor(out=tmpc[:], in0=xbc[k][:], in1=PQ[0][:],
                                                                      op=ALU.subtract),
                              deps=[t_m] + ([ap_toks[-1]] if ap_toks else [mmq]))
                    a2 = P.op("vector", lambda E, k=k: E.tensor_tensor(out=xnc[k][:], in0=tmpc[:], in1=PQ[1][:],
                                                                      op=ALU.mult),
                              deps=[a1, t_r])
                    ap_toks.append(a2)
                st_c = [P.dma("gpsimd", lambda E, k=k, tc=tc: E.dma_start(
                    out=xns_scr_d[k * 128:(k + 1) * 128, tc], in_=xnc[k][:]), deps=[ap_toks[k]]) for k in range(KT)]
                st_xn += st_c
                chunk_fence = P.fence()

            GATES = 3
            upd_last = list(ld_c0) + chunk_fence
            cb_toks = []
            xld_prev = []
            ev_toks = []
            mm_toks = []
            for t_i in range(NB):
                jbuf = 0
                xld = [P.dma("gpsimd", lambda E, k=k, t_i=t_i, jbuf=jbuf: E.dma_start(
                    out=xstep[jbuf][k][:], in_=xns_scr_d[k * 128:(k + 1) * 128, t_i * SS:(t_i + 1) * SS]),
                    deps=(st_xn if t_i < 2 else []) + xld_prev + mm_toks[-1:]) for k in range(KT)]
                cb_toks = [P.op("vector", lambda E, k=k: E.tensor_copy(out=cb[k][:], in_=ct_[k][:]),
                                deps=upd_last + cb_toks[-1:]) for k in range(KT)]
                ev_base = len(ev_toks)
                for ot in range(KT):
                    for g in range(GATES):
                        mm = None
                        for k in range(2 * KT):
                            if k < KT:
                                lhs = (lambda E, k=k, g=g, ot=ot, jbuf=jbuf: E.matmul(
                                    PG[g][:, 0:SS], wp[k][:, g * D + ot * 128: g * D + (ot + 1) * 128],
                                    xstep[jbuf][k][:], start=(k == 0), stop=False))
                            else:
                                kk = k - KT
                                lhs = (lambda E, kk=kk, g=g, ot=ot: E.matmul(
                                    PG[g][:, 0:SS], wp[kk][:, (3 + g) * D + ot * 128: (3 + g) * D + (ot + 1) * 128],
                                    cb[kk][:], start=False, stop=(kk == KT - 1)))
                            mm = P.op("tensor", lhs,
                                      deps=(list(xld) + [cb_toks[-1], ldw[-1]] + ev_toks[-3:]) if k == 0 else [])
                        mm_toks.append(mm)
                        func = AF.Sigmoid if g < 2 else AF.Tanh
                        tok = P.op("scalar", lambda E, g=g, ot=ot, func=func: E.activation(
                            out=(fts if g == 0 else its if g == 1 else gts)[ot][:], in_=PG[g][:, 0:SS],
                            func=func, bias=b3[:, g * KT + ot:g * KT + ot + 1]),
                            deps=[mm] + upd_last)
                        ev_toks.append(tok)
                upd = []
                for ot in range(KT):
                    u1 = P.op("vector", lambda E, ot=ot: E.tensor_tensor(out=tmpa[:], in0=fts[ot][:], in1=ct_[ot][:],
                                                                        op=ALU.mult),
                              deps=[ev_toks[ev_base + ot * 3], upd[-1] if upd else None])
                    u2 = P.op("vector", lambda E, ot=ot: E.tensor_tensor(out=tmpb[:], in0=its[ot][:], in1=gts[ot][:],
                                                                        op=ALU.mult),
                              deps=[ev_toks[ev_base + ot * 3 + 1], ev_toks[ev_base + ot * 3 + 2]])
                    u3 = P.op("vector", lambda E, ot=ot: E.tensor_tensor(out=ct_[ot][:], in0=tmpa[:], in1=tmpb[:],
                                                                        op=ALU.add),
                              deps=[u1, u2])
                    upd.append(u3)
                upd_last = [upd[-1]]
                xld_prev = xld

            cbf = [P.op("vector", lambda E, k=k: E.tensor_copy(out=cb[k][:], in_=ct_[k][:]), deps=upd_last + cb_toks[-1:])
                   for k in range(KT)]
            cbf = [P.op("vector", lambda E, k=k: E.tensor_copy(out=cb[k][:], in_=ct_[k][:]), deps=upd_last + cb_toks[-1:])
                   for k in range(KT)]
            z_toks = []
            pqi_free = [[], []]
            for ot in range(KT):
                mm = None
                for k in range(KT):
                    mm = P.op("tensor", lambda E, ot=ot, k=k: E.matmul(
                        PG[0][:, 0:SS], wp[k][:, 6 * D + ot * 128: 6 * D + (ot + 1) * 128], cb[k][:],
                        start=(k == 0), stop=(k == KT - 1)),
                        deps=([cbf[-1], ldw[-1]] + z_toks[-1:] + ev_toks[-3:]) if k == 0 else [])
                evz = P.op("scalar", lambda E, ot=ot: E.copy(out=zstage[:, 0:SS], in_=PG[0][:, 0:SS]), deps=[mm])
                stz = P.dma("gpsimd", lambda E, ot=ot: E.dma_start(
                    out=zsh_d[ot * 128:(ot + 1) * 128, :], in_=zstage[:, 0:SS]), deps=[evz])
                z_toks.append(stz)
            P.op("gpsimd", lambda E: E.engine_nop(), deps=z_toks)
    _phase2()
    P.set_floor(P.fence())
    cc = P.op("gpsimd", lambda E: E.collective_compute(
        "AllGather", ALU.bypass, replica_groups=[list(range(NCORES))],
        ins=[zsh_d[:]], outs=[zag_d[:]]))
    P.set_floor(P.fence())
    def _phase3():
        with ExitStack() as stack:
            def sb(name, shape, dt):
                return stack.enter_context(nc.sbuf_tensor("p3_" + name, shape, dt))

            def ps(name, shape, dt=F32):
                return stack.enter_context(nc.psum_tensor("p3_" + name, shape, dt))

            CH = 512
            wo = [sb(f"wo{k}", [128, D], BF16) for k in range(KT)]
            zs = [sb(f"zs{k}", [128, T], F32) for k in range(KT)]
            xn0c = [sb(f"xn0c{k}", [128, CH], BF16) for k in range(KT)]
            x0c = [sb(f"x0c{k}", [128, CH], F32) for k in range(KT)]
            a0c = [sb(f"a0c{k}", [128, CH], F32) for k in range(KT)]
            x1c = [sb(f"x1c{k}", [128, CH], F32) for k in range(KT)]
            xbc = [sb(f"xbc{k}", [128, CH], BF16) for k in range(KT)]
            t1bs = [sb(f"t1b{k}", [128, CH], BF16) for k in range(KT)]
            xnc = [sb(f"xnc{k}", [128, CH], F32) for k in range(KT)]
            tmp = sb("tmp", [128, CH], F32)
            ht = sb("ht", [128, CH], F32)
            bo_s = sb("bo_s", [128, KT], F32)
            ones_b = sb("ones_b", [128, 1], BF16)
            onesrow_b = sb("onesrow_b", [1, 128], BF16)
            m_row = sb("m_row", [1, CH], F32)
            v_row = sb("v_row", [1, CH], F32)
            vt_row = sb("vt_row", [1, CH], F32)
            mb_row = sb("mb_row", [1, CH], BF16)
            rb_row = sb("rb_row", [1, CH], BF16)
            PH = [ps(f"PH{i}", [128, 512]) for i in range(2)]
            PQ = [ps(f"PQ{i}", [128, 512]) for i in range(2)]

            warm = sb("warm", [1, 32], F32)
            w0 = P.op("vector", lambda E: E.memset(warm[:], 1.0))
            w1 = P.op("scalar", lambda E: E.activation(out=warm[:], in_=warm[:], func=AF.Square), deps=[w0])
            w2 = P.op("scalar", lambda E: E.activation(out=warm[:], in_=warm[:], func=AF.Ln), deps=[w1])
            w3 = P.op("scalar", lambda E: E.activation(out=warm[:], in_=warm[:], func=AF.Tanh), deps=[w2])
            w4 = P.op("scalar", lambda E: E.activation(out=warm[:], in_=warm[:], func=AF.Exp), deps=[w3])
            warm_tok = [w4]

            ldw = [P.dma("gpsimd", lambda E, k=k: E.dma_start(out=wo[k][:], in_=woT_d[k * 128:(k + 1) * 128, :]), ch="cast")
                   for k in range(KT)]
            ldz = [P.dma("sync", lambda E, k=k, r=r: E.dma_start(
                out=zs[k][:, r * SS:(r + 1) * SS], in_=zag_d[r * D + k * 128: r * D + (k + 1) * 128, :]))
                   for k in range(KT) for r in range(NCORES)]
            ld_ob = P.dma("gpsimd", lambda E: E.dma_start(out=ones_b[:], in_=ones_d[:]), ch="cast")
            ld_or = P.dma("gpsimd", lambda E: E.dma_start(
                out=onesrow_b[:], in_=ones_d[:, 0].unsqueeze(0).broadcast_to([1, 128])), ch="castp0")
            ld_bo = [P.dma("sync", lambda E, o=o: E.dma_start(
                out=bo_s[:, o:o + 1], in_=bo_d[o * 128:(o + 1) * 128].unsqueeze(1))) for o in range(KT)]

            chunk_fence = [ldw[-1], ldz[-1], ld_ob, ld_or] + ld_bo + warm_tok
            out_toks = []
            for c in range(8):
                bi = c // 4
                tc = slice(c * CH, (c + 1) * CH)
                zc = slice((c % 4) * CH, (c % 4 + 1) * CH)
                ld_xn0 = [P.dma("gpsimd", lambda E, k=k, tc=tc: E.dma_start(
                    out=xn0c[k][:], in_=xn0T_d[k * 128:(k + 1) * 128, tc]), deps=chunk_fence) for k in range(KT)]
                ld_x0 = [P.dma("sync", lambda E, k=k, tc=tc: E.dma_start(
                    out=x0c[k][:], in_=x0T_d[k * 128:(k + 1) * 128, tc]), deps=chunk_fence) for k in range(KT)]
                ld_a0 = [P.dma("sync", lambda E, k=k, tc=tc: E.dma_start(
                    out=a0c[k][:], in_=attn0T_d[k * 128:(k + 1) * 128, tc]), deps=chunk_fence) for k in range(KT)]
                x1_toks = []
                phi_free = [list(chunk_fence), list(chunk_fence)]
                for ot in range(KT):
                    phi = ot % 2
                    mm = None
                    for k in range(KT):
                        mm = P.op("tensor", lambda E, ot=ot, k=k, phi=phi: E.matmul(
                            PH[phi][:], wo[k][:, ot * 128:(ot + 1) * 128], xn0c[k][:],
                            start=(k == 0), stop=(k == KT - 1)),
                            deps=(list(ld_xn0) + [ldw[-1]] + warm_tok + phi_free[phi]) if k == 0 else [])
                    tz = P.op("vector", lambda E, ot=ot, zc=zc, phi=phi: E.tensor_tensor(
                        out=tmp[:], in0=PH[phi][:], in1=zs[ot][:, zc], op=ALU.add),
                        deps=[mm, ldz[-1], (x1_toks[-1][0] if x1_toks else None),
                              (phi_free[0][0] if phi_free[0] else None),
                              (phi_free[1][0] if phi_free[1] else None)] + chunk_fence)
                    th = P.op("scalar", lambda E, ot=ot: E.activation(
                        out=ht[:], in_=tmp[:], func=AF.Tanh, bias=bo_s[:, ot:ot + 1]),
                        deps=[tz, (x1_toks[-1][0] if x1_toks else None)] + ld_bo + chunk_fence)
                    phi_free[phi] = [th]
                    a1 = P.op("vector", lambda E, ot=ot: E.tensor_tensor(
                        out=x1c[ot][:], in0=x0c[ot][:], in1=a0c[ot][:], op=ALU.add),
                        deps=[ld_x0[ot], ld_a0[ot]] + chunk_fence)
                    a2 = P.op("vector", lambda E, ot=ot: E.tensor_tensor(
                        out=x1c[ot][:], in0=x1c[ot][:], in1=ht[:], op=ALU.add), deps=[a1, th])
                    st1 = P.dma("gpsimd", lambda E, ot=ot, tc=tc: E.dma_start(
                        out=x1T_d[ot * 128:(ot + 1) * 128, tc], in_=x1c[ot][:]), deps=[a2])
                    cb = P.op("vector", lambda E, ot=ot: E.tensor_copy(out=xbc[ot][:], in_=x1c[ot][:]),
                              deps=[a2] + chunk_fence)
                    x1_toks.append((a2, cb, st1))
                    out_toks.append(st1)
                cps = [t[1] for t in x1_toks]
                mm = None
                for k in range(KT):
                    mm = P.op("tensor", lambda E, k=k: E.matmul(
                        PQ[0][0:1, :], ones_b[:], xbc[k][:], start=(k == 0), stop=(k == KT - 1)),
                        deps=([cps[-1], ld_ob] + chunk_fence) if k == 0 else [])
                em0 = P.op("scalar", lambda E: E.copy(out=m_row[:], in_=PQ[0][0:1, :]), deps=[mm])
                em = P.op("vector", lambda E: E.tensor_scalar_mul(out=m_row[:], in0=m_row[:], scalar1=1.0 / D), deps=[em0])
                sq_toks = [P.op("scalar", lambda E, k=k: E.activation(out=t1bs[k][:], in_=xbc[k][:], func=AF.Square),
                               deps=[cps[k]] + chunk_fence) for k in range(KT)]
                mmq = None
                for k in range(KT):
                    mmq = P.op("tensor", lambda E, k=k: E.matmul(
                        PQ[1][0:1, :], ones_b[:], t1bs[k][:], start=(k == 0), stop=(k == KT - 1)),
                        deps=([sq_toks[-1], ld_ob] + chunk_fence) if k == 0 else [])
                ev0 = P.op("scalar", lambda E: E.copy(out=v_row[:], in_=PQ[1][0:1, :]), deps=[mmq])
                ev = P.op("vector", lambda E: E.tensor_scalar_mul(out=v_row[:], in0=v_row[:], scalar1=1.0 / D), deps=[ev0])
                tv1 = P.op("vector", lambda E: E.tensor_mul(out=vt_row[:], in0=m_row[:], in1=m_row[:]), deps=[em])
                tv2 = P.op("vector", lambda E: E.tensor_sub(out=v_row[:], in0=v_row[:], in1=vt_row[:]), deps=[tv1, ev])
                tv3 = P.op("vector", lambda E: E.tensor_scalar_add(out=v_row[:], in0=v_row[:], scalar1=LN_EPS), deps=[tv2])
                tr1 = P.op("scalar", lambda E: E.activation(out=v_row[:], in_=v_row[:], func=AF.Ln), deps=[tv3])
                tr1b = P.op("vector", lambda E: E.tensor_scalar_mul(out=v_row[:], in0=v_row[:], scalar1=-0.5), deps=[tr1])
                tr2 = P.op("scalar", lambda E: E.activation(out=v_row[:], in_=v_row[:], func=AF.Exp), deps=[tr1b])
                cmb = P.op("vector", lambda E: E.tensor_copy(out=mb_row[:], in_=m_row[:]), deps=[em])
                crb = P.op("vector", lambda E: E.tensor_copy(out=rb_row[:], in_=v_row[:]), deps=[tr2])
                t_m = P.op("tensor", lambda E: E.matmul(PQ[0][:], onesrow_b[:], mb_row[:], start=True, stop=True),
                           deps=[cmb, ld_or, em0])
                t_r = P.op("tensor", lambda E: E.matmul(PQ[1][:], onesrow_b[:], rb_row[:], start=True, stop=True),
                           deps=[crb, ld_or, ev0])
                ap_toks = []
                for k in range(KT):
                    a1 = P.op("vector", lambda E, k=k: E.tensor_tensor(out=tmp[:], in0=x1c[k][:], in1=PQ[0][:],
                                                                      op=ALU.subtract),
                              deps=[t_m, x1_toks[k][0]] + ([ap_toks[-1]] if ap_toks else [mmq]) + [phi_free[0][0], phi_free[1][0]])
                    a2 = P.op("vector", lambda E, k=k: E.tensor_tensor(out=xnc[k][:], in0=tmp[:], in1=PQ[1][:],
                                                                      op=ALU.mult),
                              deps=[a1, t_r])
                    ap_toks.append(a2)
                st_xn = [P.dma("gpsimd", lambda E, k=k, tc=tc: E.dma_start(
                    out=xn1T_d[k * 128:(k + 1) * 128, tc], in_=xnc[k][:]), deps=[ap_toks[k]]) for k in range(KT)]
                st_xnb = [P.dma("gpsimd", lambda E, k=k, tc=tc: E.dma_start(
                    out=xn1Tb_d[k * 128:(k + 1) * 128, tc], in_=xnc[k][:]), deps=[ap_toks[k]], ch="cast")
                    for k in range(KT)]
                out_toks += st_xn + st_xnb
                chunk_fence = P.fence()

            P.op("gpsimd", lambda E: E.engine_nop(), deps=out_toks)
    _phase3()
    P.set_floor(P.fence())
    def _phase4():
        with ExitStack() as stack:
            CH = 512
            NCH = T // CH
            NCH = T // CH
            def sb(name, shape, dt):
                return stack.enter_context(nc.sbuf_tensor("p4_" + name, shape, dt))

            def ps(name, shape, dt=F32):
                return stack.enter_context(nc.psum_tensor("p4_" + name, shape, dt))

            xbc = [sb(f"xbc{k}", [128, CH], BF16) for k in range(KT)]
            xprj = sb("xprj", [128, 96], BF16)
            xprjs = [sb(f"xprj{k}", [128, 96], BF16) for k in range(KT)]
            dtp = sb("dtp", [64, D], BF16)
            dtb_s = sb("dtb_s", [128, KT], F32)
            xp_sb = sb("xp_sb", [96, T], F32)
            drb = sb("drb", [64, T], BF16)
            A_s = [sb(f"A{k}", [128, NSt], F32) for k in range(KT)]
            D_s = sb("D_s", [128, KT], F32)
            dc2 = [sb(f"dc{i}", [128, CH], F32) for i in range(2)]
            xc2 = [sb(f"xc{i}", [128, CH], F32) for i in range(2)]
            uc2 = [sb(f"uc{i}", [128, CH], F32) for i in range(2)]
            dA = sb("dA", [128, NSt * CH], F32)
            dBu = sb("dBu", [128, NSt * CH], F32)
            hbuf = sb("hbuf", [128, NSt * CH], F32)
            zb2 = [sb(f"zb{i}", [128, NSt * CH], BF16) for i in range(2)]
            Brep = sb("Brep", [128, NSt * CH], BF16)
            Crep = sb("Crep", [128, NSt * CH], BF16)
            hcarry = [sb(f"hc{k}", [128, NSt], F32) for k in range(KT)]
            y1c2 = [sb(f"y1c{i}", [128, CH], F32) for i in range(2)]
            sp1 = sb("sp1", [128, CH], F32)
            sp2 = sb("sp2", [128, CH], F32)
            PXP = ps("PXP", [96, 512])
            PDT = ps("PDT", [128, 512])

            warm = sb("warm", [1, 32], F32)
            w0 = P.op("vector", lambda E: E.memset(warm[:], 1.0))
            w1 = P.op("scalar", lambda E: E.activation(out=warm[:], in_=warm[:], func=AF.Abs), deps=[w0])
            w2 = P.op("scalar", lambda E: E.activation(out=warm[:], in_=warm[:], func=AF.Ln), deps=[w1])
            w3 = P.op("scalar", lambda E: E.activation(out=warm[:], in_=warm[:], func=AF.Exp), deps=[w2])
            warm_tok = [w3]

            ld_xp = [P.dma("gpsimd", lambda E, k=k: E.dma_start(
                out=xprjs[k][:], in_=xprojT_d[k * 128:(k + 1) * 128, :]), ch="cast") for k in range(KT)]
            ld_dtp = P.dma("gpsimd", lambda E: E.dma_start(out=dtp[:], in_=dtprojT_d[:]), ch="castn0")
            ld_dtb = [P.dma("sync", lambda E, o=o: E.dma_start(
                out=dtb_s[:, o:o + 1], in_=dtb_d[o * 128:(o + 1) * 128].unsqueeze(1))) for o in range(KT)]
            ld_al = [P.dma("sync", lambda E, k=k: E.dma_start(
                out=A_s[k][:], in_=alog_d[k * 128:(k + 1) * 128, :])) for k in range(KT)]
            ld_dv = [P.dma("sync", lambda E, o=o: E.dma_start(
                out=D_s[:, o:o + 1], in_=dvec_d[o * 128:(o + 1) * 128].unsqueeze(1))) for o in range(KT)]
            load_fence = ld_xp + [ld_dtp] + ld_dtb + ld_al + ld_dv + warm_tok
            a_toks = []
            for k in range(KT):
                t1 = P.op("scalar", lambda E, k=k: E.activation(out=A_s[k][:], in_=A_s[k][:], func=AF.Exp),
                          deps=load_fence)
                t2 = P.op("vector", lambda E, k=k: E.tensor_scalar_mul(out=A_s[k][:], in0=A_s[k][:], scalar1=-1.0),
                          deps=[t1])
                a_toks.append(t2)

            out_toks = []
            batch_fence = list(load_fence) + a_toks
            for bi in range(B2):
                xp_toks = []
                pxp_free = list(batch_fence)
                for c in range(NCH):
                    tc = slice(bi * T + c * CH, bi * T + (c + 1) * CH)
                    ld_x = [P.dma("gpsimd", lambda E, k=k, tc=tc: E.dma_start(
                        out=xbc[k][:], in_=xn1T_d[k * 128:(k + 1) * 128, tc]),
                        deps=batch_fence + xp_toks[-1:], ch="cast") for k in range(KT)]
                    mm = None
                    for k in range(KT):
                        mm = P.op("tensor", lambda E, k=k: E.matmul(
                            PXP[:], xprjs[k][:], xbc[k][:], start=(k == 0), stop=(k == KT - 1)),
                            deps=(list(ld_x) + pxp_free) if k == 0 else [])
                    evt = P.op("scalar", lambda E, c=c: E.copy(out=xp_sb[:, c * CH:(c + 1) * CH], in_=PXP[:]),
                               deps=[mm] + batch_fence)
                    pxp_free = [evt]
                    xp_toks.append(evt)
                cdr = P.op("vector", lambda E: E.tensor_copy(out=drb[:], in_=xp_sb[0:64, :]), deps=xp_toks + batch_fence)
                del_toks = []
                pdt_free = list(batch_fence)
                for ot in range(KT):
                    for c in range(NCH):
                        mm = P.op("tensor", lambda E, ot=ot, c=c: E.matmul(
                            PDT[:], dtp[:, ot * 128:(ot + 1) * 128], drb[:, c * CH:(c + 1) * CH],
                            start=True, stop=True), deps=([cdr] + pdt_free))
                        s0 = P.op("vector", lambda E, ot=ot: E.tensor_scalar_add(
                            out=sp1[:], in0=PDT[:], scalar1=dtb_s[:, ot:ot + 1]), deps=[mm] + del_toks[-1:])
                        pdt_free = [s0]
                        sa = P.op("scalar", lambda E: E.activation(out=sp2[:], in_=sp1[:], func=AF.Abs), deps=[s0] + del_toks[-1:])
                        sn = P.op("vector", lambda E: E.tensor_scalar_mul(out=sp2[:], in0=sp2[:], scalar1=-1.0), deps=[sa])
                        se = P.op("scalar", lambda E: E.activation(out=sp2[:], in_=sp2[:], func=AF.Exp), deps=[sn])
                        s1 = P.op("vector", lambda E: E.tensor_scalar_add(out=sp2[:], in0=sp2[:], scalar1=1.0), deps=[se])
                        sl = P.op("scalar", lambda E: E.activation(out=sp2[:], in_=sp2[:], func=AF.Ln), deps=[s1])
                        sr = P.op("vector", lambda E: E.tensor_scalar_max(out=sp1[:], in0=sp1[:], scalar1=0.0), deps=[sa])
                        sd = P.op("vector", lambda E: E.tensor_add(out=sp1[:], in0=sp1[:], in1=sp2[:]), deps=[sl, sr])
                        std = P.dma("gpsimd", lambda E, ot=ot, c=c: E.dma_start(
                            out=delta_scr_d[ot * 128:(ot + 1) * 128, c * CH:(c + 1) * CH], in_=sp1[:]), deps=[sd])
                        del_toks.append(std)
                rep_toks = []
                scan_fence = list(del_toks)
                prev_dma = {0: [], 1: []}
                y_toks = []
                for c in range(NCH):
                    prel = P.fence()
                    b1 = P.dma("gpsimd", lambda E, c=c: E.dma_start(
                        out=bc_scr_d[0, :].rearrange("(n t) -> n t", t=CH), in_=xp_sb[64:80, c * CH:(c + 1) * CH]),
                        deps=xp_toks + prel, ch="n64")
                    b2 = P.dma("gpsimd", lambda E, c=c: E.dma_start(
                        out=bc_scr_d[1, :].rearrange("(n t) -> n t", t=CH), in_=xp_sb[80:96, c * CH:(c + 1) * CH]),
                        deps=xp_toks + prel, ch="n64")
                    b3 = P.dma("gpsimd", lambda E: E.dma_start(
                        out=Brep[:], in_=bc_scr_d[0:1, :].broadcast_to([128, NSt * CH])), deps=[b1, b2] + prel,
                        ch="bcast")
                    b4 = P.dma("gpsimd", lambda E: E.dma_start(
                        out=Crep[:], in_=bc_scr_d[1:2, :].broadcast_to([128, NSt * CH])), deps=[b1, b2] + prel,
                        ch="bcast")
                    rep_done = [b3, b4]
                    dt_toks = []
                    for dt in range(KT):
                        pb = (c * KT + dt) % 2
                        dc, xc, uc = dc2[pb], xc2[pb], uc2[pb]
                        tc = slice(bi * T + c * CH, bi * T + (c + 1) * CH)
                        l1 = P.dma("sync", lambda E, dt=dt, c=c, dc=dc: E.dma_start(
                            out=dc[:], in_=delta_scr_d[dt * 128:(dt + 1) * 128, c * CH:(c + 1) * CH]),
                            deps=scan_fence + list(dt_toks))
                        l2 = P.dma("sync", lambda E, dt=dt, tc=tc, xc=xc: E.dma_start(
                            out=xc[:], in_=xn1T_d[dt * 128:(dt + 1) * 128, tc]), deps=scan_fence + list(dt_toks))
                        tu = P.op("vector", lambda E, dc=dc, xc=xc, uc=uc: E.tensor_mul(out=uc[:], in0=dc[:], in1=xc[:]),
                                  deps=[l1, l2] + list(dt_toks))
                        ex_toks = []
                        for n in range(NSt):
                            ex_toks.append(P.op("scalar", lambda E, n=n, dt=dt, dc=dc: E.activation(
                                out=dA[:, n * CH:(n + 1) * CH], in_=dc[:], func=AF.Exp,
                                scale=A_s[dt][:, n:n + 1]),
                                deps=[l1] + list(dt_toks) + a_toks + (list(rep_toks) if dt == 0 else [])))
                        texp = ex_toks[-1]
                        tbu = P.op("vector", lambda E, uc=uc: E.tensor_tensor(
                            out=dBu[:].rearrange("p (n t) -> p n t", n=NSt),
                            in0=uc[:].unsqueeze(1).broadcast_to([128, NSt, CH]),
                            in1=Brep[:].rearrange("p (n t) -> p n t", n=NSt), op=ALU.mult),
                            deps=[tu] + rep_done + list(dt_toks))
                        sc_toks = []
                        fold_toks = []
                        if c > 0:
                            for n in range(NSt):
                                fold_toks.append(P.op("vector", lambda E, n=n, dt=dt: E.scalar_tensor_tensor(
                                    out=dBu[:, n * CH:n * CH + 1], in0=dA[:, n * CH:n * CH + 1],
                                    scalar=hcarry[dt][:, n:n + 1], in1=dBu[:, n * CH:n * CH + 1],
                                    op0=ALU.mult, op1=ALU.add), deps=[ex_toks[n], tbu]))
                        for n in range(NSt):
                            sc_toks.append(P.op("vector", lambda E, n=n: E.tensor_tensor_scan(
                                out=hbuf[:, n * CH:(n + 1) * CH], data0=dA[:, n * CH:(n + 1) * CH],
                                data1=dBu[:, n * CH:(n + 1) * CH], initial=0.0, op0=ALU.mult, op1=ALU.add),
                                deps=[ex_toks[n], tbu] + fold_toks[-1:] + (dt_toks[-1:] if c == 0 else [])))
                        hc = P.op("vector", lambda E, dt=dt: E.tensor_copy(
                            out=hcarry[dt][:].unsqueeze(2),
                            in_=hbuf[:].rearrange("p (n t) -> p n t", n=NSt)[:, :, CH - 1:CH]), deps=sc_toks)
                        ybi0 = (c * KT + dt) % 2
                        zb = zb2[ybi0]
                        tz = P.op("vector", lambda E, zb=zb: E.tensor_tensor(out=zb[:], in0=hbuf[:], in1=Crep[:], op=ALU.mult),
                                  deps=sc_toks + rep_done)
                        f1 = P.op("vector", lambda E, zb=zb: E.tensor_tensor(
                            out=zb[:, 0:8 * CH], in0=zb[:, 0:8 * CH], in1=zb[:, 8 * CH:16 * CH], op=ALU.add), deps=[tz])
                        f2 = P.op("vector", lambda E, zb=zb: E.tensor_tensor(
                            out=zb[:, 0:4 * CH], in0=zb[:, 0:4 * CH], in1=zb[:, 4 * CH:8 * CH], op=ALU.add), deps=[f1])
                        f3 = P.op("vector", lambda E, zb=zb: E.tensor_tensor(
                            out=zb[:, 0:2 * CH], in0=zb[:, 0:2 * CH], in1=zb[:, 2 * CH:4 * CH], op=ALU.add), deps=[f2])
                        f4 = P.op("vector", lambda E, zb=zb: E.tensor_tensor(
                            out=zb[:, 0:CH], in0=zb[:, 0:CH], in1=zb[:, CH:2 * CH], op=ALU.add), deps=[f3])
                        ybi = (c * KT + dt) % 2
                        ty = P.op("vector", lambda E, dt=dt, ybi=ybi, zb=zb, xc=xc: E.scalar_tensor_tensor(
                            out=y1c2[ybi][:], in0=xc[:], scalar=D_s[:, dt:dt + 1], in1=zb[:, 0:CH],
                            op0=ALU.mult, op1=ALU.add), deps=[f4, prev_dma[ybi][0] if prev_dma[ybi] else None])
                        sty = P.dma("gpsimd", lambda E, dt=dt, tc=tc, ybi=ybi: E.dma_start(
                            out=y1T_d[dt * 128:(dt + 1) * 128, tc], in_=y1c2[ybi][:]), deps=[ty])
                        prev_dma[ybi] = [sty]
                        out_toks.append(sty)
                        dt_toks = [ty, texp, tbu, hc]
                    rep_toks = dt_toks
                batch_fence = P.fence()

            P.op("gpsimd", lambda E: E.engine_nop(), deps=out_toks)
    _phase4()
    P.set_floor(P.fence())
    def _phase5():
        with ExitStack() as stack:
            def sb(name, shape, dt):
                return stack.enter_context(nc.sbuf_tensor("p5_" + name, shape, dt))

            def ps(name, shape, dt=F32):
                return stack.enter_context(nc.psum_tensor("p5_" + name, shape, dt))

            CH = 512
            NCH = TT_ // CH
            pool = sb("pool", [128, 8 * IM], BF16)
            qin_f = sb("qin_f", [128, IM], F32)
            qin_b = sb("qin_b", [128, IM], BF16)
            qin_bb = sb("qin_bb", [128, IM], BF16)
            u2 = sb("u2", [128, IM], F32)
            t1q = sb("t1q", [128, IM], F32)
            loq = sb("loq", [128, IM], F32)
            mkq = sb("mkq", [128, IM], U8)
            sgq = sb("sgq", [128, IM], F32)
            qout = sb("qout", [128, IM], BF16)
            bmq = sb("bmq", [128, IM // 16], F32)
            rs2q = sb("rs2q", [128, IM // 16], F32)
            shq = sb("shq", [128, IM // 16], F32)
            xqT = [sb(f"xqT{k}", [128, CH], BF16) for k in range(KT)]
            hqT = [sb(f"hqT{k}", [128, CH], BF16) for k in range(OT_I)]
            gsb = [sb(f"gsb{i}", [128, CH], BF16) for i in range(2)]
            hsb = [sb(f"hsb{i}", [128, CH], BF16) for i in range(2)]
            x1c2 = [sb(f"x1c{i}", [128, CH], F32) for i in range(2)]
            y1c2 = [sb(f"y1c{i}", [128, CH], F32) for i in range(2)]
            x2c2 = [sb(f"x2c{i}", [128, CH], F32) for i in range(2)]
            PG = ps("PG", [128, 512])
            PU = ps("PU", [128, 512])
            PD = ps("PD", [128, 512])

            warm = sb("warmq", [1, 32], F32)
            w0 = P.op("vector", lambda E: E.memset(warm[:], 1.0))
            w1 = P.op("scalar", lambda E: E.activation(out=warm[:], in_=warm[:], func=AF.Sign), deps=[w0])
            w2 = P.op("scalar", lambda E: E.activation(out=warm[:], in_=warm[:], func=AF.Silu), deps=[w1])
            warm_tok = [w2]

            def emit_quant(qin_ap, W, deps, act_deps=(), out_deps=()):
                NBq = W // 16
                t0 = P.op("vector", lambda E, qin_ap=qin_ap, W=W, NBq=NBq: E.tensor_reduce(
                    out=bmq[:, 0:NBq], in_=qin_ap.rearrange("p (nb blk) -> p nb blk", blk=16),
                    axis=AX.X, op=ALU.max, apply_absolute_value=True), deps=deps)
                t1_ = P.op("vector", lambda E, NBq=NBq: E.tensor_scalar_max(out=bmq[:, 0:NBq], in0=bmq[:, 0:NBq],
                                                                           scalar1=6e-12), deps=[t0])
                t2 = P.op("vector", lambda E, NBq=NBq: E.reciprocal(out=rs2q[:, 0:NBq], in_=bmq[:, 0:NBq]), deps=[t1_])
                t3 = P.op("vector", lambda E, NBq=NBq: E.tensor_scalar_mul(out=rs2q[:, 0:NBq], in0=rs2q[:, 0:NBq],
                                                                          scalar1=12.0), deps=[t2])
                t4 = P.op("vector", lambda E, NBq=NBq: E.tensor_scalar_mul(out=shq[:, 0:NBq], in0=bmq[:, 0:NBq],
                                                                          scalar1=1.0 / 12.0), deps=[t2])
                t5 = P.op("scalar", lambda E, qin_ap=qin_ap, W=W: E.activation(out=sgq[:, 0:W], in_=qin_ap, func=AF.Sign),
                          deps=deps + warm_tok + list(act_deps))
                t6 = P.op("vector", lambda E, qin_ap=qin_ap, W=W, NBq=NBq: E.tensor_tensor(
                    out=u2[:, 0:W].rearrange("p (nb blk) -> p nb blk", blk=16),
                    in0=qin_ap.rearrange("p (nb blk) -> p nb blk", blk=16),
                    in1=rs2q[:, 0:NBq].unsqueeze(2).broadcast_to([128, NBq, 16]), op=ALU.mult), deps=[t3] + deps)
                t7 = P.op("vector", lambda E, W=W: E.scalar_tensor_tensor(
                    out=u2[:, 0:W], in0=u2[:, 0:W], scalar=-1.0, in1=u2[:, 0:W], op0=ALU.mult, op1=ALU.max), deps=[t6])
                t8 = P.op("vector", lambda E, W=W: E.scalar_tensor_tensor(
                    out=t1q[:, 0:W], in0=u2[:, 0:W], scalar=C_VELT, in1=u2[:, 0:W],
                    op0=ALU.mult, op1=ALU.subtract), deps=[t7])
                t9 = P.op("vector", lambda E, W=W: E.scalar_tensor_tensor(
                    out=t1q[:, 0:W], in0=u2[:, 0:W], scalar=C_VELT, in1=t1q[:, 0:W],
                    op0=ALU.mult, op1=ALU.subtract), deps=[t8])
                ta = P.op("vector", lambda E, W=W: E.tensor_single_scalar(
                    out=loq[:, 0:W], in_=u2[:, 0:W], scalar=0.5, op=ALU.is_gt), deps=[t7])
                tb = P.op("vector", lambda E, W=W: E.scalar_tensor_tensor(
                    out=loq[:, 0:W], in0=u2[:, 0:W], scalar=1.5, in1=loq[:, 0:W], op0=ALU.is_gt, op1=ALU.add), deps=[ta])
                tcm = P.op("vector", lambda E, W=W: E.tensor_single_scalar(
                    out=mkq[:, 0:W], in_=u2[:, 0:W], scalar=2.0, op=ALU.is_gt), deps=[t7])
                td = P.op("vector", lambda E, W=W: E.select(
                    out=u2[:, 0:W], mask=mkq[:, 0:W], on_true=t1q[:, 0:W], on_false=loq[:, 0:W]), deps=[t9, tb, tcm])
                te = P.op("vector", lambda E, W=W: E.tensor_tensor(
                    out=u2[:, 0:W], in0=u2[:, 0:W], in1=sgq[:, 0:W], op=ALU.mult), deps=[td, t5])
                tf = P.op("vector", lambda E, W=W, NBq=NBq: E.tensor_tensor(
                    out=qout[:, 0:W].rearrange("p (nb blk) -> p nb blk", blk=16),
                    in0=u2[:, 0:W].rearrange("p (nb blk) -> p nb blk", blk=16),
                    in1=shq[:, 0:NBq].unsqueeze(2).broadcast_to([128, NBq, 16]), op=ALU.mult),
                    deps=[te, t4] + list(out_deps))
                return tf

            out_toks = []
            prevq = list(warm_tok)
            for rt in range(TT_ // 128):
                ld = P.dma("sync", lambda E, rt=rt: E.dma_start(
                    out=qin_b[:, 0:D], in_=xn1Tb_d[:, rt * 128:(rt + 1) * 128], transpose=True),
                    deps=prevq, ch="tp")
                tf = emit_quant(qin_b[:, 0:D], D, [ld])
                st = P.dma("gpsimd", lambda E, rt=rt: E.dma_start(
                    out=xq_scr_d[rt * 128:(rt + 1) * 128, :], in_=qout[:, 0:D]), deps=[tf])
                prevq = [tf, st]
            for rt in range(OT_I):
                ld = P.dma("sync", lambda E, rt=rt: E.dma_start(
                    out=qin_f[:, 0:D], in_=wg_d[rt * 128:(rt + 1) * 128, :]), deps=prevq)
                tf = emit_quant(qin_f[:, 0:D], D, [ld])
                st = P.dma("gpsimd", lambda E, rt=rt: E.dma_start(
                    out=wgq_scr_d[rt * 128:(rt + 1) * 128, :], in_=qout[:, 0:D]), deps=[tf])
                prevq = [tf, st]
            gx_done = P.fence()
            pend_w = []
            for (src_d, dst_d, nt, W) in ((wu_d, wuq_scr_d, OT_I, D),
                                          (wd_d, wdq_scr_d, KT, IM)):
                for rt in range(nt):
                    pend_w.append((src_d, dst_d, rt, W))

            def emit_wq_unit():
                nonlocal prevq
                src_d, dst_d, rt, W = pend_w.pop(0)
                ld = P.dma("gpsimd", lambda E, src_d=src_d, rt=rt, W=W: E.dma_start(
                    out=qin_f[:, 0:W], in_=src_d[rt * 128:(rt + 1) * 128, :]), deps=prevq, ch="wql")
                tf = emit_quant(qin_f[:, 0:W], W, [ld])
                st = P.dma("gpsimd", lambda E, dst_d=dst_d, rt=rt, W=W: E.dma_start(
                    out=dst_d[rt * 128:(rt + 1) * 128, :], in_=qout[:, 0:W]), deps=[tf])
                prevq = [tf, st]

            ldp = [P.dma("sync", lambda E, k=k: E.dma_start(
                out=pool[:, k * IM:(k + 1) * IM], in_=wgq_scr_d[:, k * 128:(k + 1) * 128], transpose=True),
                deps=gx_done, ch="tp") for k in range(KT)]
            pg_free = []
            gs_toks = []
            for ch in range(NCH):
                ldx = [P.dma("sync", lambda E, k=k, ch=ch: E.dma_start(
                    out=xqT[k][:], in_=xq_scr_d[ch * CH:(ch + 1) * CH, k * 128:(k + 1) * 128], transpose=True),
                    deps=gx_done + gs_toks[-2:], ch="tp") for k in range(KT)]
                for ot in range(OT_I):
                    gi = ot % 2
                    mm = None
                    for k in range(KT):
                        mm = P.op("tensor", lambda E, ot=ot, k=k: E.matmul(
                            PG[:], pool[:, k * IM + ot * 128: k * IM + (ot + 1) * 128], xqT[k][:],
                            start=(k == 0), stop=(k == KT - 1)),
                            deps=(list(ldx) + list(ldp) + pg_free) if k == 0 else [])
                    ev = P.op("scalar", lambda E, gi=gi: E.activation(out=gsb[gi][:], in_=PG[:], func=AF.Silu),
                              deps=[mm] + gs_toks[-2:] + warm_tok)
                    pg_free = [ev]
                    st = P.dma("scalar", lambda E, ot=ot, ch=ch, gi=gi: E.dma_start(
                        out=gs_scr_d[ot * 128:(ot + 1) * 128, ch * CH:(ch + 1) * CH], in_=gsb[gi][:]), deps=[ev],
                        ch="sst")
                    gs_toks.append(st)
                    if pend_w and len(gs_toks) % 6 == 3:
                        emit_wq_unit()
            while pend_w:
                emit_wq_unit()
            g_done = P.fence()

            ldp2 = [P.dma("sync", lambda E, k=k: E.dma_start(
                out=pool[:, k * IM:(k + 1) * IM], in_=wuq_scr_d[:, k * 128:(k + 1) * 128], transpose=True),
                deps=g_done, ch="tp") for k in range(KT)]
            pu_free = []
            h_toks = []
            hq_toks = []
            pend_hq = []
            hq_prev = [list(prevq), list(prevq)]

            def emit_hq_unit():
                gt, dep_sts = pend_hq.pop(0)
                qb = [qin_b, qin_bb][gt % 2]
                ld = P.dma("sync", lambda E, gt=gt, qb=qb: E.dma_start(
                    out=qb[:], in_=h_scr_d[:, gt * 128:(gt + 1) * 128], transpose=True),
                    deps=dep_sts + hq_prev[0], ch="tp")
                tf = emit_quant(qb[:], IM, [ld], act_deps=hq_prev[1], out_deps=hq_prev[1])
                st2 = P.dma("gpsimd", lambda E, gt=gt: E.dma_start(
                    out=hq_scr_d[gt * 128:(gt + 1) * 128, :], in_=qout[:, 0:IM]), deps=[tf])
                hq_prev[0] = hq_prev[1]
                hq_prev[1] = [tf, st2]
                hq_toks.append(st2)

            for ch in range(NCH):
                ldx = [P.dma("sync", lambda E, k=k, ch=ch: E.dma_start(
                    out=xqT[k][:], in_=xq_scr_d[ch * CH:(ch + 1) * CH, k * 128:(k + 1) * 128], transpose=True),
                    deps=g_done + h_toks[-2:], ch="tp") for k in range(KT)]
                ch_sts = []
                for ot in range(OT_I):
                    gi = ot % 2
                    mm = None
                    for k in range(KT):
                        mm = P.op("tensor", lambda E, ot=ot, k=k: E.matmul(
                            PU[:], pool[:, k * IM + ot * 128: k * IM + (ot + 1) * 128], xqT[k][:],
                            start=(k == 0), stop=(k == KT - 1)),
                            deps=(list(ldx) + list(ldp2) + pu_free) if k == 0 else [])
                    lg = P.dma("sync", lambda E, ot=ot, ch=ch, gi=gi: E.dma_start(
                        out=gsb[gi][:], in_=gs_scr_d[ot * 128:(ot + 1) * 128, ch * CH:(ch + 1) * CH]),
                        deps=g_done + h_toks[-2:])
                    th = P.op("vector", lambda E, gi=gi: E.tensor_tensor(
                        out=hsb[gi][:], in0=PU[:], in1=gsb[gi][:], op=ALU.mult), deps=[mm, lg] + h_toks[-2:])
                    pu_free = [th]
                    st = P.dma("gpsimd", lambda E, ot=ot, ch=ch, gi=gi: E.dma_start(
                        out=h_scr_d[ot * 128:(ot + 1) * 128, ch * CH:(ch + 1) * CH], in_=hsb[gi][:]), deps=[th])
                    h_toks.append(st)
                    ch_sts.append(st)
                    if pend_hq and ot % 5 == 2:
                        emit_hq_unit()
                for rt in range(CH // 128):
                    pend_hq.append((ch * (CH // 128) + rt, list(ch_sts)))
            while pend_hq:
                emit_hq_unit()
            hq_done = P.fence()

            ldp3 = [P.dma("sync", lambda E, kk=kk: E.dma_start(
                out=pool[:, kk * D:(kk + 1) * D], in_=wdq_scr_d[:, kk * 128:(kk + 1) * 128], transpose=True),
                deps=hq_done, ch="tp") for kk in range(OT_I)]
            pd_free = []
            x2_toks = []
            for ch in range(NCH):
                ldh = [P.dma("sync", lambda E, kk=kk, ch=ch: E.dma_start(
                    out=hqT[kk][:], in_=hq_scr_d[ch * CH:(ch + 1) * CH, kk * 128:(kk + 1) * 128], transpose=True),
                    deps=hq_done + x2_toks[-2:], ch="tp") for kk in range(OT_I)]
                for ot in range(KT):
                    gi = ot % 2
                    l1 = P.dma("sync", lambda E, ot=ot, ch=ch, gi=gi: E.dma_start(
                        out=x1c2[gi][:], in_=x1T_d[ot * 128:(ot + 1) * 128, ch * CH:(ch + 1) * CH]),
                        deps=hq_done + x2_toks[-2:])
                    l2 = P.dma("sync", lambda E, ot=ot, ch=ch, gi=gi: E.dma_start(
                        out=y1c2[gi][:], in_=y1T_d[ot * 128:(ot + 1) * 128, ch * CH:(ch + 1) * CH]),
                        deps=hq_done + x2_toks[-2:])
                    mm = None
                    for kk in range(OT_I):
                        mm = P.op("tensor", lambda E, ot=ot, kk=kk: E.matmul(
                            PD[:], pool[:, kk * D + ot * 128: kk * D + (ot + 1) * 128], hqT[kk][:],
                            start=(kk == 0), stop=(kk == OT_I - 1)),
                            deps=(list(ldh) + list(ldp3) + pd_free) if kk == 0 else [])
                    a1 = P.op("vector", lambda E, gi=gi: E.tensor_tensor(
                        out=x2c2[gi][:], in0=PD[:], in1=x1c2[gi][:], op=ALU.add), deps=[mm, l1] + x2_toks[-2:])
                    pd_free = [a1]
                    a2 = P.op("vector", lambda E, gi=gi: E.tensor_tensor(
                        out=x2c2[gi][:], in0=x2c2[gi][:], in1=y1c2[gi][:], op=ALU.add), deps=[a1, l2])
                    st = P.dma("gpsimd", lambda E, ot=ot, ch=ch, gi=gi: E.dma_start(
                        out=x2T_d[ot * 128:(ot + 1) * 128, ch * CH:(ch + 1) * CH], in_=x2c2[gi][:]), deps=[a2])
                    x2_toks.append(st)
                    out_toks.append(st)
            P.op("gpsimd", lambda E: E.engine_nop(), deps=out_toks)
    _phase5()
    with ExitStack() as stack:
        block = stack.enter_context(nc.Block())
        P.build(stack, block)
    return nc


def kernel(**inputs):
    x = np.asarray(inputs["L0_attn_x"])
    wqkvT = np.ascontiguousarray(np.asarray(inputs["L0_attn_w_qkv"]).T)
    woutT = np.ascontiguousarray(np.asarray(inputs["L0_attn_w_out"]).T)
    bout = np.ascontiguousarray(np.asarray(inputs["L0_attn_b_out"]))
    g16 = np.ascontiguousarray(np.asarray(inputs["L0_attn_norm_g"]).reshape(1, H).astype(np.float32))
    b16 = np.ascontiguousarray(np.asarray(inputs["L0_attn_norm_b"]).reshape(1, H).astype(np.float32))
    eye = np.eye(128, dtype=np.float32)
    ones = np.ones((128, 1), dtype=np.float32)
    wpackT = np.ascontiguousarray(np.concatenate(
        [np.asarray(inputs[f"L0_ffn_{n}"]).T for n in ("wf", "wi", "wc", "uf", "ui", "uc", "uo")], axis=1))
    bias3 = np.ascontiguousarray(np.stack([np.asarray(inputs["L0_ffn_bf"]),
                                           np.asarray(inputs["L0_ffn_bi"]),
                                           np.asarray(inputs["L0_ffn_bc"])]))
    incv = np.ascontiguousarray(np.asarray(inputs["L0_ffn_in_c"]))
    woT = np.ascontiguousarray(np.asarray(inputs["L0_ffn_wo"]).T)
    bo = np.ascontiguousarray(np.asarray(inputs["L0_ffn_bo"]))
    xprojT = np.ascontiguousarray(np.asarray(inputs["L1_attn_x_proj_w"]).T)
    dtprojT = np.ascontiguousarray(np.asarray(inputs["L1_attn_dt_proj_w"]).T)
    dtb = np.ascontiguousarray(np.asarray(inputs["L1_attn_dt_proj_b"]))
    alog = np.ascontiguousarray(np.asarray(inputs["L1_attn_A_log"]))
    dvec = np.ascontiguousarray(np.asarray(inputs["L1_attn_D"]))
    wg = np.ascontiguousarray(np.asarray(inputs["L1_ffn_w_gate"]))
    wu = np.ascontiguousarray(np.asarray(inputs["L1_ffn_w_up"]))
    wd = np.ascontiguousarray(np.asarray(inputs["L1_ffn_w_down"]))
    nc = build_fused()
    in_maps = []
    for c in range(NCORES):
        x0T = np.ascontiguousarray(np.concatenate([x[2 * c].T, x[2 * c + 1].T], axis=1))
        sl = slice(c * SS, (c + 1) * SS)
        x0Ts = np.ascontiguousarray(np.concatenate([x[b, sl, :].T for b in range(NB)], axis=1))
        in_maps.append({"x0T": x0T, "x0Ts": x0Ts, "wqkvT": wqkvT, "woutT": woutT, "bout": bout,
                        "g16": g16, "b16": b16, "eye": eye, "ones": ones, "wpackT": wpackT,
                        "bias3": bias3, "incv": incv, "woT": woT, "bo": bo, "xprojT": xprojT,
                        "dtprojT": dtprojT, "dtb": dtb, "alog": alog, "dvec": dvec,
                        "wg": wg, "wu": wu, "wd": wd})
    res = run_bass_kernel_spmd(nc, in_maps, core_ids=list(range(NCORES)))
    out = np.empty((NB, T, D), dtype=np.float32)
    for c in range(NCORES):
        x2T = np.asarray(res.results[c]["x2T"])
        out[2 * c] = x2T[:, 0:T].T
        out[2 * c + 1] = x2T[:, T:2 * T].T
    return out
